# version 01
# baseline (speedup 1.0000x reference)
from contextlib import ExitStack
import numpy as np
import ml_dtypes
import concourse.bass as bass
import concourse.mybir as mybir
from concourse.bass_utils import run_bass_kernel_spmd

F32 = mybir.dt.float32
BF16 = mybir.dt.bfloat16
I32 = mybir.dt.int32
AF = mybir.ActivationFunctionType
ALU = mybir.AluOpType
AX = mybir.AxisListType
NEG = -30000.0


class Prog:

    def __init__(self, nc, name="prog"):
        self.nc = nc
        self.items = []
        self.sem_vals = {}
        self.sems = {}
        self._semnames = set()
        self._pending_barrier = None
        self._barrier_applied = set()

    def phase_barrier(self):
        self._pending_barrier = self.all_done()
        self._barrier_applied = set()

    def _bump(self, semname, n):
        v = self.sem_vals.get(semname, 0) + n
        self.sem_vals[semname] = v
        self._semnames.add(semname)
        return (semname, v)

    def op(self, eng, fn, wait=(), dma=False, stream=None):
        if dma:
            semname = f"d_{stream}" if stream else f"d_{eng}"
        else:
            semname = f"c_{eng}"
        tok = self._bump(semname, 16 if dma else 1)
        wait = list(wait)
        if self._pending_barrier is not None and eng not in self._barrier_applied:
            wait = wait + [self._pending_barrier]
            self._barrier_applied.add(eng)
        seen = {}
        for w in wait:
            if w is None:
                continue
            if isinstance(w, list):
                for (s, v) in w:
                    if s not in seen or seen[s] < v:
                        seen[s] = v
            else:
                s, v = w
                if s not in seen or seen[s] < v:
                    seen[s] = v
        self.items.append((eng, fn, sorted(seen.items()), (semname, 16 if dma else 1)))
        return tok

    def all_done(self):
        return [(s, v) for s, v in self.sem_vals.items()]

    def emit(self, stack):
        nc = self.nc
        for s in sorted(self._semnames):
            self.sems[s] = stack.enter_context(nc.semaphore(s))
        block = stack.enter_context(nc.Block())
        prog = self

        def mk(engkey):
            def section(eng):
                last = None
                for (e, fn, waits, _inc) in prog.items:
                    if e != engkey:
                        continue
                    for (s, v) in waits:
                        eng.wait_ge(prog.sems[s], v)
                    inst = fn(eng)
                    if inst is not None:
                        inst.then_inc(prog.sems[_inc[0]], _inc[1])
                        last = inst
            return section

        block.tensor(mk("t"))
        block.vector(mk("v"))
        block.scalar(mk("s"))
        block.gpsimd(mk("g"))
        block.sync(mk("y"))


def ceil_div(a, b):
    return (a + b - 1) // b


def bfloat(x):
    return np.asarray(x).astype(ml_dtypes.bfloat16)


def f32(x):
    return np.asarray(x, dtype=np.float32)

def _ns_attna():
    D = 1024
    LAT = 512
    NSEQ = 8192
    TCH = 512
    NCH = NSEQ // TCH
    BIAS = -40.0


    def build_attna():
        nc = bass.Bass("TRN2", target_bir_lowering=False)
        x_d = nc.dram_tensor("x0", [D, NSEQ], F32, kind="ExternalInput")
        wcd_d = nc.dram_tensor("Wc_down", [D, LAT], F32, kind="ExternalInput")
        wq_d = nc.dram_tensor("Wq_sl", [LAT, 256], F32, kind="ExternalInput")
        wkv_d = nc.dram_tensor("Wkv", [LAT, 128], F32, kind="ExternalInput")
        ao_d = nc.dram_tensor("ao", [256, NSEQ], F32, kind="ExternalOutput")

        u = np.arange(128)
        mdiag_c = []
        for delta in range(4):
            tt = np.arange(512)
            m = ((128 * delta + u[:, None]) <= tt[None, :]).astype(ml_dtypes.bfloat16)
            mdiag_c.append(nc.inline_tensor(m, name=f"mdiag{delta}"))

        with ExitStack() as est:
            e = est.enter_context
            wcd = e(nc.sbuf_tensor("wcd", [128, 8 * LAT], BF16))
            wq = e(nc.sbuf_tensor("wq", [128, 4 * 256], BF16))
            wkv = e(nc.sbuf_tensor("wkv", [128, 4 * 128], BF16))
            lat = e(nc.sbuf_tensor("lat", [128, 4 * TCH], BF16))
            qh = [e(nc.sbuf_tensor(f"qh{h}", [64, NSEQ], BF16)) for h in range(4)]
            kb = e(nc.sbuf_tensor("kb", [64, NSEQ], BF16))
            vb = e(nc.sbuf_tensor("vb", [128, 64 * 65], BF16))
            xch = [e(nc.sbuf_tensor(f"xch{k}", [128, 8 * TCH], F32)) for k in range(2)]
            xcc = [e(nc.sbuf_tensor(f"xcc{k}", [128, TCH], BF16)) for k in range(2)]
            xsq = [e(nc.sbuf_tensor(f"xsq{k}", [128, TCH], BF16)) for k in range(2)]
            h1 = e(nc.sbuf_tensor("h1", [128, 8 * TCH], BF16))
            mdg = [e(nc.sbuf_tensor(f"mdg{d}", [128, 512], BF16)) for d in range(4)]
            Ab = [e(nc.sbuf_tensor(f"Ab{k}", [128, 512], BF16)) for k in range(2)]
            oev = e(nc.sbuf_tensor("oev", [64, 512], BF16))
            rrow = e(nc.sbuf_tensor("rrow", [1, 512], F32))
            onesc = e(nc.sbuf_tensor("onesc", [128, 2], BF16))
            onesf = e(nc.sbuf_tensor("onesf", [1, 128], F32))
            onc = e(nc.sbuf_tensor("onc", [1, 64], F32))
            bcol = e(nc.sbuf_tensor("bcol", [128, 1], F32))
            murow = e(nc.sbuf_tensor("murow", [1, TCH], F32))
            musq = e(nc.sbuf_tensor("musq", [1, TCH], F32))
            varrow = e(nc.sbuf_tensor("varrow", [1, TCH], F32))
            rstdrow = e(nc.sbuf_tensor("rstdrow", [1, TCH], F32))
            mrrow = e(nc.sbuf_tensor("mrrow", [1, TCH], F32))
            Rb = e(nc.sbuf_tensor("Rb", [128, TCH], BF16))
            Mb = e(nc.sbuf_tensor("Mb", [128, TCH], BF16))
            outc2 = [[e(nc.sbuf_tensor(f"outc{j}_{k}", [128, 512], F32)) for k in range(2)]
                     for j in range(2)]
            ps_r0 = e(nc.psum_tensor("ps_r0", [1, TCH], F32))
            ps_r1 = e(nc.psum_tensor("ps_r1", [1, TCH], F32))
            ps_bc = e(nc.psum_tensor("ps_bc", [128, TCH], F32))
            ps_a = e(nc.psum_tensor("ps_a", [128, 512], F32))
            ps_s = e(nc.psum_tensor("ps_s", [128, 512], F32))
            ps_av = e(nc.psum_tensor("ps_av", [65, 512], F32))

            p = Prog(nc)
            G = {}

            def gw(name):
                return list(G.get(name, []))

            w_sm = p.op("g", lambda en: en.dma_start(
                wcd[...].rearrange("p (a w) -> p a w", a=8),
                wcd_d[...].rearrange("(a p) w -> p a w", p=128)), dma=True, stream="sm")
            w_sm = p.op("g", lambda en: en.dma_start(
                wq[...].rearrange("p (a w) -> p a w", a=4),
                wq_d[...].rearrange("(a p) w -> p a w", p=128)), dma=True, stream="sm")
            w_sm = p.op("g", lambda en: en.dma_start(
                wkv[...].rearrange("p (a w) -> p a w", a=4),
                wkv_d[...].rearrange("(a p) w -> p a w", p=128)), dma=True, stream="sm")
            for d_ in range(4):
                w_sm = p.op("g", (lambda d_: lambda en: en.dma_start(
                    mdg[d_][...], mdiag_c[d_][...]))(d_), dma=True, stream="sm")
            c1 = p.op("v", lambda en: en.memset(onesc[...], 1.0))
            c2 = p.op("v", lambda en: en.memset(onesf[...], 1.0))
            c3 = p.op("v", lambda en: en.memset(onc[...], 1.0))
            c4 = p.op("v", lambda en: en.memset(bcol[...], BIAS))
            c5 = p.op("v", lambda en: en.memset(
                vb[...].rearrange("p (a w) -> p a w", a=64)[:, :, 64:65], 1.0))
            CON = [w_sm, c1, c2, c3, c4, c5]

            for ch in range(NCH):
                b = ch % 2
                t_x = p.op("y", (lambda ch, b: lambda en: en.dma_start(
                    xch[b][...].rearrange("p (a w) -> p a w", a=8),
                    x_d[:, TCH * ch:TCH * (ch + 1)].rearrange("(a p) w -> p a w", p=128)))(ch, b),
                    wait=gw(f"xch{b}"), dma=True, stream=f"x{b}")
                tprev = None
                for j in range(8):
                    sb = j % 2
                    tc = p.op("v", (lambda j, b, sb: lambda en: en.tensor_copy(
                        xcc[sb][...], xch[b][:, j * TCH:(j + 1) * TCH]))(j, b, sb),
                        wait=[t_x] + gw(f"xcc{sb}"))
                    tq = p.op("s", (lambda sb: lambda en: en.activation(
                        xsq[sb][...], xcc[sb][...], AF.Square))(sb),
                        wait=[tc] + gw(f"xsq{sb}"))
                    wl = [tc, c1] + ([tprev] if tprev else [])
                    if j == 0:
                        wl += gw("ps_r")
                    t1 = p.op("t", (lambda j, sb: lambda en: en.matmul(
                        ps_r0[0:1, :], onesc[:, 0:1], xcc[sb][...],
                        start=(j == 0), stop=(j == 7)))(j, sb), wait=wl)
                    t2 = p.op("t", (lambda j, sb: lambda en: en.matmul(
                        ps_r1[0:1, :], onesc[:, 1:2], xsq[sb][...],
                        start=(j == 0), stop=(j == 7)))(j, sb), wait=[tq, t1])
                    G[f"xcc{sb}"] = [t1, tq]
                    G[f"xsq{sb}"] = [t2]
                    tprev = t2
                t_mu = p.op("v", lambda en: en.tensor_scalar_mul(
                    murow[...], ps_r0[0:1, :], 1.0 / D), wait=[tprev])
                t_m2 = p.op("v", lambda en: en.tensor_mul(
                    musq[...], murow[...], murow[...]), wait=[t_mu])
                t_var = p.op("v", lambda en: en.scalar_tensor_tensor(
                    varrow[...], ps_r1[0:1, :], 1.0 / D, musq[...],
                    ALU.mult, ALU.subtract), wait=[t_m2, tprev])
                G["ps_r"] = [t_var]
                t_ve = p.op("v", lambda en: en.tensor_scalar_add(
                    varrow[...], varrow[...], 1e-5), wait=[t_var])
                t_ln = p.op("s", lambda en: en.activation(
                    rstdrow[...], varrow[...], AF.Ln), wait=[t_ve])
                t_rs = p.op("s", lambda en: en.activation(
                    rstdrow[...], rstdrow[...], AF.Exp, scale=-0.5), wait=[t_ln])
                t_mr = p.op("v", lambda en: en.tensor_mul(
                    mrrow[...], murow[...], rstdrow[...]), wait=[t_rs, t_mu])
                tRb, tMb = [], []
                for q_ in range(TCH // 512):
                    pass
                tb = p.op("t", lambda en: en.matmul(
                    ps_bc[:, 0:512], onesf[0:1, :], rstdrow[0:1, 0:512],
                    start=True, stop=True), wait=[t_rs, c2] + gw("ps_bc"))
                tR = p.op("s", lambda en: en.activation(
                    Rb[:, 0:512], ps_bc[:, 0:512], AF.Identity), wait=[tb] + gw("Rb"))
                tb2 = p.op("t", lambda en: en.matmul(
                    ps_bc[:, 0:512], onesf[0:1, :], mrrow[0:1, 0:512],
                    start=True, stop=True), wait=[t_mr, tR])
                tM = p.op("s", lambda en: en.activation(
                    Mb[:, 0:512], ps_bc[:, 0:512], AF.Identity, scale=-1.0),
                    wait=[tb2] + gw("Mb"))
                G["ps_bc"] = [tM]
                t_h = []
                for j in range(8):
                    tm = p.op("v", (lambda j, b: lambda en: en.tensor_mul(
                        h1[:, j * TCH:(j + 1) * TCH], xch[b][:, j * TCH:(j + 1) * TCH],
                        Rb[:, 0:TCH]))(j, b), wait=[tR, t_x] + gw("h1"))
                    ta = p.op("v", (lambda j: lambda en: en.tensor_add(
                        h1[:, j * TCH:(j + 1) * TCH], h1[:, j * TCH:(j + 1) * TCH],
                        Mb[:, 0:TCH]))(j), wait=[tm, tM])
                    t_h.append(ta)
                G[f"xch{b}"] = t_h
                G["Rb"] = t_h
                G["Mb"] = t_h
                t_lat = []
                for m in range(4):
                    last = None
                    for kk in range(8):
                        wl = [t_h[kk], CON[0]] + ([last] if last else [])
                        if kk == 0:
                            wl += gw("ps_a")
                        last = p.op("t", (lambda m, kk, ch: lambda en: en.matmul(
                            ps_a[:, 0:512], wcd[:, kk * LAT + 128 * m:kk * LAT + 128 * (m + 1)],
                            h1[:, kk * TCH:(kk + 1) * TCH],
                            start=(kk == 0), stop=(kk == 7)))(m, kk, ch), wait=wl)
                    tev = p.op("s", (lambda m: lambda en: en.activation(
                        lat[:, m * TCH:(m + 1) * TCH],
                        ps_a[:, 0:512], AF.Identity))(m), wait=[last] + (gw("lat") if m == 0 else []))
                    G["ps_a"] = [tev]
                    t_lat.append(tev)
                G["h1"] = [t_lat[-1]]
                qreaders = []
                for m in range(2):
                    last = None
                    for kk in range(4):
                        wl = [t_lat[kk], CON[0]] + ([last] if last else [])
                        if kk == 0:
                            wl += gw("ps_a")
                        last = p.op("t", (lambda m, kk: lambda en: en.matmul(
                            ps_a[:, 0:512], wq[:, kk * 256 + 128 * m:kk * 256 + 128 * (m + 1)],
                            lat[:, kk * TCH:(kk + 1) * TCH],
                            start=(kk == 0), stop=(kk == 3)))(m, kk), wait=wl)
                    tev0 = p.op("s", (lambda m, ch: lambda en: en.activation(
                        qh[2 * m][0:64, TCH * ch:TCH * (ch + 1)],
                        ps_a[0:64, 0:512], AF.Identity))(m, ch), wait=[last])
                    tev1 = p.op("s", (lambda m, ch: lambda en: en.activation(
                        qh[2 * m + 1][0:64, TCH * ch:TCH * (ch + 1)],
                        ps_a[64:128, 0:512], AF.Identity))(m, ch), wait=[last, tev0])
                    G["ps_a"] = [tev0, tev1]
                    qreaders += [tev0, tev1]
                last = None
                for kk in range(4):
                    wl = [t_lat[kk], CON[0]] + ([last] if last else [])
                    if kk == 0:
                        wl += gw("ps_a")
                    last = p.op("t", (lambda kk: lambda en: en.matmul(
                        ps_a[0:64, 0:512], wkv[:, kk * 128:kk * 128 + 64],
                        lat[:, kk * TCH:(kk + 1) * TCH],
                        start=(kk == 0), stop=(kk == 3)))(kk), wait=wl)
                tev = p.op("s", (lambda ch: lambda en: en.activation(
                    kb[0:64, TCH * ch:TCH * (ch + 1)], ps_a[0:64, 0:512],
                    AF.Identity))(ch), wait=[last])
                G["ps_a"] = [tev]
                for tb_ in range(4):
                    beta = 4 * ch + tb_
                    last = None
                    for kk in range(4):
                        wl = [t_lat[kk], CON[0]] + ([last] if last else [])
                        if kk == 0:
                            wl += gw("ps_s")
                        last = p.op("t", (lambda tb_, kk: lambda en: en.matmul(
                            ps_s[:, 0:64],
                            lat[:, kk * TCH + 128 * tb_:kk * TCH + 128 * (tb_ + 1)],
                            wkv[:, kk * 128 + 64:kk * 128 + 128],
                            start=(kk == 0), stop=(kk == 3)))(tb_, kk), wait=wl)
                    tev = p.op("s", (lambda beta: lambda en: en.activation(
                        vb[:, beta * 65:beta * 65 + 64], ps_s[:, 0:64],
                        AF.Identity))(beta), wait=[last])
                    G["ps_s"] = [tev]
                G["lat"] = [last] + qreaders

            p.phase_barrier()
            for tau in range(NCH):
                nsig = 4 * tau + 4
                tnos = {0: [], 1: []}
                for h in range(4):
                    hb = 64 * (h % 2)
                    jq = h // 2
                    last_av = None
                    for sg in range(nsig):
                        ab = sg % 2
                        t_s = p.op("t", (lambda h, tau, sg: lambda en: en.matmul(
                            ps_s[:, 0:512],
                            kb[0:64, 128 * sg:128 * (sg + 1)],
                            qh[h][0:64, TCH * tau:TCH * (tau + 1)],
                            start=True, stop=True))(h, tau, sg),
                            wait=gw("ps_s"))
                        te_ = p.op("s", (lambda ab: lambda en: en.activation(
                            Ab[ab][:, 0:512], ps_s[:, 0:512], AF.Exp,
                            scale=0.125, bias=bcol[...]))(ab),
                            wait=[t_s, CON[4]] + gw(f"Ab{ab}"))
                        G["ps_s"] = [te_]
                        tmask = None
                        delta = sg - 4 * tau
                        if 0 <= delta <= 3:
                            tmask = p.op("v", (lambda ab, delta: lambda en: en.tensor_mul(
                                Ab[ab][:, 0:512], Ab[ab][:, 0:512], mdg[delta][...]))(ab, delta),
                                wait=[te_, CON[0]])
                        rd = tmask or te_
                        wl = [rd, CON[5]] + ([last_av] if last_av else [])
                        if sg == 0:
                            wl += gw("ps_av")
                        last_av = p.op("t", (lambda ab, sg, nsig: lambda en: en.matmul(
                            ps_av[:, 0:512], vb[:, sg * 65:(sg + 1) * 65],
                            Ab[ab][:, 0:512],
                            start=(sg == 0), stop=(sg == nsig - 1)))(ab, sg, nsig), wait=wl)
                        G[f"Ab{ab}"] = [last_av]
                    trc = p.op("v", lambda en: en.reciprocal(
                        rrow[0:1, 0:512], ps_av[64:65, 0:512]), wait=[last_av] + gw("rrow"))
                    tbc = p.op("t", lambda en: en.matmul(
                        ps_bc[0:64, 0:512], onc[...], rrow[0:1, 0:512],
                        start=True, stop=True), wait=[trc, c3] + gw("ps_bc2"))
                    G["rrow"] = [tbc]
                    tev = p.op("s", lambda en: en.activation(
                        oev[...], ps_av[0:64, 0:512], AF.Identity),
                        wait=[last_av, trc] + gw("oev"))
                    G["ps_av"] = [tev]
                    cb = tau % 2
                    tno = p.op("v", (lambda hb, jq, cb: lambda en: en.tensor_mul(
                        outc2[jq][cb][hb:hb + 64, :], oev[...],
                        ps_bc[0:64, 0:512]))(hb, jq, cb),
                        wait=[tev, tbc] + gw(f"outc{jq}{cb}"))
                    G["ps_bc2"] = [tno]
                    G["oev"] = [tno]
                    tnos[jq].append(tno)
                cb = tau % 2
                for jq in range(2):
                    tst = p.op("y", (lambda jq, tau, cb: lambda en: en.dma_start(
                        ao_d[128 * jq:128 * (jq + 1), TCH * tau:TCH * (tau + 1)],
                        outc2[jq][cb][...]))(jq, tau, cb),
                        wait=tnos[jq], dma=True, stream=f"st{jq}{cb}")
                    G[f"outc{jq}{cb}"] = [tst]

            p.op("v", lambda en: en.engine_nop(), wait=p.all_done())
            p.emit(est)
        return nc
    return build_attna

build_attna = _ns_attna()

def _ns_attn():
    D = 1024
    HAL = 257
    NT = 2048
    NK = 2304
    NB = 18
    NTAU = 16
    QSCALE = 8.0


    def build_ffn_attn():
        nc = bass.Bass("TRN2", target_bir_lowering=False)
        x_d = nc.dram_tensor("x", [D, HAL + NT], F32, kind="ExternalInput")
        w_d = {nm: nc.dram_tensor(nm, [D, D], F32, kind="ExternalInput")
               for nm in ["Wq", "Wk", "Wv", "Wo"]}
        qs_d = nc.dram_tensor("qs", [128, 1], F32, kind="ExternalInput")
        ks_d = nc.dram_tensor("ks", [128, 1], F32, kind="ExternalInput")
        csq_d = nc.dram_tensor("cs_q", [128, NT], F32, kind="ExternalInput")
        ssq_d = nc.dram_tensor("ss_q", [128, NT], F32, kind="ExternalInput")
        csk_d = nc.dram_tensor("cs_k", [128, NK], F32, kind="ExternalInput")
        ssk_d = nc.dram_tensor("ss_k", [128, NK], F32, kind="ExternalInput")
        von_d = nc.dram_tensor("vones", [128, NB], F32, kind="ExternalInput")
        y_d = nc.dram_tensor("y", [D, NT], F32, kind="ExternalOutput")

        swap = np.zeros((128, 128), np.float32)
        for i in range(128):
            part = i + 32 if (i % 64) < 32 else i - 32
            swap[part, i] = -1.0 if (i % 64) < 32 else 1.0
        P_c = nc.inline_tensor(swap.astype(ml_dtypes.bfloat16), name="Pswap")
        ident_c = nc.inline_tensor(np.eye(128, dtype=ml_dtypes.bfloat16), name="id128")
        u = np.arange(128)
        m0_c = nc.inline_tensor((u[:, None] >= u[None, :]).astype(ml_dtypes.bfloat16),
                                name="mask_s0")
        m2_c = nc.inline_tensor((u[:, None] <= u[None, :]).astype(ml_dtypes.bfloat16),
                                name="mask_s2")
        indj_np = np.zeros((8, 128, 16), np.float32)
        for j in range(8):
            indj_np[j, 0:64, 2 * j] = 1.0
            indj_np[j, 64:128, 2 * j + 1] = 1.0
        indj_c = [nc.inline_tensor(indj_np[j].astype(ml_dtypes.bfloat16), name=f"indj{j}")
                  for j in range(8)]
        ind2_np = np.zeros((8, 16, 128), np.float32)
        for j in range(8):
            ind2_np[j, 2 * j, 0:64] = 1.0
            ind2_np[j, 2 * j + 1, 64:128] = 1.0
        ind2_c = nc.inline_tensor(ind2_np.reshape(8 * 16, 128).astype(ml_dtypes.bfloat16),
                                  name="indh2")

        with ExitStack() as est:
            e = est.enter_context
            qs = e(nc.sbuf_tensor("qss", [128, 1], F32))
            ks = e(nc.sbuf_tensor("kss", [128, 1], F32))
            cst = e(nc.sbuf_tensor("cst", [128, NK], BF16))
            sst = e(nc.sbuf_tensor("sst", [128, NK], BF16))
            von = e(nc.sbuf_tensor("von", [128, NB], BF16))
            Psw = e(nc.sbuf_tensor("Psw", [128, 128], BF16))
            idn = e(nc.sbuf_tensor("idn", [128, 128], BF16))
            msk0 = e(nc.sbuf_tensor("msk0", [128, 128], BF16))
            msk2 = e(nc.sbuf_tensor("msk2", [128, 128], BF16))
            indj = [e(nc.sbuf_tensor(f"indjs{j}", [128, 16], BF16)) for j in range(8)]
            ind2 = e(nc.sbuf_tensor("ind2", [16, 8 * 128], BF16))
            ones65 = e(nc.sbuf_tensor("ones65", [65, 128], BF16))
            onc = e(nc.sbuf_tensor("onc", [1, 64], F32))
            wbuf = e(nc.sbuf_tensor("wbuf", [128, 8 * D], BF16))
            wbuf2 = e(nc.sbuf_tensor("wbuf2", [128, 8 * D], BF16))
            qrot = e(nc.sbuf_tensor("qrot", [128, 8 * NT], BF16))
            krot = e(nc.sbuf_tensor("krot", [128, 8 * NK], BF16))
            vbuf = e(nc.sbuf_tensor("vbuf", [128, NB * D], BF16))
            xbch = [e(nc.sbuf_tensor(f"xbch{k}", [128, 8 * 512], BF16)) for k in range(2)]
            attb = [e(nc.sbuf_tensor(f"attb{k}", [128, 8 * 128], BF16)) for k in range(2)]
            A_T = e(nc.sbuf_tensor("A_T", [128, 16 * 384], BF16))
            negM = e(nc.sbuf_tensor("negM", [128, 16], BF16))
            negMr = e(nc.sbuf_tensor("negMr", [65, 16 * 128], BF16))
            rno = e(nc.sbuf_tensor("rno", [16, NT], BF16))
            rnk = e(nc.sbuf_tensor("rnk", [16, NK], BF16))
            rrow = e(nc.sbuf_tensor("rrow", [1, 512], F32))
            oev = e(nc.sbuf_tensor("oev", [64, 512], BF16))
            sqs = [e(nc.sbuf_tensor(f"sqs{k}", [128, 512], BF16)) for k in range(2)]
            tmp0 = e(nc.sbuf_tensor("tmp0", [128, 512], BF16))
            tmp1 = e(nc.sbuf_tensor("tmp1", [128, 512], BF16))
            tmp2 = e(nc.sbuf_tensor("tmp2", [128, 512], BF16))
            xres = [e(nc.sbuf_tensor(f"xres{k}", [128, 128], F32)) for k in range(2)]
            outb = [e(nc.sbuf_tensor(f"outbf{k}", [128, 128], F32)) for k in range(2)]
            ps_q = e(nc.psum_tensor("ps_q", [128, 512], F32))
            ps_w = e(nc.psum_tensor("ps_w", [128, 512], F32))
            ps_n = e(nc.psum_tensor("ps_n", [16, 512], F32))
            ps_bc = e(nc.psum_tensor("ps_bc", [128, 512], F32))
            ps_1 = e(nc.psum_tensor("ps_1", [128, 384], F32))
            ps_2 = e(nc.psum_tensor("ps_2", [128, 384], F32))
            ps_av = e(nc.psum_tensor("ps_av", [64, 512], F32))
            ps_dn = e(nc.psum_tensor("ps_dn", [1, 512], F32))

            p = Prog(nc)
            G = {}

            def gw(name):
                return list(G.get(name, []))

            w_sm = None
            smalls = [(qs, qs_d), (ks, ks_d), (von, von_d), (Psw, P_c), (idn, ident_c),
                      (msk0, m0_c), (msk2, m2_c)]
            smalls += [(indj[j], indj_c[j]) for j in range(8)]
            for (sb_, dr_) in smalls:
                w_sm = p.op("g", (lambda sb_, dr_: lambda en: en.dma_start(
                    sb_[...], dr_[...]))(sb_, dr_), dma=True, stream="sm")
            w_sm = p.op("g", lambda en: en.dma_start(
                ind2[...].rearrange("p (a w) -> p a w", a=8),
                ind2_c[...].rearrange("(a p) w -> p a w", p=16)), dma=True, stream="sm")
            c2 = p.op("v", lambda en: en.memset(ones65[...], 1.0))
            c3 = p.op("v", lambda en: en.memset(onc[...], 1.0))
            CON = [w_sm, c2, c3]

            def load_w(nm, buf):
                return p.op("g", (lambda nm, buf: lambda en: en.dma_start(
                    buf[...].rearrange("p (a w) -> p a w", a=8),
                    w_d[nm][...].rearrange("(a p) w -> p a w", p=128)))(nm, buf),
                    dma=True, stream="wst")

            def load_tbl(cs_src, ss_src, ncols):
                t1 = p.op("g", (lambda cs_src, ncols: lambda en: en.dma_start(
                    cst[:, 0:ncols], cs_src[...]))(cs_src, ncols), dma=True, stream="tbl")
                t2 = p.op("g", (lambda ss_src, ncols: lambda en: en.dma_start(
                    sst[:, 0:ncols], ss_src[...]))(ss_src, ncols), dma=True, stream="tbl")
                return [t1, t2]

            def emit_proj(dst, wtok, wbufx, scol, ncols, xoff, rows_out, tbl_toks):
                chunks = []
                o = 0
                while o < ncols:
                    cw = min(512, ncols - o)
                    chunks.append((o, cw))
                    o += cw
                for ci, (o, cw) in enumerate(chunks):
                    b = ci % 2
                    p.op("g", (lambda o, cw, b: lambda en: en.dma_start(
                        xbch[b][:, 0:4 * cw].rearrange("p (a w) -> p a w", a=4),
                        x_d[0:512, xoff + o:xoff + o + cw].rearrange(
                            "(a p) w -> p a w", p=128)))(o, cw, b),
                        wait=gw(f"xbch{b}"), dma=True, stream=f"xc{b}")
                    t_xc = p.op("g", (lambda o, cw, b: lambda en: en.dma_start(
                        xbch[b][:, 4 * cw:8 * cw].rearrange("p (a w) -> p a w", a=4),
                        x_d[512:1024, xoff - 1 + o:xoff - 1 + o + cw].rearrange(
                            "(a p) w -> p a w", p=128)))(o, cw, b),
                        wait=gw(f"xbch{b}"), dma=True, stream=f"xc{b}")
                    t3s = []
                    t_nmm = None
                    xc_readers = []
                    for j in range(8):
                        sb = j % 2
                        last = None
                        for kk in range(8):
                            wl = [t_xc, wtok] + ([last] if last else [])
                            if j == 0 and kk == 0:
                                wl += gw("ps_q")
                            last = p.op("t", (lambda j, kk, cw, b: lambda en: en.matmul(
                                ps_q[:, 0:cw],
                                wbufx[:, kk * D + 128 * j:kk * D + 128 * (j + 1)],
                                xbch[b][:, kk * cw:(kk + 1) * cw],
                                start=(kk == 0), stop=(kk == 7)))(j, kk, cw, b), wait=wl)
                        xc_readers.append(last)
                        tsq = p.op("s", (lambda cw, sb: lambda en: en.activation(
                            sqs[sb][:, 0:cw], ps_q[:, 0:cw], AF.Square))(cw, sb),
                            wait=[last] + gw(f"sqs{sb}"))
                        tsc = p.op("s", (lambda cw, scol: lambda en: en.activation(
                            tmp0[:, 0:cw], ps_q[:, 0:cw], AF.Identity,
                            scale=scol[...]))(cw, scol), wait=[last, tsq, CON[0]] + gw("tmp0"))
                        G["ps_q"] = [tsc]
                        wl = [tsq, CON[0]] + ([t_nmm] if t_nmm else [])
                        if j == 0:
                            wl += gw("ps_n")
                        t_nmm = p.op("t", (lambda j, cw, sb: lambda en: en.matmul(
                            ps_n[0:16, 0:cw], indj[j][...], sqs[sb][:, 0:cw],
                            start=(j == 0), stop=(j == 7)))(j, cw, sb), wait=wl)
                        G[f"sqs{sb}"] = [t_nmm]
                        tsw = p.op("t", (lambda cw: lambda en: en.matmul(
                            ps_w[:, 0:cw], Psw[...], tmp0[:, 0:cw],
                            start=True, stop=True))(cw), wait=[tsc, CON[0]] + gw("ps_w"))
                        t1 = p.op("v", (lambda o, cw: lambda en: en.tensor_mul(
                            tmp1[:, 0:cw], tmp0[:, 0:cw], cst[:, o:o + cw]))(o, cw),
                            wait=[tsc] + tbl_toks + gw("tmp1"))
                        G["tmp0"] = [t1]
                        t2 = p.op("v", (lambda o, cw: lambda en: en.tensor_mul(
                            tmp2[:, 0:cw], ps_w[:, 0:cw], sst[:, o:o + cw]))(o, cw),
                            wait=[tsw] + tbl_toks + gw("tmp2"))
                        G["ps_w"] = [t2]
                        t3 = p.op("v", (lambda j, o, cw, dst, ncols: lambda en: en.tensor_add(
                            dst[:, j * ncols + o:j * ncols + o + cw],
                            tmp1[:, 0:cw], tmp2[:, 0:cw]))(j, o, cw, dst, ncols),
                            wait=[t1, t2])
                        G["tmp1"] = [t3]
                        G["tmp2"] = [t3]
                        t3s.append(t3)
                    G[f"xbch{b}"] = xc_readers
                    tmx = p.op("v", (lambda o, cw, rows_out: lambda en: en.tensor_scalar_max(
                        rows_out[0:16, o:o + cw], ps_n[0:16, 0:cw], 1e-24))(o, cw, rows_out),
                        wait=[t_nmm])
                    G["ps_n"] = [tmx]
                    tl = p.op("s", (lambda o, cw, rows_out: lambda en: en.activation(
                        rows_out[0:16, o:o + cw], rows_out[0:16, o:o + cw], AF.Ln))(o, cw, rows_out),
                        wait=[tmx])
                    tex = p.op("s", (lambda o, cw, rows_out: lambda en: en.activation(
                        rows_out[0:16, o:o + cw], rows_out[0:16, o:o + cw], AF.Exp,
                        scale=-0.5))(o, cw, rows_out), wait=[tl])
                    for j in range(8):
                        tb = p.op("t", (lambda j, o, cw, rows_out: lambda en: en.matmul(
                            ps_bc[:, 0:cw], ind2[:, 128 * j:128 * (j + 1)],
                            rows_out[0:16, o:o + cw], start=True, stop=True))(j, o, cw, rows_out),
                            wait=[tex, CON[0]] + gw("ps_bc"))
                        tm = p.op("v", (lambda j, o, cw, dst, ncols: lambda en: en.tensor_mul(
                            dst[:, j * ncols + o:j * ncols + o + cw],
                            dst[:, j * ncols + o:j * ncols + o + cw],
                            ps_bc[:, 0:cw]))(j, o, cw, dst, ncols), wait=[tb, t3s[j]])
                        G["ps_bc"] = [tm]

            w_wq = load_w("Wq", wbuf)
            tbl = load_tbl(csq_d, ssq_d, NT)
            emit_proj(qrot, w_wq, wbuf, qs, NT, HAL, rno, tbl)
            p.phase_barrier()
            w_wk = load_w("Wk", wbuf2)
            tbl = load_tbl(csk_d, ssk_d, NK)
            emit_proj(krot, w_wk, wbuf2, ks, NK, 1, rnk, tbl)

            p.phase_barrier()
            w_wv = load_w("Wv", wbuf)
            for beta in range(NB):
                b = beta % 2
                p.op("g", (lambda beta, b: lambda en: en.dma_start(
                    xbch[b][:, 0:4 * 128].rearrange("p (a w) -> p a w", a=4),
                    x_d[0:512, 1 + 128 * beta:1 + 128 * (beta + 1)].rearrange(
                        "(a p) w -> p a w", p=128)))(beta, b),
                    wait=gw(f"xbch{b}"), dma=True, stream=f"xc{b}")
                t_xc = p.op("g", (lambda beta, b: lambda en: en.dma_start(
                    xbch[b][:, 4 * 128:8 * 128].rearrange("p (a w) -> p a w", a=4),
                    x_d[512:1024, 128 * beta:128 * (beta + 1)].rearrange(
                        "(a p) w -> p a w", p=128)))(beta, b),
                    wait=gw(f"xbch{b}"), dma=True, stream=f"xc{b}")
                readers = []
                for half in range(2):
                    ps = ps_bc if half == 0 else ps_q
                    last = None
                    for kk in range(8):
                        wl = [t_xc, w_wv] + ([last] if last else [])
                        if kk == 0:
                            wl += gw("ps_bc" if half == 0 else "ps_q")
                        last = p.op("t", (lambda b, half, kk, ps: lambda en: en.matmul(
                            ps[:, 0:512],
                            xbch[b][:, kk * 128:(kk + 1) * 128],
                            wbuf[:, kk * D + half * 512:kk * D + (half + 1) * 512],
                            start=(kk == 0), stop=(kk == 7)))(b, half, kk, ps), wait=wl)
                    readers.append(last)
                    tev = p.op("s", (lambda beta, half, ps: lambda en: en.activation(
                        vbuf[:, beta * D + half * 512:beta * D + (half + 1) * 512],
                        ps[:, 0:512], AF.Identity))(beta, half, ps), wait=[last])
                    G["ps_bc" if half == 0 else "ps_q"] = [tev]
                    tv_last = tev
                G[f"xbch{b}"] = readers

            p.phase_barrier()
            w_wo = load_w("Wo", wbuf2)
            for tau in range(NTAU):
                ab = tau % 2
                w0 = 1 + 128 * tau
                t_red = {}
                for h in range(16):
                    hb = 64 * (h % 2)
                    jt = h // 2
                    t_s1 = p.op("t", (lambda hb, jt, tau, w0: lambda en: en.matmul(
                        ps_1[:, 0:384],
                        qrot[hb:hb + 64, jt * NT + 128 * tau:jt * NT + 128 * (tau + 1)],
                        krot[hb:hb + 64, jt * NK + w0 - 1:jt * NK + w0 - 1 + 384],
                        start=True, stop=True))(hb, jt, tau, w0), wait=gw("ps_1"))
                    t_red[h] = p.op("v", (lambda h: lambda en: en.reduce_max(
                        negM[:, h:h + 1], ps_1[:, 0:384], AX.X, negate=True))(h),
                        wait=[t_s1] + gw("negM"))
                    G["ps_1"] = [t_red[h]]
                negm_readers = []
                for q4 in range(4):
                    last = None
                    for hh in range(4):
                        h = 4 * q4 + hh
                        wl = [t_red[h], CON[0]] + ([last] if last else [])
                        if hh == 0:
                            wl += gw("ps_n")
                        last = p.op("t", (lambda h, hh: lambda en: en.matmul(
                            ps_n[0:1, 128 * hh:128 * (hh + 1)], negM[:, h:h + 1],
                            idn[...], start=True, stop=True))(h, hh), wait=wl)
                    negm_readers.append(last)
                    te0 = p.op("s", (lambda q4: lambda en: en.activation(
                        negMr[0:1, 512 * q4:512 * (q4 + 1)], ps_n[0:1, 0:512],
                        AF.Identity))(q4), wait=[last] + gw("negMr"))
                    te1 = p.op("s", (lambda q4: lambda en: en.activation(
                        negMr[64:65, 512 * q4:512 * (q4 + 1)], ps_n[0:1, 0:512],
                        AF.Identity))(q4), wait=[last, te0])
                    G["ps_n"] = [te0, te1]
                    G[f"negMr{q4}"] = [te0, te1]
                G["negM"] = negm_readers
                t_exp = {}
                for h in range(16):
                    hb = 64 * (h % 2)
                    jt = h // 2
                    last = None
                    for sg in range(3):
                        wl = G[f"negMr{h // 4}"] + ([last] if last else [])
                        if sg == 0:
                            wl += gw("ps_2")
                        last = p.op("t", (lambda hb, jt, sg, tau, w0: lambda en: en.matmul(
                            ps_2[:, 128 * sg:128 * (sg + 1)],
                            krot[hb:hb + 64, jt * NK + w0 - 1 + 128 * sg:jt * NK + w0 - 1 + 128 * (sg + 1)],
                            qrot[hb:hb + 64, jt * NT + 128 * tau:jt * NT + 128 * (tau + 1)],
                            start=True, stop=False))(hb, jt, sg, tau, w0), wait=wl)
                        last = p.op("t", (lambda hb, h, sg: lambda en: en.matmul(
                            ps_2[:, 128 * sg:128 * (sg + 1)],
                            ones65[hb:hb + 1, 0:128],
                            negMr[hb:hb + 1, 128 * h:128 * (h + 1)],
                            start=False, stop=True))(hb, h, sg), wait=[last, CON[1]])
                    t_exp[h] = p.op("s", (lambda h: lambda en: en.activation(
                        A_T[:, 384 * h:384 * (h + 1)], ps_2[:, 0:384], AF.Exp,
                        scale=QSCALE))(h), wait=[last] + gw("A_T"))
                    G["ps_2"] = [t_exp[h]]
                expall = [t_exp[h] for h in range(16)]
                G["negMr"] = expall
                tm0 = p.op("v", lambda en: en.tensor_mul(
                    A_T[...].rearrange("p (h s) -> p h s", h=16)[:, :, 0:128],
                    A_T[...].rearrange("p (h s) -> p h s", h=16)[:, :, 0:128],
                    msk0[...].rearrange("p (o w) -> p o w", o=1).broadcast_to((128, 16, 128))),
                    wait=expall + [CON[0]])
                tm2 = p.op("v", lambda en: en.tensor_mul(
                    A_T[...].rearrange("p (h s) -> p h s", h=16)[:, :, 256:384],
                    A_T[...].rearrange("p (h s) -> p h s", h=16)[:, :, 256:384],
                    msk2[...].rearrange("p (o w) -> p o w", o=1).broadcast_to((128, 16, 128))),
                    wait=expall)
                at_readers = []
                for q4 in range(4):
                    last_av = None
                    last_dn = None
                    for hh in range(4):
                        h = 4 * q4 + hh
                        for sg in range(3):
                            beta = tau + sg
                            wl = [tm0, tm2, tv_last] + ([last_av] if last_av else [])
                            if hh == 0 and sg == 0:
                                wl += gw("ps_av")
                            last_av = p.op("t", (lambda h, hh, sg, beta: lambda en: en.matmul(
                                ps_av[0:64, 128 * hh:128 * (hh + 1)],
                                vbuf[:, beta * D + 64 * h:beta * D + 64 * (h + 1)],
                                A_T[:, 384 * h + 128 * sg:384 * h + 128 * (sg + 1)],
                                start=(sg == 0), stop=(sg == 2)))(h, hh, sg, beta), wait=wl)
                            wl2 = [tm0, tm2, CON[0]] + ([last_dn] if last_dn else [])
                            if hh == 0 and sg == 0:
                                wl2 += gw("ps_dn")
                            last_dn = p.op("t", (lambda h, hh, sg, beta: lambda en: en.matmul(
                                ps_dn[0:1, 128 * hh:128 * (hh + 1)],
                                von[:, beta:beta + 1],
                                A_T[:, 384 * h + 128 * sg:384 * h + 128 * (sg + 1)],
                                start=(sg == 0), stop=(sg == 2)))(h, hh, sg, beta), wait=wl2)
                    trc = p.op("v", lambda en: en.reciprocal(
                        rrow[0:1, 0:512], ps_dn[0:1, 0:512]), wait=[last_dn] + gw("rrow"))
                    G["ps_dn"] = [trc]
                    tbc = p.op("t", lambda en: en.matmul(
                        ps_bc[0:64, 0:512], onc[...], rrow[0:1, 0:512],
                        start=True, stop=True), wait=[trc, CON[2]] + gw("ps_bc"))
                    G["rrow"] = [tbc]
                    tev = p.op("s", lambda en: en.activation(
                        oev[...], ps_av[0:64, 0:512], AF.Identity),
                        wait=[last_av] + gw("oev"))
                    G["ps_av"] = [tev]
                    tne = p.op("v", (lambda q4, ab: lambda en: en.tensor_mul(
                        attb[ab][0:64, 256 * q4:256 * q4 + 256].rearrange(
                            "p (r w) -> p r w", r=2),
                        oev[...].rearrange("p (r c w) -> p r c w", r=2, c=2)[:, :, 0, :],
                        ps_bc[0:64, 0:512].rearrange(
                            "p (r c w) -> p r c w", r=2, c=2)[:, :, 0, :]))(q4, ab),
                        wait=[tev, tbc] + gw(f"attb{ab}"))
                    tno = p.op("v", (lambda q4, ab: lambda en: en.tensor_mul(
                        attb[ab][64:128, 256 * q4:256 * q4 + 256].rearrange(
                            "p (r w) -> p r w", r=2),
                        oev[...].rearrange("p (r c w) -> p r c w", r=2, c=2)[:, :, 1, :],
                        ps_bc[0:64, 0:512].rearrange(
                            "p (r c w) -> p r c w", r=2, c=2)[:, :, 1, :]))(q4, ab),
                        wait=[tev, tbc])
                    G["ps_bc"] = [tne, tno]
                    G["oev"] = [tne, tno]
                    at_readers += [tne, tno]
                G["A_T"] = at_readers
                G[f"attb_w{ab}"] = at_readers
                wo_readers = []
                for n in range(8):
                    rb = n % 2
                    t_xr = p.op("y", (lambda n, tau, rb: lambda en: en.dma_start(
                        xres[rb][...],
                        x_d[128 * n:128 * (n + 1), HAL + 128 * tau:HAL + 128 * (tau + 1)]))(n, tau, rb),
                        wait=gw(f"xres{rb}"), dma=True, stream=f"xr{rb}")
                    last = None
                    for kk in range(8):
                        wl = at_readers + [w_wo] + ([last] if last else [])
                        if kk == 0:
                            wl += gw("ps_q")
                        last = p.op("t", (lambda n, kk, ab: lambda en: en.matmul(
                            ps_q[:, 0:128],
                            wbuf2[:, kk * D + 128 * n:kk * D + 128 * (n + 1)],
                            attb[ab][:, 128 * kk:128 * (kk + 1)],
                            start=(kk == 0), stop=(kk == 7)))(n, kk, ab), wait=wl)
                    wo_readers.append(last)
                    tad = p.op("v", (lambda rb: lambda en: en.tensor_add(
                        outb[rb][...], ps_q[:, 0:128], xres[rb][...]))(rb),
                        wait=[last, t_xr] + gw(f"outb{rb}"))
                    G["ps_q"] = [tad]
                    G[f"xres{rb}"] = [tad]
                    tst = p.op("y", (lambda n, tau, rb: lambda en: en.dma_start(
                        y_d[128 * n:128 * (n + 1), 128 * tau:128 * (tau + 1)],
                        outb[rb][...]))(n, tau, rb), wait=[tad], dma=True, stream=f"yo{rb}")
                    G[f"outb{rb}"] = [tst]
                G[f"attb{ab}"] = wo_readers

            p.op("v", lambda en: en.engine_nop(), wait=p.all_done())
            p.emit(est)
        return nc


    def attn_inputs(x_own_cm, x_halo_cm, Wqkv, q_scale, k_scale, Wo, pos0, seq_start):
        DIMH = 64
        WINDOW = 256
        xin = np.concatenate([x_halo_cm, x_own_cm], axis=1).astype(np.float32)
        Wq = Wqkv[:, 0:1024]
        Wk = Wqkv[:, 1024:2048]
        Wv = Wqkv[:, 2048:3072]
        qs = np.tile(q_scale, 2).reshape(128, 1).astype(np.float32)
        ks = np.tile(k_scale, 2).reshape(128, 1).astype(np.float32)

        def tables(positions, inv):
            half = np.arange(0, DIMH, 2, dtype=np.float32)
            inv_freq = 1.0 / 10000.0 ** (half / DIMH)
            t = positions.astype(np.float32)
            freqs = t[:, None] * inv_freq[None, :]
            freqs = np.concatenate([freqs, freqs], -1)
            base_scale = (half + 0.4 * DIMH) / (1.4 * DIMH)
            power = (t - 8192 // 2) / (WINDOW // 2)
            scale = base_scale[None, :] ** power[:, None]
            scale = np.concatenate([scale, scale], -1)
            if inv:
                scale = 1.0 / scale
            cs = np.cos(freqs) * scale
            ss = np.sin(freqs) * scale
            cs = np.tile(cs.T, (2, 1)).astype(np.float32)
            ss = np.tile(ss.T, (2, 1)).astype(np.float32)
            return np.ascontiguousarray(cs), np.ascontiguousarray(ss)

        pos_q = pos0 + np.arange(2048)
        pos_k = pos0 - 256 + np.arange(2304)
        cs_q, ss_q = tables(pos_q, inv=False)
        cs_k, ss_k = tables(pos_k, inv=True)
        vones = np.zeros((128, NB), np.float32)
        for beta in range(NB):
            for pp in range(128):
                kpos = pos0 - 256 + 128 * beta + pp
                vones[pp, beta] = 1.0 if (kpos >= 0 or not seq_start) else 0.0
        if not seq_start:
            vones[:] = 1.0
        return {
            "x": xin,
            "Wq": np.ascontiguousarray(Wq.astype(np.float32)),
            "Wk": np.ascontiguousarray(Wk.astype(np.float32)),
            "Wv": np.ascontiguousarray(Wv.astype(np.float32)),
            "Wo": np.ascontiguousarray(Wo.astype(np.float32)),
            "qs": qs, "ks": ks,
            "cs_q": cs_q, "ss_q": ss_q, "cs_k": cs_k, "ss_k": ss_k,
            "vones": vones,
        }
    return build_ffn_attn, attn_inputs

build_ffn_attn, attn_inputs = _ns_attn()

def _ns_ffn():
    D = 1024
    DI = 2730
    NMB = 22
    TCH = 1024


    def mblk_sz(m):
        return 42 if m == NMB - 1 else 128


    def build_ffn_ffn(nt=2048, eps=1e-5):
        nc = bass.Bass("TRN2", target_bir_lowering=False)
        x_d = nc.dram_tensor("x", [D, nt + 1], F32, kind="ExternalInput")
        w1a_d = nc.dram_tensor("W1a", [D, DI], F32, kind="ExternalInput")
        w1g_d = nc.dram_tensor("W1g", [D, DI], F32, kind="ExternalInput")
        b1a_d = nc.dram_tensor("b1a", [128, NMB], F32, kind="ExternalInput")
        b1g_d = nc.dram_tensor("b1g", [128, NMB], F32, kind="ExternalInput")
        w2_d = nc.dram_tensor("W2", [DI, D], F32, kind="ExternalInput")
        b2_d = nc.dram_tensor("b2r", [1, D], F32, kind="ExternalInput")
        lng_d = nc.dram_tensor("lng", [128, 8], F32, kind="ExternalInput")
        lnb_d = nc.dram_tensor("lnb", [128, 8], F32, kind="ExternalInput")
        y_d = nc.dram_tensor("y", [D, nt], F32, kind="ExternalOutput")

        nchunk = nt // TCH
        with ExitStack() as st:
            e = st.enter_context
            x = [e(nc.sbuf_tensor(f"xt{i}", [128, nt + 1], F32)) for i in range(8)]
            lng = e(nc.sbuf_tensor("lngs", [128, 8], F32))
            lnb = e(nc.sbuf_tensor("lnbs", [128, 8], F32))
            b1a = e(nc.sbuf_tensor("b1as", [128, NMB], F32))
            b1g = e(nc.sbuf_tensor("b1gs", [128, NMB], F32))
            b2r = e(nc.sbuf_tensor("b2rs", [1, D], BF16))
            onesr = e(nc.sbuf_tensor("onesr", [1, TCH], BF16))
            ones2 = e(nc.sbuf_tensor("ones2", [128, 2], BF16))
            onesf = e(nc.sbuf_tensor("onesf", [1, 128], F32))
            xn = [e(nc.sbuf_tensor(f"xn{i}", [128, TCH], BF16)) for i in range(8)]
            xcast = [e(nc.sbuf_tensor(f"xcast{k}", [128, TCH], BF16)) for k in range(2)]
            xsq = [e(nc.sbuf_tensor(f"xsq{k}", [128, TCH], BF16)) for k in range(2)]
            hbuf = e(nc.sbuf_tensor("hbuf", [128, NMB * TCH], BF16))
            gelu_s = e(nc.sbuf_tensor("gelu_s", [128, TCH], BF16))
            a_s = e(nc.sbuf_tensor("a_s", [128, TCH], BF16))
            w1buf = [e(nc.sbuf_tensor(f"w1b{k}", [128, 16 * 128], BF16)) for k in range(2)]
            w2buf = [e(nc.sbuf_tensor(f"w2b{k}", [128, NMB * 128], BF16)) for k in range(2)]
            outbuf = [e(nc.sbuf_tensor(f"outb{k}", [128, TCH], F32)) for k in range(2)]
            murow = e(nc.sbuf_tensor("murow", [1, TCH], F32))
            musq = e(nc.sbuf_tensor("musq", [1, TCH], F32))
            varrow = e(nc.sbuf_tensor("varrow", [1, TCH], F32))
            lnvrow = e(nc.sbuf_tensor("lnvrow", [1, TCH], F32))
            rstdrow = e(nc.sbuf_tensor("rstdrow", [1, TCH], F32))
            mrrow = e(nc.sbuf_tensor("mrrow", [1, TCH], F32))
            Rb = e(nc.sbuf_tensor("Rb", [128, TCH], BF16))
            Mb = e(nc.sbuf_tensor("Mb", [128, TCH], BF16))
            ps_r0 = e(nc.psum_tensor("ps_r0", [1, TCH], F32))
            ps_r1 = e(nc.psum_tensor("ps_r1", [1, TCH], F32))
            ps_A = e(nc.psum_tensor("ps_A", [128, TCH], F32))
            ps_B = e(nc.psum_tensor("ps_B", [128, TCH], F32))

            p = Prog(nc)
            for i in range(8):
                w_x_all = p.op("y", (lambda i: lambda en: en.dma_start(
                    x[i][...], x_d[128 * i:128 * (i + 1), :]))(i), dma=True, stream="x")
            w_x = [w_x_all] * 8
            for (sb_, dr_) in [(lng, lng_d), (lnb, lnb_d), (b1a, b1a_d), (b1g, b1g_d)]:
                w_sm = p.op("y", (lambda sb_, dr_: lambda en: en.dma_start(
                    sb_[...], dr_[...]))(sb_, dr_), dma=True, stream="sm")
            w_lng = w_lnb = w_b1a = w_b1g = w_sm
            w_b2 = p.op("g", lambda en: en.dma_start(b2r[...], b2_d[...]), dma=True, stream="b2")
            c1 = p.op("v", lambda en: en.memset(onesr[...], 1.0))
            c2 = p.op("v", lambda en: en.memset(ones2[...], 1.0))
            c3 = p.op("v", lambda en: en.memset(onesf[...], 1.0))

            pv_var = None
            pv_psA_readers = []
            pv_h_last = None
            pv_w2_last = None

            for ch in range(nchunk):
                if ch > 0:
                    p.phase_barrier()
                c0 = ch * TCH

                def xsh_ap(i):
                    off = 1 if i < 4 else 0
                    return x[i][:, c0 + off: c0 + off + TCH]

                def xres_ap(i):
                    return x[i][:, c0 + 1:c0 + 1 + TCH]

                tprev_mm = None
                tc_list = []
                for i in range(8):
                    b = i % 2
                    wl = [w_x[i], c1, c2]
                    if i >= 2:
                        wl.append(tprev_mm)
                    tc = p.op("v", (lambda xv, b: lambda en: en.tensor_copy(
                        xcast[b][...], xv))(xsh_ap(i), b), wait=wl)
                    tsq = p.op("s", (lambda b: lambda en: en.activation(
                        xsq[b][...], xcast[b][...], AF.Square))(b), wait=[tc])
                    mmw = [tc]
                    if tprev_mm:
                        mmw.append(tprev_mm)
                    if i == 0 and pv_var is not None:
                        mmw.append(pv_var)
                    for q in range(TCH // 512):
                        t1 = p.op("t", (lambda b, i, q: lambda en: en.matmul(
                            ps_r0[0:1, q * 512:(q + 1) * 512], ones2[:, 0:1],
                            xcast[b][:, q * 512:(q + 1) * 512],
                            start=(i == 0), stop=(i == 7)))(b, i, q), wait=mmw)
                        mmw = [t1]
                    for q in range(TCH // 512):
                        t2 = p.op("t", (lambda b, i, q: lambda en: en.matmul(
                            ps_r1[0:1, q * 512:(q + 1) * 512], ones2[:, 1:2],
                            xsq[b][:, q * 512:(q + 1) * 512],
                            start=(i == 0), stop=(i == 7)))(b, i, q), wait=[tsq, t1, t2] if i else [tsq, t1])
                    tc_list.append(tc)
                    tprev_mm = t2
                t_mu = p.op("v", lambda en: en.tensor_scalar_mul(
                    murow[...], ps_r0[0:1, :], 1.0 / D), wait=[tprev_mm])
                t_m2 = p.op("v", lambda en: en.tensor_mul(
                    musq[...], murow[...], murow[...]), wait=[t_mu])
                t_var = p.op("v", lambda en: en.scalar_tensor_tensor(
                    varrow[...], ps_r1[0:1, :], 1.0 / D, musq[...],
                    ALU.mult, ALU.subtract), wait=[t_m2, tprev_mm])
                pv_var = t_var
                t_vpe = p.op("v", lambda en: en.tensor_scalar_add(
                    varrow[...], varrow[...], eps), wait=[t_var])
                t_lnv = p.op("s", lambda en: en.activation(
                    lnvrow[...], varrow[...], AF.Ln), wait=[t_vpe])
                t_rstd = p.op("s", lambda en: en.activation(
                    rstdrow[...], lnvrow[...], AF.Exp, scale=-0.5), wait=[t_lnv])
                t_mr = p.op("v", lambda en: en.tensor_mul(
                    mrrow[...], murow[...], rstdrow[...]), wait=[t_rstd, t_mu])
                tRb, tMb = [], []
                for q in range(TCH // 512):
                    bw = [t_rstd, c1] + pv_psA_readers
                    tb = p.op("t", (lambda q: lambda en: en.matmul(
                        ps_A[:, q * 512:(q + 1) * 512], onesf[0:1, :],
                        rstdrow[0:1, q * 512:(q + 1) * 512], start=True, stop=True))(q),
                        wait=bw + [c3])
                    tRb.append(p.op("s", (lambda q: lambda en: en.activation(
                        Rb[:, q * 512:(q + 1) * 512], ps_A[:, q * 512:(q + 1) * 512],
                        AF.Identity))(q), wait=[tb]))
                for q in range(TCH // 512):
                    tb = p.op("t", (lambda q: lambda en: en.matmul(
                        ps_A[:, q * 512:(q + 1) * 512], onesf[0:1, :],
                        mrrow[0:1, q * 512:(q + 1) * 512], start=True, stop=True))(q),
                        wait=[t_mr, tRb[q], c3])
                    tMb.append(p.op("s", (lambda q: lambda en: en.activation(
                        Mb[:, q * 512:(q + 1) * 512], ps_A[:, q * 512:(q + 1) * 512],
                        AF.Identity, scale=-1.0))(q), wait=[tb]))
                t_xn = []
                for i in range(8):
                    tm = p.op("v", (lambda xv, i: lambda en: en.tensor_mul(
                        xn[i][...], xv, Rb[...]))(xsh_ap(i), i),
                        wait=tRb + [w_x[i]])
                    ta = p.op("v", (lambda i: lambda en: en.tensor_add(
                        xn[i][...], xn[i][...], Mb[...]))(i), wait=[tm] + tMb)
                    tg = p.op("s", (lambda i: lambda en: en.activation(
                        xn[i][...], xn[i][...], AF.Identity,
                        scale=lng[:, i:i + 1], bias=lnb[:, i:i + 1]))(i),
                        wait=[ta, w_lng, w_lnb])
                    t_xn.append(tg)

                p.phase_barrier()
                t_w1 = {}
                t_h = {}
                for m in range(NMB):
                    b = m % 2
                    mw = mblk_sz(m)
                    wwait = [t_h[m - 2]] if m >= 2 else ([pv_w2_last] if pv_w2_last else [])
                    t_w1a = p.op("g", (lambda m, b, mw: lambda en: en.dma_start(
                        w1buf[b][:, 0:8 * mw].rearrange("p (a w) -> p a w", a=8),
                        w1a_d[:, 128 * m:128 * m + mw].rearrange("(a p) w -> p a w", p=128)))(m, b, mw),
                        wait=wwait, dma=True, stream=f"w1p{b}")
                    t_w1g = p.op("g", (lambda m, b, mw: lambda en: en.dma_start(
                        w1buf[b][:, 8 * mw:16 * mw].rearrange("p (a w) -> p a w", a=8),
                        w1g_d[:, 128 * m:128 * m + mw].rearrange("(a p) w -> p a w", p=128)))(m, b, mw),
                        wait=wwait, dma=True, stream=f"w1p{b}")
                    psw = [t_h[m - 1]] if m >= 1 else ([pv_w2_last] if pv_w2_last else [])
                    last = None
                    for q in range(TCH // 512):
                        for k in range(8):
                            wl = [t_w1g, t_xn[k]] + psw + ([last] if last else []) + \
                                (tMb if m == 0 else [])
                            last = p.op("t", (lambda b, mw, q, k: lambda en: en.matmul(
                                ps_A[0:mw, q * 512:(q + 1) * 512],
                                w1buf[b][:, k * mw:(k + 1) * mw],
                                xn[k][:, q * 512:(q + 1) * 512],
                                start=(k == 0), stop=(k == 7)))(b, mw, q, k), wait=wl)
                    t_a = last
                    for q in range(TCH // 512):
                        for k in range(8):
                            wl = [t_w1g, t_xn[k]] + psw + [last]
                            last = p.op("t", (lambda b, mw, q, k: lambda en: en.matmul(
                                ps_B[0:mw, q * 512:(q + 1) * 512],
                                w1buf[b][:, (8 + k) * mw:(9 + k) * mw],
                                xn[k][:, q * 512:(q + 1) * 512],
                                start=(k == 0), stop=(k == 7)))(b, mw, q, k), wait=wl)
                    t_g = last
                    tge = p.op("s", (lambda m, mw: lambda en: en.activation(
                        gelu_s[0:mw, :], ps_B[0:mw, :], AF.Gelu,
                        bias=b1g[0:mw, m:m + 1]))(m, mw), wait=[t_g, w_b1g])
                    tas = p.op("s", (lambda m, mw: lambda en: en.activation(
                        a_s[0:mw, :], ps_A[0:mw, :], AF.Identity,
                        bias=b1a[0:mw, m:m + 1]))(m, mw), wait=[t_a, w_b1a, tge])
                    hw = [tge, tas]
                    if pv_w2_last is not None:
                        hw.append(pv_w2_last)
                    t_h[m] = p.op("v", (lambda m, mw: lambda en: en.tensor_mul(
                        hbuf[0:mw, m * TCH:(m + 1) * TCH], a_s[0:mw, :],
                        gelu_s[0:mw, :]))(m, mw), wait=hw)
                pv_h_last = t_h[NMB - 1]

                p.phase_barrier()
                t_out_dma = {}
                tadds = []
                for n in range(8):
                    b = n % 2
                    wwait = [t_out_dma[n - 2]] if n >= 2 else []
                    t_w2a = p.op("g", (lambda n, b: lambda en: en.dma_start(
                        w2buf[b][:, 0:21 * 128].rearrange("p (a w) -> p a w", a=21),
                        w2_d[0:2688, 128 * n:128 * (n + 1)].rearrange("(a p) w -> p a w", p=128)))(n, b),
                        wait=wwait, dma=True, stream=f"w2p{b}")
                    t_w2 = p.op("g", (lambda n, b: lambda en: en.dma_start(
                        w2buf[b][0:42, 21 * 128:22 * 128],
                        w2_d[2688:2730, 128 * n:128 * (n + 1)]))(n, b),
                        wait=wwait, dma=True, stream=f"w2p{b}")
                    last = None
                    for q in range(TCH // 512):
                        for m in range(NMB):
                            mw = mblk_sz(m)
                            wl = [t_w2, t_w2a, t_h[m]] + ([last] if last else [])
                            if n == 0 and q == 0 and m == 0:
                                wl.append(t_h[NMB - 1])
                            if n >= 1 and q == 0 and m == 0:
                                wl.append(tadds[n - 1])
                            last = p.op("t", (lambda b, q, m, mw: lambda en: en.matmul(
                                ps_A[:, q * 512:(q + 1) * 512],
                                w2buf[b][0:mw, m * 128:(m + 1) * 128],
                                hbuf[0:mw, m * TCH + q * 512:m * TCH + (q + 1) * 512],
                                start=(m == 0), stop=False))(b, q, m, mw), wait=wl)
                        last = p.op("t", (lambda n, q: lambda en: en.matmul(
                            ps_A[:, q * 512:(q + 1) * 512],
                            b2r[0:1, 128 * n:128 * (n + 1)],
                            onesr[0:1, q * 512:(q + 1) * 512],
                            start=False, stop=True))(n, q), wait=[last, w_b2, c1])
                    pv_w2_last = last
                    tadd = p.op("v", (lambda xv, b: lambda en: en.tensor_add(
                        outbuf[b][...], ps_A[...], xv))(xres_ap(n), b),
                        wait=[last] + wwait)
                    tadds.append(tadd)
                    t_out_dma[n] = p.op("y", (lambda n, b, c0: lambda en: en.dma_start(
                        y_d[128 * n:128 * (n + 1), c0:c0 + TCH], outbuf[b][...]))(n, b, c0),
                        wait=[tadd], dma=True, stream=f"outp{b}")
                pv_psA_readers = tadds[-2:]

            p.op("v", lambda en: en.engine_nop(), wait=p.all_done())
            p.emit(st)
        return nc


    def ffn_inputs(x_own, x_halo_col, W1, b1, W2, b2, g, bln):
        nt = x_own.shape[1]
        xin = np.concatenate([x_halo_col.reshape(-1, 1), x_own], axis=1).astype(np.float32)
        b1a = np.zeros((128, NMB), np.float32)
        b1g = np.zeros((128, NMB), np.float32)
        for m in range(NMB):
            mw = mblk_sz(m)
            b1a[:mw, m] = b1[128 * m:128 * m + mw]
            b1g[:mw, m] = b1[DI + 128 * m:DI + 128 * m + mw]
        return {
            "x": np.ascontiguousarray(xin),
            "W1a": np.ascontiguousarray(W1[:, :DI].astype(np.float32)),
            "W1g": np.ascontiguousarray(W1[:, DI:].astype(np.float32)),
            "b1a": b1a, "b1g": b1g,
            "W2": np.ascontiguousarray(W2.astype(np.float32)),
            "b2r": np.ascontiguousarray(b2.reshape(1, -1).astype(np.float32)),
            "lng": np.ascontiguousarray(g.reshape(8, 128).T.astype(np.float32)),
            "lnb": np.ascontiguousarray(bln.reshape(8, 128).T.astype(np.float32)),
        }
    return build_ffn_ffn, ffn_inputs

build_ffn_ffn, ffn_inputs = _ns_ffn()

def _ns_lru():
    D = 1024
    NST = 256
    NT = 2048
    NCK = 16
    TCH = 512


    def _ln_alloc(nc, e):
        t = {}
        t["onesc"] = e(nc.sbuf_tensor("l_onesc", [128, 2], BF16))
        t["onesf"] = e(nc.sbuf_tensor("l_onesf", [1, 128], F32))
        t["xcc"] = [e(nc.sbuf_tensor(f"l_xcc{k}", [128, TCH], BF16)) for k in range(2)]
        t["xsq"] = [e(nc.sbuf_tensor(f"l_xsq{k}", [128, TCH], BF16)) for k in range(2)]
        for nm in ["mu", "musq", "var", "rstd", "mr"]:
            t[nm] = e(nc.sbuf_tensor(f"l_{nm}", [1, TCH], F32))
        t["Rb"] = e(nc.sbuf_tensor("l_Rb", [128, TCH], BF16))
        t["Mb"] = e(nc.sbuf_tensor("l_Mb", [128, TCH], BF16))
        t["ps_r0"] = e(nc.psum_tensor("l_psr0", [1, TCH], F32))
        t["ps_r1"] = e(nc.psum_tensor("l_psr1", [1, TCH], F32))
        t["ps_bc"] = e(nc.psum_tensor("l_psbc", [128, TCH], F32))
        return t


    def _ln_init(p, t):
        c1 = p.op("v", lambda en: en.memset(t["onesc"][...], 1.0))
        c2 = p.op("v", lambda en: en.memset(t["onesf"][...], 1.0))
        return [c1, c2]


    def _ln_chunk(p, G, gw, t, cons, xap, h3ap):
        tprev = None
        for j in range(8):
            sb = j % 2
            tc = p.op("v", (lambda j, sb: lambda en: en.tensor_copy(
                t["xcc"][sb][...], xap(j)))(j, sb), wait=cons + gw(f"lxcc{sb}"))
            tq = p.op("s", (lambda sb: lambda en: en.activation(
                t["xsq"][sb][...], t["xcc"][sb][...], AF.Square))(sb),
                wait=[tc] + gw(f"lxsq{sb}"))
            wl = [tc] + cons + ([tprev] if tprev else [])
            if j == 0:
                wl += gw("lpsr")
            t1 = p.op("t", (lambda j, sb: lambda en: en.matmul(
                t["ps_r0"][0:1, :], t["onesc"][:, 0:1], t["xcc"][sb][...],
                start=(j == 0), stop=(j == 7)))(j, sb), wait=wl)
            t2 = p.op("t", (lambda j, sb: lambda en: en.matmul(
                t["ps_r1"][0:1, :], t["onesc"][:, 1:2], t["xsq"][sb][...],
                start=(j == 0), stop=(j == 7)))(j, sb), wait=[tq, t1])
            G[f"lxcc{sb}"] = [t1, tq]
            G[f"lxsq{sb}"] = [t2]
            tprev = t2
        t_mu = p.op("v", lambda en: en.tensor_scalar_mul(
            t["mu"][...], t["ps_r0"][0:1, :], 1.0 / D), wait=[tprev])
        t_m2 = p.op("v", lambda en: en.tensor_mul(
            t["musq"][...], t["mu"][...], t["mu"][...]), wait=[t_mu])
        t_var = p.op("v", lambda en: en.scalar_tensor_tensor(
            t["var"][...], t["ps_r1"][0:1, :], 1.0 / D, t["musq"][...],
            ALU.mult, ALU.subtract), wait=[t_m2, tprev])
        G["lpsr"] = [t_var]
        t_ve = p.op("v", lambda en: en.tensor_scalar_add(
            t["var"][...], t["var"][...], 1e-5), wait=[t_var])
        t_ln = p.op("s", lambda en: en.activation(
            t["rstd"][...], t["var"][...], AF.Ln), wait=[t_ve])
        t_rs = p.op("s", lambda en: en.activation(
            t["rstd"][...], t["rstd"][...], AF.Exp, scale=-0.5), wait=[t_ln])
        t_mr = p.op("v", lambda en: en.tensor_mul(
            t["mr"][...], t["mu"][...], t["rstd"][...]), wait=[t_rs, t_mu])
        tb = p.op("t", lambda en: en.matmul(
            t["ps_bc"][:, 0:512], t["onesf"][0:1, :], t["rstd"][0:1, 0:512],
            start=True, stop=True), wait=[t_rs] + cons + gw("lpsbc"))
        tR = p.op("s", lambda en: en.activation(
            t["Rb"][:, 0:512], t["ps_bc"][:, 0:512], AF.Identity),
            wait=[tb] + gw("lRb"))
        tb2 = p.op("t", lambda en: en.matmul(
            t["ps_bc"][:, 0:512], t["onesf"][0:1, :], t["mr"][0:1, 0:512],
            start=True, stop=True), wait=[t_mr, tR])
        tM = p.op("s", lambda en: en.activation(
            t["Mb"][:, 0:512], t["ps_bc"][:, 0:512], AF.Identity, scale=-1.0),
            wait=[tb2] + gw("lMb"))
        G["lpsbc"] = [tM]
        t_h = []
        for j in range(8):
            tm = p.op("v", (lambda j: lambda en: en.tensor_mul(
                h3ap(j), xap(j), t["Rb"][:, 0:512]))(j), wait=[tR] + gw("lh3"))
            ta = p.op("v", (lambda j: lambda en: en.tensor_add(
                h3ap(j), h3ap(j), t["Mb"][:, 0:512]))(j), wait=[tm, tM])
            t_h.append(ta)
        G["lRb"] = t_h
        G["lMb"] = t_h
        G["lh3"] = []
        return t_h


    def _lam_cols(p, G, gw, nc, e, nu2, th2, w_in):
        r = {}
        for nm in ["nuv", "th", "eneg", "s2", "sth", "cth", "lamR", "lamI",
                   "gam", "tmp"]:
            r[nm] = e(nc.sbuf_tensor(f"rc_{nm}", [128, 2], F32))
        t1 = p.op("s", lambda en: en.activation(r["nuv"][...], nu2[...], AF.Exp),
                  wait=w_in)
        t2 = p.op("s", lambda en: en.activation(r["th"][...], th2[...], AF.Exp),
                  wait=w_in)
        t3 = p.op("v", lambda en: en.tensor_single_scalar(
            r["tmp"][...], r["th"][...], float(np.pi), ALU.is_gt), wait=[t2])
        t4 = p.op("v", lambda en: en.scalar_tensor_tensor(
            r["th"][...], r["tmp"][...], float(-2.0 * np.pi), r["th"][...],
            ALU.mult, ALU.add), wait=[t3])
        t5 = p.op("s", lambda en: en.activation(r["sth"][...], r["th"][...], AF.Sin),
                  wait=[t4])
        t6 = p.op("s", lambda en: en.activation(r["s2"][...], r["th"][...], AF.Sin,
                                                scale=0.5), wait=[t4, t5])
        t7 = p.op("v", lambda en: en.tensor_mul(r["s2"][...], r["s2"][...],
                                                r["s2"][...]), wait=[t6])
        t8 = p.op("v", lambda en: en.tensor_scalar(
            r["cth"][...], r["s2"][...], -2.0, 1.0, ALU.mult, ALU.add), wait=[t7])
        t9 = p.op("s", lambda en: en.activation(r["eneg"][...], r["nuv"][...], AF.Exp,
                                                scale=-1.0), wait=[t1, t6])
        ta = p.op("v", lambda en: en.tensor_mul(r["lamR"][...], r["eneg"][...],
                                                r["cth"][...]), wait=[t9, t8])
        tb = p.op("v", lambda en: en.tensor_mul(r["lamI"][...], r["eneg"][...],
                                                r["sth"][...]), wait=[t9, t5])
        tc = p.op("v", lambda en: en.tensor_mul(r["tmp"][...], r["eneg"][...],
                                                r["eneg"][...]), wait=[t9])
        td = p.op("v", lambda en: en.tensor_scalar(
            r["tmp"][...], r["tmp"][...], -1.0, 1.0, ALU.mult, ALU.add), wait=[tc])
        te = p.op("s", lambda en: en.activation(r["gam"][...], r["tmp"][...], AF.Sqrt),
                  wait=[td])
        G["lam"] = [ta, tb, te]
        return r


    def _build_pow(p, G, gw, nc, e, r, sign, width, name):
        TR = e(nc.sbuf_tensor(f"{name}R", [128, 2 * width], F32))
        TI = e(nc.sbuf_tensor(f"{name}I", [128, 2 * width], F32))
        t1r = e(nc.sbuf_tensor(f"{name}t1", [128, width], F32))
        t2r = e(nc.sbuf_tensor(f"{name}t2", [128, width], F32))
        outs = []
        for tile in range(2):
            off = tile * width
            lamR = r["lamR"][:, tile:tile + 1]
            lamI = r["lamI"][:, tile:tile + 1]
            last = [p.op("v", (lambda off: lambda en: en.memset(
                TR[:, off:off + 1], 1.0))(off), wait=gw("lam")),
                p.op("v", (lambda off: lambda en: en.memset(
                    TI[:, off:off + 1], 0.0))(off), wait=gw("lam"))]
            if sign > 0:
                last.append(p.op("v", (lambda off, lamR: lambda en: en.tensor_copy(
                    TR[:, off + 1:off + 2], lamR))(off, lamR), wait=gw("lam")))
                last.append(p.op("v", (lambda off, lamI: lambda en: en.tensor_copy(
                    TI[:, off + 1:off + 2], lamI))(off, lamI), wait=gw("lam")))
            else:
                a1 = p.op("v", (lambda lamR: lambda en: en.tensor_mul(
                    t1r[:, 0:1], lamR, lamR))(lamR), wait=gw("lam") + gw(f"{name}tmp"))
                a2 = p.op("v", (lambda lamI: lambda en: en.tensor_mul(
                    t2r[:, 0:1], lamI, lamI))(lamI), wait=gw("lam") + gw(f"{name}tmp"))
                a3 = p.op("v", lambda en: en.tensor_add(
                    t1r[:, 0:1], t1r[:, 0:1], t2r[:, 0:1]), wait=[a1, a2])
                a4 = p.op("v", lambda en: en.reciprocal(
                    t1r[:, 1:2], t1r[:, 0:1]), wait=[a3])
                a5 = p.op("v", (lambda off, lamR: lambda en: en.tensor_mul(
                    TR[:, off + 1:off + 2], lamR, t1r[:, 1:2]))(off, lamR), wait=[a4])
                a6 = p.op("v", (lambda off, lamI: lambda en: en.tensor_mul(
                    TI[:, off + 1:off + 2], lamI, t1r[:, 1:2]))(off, lamI), wait=[a4])
                a7 = p.op("v", (lambda off: lambda en: en.tensor_scalar_mul(
                    TI[:, off + 1:off + 2], TI[:, off + 1:off + 2], -1.0))(off),
                    wait=[a6])
                last += [a5, a7]
            filled = 2
            while filled < width:
                s = min(filled - 1, width - filled)
                cR = TR[:, off + filled - 1:off + filled]
                cI = TI[:, off + filled - 1:off + filled]
                m1 = p.op("v", (lambda off, s, cR: lambda en: en.tensor_scalar_mul(
                    t1r[:, 0:s], TR[:, off + 1:off + 1 + s], cR))(off, s, cR), wait=last)
                m2 = p.op("v", (lambda off, s, cI: lambda en: en.tensor_scalar_mul(
                    t2r[:, 0:s], TI[:, off + 1:off + 1 + s], cI))(off, s, cI), wait=last)
                m3 = p.op("v", (lambda off, s, filled: lambda en: en.tensor_sub(
                    TR[:, off + filled:off + filled + s], t1r[:, 0:s],
                    t2r[:, 0:s]))(off, s, filled), wait=[m1, m2])
                m4 = p.op("v", (lambda off, s, cI: lambda en: en.tensor_scalar_mul(
                    t1r[:, 0:s], TR[:, off + 1:off + 1 + s], cI))(off, s, cI), wait=[m3])
                m5 = p.op("v", (lambda off, s, cR: lambda en: en.tensor_scalar_mul(
                    t2r[:, 0:s], TI[:, off + 1:off + 1 + s], cR))(off, s, cR), wait=[m4])
                m6 = p.op("v", (lambda off, s, filled: lambda en: en.tensor_add(
                    TI[:, off + filled:off + filled + s], t1r[:, 0:s],
                    t2r[:, 0:s]))(off, s, filled), wait=[m5])
                last = [m3, m6]
                filled += s
            outs += last
            G[f"{name}tmp"] = last
        G[name] = outs
        return TR, TI


    def build_lru1():
        nc = bass.Bass("TRN2", target_bir_lowering=False)
        x_d = nc.dram_tensor("x3", [D, NT], F32, kind="ExternalInput")
        nu_d = nc.dram_tensor("nu2", [128, 2], F32, kind="ExternalInput")
        th_d = nc.dram_tensor("th2", [128, 2], F32, kind="ExternalInput")
        br_d = nc.dram_tensor("BreT", [D, NST], F32, kind="ExternalInput")
        bi_d = nc.dram_tensor("BimT", [D, NST], F32, kind="ExternalInput")
        acc_d = nc.dram_tensor("acc", [NT, 512], F32, kind="ExternalOutput")

        triu_c = nc.inline_tensor(
            np.triu(np.ones((128, 128), np.float32)).astype(np.float32), name="triu")
        id_c = nc.inline_tensor(np.eye(128, dtype=np.float32), name="idf32")

        with ExitStack() as est:
            e = est.enter_context
            nu2 = e(nc.sbuf_tensor("nu2s", [128, 2], F32))
            th2 = e(nc.sbuf_tensor("th2s", [128, 2], F32))
            brb = e(nc.sbuf_tensor("brb", [128, 8 * NST], BF16))
            bib = e(nc.sbuf_tensor("bib", [128, 8 * NST], BF16))
            triu = e(nc.sbuf_tensor("trius", [128, 128], F32))
            idf = e(nc.sbuf_tensor("idfs", [128, 128], F32))
            xch = [e(nc.sbuf_tensor(f"xch{k}", [128, 8 * TCH], F32)) for k in range(2)]
            h3 = e(nc.sbuf_tensor("h3c", [128, 8 * TCH], BF16))
            LnT = [e(nc.sbuf_tensor(f"LnT{ri}", [128, NST], F32)) for ri in range(2)]
            scl = e(nc.sbuf_tensor("scl", [128, 512], F32))
            accs = [e(nc.sbuf_tensor(f"accs{k}", [128, 512], F32)) for k in range(2)]
            lnt = _ln_alloc(nc, e)
            ps_bu = e(nc.psum_tensor("ps_bu", [128, 512], F32))
            ps_ac = e(nc.psum_tensor("ps_ac", [128, 512], F32))
            ps_tr = e(nc.psum_tensor("ps_tr", [128, 256], F32))

            p = Prog(nc)
            G = {}

            def gw(n):
                return list(G.get(n, []))

            w_in = []
            for wi, (sb_, dr_) in enumerate([(nu2, nu_d), (th2, th_d),
                                             (triu, triu_c), (idf, id_c)]):
                w_in.append(p.op("y", (lambda sb_, dr_: lambda en: en.dma_start(
                    sb_[...], dr_[...]))(sb_, dr_), dma=True, stream=f"sm{wi}"))
            w_b = p.op("g", lambda en: en.dma_start(
                brb[...].rearrange("p (a w) -> p a w", a=8),
                br_d[...].rearrange("(a p) w -> p a w", p=128)), dma=True, stream="wb")
            w_b = p.op("g", lambda en: en.dma_start(
                bib[...].rearrange("p (a w) -> p a w", a=8),
                bi_d[...].rearrange("(a p) w -> p a w", p=128)), dma=True, stream="wb")
            cons = _ln_init(p, lnt)
            G["in_rows"] = [w_in[0], w_in[1]]
            r = _lam_cols(p, G, gw, nc, e, nu2, th2, [w_in[0], w_in[1]])
            TnR, TnI = _build_pow(p, G, gw, nc, e, r, -1, 128, "Ln")
            ttoks = []
            for ri, (T, LT) in enumerate([(TnR, LnT[0]), (TnI, LnT[1])]):
                for tile in range(2):
                    tt = p.op("t", (lambda T, tile: lambda en: en.transpose(
                        ps_tr[:, 0:128], T[:, tile * 128:(tile + 1) * 128],
                        idf[...]))(T, tile), wait=G["Ln"] + [w_in[3]] + gw("ps_tr"))
                    tv = p.op("s", (lambda LT, tile: lambda en: en.activation(
                        LT[:, tile * 128:(tile + 1) * 128], ps_tr[:, 0:128],
                        AF.Identity))(LT, tile), wait=[tt])
                    G["ps_tr"] = [tv]
                    ttoks.append(tv)
            G["LnT"] = ttoks

            for ch in range(4):
                b = ch % 2
                t_x = p.op("y", (lambda ch, b: lambda en: en.dma_start(
                    xch[b][...].rearrange("p (a w) -> p a w", a=8),
                    x_d[:, TCH * ch:TCH * (ch + 1)].rearrange("(a p) w -> p a w", p=128)))(ch, b),
                    wait=gw(f"xch{b}"), dma=True, stream=f"x{b}")
                t_h = _ln_chunk(
                    p, G, gw, lnt, cons + [t_x],
                    (lambda b: lambda j: xch[b][:, j * TCH:(j + 1) * TCH])(b),
                    lambda j: h3[:, j * TCH:(j + 1) * TCH])
                G[f"xch{b}"] = t_h
                readers = []
                for tb in range(4):
                    c_idx = 4 * ch + tb
                    last = None
                    for ri, wsb in enumerate([brb, bib]):
                        for kk in range(8):
                            wl = [t_h[kk], w_b] + ([last] if last else [])
                            if ri == 0 and kk == 0:
                                wl += gw("ps_bu")
                            last = p.op("t", (lambda ri, kk, tb, wsb: lambda en: en.matmul(
                                ps_bu[:, ri * 256:(ri + 1) * 256],
                                h3[:, kk * TCH + 128 * tb:kk * TCH + 128 * (tb + 1)],
                                wsb[:, kk * NST:(kk + 1) * NST],
                                start=(kk == 0), stop=(kk == 7)))(ri, kk, tb, wsb), wait=wl)
                    m1 = p.op("v", lambda en: en.tensor_mul(
                        scl[:, 0:256], ps_bu[:, 0:256], LnT[0][...]),
                        wait=[last] + G["LnT"] + gw("scl"))
                    m2 = p.op("v", lambda en: en.tensor_mul(
                        scl[:, 256:512], ps_bu[:, 256:512], LnT[1][...]), wait=[last, m1])
                    m3 = p.op("v", lambda en: en.tensor_sub(
                        scl[:, 0:256], scl[:, 0:256], scl[:, 256:512]), wait=[m2])
                    m4 = p.op("v", lambda en: en.tensor_mul(
                        scl[:, 256:512], ps_bu[:, 0:256], LnT[1][...]), wait=[m3, last])
                    m5 = p.op("v", lambda en: en.scalar_tensor_tensor(
                        scl[:, 256:512], ps_bu[:, 256:512], 1.0, scl[:, 256:512],
                        ALU.mult, ALU.bypass), wait=[m4]) if False else None
                    m5 = p.op("v", (lambda b: lambda en: en.tensor_mul(
                        accs[b][:, 0:256], ps_bu[:, 256:512], LnT[0][...]))(b),
                        wait=[m4] + gw(f"accs{b}"))
                    m6 = p.op("v", (lambda b: lambda en: en.tensor_add(
                        scl[:, 256:512], scl[:, 256:512], accs[b][:, 0:256]))(b),
                        wait=[m5])
                    G["ps_bu"] = [m4, m6, m1, m2]
                    tcs = p.op("t", lambda en: en.matmul(
                        ps_ac[:, 0:512], triu[...], scl[:, 0:512],
                        start=True, stop=True), wait=[m3, m6, w_in[2]] + gw("ps_ac"))
                    G["scl"] = [tcs]
                    tev = p.op("s", (lambda b: lambda en: en.activation(
                        accs[b][:, 0:512], ps_ac[:, 0:512], AF.Identity))(b),
                        wait=[tcs, m5])
                    G["ps_ac"] = [tev]
                    tst = p.op("y", (lambda c_idx, b: lambda en: en.dma_start(
                        acc_d[128 * c_idx:128 * (c_idx + 1), :], accs[b][...]))(c_idx, b),
                        wait=[tev], dma=True, stream=f"st{b}")
                    G[f"accs{b}"] = [tst]
                    readers.append(tst)
                G["lh3"] = readers + [G["ps_bu"][0]]

            p.op("v", lambda en: en.engine_nop(), wait=p.all_done())
            p.emit(est)
        return nc


    def build_lru2a():
        nc = bass.Bass("TRN2", target_bir_lowering=False)
        x_d = nc.dram_tensor("x3", [D, NT], F32, kind="ExternalInput")
        acc_d = nc.dram_tensor("acc", [NT, 512], F32, kind="ExternalInput")
        alr_d = nc.dram_tensor("aclTR", [128, 2 * 64], F32, kind="ExternalInput")
        ali_d = nc.dram_tensor("aclTI", [128, 2 * 64], F32, kind="ExternalInput")
        sel_d = nc.dram_tensor("Ssel", [64, 16], F32, kind="ExternalInput")
        nu_d = nc.dram_tensor("nu2", [128, 2], F32, kind="ExternalInput")
        th_d = nc.dram_tensor("th2", [128, 2], F32, kind="ExternalInput")
        cr_d = nc.dram_tensor("CreT", [NST, D], F32, kind="ExternalInput")
        ci_d = nc.dram_tensor("CimT", [NST, D], F32, kind="ExternalInput")
        dm_d = nc.dram_tensor("Dmat", [D, D], F32, kind="ExternalInput")
        y_d = nc.dram_tensor("yt", [D, NT], F32, kind="ExternalOutput")
        h3_d = nc.dram_tensor("h3o", [D, NT], F32, kind="ExternalOutput")

        id_c = nc.inline_tensor(np.eye(128, dtype=np.float32), name="idf32")
        idb_c = nc.inline_tensor(np.eye(128, dtype=ml_dtypes.bfloat16), name="idb16")
        ec_np = np.zeros((16, 16 * 128), np.float32)
        for c in range(16):
            ec_np[c, 128 * c:128 * (c + 1)] = 1.0
        ec_c = nc.inline_tensor(ec_np, name="ec16")

        with ExitStack() as est:
            e = est.enter_context
            nu2 = e(nc.sbuf_tensor("nu2s", [128, 2], F32))
            th2 = e(nc.sbuf_tensor("th2s", [128, 2], F32))
            alr = e(nc.sbuf_tensor("alrs", [128, 2 * 64], F32))
            ali = e(nc.sbuf_tensor("alis", [128, 2 * 64], F32))
            sel = e(nc.sbuf_tensor("sels", [64, 16], F32))
            idf = e(nc.sbuf_tensor("idfs", [128, 128], F32))
            idb = e(nc.sbuf_tensor("idbs", [128, 128], BF16))
            ec16 = e(nc.sbuf_tensor("ec16s", [16, 16 * 128], F32))
            crb = e(nc.sbuf_tensor("crb", [128, 2 * D], BF16))
            cib = e(nc.sbuf_tensor("cib", [128, 2 * D], BF16))
            dmb = e(nc.sbuf_tensor("dmb", [128, 8 * D], BF16))
            xch = [e(nc.sbuf_tensor(f"xch{k}", [128, 8 * TCH], F32)) for k in range(2)]
            h3f = e(nc.sbuf_tensor("h3f", [128, 8 * NT], BF16))
            LpT = [e(nc.sbuf_tensor(f"LpT{ri}", [128, NST], F32)) for ri in range(2)]
            lamA = [e(nc.sbuf_tensor(f"lamA{ri}", [128, 2 * 64], F32)) for ri in range(2)]
            acp = e(nc.sbuf_tensor("acp", [64, 512], F32))
            mycar = e(nc.sbuf_tensor("mycar", [16, 512], F32))
            accb = [e(nc.sbuf_tensor(f"accb{k}", [128, 512], F32)) for k in range(2)]
            accp = e(nc.sbuf_tensor("accp", [128, 512], F32))
            stw = e(nc.sbuf_tensor("stw", [128, 512], BF16))
            stT = [e(nc.sbuf_tensor(f"stT{ri}", [128, 2 * NT], BF16)) for ri in range(2)]
            ytb = [e(nc.sbuf_tensor(f"ytb{k}", [128, 512], F32)) for k in range(2)]
            lnt = _ln_alloc(nc, e)
            ps_a = e(nc.psum_tensor("ps_a2", [128, 512], F32))
            ps_tr = e(nc.psum_tensor("ps_tr2", [128, 256], F32))
            ps_y = e(nc.psum_tensor("ps_y", [128, 512], F32))

            p = Prog(nc)
            G = {}

            def gw(n):
                return list(G.get(n, []))

            w_in = []
            for wi, (sb_, dr_) in enumerate([(nu2, nu_d), (th2, th_d), (alr, alr_d),
                                             (ali, ali_d), (sel, sel_d), (idf, id_c),
                                             (idb, idb_c), (ec16, ec_c)]):
                w_in.append(p.op("y", (lambda sb_, dr_: lambda en: en.dma_start(
                    sb_[...], dr_[...]))(sb_, dr_), dma=True, stream=f"sm{wi}"))
            w_c = []
            for wi, (sb_, dr_) in enumerate([(crb, cr_d), (cib, ci_d)]):
                w_c.append(p.op("g", (lambda sb_, dr_: lambda en: en.dma_start(
                    sb_[...].rearrange("p (a w) -> p a w", a=2),
                    dr_[...].rearrange("(a p) w -> p a w", p=128)))(sb_, dr_),
                    dma=True, stream=f"wc{wi}"))
            w_dm = p.op("g", lambda en: en.dma_start(
                dmb[...].rearrange("p (a w) -> p a w", a=8),
                dm_d[...].rearrange("(a p) w -> p a w", p=128)), dma=True, stream="wdm")
            cons = _ln_init(p, lnt)

            r = _lam_cols(p, G, gw, nc, e, nu2, th2, [w_in[0], w_in[1]])
            TpR, TpI = _build_pow(p, G, gw, nc, e, r, +1, 129, "Lp")
            tcw = []
            for tile in range(2):
                t1 = p.op("v", (lambda tile: lambda en: en.tensor_scalar_mul(
                    crb[:, tile * D:(tile + 1) * D], crb[:, tile * D:(tile + 1) * D],
                    r["gam"][:, tile:tile + 1]))(tile), wait=[w_c[0]] + G["lam"])
                t2 = p.op("v", (lambda tile: lambda en: en.tensor_scalar(
                    cib[:, tile * D:(tile + 1) * D], cib[:, tile * D:(tile + 1) * D],
                    r["gam"][:, tile:tile + 1], -1.0, ALU.mult, ALU.mult))(tile),
                    wait=[w_c[1]] + G["lam"])
                tcw += [t1, t2]
            lcs = []
            lR = e(nc.sbuf_tensor("lR", [128, 2 * 64], F32))
            lI = e(nc.sbuf_tensor("lI", [128, 2 * 64], F32))
            t1r = e(nc.sbuf_tensor("ch_t1", [128, 2 * 64], F32))
            t2r = e(nc.sbuf_tensor("ch_t2", [128, 2 * 64], F32))
            for tile in range(2):
                o = tile * 64
                pb = tile * 129
                l7R = TpR[:, pb + 127:pb + 128]
                l7I = TpI[:, pb + 127:pb + 128]
                wl = [w_in[2], w_in[3]] + G["Lp"]
                m1 = p.op("v", (lambda o, l7R: lambda en: en.tensor_scalar_mul(
                    t1r[:, o:o + 64], alr[:, o:o + 64], l7R))(o, l7R), wait=wl)
                m2 = p.op("v", (lambda o, l7I: lambda en: en.tensor_scalar_mul(
                    t2r[:, o:o + 64], ali[:, o:o + 64], l7I))(o, l7I), wait=wl)
                m3 = p.op("v", (lambda o: lambda en: en.tensor_sub(
                    lR[:, o:o + 64], t1r[:, o:o + 64], t2r[:, o:o + 64]))(o),
                    wait=[m1, m2])
                m4 = p.op("v", (lambda o, l7I: lambda en: en.tensor_scalar_mul(
                    t1r[:, o:o + 64], alr[:, o:o + 64], l7I))(o, l7I), wait=[m3])
                m5 = p.op("v", (lambda o, l7R: lambda en: en.tensor_scalar_mul(
                    t2r[:, o:o + 64], ali[:, o:o + 64], l7R))(o, l7R), wait=[m3])
                m6 = p.op("v", (lambda o: lambda en: en.tensor_add(
                    lI[:, o:o + 64], t1r[:, o:o + 64], t2r[:, o:o + 64]))(o),
                    wait=[m4, m5])
                lcs += [m3, m6]
            AR = e(nc.sbuf_tensor("AR", [128, 2 * 64], F32))
            AI = e(nc.sbuf_tensor("AI", [128, 2 * 64], F32))
            za = p.op("v", lambda en: en.memset(AR[...], 0.0), wait=lcs)
            zb = p.op("v", lambda en: en.memset(AI[...], 0.0), wait=lcs)
            pwR = e(nc.sbuf_tensor("pwR", [128, 8], F32))
            pwI = e(nc.sbuf_tensor("pwI", [128, 8], F32))
            qs_ = []
            for tile in range(2):
                po = tile * 4
                pb = tile * 129
                lWR = TpR[:, pb + 128:pb + 129]
                lWI = TpI[:, pb + 128:pb + 129]
                q0 = p.op("v", (lambda po: lambda en: en.memset(
                    pwR[:, po:po + 1], 1.0))(po))
                q1 = p.op("v", (lambda po: lambda en: en.memset(
                    pwI[:, po:po + 1], 0.0))(po))
                q2 = p.op("v", (lambda po, lWR: lambda en: en.tensor_copy(
                    pwR[:, po + 1:po + 2], lWR))(po, lWR), wait=G["Lp"])
                q3 = p.op("v", (lambda po, lWI: lambda en: en.tensor_copy(
                    pwI[:, po + 1:po + 2], lWI))(po, lWI), wait=G["Lp"])
                qt = [q0, q1, q2, q3]
                for k in [2, 3]:
                    a1 = p.op("v", (lambda po, k, lWR: lambda en: en.tensor_scalar_mul(
                        t1r[:, 0:1], pwR[:, po + k - 1:po + k], lWR))(po, k, lWR),
                        wait=qt + lcs)
                    a2 = p.op("v", (lambda po, k, lWI: lambda en: en.tensor_scalar_mul(
                        t2r[:, 0:1], pwI[:, po + k - 1:po + k], lWI))(po, k, lWI),
                        wait=qt + lcs)
                    a3 = p.op("v", (lambda po, k: lambda en: en.tensor_sub(
                        pwR[:, po + k:po + k + 1], t1r[:, 0:1], t2r[:, 0:1]))(po, k),
                        wait=[a1, a2])
                    a4 = p.op("v", (lambda po, k, lWI: lambda en: en.tensor_scalar_mul(
                        t1r[:, 0:1], pwR[:, po + k - 1:po + k], lWI))(po, k, lWI),
                        wait=[a3])
                    a5 = p.op("v", (lambda po, k, lWR: lambda en: en.tensor_scalar_mul(
                        t2r[:, 0:1], pwI[:, po + k - 1:po + k], lWR))(po, k, lWR),
                        wait=[a3])
                    a6 = p.op("v", (lambda po, k: lambda en: en.tensor_add(
                        pwI[:, po + k:po + k + 1], t1r[:, 0:1], t2r[:, 0:1]))(po, k),
                        wait=[a4, a5])
                    qt = [a3, a6]
                qs_ += qt
            chain_t = [za, zb] + qs_
            for k in range(1, 5):
                for tile in range(2):
                    o = tile * 64
                    po = tile * 4
                    cnt = 64 - k
                    b1 = p.op("v", (lambda o, po, k, cnt: lambda en: en.tensor_scalar_mul(
                        t1r[:, 0:cnt], lR[:, o:o + cnt], pwR[:, po + k - 1:po + k]))(o, po, k, cnt),
                        wait=chain_t)
                    b2 = p.op("v", (lambda o, po, k, cnt: lambda en: en.tensor_scalar_mul(
                        t2r[:, 0:cnt], lI[:, o:o + cnt], pwI[:, po + k - 1:po + k]))(o, po, k, cnt),
                        wait=chain_t)
                    b3 = p.op("v", (lambda cnt: lambda en: en.tensor_sub(
                        t1r[:, 0:cnt], t1r[:, 0:cnt], t2r[:, 0:cnt]))(cnt),
                        wait=[b1, b2])
                    b4 = p.op("v", (lambda o, k, cnt: lambda en: en.tensor_add(
                        AR[:, o + k:o + 64], AR[:, o + k:o + 64], t1r[:, 0:cnt]))(o, k, cnt),
                        wait=[b3])
                    b5 = p.op("v", (lambda o, po, k, cnt: lambda en: en.tensor_scalar_mul(
                        t1r[:, 0:cnt], lR[:, o:o + cnt], pwI[:, po + k - 1:po + k]))(o, po, k, cnt),
                        wait=[b4])
                    b6 = p.op("v", (lambda o, po, k, cnt: lambda en: en.tensor_scalar_mul(
                        t2r[:, 0:cnt], lI[:, o:o + cnt], pwR[:, po + k - 1:po + k]))(o, po, k, cnt),
                        wait=[b4])
                    b7 = p.op("v", (lambda cnt: lambda en: en.tensor_add(
                        t1r[:, 0:cnt], t1r[:, 0:cnt], t2r[:, 0:cnt]))(cnt),
                        wait=[b5, b6])
                    b8 = p.op("v", (lambda o, k, cnt: lambda en: en.tensor_add(
                        AI[:, o + k:o + 64], AI[:, o + k:o + 64], t1r[:, 0:cnt]))(o, k, cnt),
                        wait=[b7])
                    chain_t = [b4, b8]
            la_t = []
            for tile in range(2):
                o = tile * 64
                lamRc = r["lamR"][:, tile:tile + 1]
                lamIc = r["lamI"][:, tile:tile + 1]
                c1_ = p.op("v", (lambda o, lamRc: lambda en: en.tensor_scalar_mul(
                    t1r[:, 0:64], AR[:, o:o + 64], lamRc))(o, lamRc),
                    wait=chain_t + la_t)
                c2_ = p.op("v", (lambda o, lamIc: lambda en: en.tensor_scalar_mul(
                    t2r[:, 0:64], AI[:, o:o + 64], lamIc))(o, lamIc),
                    wait=chain_t + la_t)
                c3_ = p.op("v", (lambda o: lambda en: en.tensor_sub(
                    lamA[0][:, o:o + 64], t1r[:, 0:64], t2r[:, 0:64]))(o), wait=[c1_, c2_])
                c4_ = p.op("v", (lambda o, lamIc: lambda en: en.tensor_scalar_mul(
                    t1r[:, 0:64], AR[:, o:o + 64], lamIc))(o, lamIc), wait=[c3_])
                c5_ = p.op("v", (lambda o, lamRc: lambda en: en.tensor_scalar_mul(
                    t2r[:, 0:64], AI[:, o:o + 64], lamRc))(o, lamRc), wait=[c3_])
                c6_ = p.op("v", (lambda o: lambda en: en.tensor_add(
                    lamA[1][:, o:o + 64], t1r[:, 0:64], t2r[:, 0:64]))(o), wait=[c4_, c5_])
                la_t += [c3_, c6_]
            tr_t = []
            for ri in range(2):
                for tile in range(2):
                    tt = p.op("t", (lambda ri, tile: lambda en: en.transpose(
                        ps_tr[0:64, 0:128],
                        lamA[ri][...].rearrange("p (a w) -> p a w", a=2)[:, tile, :],
                        idf[...]))(ri, tile), wait=la_t + [w_in[5]] + gw("ps_tr"))
                    tv = p.op("s", (lambda ri, tile: lambda en: en.activation(
                        acp[0:64, (2 * ri + tile) * 128:(2 * ri + tile + 1) * 128],
                        ps_tr[0:64, 0:128], AF.Identity))(ri, tile), wait=[tt])
                    G["ps_tr"] = [tv]
                    tr_t.append(tv)
            tsel = p.op("t", lambda en: en.matmul(
                ps_a[0:16, 0:512], sel[...], acp[0:64, 0:512],
                start=True, stop=True), wait=tr_t + [w_in[4]] + gw("ps_a"))
            tmc = p.op("s", lambda en: en.activation(
                mycar[...], ps_a[0:16, 0:512], AF.Identity), wait=[tsel])
            G["ps_a"] = [tmc]
            lp_t = []
            for (TT_, dstl) in [(TpR, LpT[0]), (TpI, LpT[1])]:
                for tile in range(2):
                    tt = p.op("t", (lambda TT_, tile: lambda en: en.transpose(
                        ps_tr[:, 0:128],
                        TT_[:, tile * 129:tile * 129 + 128],
                        idf[...]))(TT_, tile), wait=G["Lp"] + [w_in[5]] + gw("ps_tr"))
                    tv = p.op("s", (lambda dstl, tile: lambda en: en.activation(
                        dstl[:, tile * 128:(tile + 1) * 128], ps_tr[:, 0:128],
                        AF.Identity))(dstl, tile), wait=[tt])
                    G["ps_tr"] = [tv]
                    lp_t.append(tv)

            h3done = []
            for ch in range(4):
                b = ch % 2
                t_x = p.op("y", (lambda ch, b: lambda en: en.dma_start(
                    xch[b][...].rearrange("p (a w) -> p a w", a=8),
                    x_d[:, TCH * ch:TCH * (ch + 1)].rearrange("(a p) w -> p a w", p=128)))(ch, b),
                    wait=gw(f"xch{b}"), dma=True, stream=f"x{b}")
                t_h = _ln_chunk(
                    p, G, gw, lnt, cons + [t_x],
                    (lambda b: lambda j: xch[b][:, j * TCH:(j + 1) * TCH])(b),
                    (lambda ch: lambda j: h3f[:, j * NT + TCH * ch:j * NT + TCH * (ch + 1)])(ch))
                G[f"xch{b}"] = t_h
                G["lh3"] = []
                h3done += t_h
                for j in range(8):
                    p.op("g", (lambda j, ch: lambda en: en.dma_start(
                        h3_d[128 * j:128 * (j + 1), TCH * ch:TCH * (ch + 1)],
                        h3f[:, j * NT + TCH * ch:j * NT + TCH * (ch + 1)]))(j, ch),
                        wait=[t_h[j]], dma=True, stream=f"h3s{ch % 2}")

            p.phase_barrier()
            st_t = []
            for c in range(16):
                b = c % 2
                t_ac = p.op("y", (lambda c, b: lambda en: en.dma_start(
                    accb[b][...], acc_d[128 * c:128 * (c + 1), :]))(c, b),
                    wait=gw(f"accb{b}"), dma=True, stream=f"ac{b}")
                tbc = p.op("t", (lambda c: lambda en: en.matmul(
                    ps_a[:, 0:512], ec16[:, 128 * c:128 * (c + 1)], mycar[...],
                    start=True, stop=True))(c), wait=[tmc, w_in[7]] + gw("ps_a"))
                ta1 = p.op("v", (lambda b: lambda en: en.tensor_add(
                    accp[:, 0:512], accb[b][...], ps_a[:, 0:512]))(b),
                    wait=[t_ac, tbc] + gw("accp"))
                G["ps_a"] = [ta1]
                s1 = p.op("v", (lambda b: lambda en: en.tensor_mul(
                    ytb[b][:, 0:256], accp[:, 0:256], LpT[0][...]))(b),
                    wait=[ta1] + lp_t + gw(f"ytb{b}"))
                s2 = p.op("v", (lambda b: lambda en: en.tensor_mul(
                    ytb[b][:, 256:512], accp[:, 256:512], LpT[1][...]))(b),
                    wait=[ta1] + lp_t)
                s3 = p.op("v", (lambda b: lambda en: en.tensor_sub(
                    ytb[b][:, 0:256], ytb[b][:, 0:256], ytb[b][:, 256:512]))(b),
                    wait=[s2, s1])
                sR = p.op("v", (lambda b: lambda en: en.tensor_copy(
                    stw[:, 0:256], ytb[b][:, 0:256]))(b), wait=[s3] + gw("stw"))
                s4 = p.op("v", (lambda b: lambda en: en.tensor_mul(
                    ytb[b][:, 0:256], accp[:, 0:256], LpT[1][...]))(b), wait=[sR])
                s5 = p.op("v", (lambda b: lambda en: en.tensor_mul(
                    ytb[b][:, 256:512], accp[:, 256:512], LpT[0][...]))(b), wait=[s3])
                s6 = p.op("v", (lambda b: lambda en: en.tensor_add(
                    ytb[b][:, 0:256], ytb[b][:, 0:256], ytb[b][:, 256:512]))(b),
                    wait=[s4, s5])
                sI = p.op("v", (lambda b: lambda en: en.tensor_copy(
                    stw[:, 256:512], ytb[b][:, 0:256]))(b), wait=[s6])
                G["accp"] = [s4, s5, sR]
                G[f"accb{b}"] = [ta1]
                G[f"ytb{b}"] = [sR, sI, s6]
                my_tr = []
                for ri in range(2):
                    for tile in range(2):
                        tt = p.op("t", (lambda ri, tile: lambda en: en.transpose(
                            ps_tr[...].bitcast(BF16)[:, 0:128],
                            stw[:, ri * 256 + tile * 128:ri * 256 + (tile + 1) * 128],
                            idb[...]))(ri, tile),
                            wait=[sR, sI, w_in[6]] + gw("ps_tr"))
                        tv = p.op("s", (lambda ri, tile, c: lambda en: en.activation(
                            stT[ri][:, tile * NT + 128 * c:tile * NT + 128 * (c + 1)],
                            ps_tr[...].bitcast(BF16)[:, 0:128], AF.Identity))(ri, tile, c),
                            wait=[tt])
                        G["ps_tr"] = [tv]
                        my_tr.append(tv)
                st_t += my_tr
                G["stw"] = my_tr

            p.phase_barrier()
            for ch in range(4):
                b = ch % 2
                for m in range(8):
                    last = None
                    for ri in range(2):
                        for tile in range(2):
                            csb = crb if ri == 0 else cib
                            wl = [last] if last else gw("ps_y")
                            last = p.op("t", (lambda m, ri, tile, ch, csb: lambda en: en.matmul(
                                ps_y[:, 0:512],
                                csb[:, tile * D + 128 * m:tile * D + 128 * (m + 1)],
                                stT[ri][:, tile * NT + TCH * ch:tile * NT + TCH * (ch + 1)],
                                start=(ri == 0 and tile == 0), stop=False))(m, ri, tile, ch, csb),
                                wait=wl)
                    for kk in range(8):
                        last = p.op("t", (lambda m, kk, ch: lambda en: en.matmul(
                            ps_y[:, 0:512],
                            dmb[:, kk * D + 128 * m:kk * D + 128 * (m + 1)],
                            h3f[:, kk * NT + TCH * ch:kk * NT + TCH * (ch + 1)],
                            start=False, stop=(kk == 7)))(m, kk, ch),
                            wait=[last])
                    tev = p.op("v", (lambda b: lambda en: en.tensor_copy(
                        ytb[b][...], ps_y[:, 0:512]))(b), wait=[last] + gw(f"ytb{b}"))
                    G["ps_y"] = [tev]
                    tst = p.op("y", (lambda m, ch, b: lambda en: en.dma_start(
                        y_d[128 * m:128 * (m + 1), TCH * ch:TCH * (ch + 1)],
                        ytb[b][...]))(m, ch, b), wait=[tev], dma=True, stream=f"ys{b}")
                    G[f"ytb{b}"] = [tst]

            p.op("v", lambda en: en.engine_nop(), wait=p.all_done())
            p.emit(est)
        return nc


    def build_lru2b():
        nc = bass.Bass("TRN2", target_bir_lowering=False)
        y_di = nc.dram_tensor("yt", [D, NT], F32, kind="ExternalInput")
        w1_d = nc.dram_tensor("W1", [D, 4096], F32, kind="ExternalInput")
        w2_d = nc.dram_tensor("W2", [4096, D], F32, kind="ExternalInput")
        x_d = nc.dram_tensor("x3", [D, NT], F32, kind="ExternalInput")
        h3_d = nc.dram_tensor("h3o", [D, NT], F32, kind="ExternalInput")
        rs_d = nc.dram_tensor("rescol", [128, 1], F32, kind="ExternalInput")
        o_d = nc.dram_tensor("x4p", [D, NT], F32, kind="ExternalOutput")

        NMB = 32
        TC2 = 1024
        C2 = 0.044715
        C1 = 0.7978845608028654

        with ExitStack() as est:
            e = est.enter_context
            yb = [e(nc.sbuf_tensor(f"yb{i}", [128, TC2], BF16)) for i in range(8)]
            hbuf = e(nc.sbuf_tensor("hbuf", [128, NMB * TC2], BF16))
            hb_s = e(nc.sbuf_tensor("hb_s", [128, TC2], BF16))
            hs_s = e(nc.sbuf_tensor("hs_s", [128, TC2], BF16))
            arg_s = e(nc.sbuf_tensor("arg_s", [128, TC2], BF16))
            th_s = e(nc.sbuf_tensor("th_s", [128, TC2], BF16))
            w1b = [e(nc.sbuf_tensor(f"w1b{k}", [128, 8 * 128], BF16)) for k in range(2)]
            w2b = [e(nc.sbuf_tensor(f"w2b{k}", [128, NMB * 128], BF16)) for k in range(2)]
            rsc = e(nc.sbuf_tensor("rsc", [128, 1], F32))
            sigc = e(nc.sbuf_tensor("sigc", [128, 1], F32))
            xrb = [e(nc.sbuf_tensor(f"xrb{k}", [128, TC2], F32)) for k in range(2)]
            hrb = [e(nc.sbuf_tensor(f"hrb{k}", [128, TC2], F32)) for k in range(2)]
            ob = [e(nc.sbuf_tensor(f"ob{k}", [128, TC2], F32)) for k in range(2)]
            ps_A = e(nc.psum_tensor("ps_A", [128, TC2], F32))
            ps_B = e(nc.psum_tensor("ps_B", [128, TC2], F32))

            p = Prog(nc)
            G = {}

            def gw(n):
                return list(G.get(n, []))

            w_rs = p.op("y", lambda en: en.dma_start(rsc[...], rs_d[...]),
                        dma=True, stream="sm")
            t_sg = p.op("s", lambda en: en.activation(
                sigc[...], rsc[...], AF.Sigmoid), wait=[w_rs])
            t_sg2 = p.op("v", lambda en: en.tensor_scalar_mul(
                sigc[...], sigc[...], 0.5), wait=[t_sg])

            for ci in range(NT // TC2):
                c0 = ci * TC2
                t_y = []
                for j in range(8):
                    t_y.append(p.op("g", (lambda j, c0: lambda en: en.dma_start(
                        yb[j][...], y_di[128 * j:128 * (j + 1), c0:c0 + TC2]))(j, c0),
                        wait=gw(f"yb{j}"), dma=True, stream=f"yl{j}"))
                t_h = {}
                for m in range(NMB):
                    b = m % 2
                    wwait = [t_h[m - 2]] if m >= 2 else gw("w1pv")
                    t_w1 = p.op("g", (lambda m, b: lambda en: en.dma_start(
                        w1b[b][...].rearrange("p (a w) -> p a w", a=8),
                        w1_d[:, 128 * m:128 * (m + 1)].rearrange("(a p) w -> p a w", p=128)))(m, b),
                        wait=wwait, dma=True, stream=f"w1p{b}")
                    psw = [t_h[m - 1]] if m >= 1 else gw("psA_prev")
                    last = None
                    for q in range(TC2 // 512):
                        for kk in range(8):
                            wl = [t_w1, t_y[kk]] + psw + ([last] if last else [])
                            last = p.op("t", (lambda b, q, kk: lambda en: en.matmul(
                                ps_A[:, q * 512:(q + 1) * 512],
                                w1b[b][:, kk * 128:(kk + 1) * 128],
                                yb[kk][:, q * 512:(q + 1) * 512],
                                start=(kk == 0), stop=(kk == 7)))(b, q, kk), wait=wl)
                    te1 = p.op("s", (lambda: lambda en: en.activation(
                        hb_s[...], ps_A[...], AF.Identity))(), wait=[last] + gw("hb_s"))
                    te2 = p.op("s", (lambda: lambda en: en.activation(
                        hs_s[...], ps_A[...], AF.Square))(), wait=[last, te1])
                    tv1 = p.op("v", (lambda: lambda en: en.tensor_mul(
                        arg_s[...], hs_s[...], hb_s[...]))(), wait=[te1, te2] + gw("arg_s"))
                    tv2 = p.op("v", (lambda: lambda en: en.scalar_tensor_tensor(
                        arg_s[...], arg_s[...], C2, hb_s[...], ALU.mult, ALU.add))(),
                        wait=[tv1])
                    te3 = p.op("s", (lambda: lambda en: en.activation(
                        th_s[...], arg_s[...], AF.Tanh, scale=C1))(), wait=[tv2] + gw("th_s"))
                    tv3 = p.op("v", (lambda: lambda en: en.tensor_scalar_add(
                        th_s[...], th_s[...], 1.0))(), wait=[te3])
                    t_h[m] = p.op("v", (lambda m: lambda en: en.tensor_mul(
                        hbuf[:, m * TC2:(m + 1) * TC2], hb_s[...], th_s[...]))(m),
                        wait=[tv3, te1] + gw(f"hbuf{m}"))
                    G["hb_s"] = [t_h[m]]
                    G["arg_s"] = [tv2]
                    G["th_s"] = [t_h[m]]
                G["w1pv"] = [t_h[NMB - 2], t_h[NMB - 1]]
                G["psA_prev"] = [t_h[NMB - 1]]

                for n in range(8):
                    b = n % 2
                    wwait = gw(f"w2rd{b}")
                    t_w2 = p.op("g", (lambda n, b: lambda en: en.dma_start(
                        w2b[b][...].rearrange("p (a w) -> p a w", a=NMB),
                        w2_d[:, 128 * n:128 * (n + 1)].rearrange("(a p) w -> p a w", p=128)))(n, b),
                        wait=wwait, dma=True, stream=f"w2p{b}")
                    t_xr = p.op("y", (lambda n, c0, b: lambda en: en.dma_start(
                        xrb[b][...], x_d[128 * n:128 * (n + 1), c0:c0 + TC2]))(n, c0, b),
                        wait=gw(f"xrb{b}"), dma=True, stream=f"xr{b}")
                    t_hr = p.op("y", (lambda n, c0, b: lambda en: en.dma_start(
                        hrb[b][...], h3_d[128 * n:128 * (n + 1), c0:c0 + TC2]))(n, c0, b),
                        wait=gw(f"hrb{b}"), dma=True, stream=f"hr{b}")
                    last = None
                    rd = []
                    for q in range(TC2 // 512):
                        for m in range(NMB):
                            wl = [t_w2, t_h[m]] + ([last] if last else [])
                            if n >= 1 and q == 0 and m == 0:
                                wl += gw("psB_rd")
                            last = p.op("t", (lambda b, q, m: lambda en: en.matmul(
                                ps_B[:, q * 512:(q + 1) * 512],
                                w2b[b][:, m * 128:(m + 1) * 128],
                                hbuf[:, m * TC2 + q * 512:m * TC2 + (q + 1) * 512],
                                start=(m == 0), stop=(m == NMB - 1)))(b, q, m), wait=wl)
                    G[f"w2rd{b}"] = [last]
                    tz = p.op("v", (lambda b: lambda en: en.tensor_scalar_mul(
                        ob[b][...], ps_B[...], sigc[...]))(b),
                        wait=[last, t_sg2] + gw(f"ob{b}"))
                    G["psB_rd"] = [tz]
                    ta1 = p.op("v", (lambda b: lambda en: en.tensor_add(
                        ob[b][...], ob[b][...], xrb[b][...]))(b), wait=[tz, t_xr])
                    ta2 = p.op("v", (lambda b: lambda en: en.tensor_add(
                        ob[b][...], ob[b][...], hrb[b][...]))(b), wait=[ta1, t_hr])
                    G[f"xrb{b}"] = [ta1]
                    G[f"hrb{b}"] = [ta2]
                    tst = p.op("y", (lambda n, c0, b: lambda en: en.dma_start(
                        o_d[128 * n:128 * (n + 1), c0:c0 + TC2], ob[b][...]))(n, c0, b),
                        wait=[ta2], dma=True, stream=f"os{b}")
                    G[f"ob{b}"] = [tst]
                for j in range(8):
                    G[f"yb{j}"] = [t_h[NMB - 1]]
                for m in range(NMB):
                    G[f"hbuf{m}"] = [G["psB_rd"][0]]

            p.op("v", lambda en: en.engine_nop(), wait=p.all_done())
            p.emit(est)
        return nc
    return build_lru1, build_lru2a, build_lru2b, _ln_alloc, _ln_init, _ln_chunk

build_lru1, build_lru2a, build_lru2b, _ln_alloc, _ln_init, _ln_chunk = _ns_lru()

def _ns_proj():
    D = 1024
    NT = 2048
    TCH = 512


    def build_proj():
        nc = bass.Bass("TRN2", target_bir_lowering=False)
        x_d = nc.dram_tensor("x0", [D, NT], F32, kind="ExternalInput")
        a_d = nc.dram_tensor("aog", [D, NT], F32, kind="ExternalInput")
        w_d = nc.dram_tensor("Wp", [D, D], F32, kind="ExternalInput")
        o_d = nc.dram_tensor("x1", [D, NT], F32, kind="ExternalOutput")

        with ExitStack() as est:
            e = est.enter_context
            wpb = e(nc.sbuf_tensor("wpb", [128, 8 * D], BF16))
            ab = [e(nc.sbuf_tensor(f"ab{k}", [128, 8 * TCH], BF16)) for k in range(2)]
            xb = [e(nc.sbuf_tensor(f"xb{k}", [128, TCH], F32)) for k in range(2)]
            ob = [e(nc.sbuf_tensor(f"ob{k}", [128, TCH], F32)) for k in range(2)]
            ps = e(nc.psum_tensor("ps_p", [128, TCH], F32))

            p = Prog(nc)
            G = {}

            def gw(n):
                return list(G.get(n, []))

            w_w = p.op("g", lambda en: en.dma_start(
                wpb[...].rearrange("p (a w) -> p a w", a=8),
                w_d[...].rearrange("(a p) w -> p a w", p=128)), dma=True, stream="wp")
            for ch in range(4):
                b = ch % 2
                t_a = p.op("g", (lambda ch, b: lambda en: en.dma_start(
                    ab[b][...].rearrange("p (a w) -> p a w", a=8),
                    a_d[:, TCH * ch:TCH * (ch + 1)].rearrange("(a p) w -> p a w", p=128)))(ch, b),
                    wait=gw(f"ab{b}"), dma=True, stream=f"al{b}")
                rds = []
                for m in range(8):
                    mb = m % 2
                    t_x = p.op("y", (lambda ch, m, mb: lambda en: en.dma_start(
                        xb[mb][...], x_d[128 * m:128 * (m + 1), TCH * ch:TCH * (ch + 1)]))(ch, m, mb),
                        wait=gw(f"xb{mb}"), dma=True, stream=f"xl{mb}")
                    last = None
                    for kk in range(8):
                        wl = [t_a, w_w] + ([last] if last else [])
                        if kk == 0:
                            wl += gw("ps")
                        last = p.op("t", (lambda b, m, kk: lambda en: en.matmul(
                            ps[...], wpb[:, kk * D + 128 * m:kk * D + 128 * (m + 1)],
                            ab[b][:, kk * TCH:(kk + 1) * TCH],
                            start=(kk == 0), stop=(kk == 7)))(b, m, kk), wait=wl)
                    rds.append(last)
                    tad = p.op("v", (lambda mb: lambda en: en.tensor_add(
                        ob[mb][...], ps[...], xb[mb][...]))(mb),
                        wait=[last, t_x] + gw(f"ob{mb}"))
                    G["ps"] = [tad]
                    G[f"xb{mb}"] = [tad]
                    tst = p.op("y", (lambda ch, m, mb: lambda en: en.dma_start(
                        o_d[128 * m:128 * (m + 1), TCH * ch:TCH * (ch + 1)],
                        ob[mb][...]))(ch, m, mb), wait=[tad], dma=True, stream=f"os{mb}")
                    G[f"ob{mb}"] = [tst]
                G[f"ab{b}"] = rds

            p.op("v", lambda en: en.engine_nop(), wait=p.all_done())
            p.emit(est)
        return nc


    def build_addln():
        nc = bass.Bass("TRN2", target_bir_lowering=False)
        xa_d = nc.dram_tensor("xa", [D, NT], F32, kind="ExternalInput")
        xb_d = nc.dram_tensor("xb", [D, NT], F32, kind="ExternalInput")
        s_d = nc.dram_tensor("s", [D, NT], F32, kind="ExternalOutput")
        l_d = nc.dram_tensor("sln", [D, NT], F32, kind="ExternalOutput")

        with ExitStack() as est:
            e = est.enter_context
            xab = [e(nc.sbuf_tensor(f"xab{k}", [128, 8 * TCH], F32)) for k in range(2)]
            xbb = [e(nc.sbuf_tensor(f"xbb{k}", [128, 8 * TCH], F32)) for k in range(2)]
            snb = [e(nc.sbuf_tensor(f"snb{k}", [128, 8 * TCH], F32)) for k in range(2)]
            lnb = [e(nc.sbuf_tensor(f"lnb{k}", [128, 8 * TCH], F32)) for k in range(2)]
            lnt = _ln_alloc(nc, e)

            p = Prog(nc)
            G = {}

            def gw(n):
                return list(G.get(n, []))

            cons = _ln_init(p, lnt)
            for ch in range(4):
                b = ch % 2
                t_a = p.op("y", (lambda ch, b: lambda en: en.dma_start(
                    xab[b][...].rearrange("p (a w) -> p a w", a=8),
                    xa_d[:, TCH * ch:TCH * (ch + 1)].rearrange("(a p) w -> p a w", p=128)))(ch, b),
                    wait=gw(f"xab{b}"), dma=True, stream=f"la{b}")
                t_b = p.op("g", (lambda ch, b: lambda en: en.dma_start(
                    xbb[b][...].rearrange("p (a w) -> p a w", a=8),
                    xb_d[:, TCH * ch:TCH * (ch + 1)].rearrange("(a p) w -> p a w", p=128)))(ch, b),
                    wait=gw(f"xbb{b}"), dma=True, stream=f"lb{b}")
                t_s = []
                t_st = []
                for j in range(8):
                    ts = p.op("v", (lambda b, j: lambda en: en.tensor_add(
                        snb[b][:, j * TCH:(j + 1) * TCH], xab[b][:, j * TCH:(j + 1) * TCH],
                        xbb[b][:, j * TCH:(j + 1) * TCH]))(b, j),
                        wait=[t_a, t_b] + gw(f"snb{b}{j}"))
                    t_s.append(ts)
                    t_st.append(p.op("y", (lambda b, j, ch: lambda en: en.dma_start(
                        s_d[128 * j:128 * (j + 1), TCH * ch:TCH * (ch + 1)],
                        snb[b][:, j * TCH:(j + 1) * TCH]))(b, j, ch),
                        wait=[ts], dma=True, stream=f"ss{b}"))
                G[f"xab{b}"] = t_s
                G[f"xbb{b}"] = t_s
                G["lh3"] = gw(f"lnb{b}")
                t_h = _ln_chunk(
                    p, G, gw, lnt, cons + t_s,
                    (lambda b: lambda j: snb[b][:, j * TCH:(j + 1) * TCH])(b),
                    (lambda b: lambda j: lnb[b][:, j * TCH:(j + 1) * TCH])(b))
                G["lh3"] = []
                t_l = []
                for j in range(8):
                    tl = p.op("g", (lambda b, j, ch: lambda en: en.dma_start(
                        l_d[128 * j:128 * (j + 1), TCH * ch:TCH * (ch + 1)],
                        lnb[b][:, j * TCH:(j + 1) * TCH]))(b, j, ch),
                        wait=[t_h[j]], dma=True, stream=f"ls{b}")
                    t_l.append(tl)
                for j in range(8):
                    G[f"snb{b}{j}"] = [t_h[j], t_st[-1]]
                G[f"lnb{b}"] = t_l
                G[f"xab{b}"] = t_h + [G["lpsr"][0]]
                G[f"xbb{b}"] = G[f"xab{b}"]

            p.op("v", lambda en: en.engine_nop(), wait=p.all_done())
            p.emit(est)
        return nc
    return build_proj, build_addln

build_proj, build_addln = _ns_proj()

_PROGS = {}


def _prog(name, fn):
    if name not in _PROGS:
        _PROGS[name] = fn()
    return _PROGS[name]


_CORES = list(range(8))


def _gather(res, name):
    return [np.concatenate([res.results[4 * b + j][name] for j in range(4)], 1)
            for b in range(2)]


def kernel(**inputs):
    X = np.asarray(inputs["L0_attn_a_x"], np.float32)
    Wcd = np.ascontiguousarray(np.asarray(inputs["L0_attn_a_Wc_down"], np.float32))
    Wqu = np.asarray(inputs["L0_attn_a_Wq_up"], np.float32)
    Wkv = np.ascontiguousarray(np.asarray(inputs["L0_attn_a_Wkv_up"], np.float32))
    Wcp = np.ascontiguousarray(np.asarray(inputs["L0_attn_a_Wc_proj"], np.float32))
    nu_log = np.asarray(inputs["L0_attn_b_nu_log"], np.float32)
    theta_log = np.asarray(inputs["L0_attn_b_theta_log"], np.float32)
    B_re = np.asarray(inputs["L0_attn_b_B_re"], np.float32)
    B_im = np.asarray(inputs["L0_attn_b_B_im"], np.float32)
    C_re = np.asarray(inputs["L0_attn_b_C_re"], np.float32)
    C_im = np.asarray(inputs["L0_attn_b_C_im"], np.float32)
    D_mat = np.asarray(inputs["L0_attn_b_D_mat"], np.float32)
    W1b_ = np.ascontiguousarray(np.asarray(inputs["L0_attn_b_W1"], np.float32))
    W2b_ = np.ascontiguousarray(np.asarray(inputs["L0_attn_b_W2"], np.float32))
    res_logit = np.float32(np.asarray(inputs["L0_attn_b_res_logit"]).reshape(()))
    Wqkv = np.asarray(inputs["L0_ffn_Wqkv"], np.float32)
    q_scale = np.asarray(inputs["L0_ffn_q_scale"], np.float32)
    k_scale = np.asarray(inputs["L0_ffn_k_scale"], np.float32)
    Wo = np.asarray(inputs["L0_ffn_Wo"], np.float32)
    ln_g = np.asarray(inputs["L0_ffn_ln_g"], np.float32)
    ln_b = np.asarray(inputs["L0_ffn_ln_b"], np.float32)
    W1f = np.asarray(inputs["L0_ffn_W1"], np.float32)
    b1f = np.asarray(inputs["L0_ffn_b1"], np.float32)
    W2f = np.asarray(inputs["L0_ffn_W2"], np.float32)
    b2f = np.asarray(inputs["L0_ffn_b2"], np.float32)

    Xcm = [np.ascontiguousarray(X[b].T) for b in range(2)]

    nc = _prog("attna", build_attna)
    ins = []
    for c in _CORES:
        b, j = c // 4, c % 4
        ins.append({"x0": Xcm[b], "Wc_down": Wcd,
                    "Wq_sl": np.ascontiguousarray(Wqu[:, 256 * j:256 * (j + 1)]),
                    "Wkv": Wkv})
    res = run_bass_kernel_spmd(nc, ins, core_ids=_CORES)
    aog = [np.concatenate([res.results[4 * b + j]["ao"] for j in range(4)], 0)
           for b in range(2)]

    nc = _prog("proj", build_proj)
    ins = []
    for c in _CORES:
        b, j = c // 4, c % 4
        sl = slice(2048 * j, 2048 * (j + 1))
        ins.append({"x0": np.ascontiguousarray(Xcm[b][:, sl]),
                    "aog": np.ascontiguousarray(aog[b][:, sl]), "Wp": Wcp})
    res = run_bass_kernel_spmd(nc, ins, core_ids=_CORES)
    X1 = _gather(res, "x1")

    nc_addln = _prog("addln", build_addln)

    def addln(A, Bm):
        ins = []
        for c in _CORES:
            b, j = c // 4, c % 4
            sl = slice(2048 * j, 2048 * (j + 1))
            xb = (np.zeros((1024, 2048), np.float32) if Bm is None
                  else np.ascontiguousarray(Bm[b][:, sl]))
            ins.append({"xa": np.ascontiguousarray(A[b][:, sl]), "xb": xb})
        res = run_bass_kernel_spmd(nc_addln, ins, core_ids=_CORES)
        return _gather(res, "s"), _gather(res, "sln")

    nc_attn = _prog("attn", build_ffn_attn)
    nc_ffn = _prog("ffn", build_ffn_ffn)

    def ffn_block(Y):
        for i in range(6):
            ins = []
            for c in _CORES:
                b, j = c // 4, c % 4
                pos0 = 2048 * j
                own = np.ascontiguousarray(Y[b][:, pos0:pos0 + 2048])
                halo = (np.zeros((1024, 257), np.float32) if j == 0
                        else np.ascontiguousarray(Y[b][:, pos0 - 257:pos0]))
                ins.append(attn_inputs(own, halo, Wqkv[i], q_scale[i],
                                       k_scale[i], Wo[i], pos0, j == 0))
            res = run_bass_kernel_spmd(nc_attn, ins, core_ids=_CORES)
            Y = _gather(res, "y")
            ins = []
            for c in _CORES:
                b, j = c // 4, c % 4
                pos0 = 2048 * j
                own = np.ascontiguousarray(Y[b][:, pos0:pos0 + 2048])
                halo = (np.zeros(1024, np.float32) if j == 0
                        else np.ascontiguousarray(Y[b][:, pos0 - 1]))
                ins.append(ffn_inputs(own, halo, W1f[i], b1f[i], W2f[i],
                                      b2f[i], ln_g[i], ln_b[i]))
            res = run_bass_kernel_spmd(nc_ffn, ins, core_ids=_CORES)
            Y = _gather(res, "y")
        return Y

    _, Y0 = addln(X1, None)
    Y6 = ffn_block(Y0)
    X2, _ = addln(X1, Y6)

    nu2 = np.ascontiguousarray(nu_log.reshape(2, 128).T)
    th2 = np.ascontiguousarray(theta_log.reshape(2, 128).T)
    BreT = np.ascontiguousarray(B_re.T)
    BimT = np.ascontiguousarray(B_im.T)
    nc = _prog("lru1", build_lru1)
    ins = []
    for c in _CORES:
        b, j = c // 4, c % 4
        sl = slice(2048 * j, 2048 * (j + 1))
        ins.append({"x3": np.ascontiguousarray(X2[b][:, sl]),
                    "nu2": nu2, "th2": th2, "BreT": BreT, "BimT": BimT})
    res = run_bass_kernel_spmd(nc, ins, core_ids=_CORES)
    accs = [np.asarray(res.results[c]["acc"]) for c in _CORES]
    acl = []
    for b in range(2):
        Lall = np.concatenate([accs[4 * b + j][127::128] for j in range(4)], 0)
        aR = np.ascontiguousarray(
            Lall[:, 0:256].T.reshape(2, 128, 64).transpose(1, 0, 2).reshape(128, 128))
        aI = np.ascontiguousarray(
            Lall[:, 256:512].T.reshape(2, 128, 64).transpose(1, 0, 2).reshape(128, 128))
        acl.append((aR, aI))
    nc = _prog("lru2a", build_lru2a)
    ins = []
    for c in _CORES:
        b, j = c // 4, c % 4
        sl = slice(2048 * j, 2048 * (j + 1))
        ssel = np.zeros((64, 16), np.float32)
        for g in range(16):
            ssel[16 * j + g, g] = 1.0
        ins.append({"x3": np.ascontiguousarray(X2[b][:, sl]), "acc": accs[c],
                    "aclTR": acl[b][0], "aclTI": acl[b][1], "Ssel": ssel,
                    "nu2": nu2, "th2": th2,
                    "CreT": np.ascontiguousarray(C_re.T),
                    "CimT": np.ascontiguousarray(C_im.T),
                    "Dmat": np.ascontiguousarray(D_mat.T)})
    res = run_bass_kernel_spmd(nc, ins, core_ids=_CORES)
    yts = [np.asarray(res.results[c]["yt"]) for c in _CORES]
    h3s = [np.asarray(res.results[c]["h3o"]) for c in _CORES]
    nc = _prog("lru2b", build_lru2b)
    ins = []
    for c in _CORES:
        b, j = c // 4, c % 4
        sl = slice(2048 * j, 2048 * (j + 1))
        ins.append({"yt": yts[c], "W1": W1b_, "W2": W2b_,
                    "x3": np.ascontiguousarray(X2[b][:, sl]), "h3o": h3s[c],
                    "rescol": np.full((128, 1), res_logit, np.float32)})
    res = run_bass_kernel_spmd(nc, ins, core_ids=_CORES)
    X4 = _gather(res, "x4p")

    _, Y0b = addln(X4, None)
    Y6b = ffn_block(Y0b)
    X5, _ = addln(X4, Y6b)

    return np.ascontiguousarray(
        np.stack([X5[b].T for b in range(2)], 0).astype(np.float32))


# version 03
# speedup vs baseline: 1.7799x; 1.7799x over previous
from contextlib import ExitStack
import numpy as np
import ml_dtypes
import concourse.bass as bass
import concourse.mybir as mybir
from concourse.bass_utils import run_bass_kernel_spmd

F32 = mybir.dt.float32
BF16 = mybir.dt.bfloat16
I32 = mybir.dt.int32
AF = mybir.ActivationFunctionType
ALU = mybir.AluOpType
AX = mybir.AxisListType
NEG = -30000.0
ELIDE = True


class Prog:

    def __init__(self, nc, name="prog"):
        self.nc = nc
        self.items = []
        self.sem_vals = {}
        self.sems = {}
        self._semnames = set()
        self._pending_barrier = None
        self._barrier_applied = set()

    def phase_barrier(self):
        self._pending_barrier = self.all_done()
        self._barrier_applied = set()

    def _bump(self, semname, n):
        v = self.sem_vals.get(semname, 0) + n
        self.sem_vals[semname] = v
        self._semnames.add(semname)
        return (semname, v)

    def op(self, eng, fn, wait=(), dma=False, stream=None):
        if dma:
            semname = f"d_{stream}" if stream else f"d_{eng}"
        else:
            semname = f"c_{eng}"
        tok = self._bump(semname, 16 if dma else 1)
        wait = list(wait)
        if self._pending_barrier is not None and eng not in self._barrier_applied:
            wait = wait + [self._pending_barrier]
            self._barrier_applied.add(eng)
        seen = {}
        for w in wait:
            if w is None:
                continue
            if isinstance(w, list):
                for (s, v) in w:
                    if s not in seen or seen[s] < v:
                        seen[s] = v
            else:
                s, v = w
                if s not in seen or seen[s] < v:
                    seen[s] = v
        self.items.append((eng, fn, sorted(seen.items()), (semname, 16 if dma else 1)))
        return tok

    def all_done(self):
        return [(s, v) for s, v in self.sem_vals.items()]

    def emit(self, stack):
        nc = self.nc
        for s in sorted(self._semnames):
            self.sems[s] = stack.enter_context(nc.semaphore(s))
        block = stack.enter_context(nc.Block())
        prog = self
        elide = getattr(self, "elide", ELIDE)

        def mk(engkey):
            def section(eng):
                own = f"c_{engkey}"
                own_count = 0
                hw = {}
                if isinstance(elide, (set, frozenset)):
                    el_own = engkey in elide
                    el_dom = True
                elif elide:
                    el_own = engkey != "v"
                    el_dom = True
                else:
                    el_own = el_dom = False
                for (e, fn, waits, _inc) in prog.items:
                    if e != engkey:
                        continue
                    for (s, v) in waits:
                        if el_own and s == own and v <= own_count:
                            continue
                        if el_dom:
                            if hw.get(s, -1) >= v and not (s == own and not el_own):
                                continue
                            hw[s] = v
                        eng.wait_ge(prog.sems[s], v)
                    inst = fn(eng)
                    if inst is not None:
                        inst.then_inc(prog.sems[_inc[0]], _inc[1])
                        if _inc[0] == own:
                            own_count += _inc[1]
            return section

        block.tensor(mk("t"))
        block.vector(mk("v"))
        block.scalar(mk("s"))
        block.gpsimd(mk("g"))
        block.sync(mk("y"))


def ceil_div(a, b):
    return (a + b - 1) // b


def bfloat(x):
    return np.asarray(x).astype(ml_dtypes.bfloat16)


def f32(x):
    return np.asarray(x, dtype=np.float32)

def _ns_attna():
    D = 1024
    LAT = 512
    NSEQ = 8192
    TCH = 512
    NCH = NSEQ // TCH
    BIAS = -40.0


    def build_attna():
        nc = bass.Bass("TRN2", target_bir_lowering=False)
        x_d = nc.dram_tensor("x0", [D, NSEQ], F32, kind="ExternalInput")
        wcd_d = nc.dram_tensor("Wc_down", [D, LAT], F32, kind="ExternalInput")
        wq_d = nc.dram_tensor("Wq_sl", [LAT, 256], F32, kind="ExternalInput")
        wkv_d = nc.dram_tensor("Wkv", [LAT, 128], F32, kind="ExternalInput")
        ao_d = nc.dram_tensor("ao", [256, NSEQ], F32, kind="ExternalOutput")

        u = np.arange(128)
        mdiag_c = []
        for delta in range(4):
            tt = np.arange(512)
            m = ((128 * delta + u[:, None]) <= tt[None, :]).astype(ml_dtypes.bfloat16)
            mdiag_c.append(nc.inline_tensor(m, name=f"mdiag{delta}"))

        with ExitStack() as est:
            e = est.enter_context
            wcd = e(nc.sbuf_tensor("wcd", [128, 8 * LAT], BF16))
            wq = e(nc.sbuf_tensor("wq", [128, 4 * 256], BF16))
            wkv = e(nc.sbuf_tensor("wkv", [128, 4 * 128], BF16))
            lat = e(nc.sbuf_tensor("lat", [128, 4 * TCH], BF16))
            qh = [e(nc.sbuf_tensor(f"qh{h}", [64, NSEQ], BF16)) for h in range(4)]
            kb = e(nc.sbuf_tensor("kb", [64, NSEQ], BF16))
            vb = e(nc.sbuf_tensor("vb", [128, 64 * 65], BF16))
            xch = [e(nc.sbuf_tensor(f"xch{k}", [128, 8 * TCH], F32)) for k in range(2)]
            xcc = [e(nc.sbuf_tensor(f"xcc{k}", [128, TCH], BF16)) for k in range(2)]
            xsq = [e(nc.sbuf_tensor(f"xsq{k}", [128, TCH], BF16)) for k in range(2)]
            h1 = e(nc.sbuf_tensor("h1", [128, 8 * TCH], BF16))
            mdg = [e(nc.sbuf_tensor(f"mdg{d}", [128, 512], BF16)) for d in range(4)]
            Ab = [e(nc.sbuf_tensor(f"Ab{k}", [128, 512], BF16)) for k in range(2)]
            oev = e(nc.sbuf_tensor("oev", [64, 512], BF16))
            rrow = e(nc.sbuf_tensor("rrow", [1, 512], F32))
            onesc = e(nc.sbuf_tensor("onesc", [128, 2], BF16))
            onesf = e(nc.sbuf_tensor("onesf", [1, 128], F32))
            onc = e(nc.sbuf_tensor("onc", [1, 64], F32))
            bcol = e(nc.sbuf_tensor("bcol", [128, 1], F32))
            murow = e(nc.sbuf_tensor("murow", [1, TCH], F32))
            musq = e(nc.sbuf_tensor("musq", [1, TCH], F32))
            varrow = e(nc.sbuf_tensor("varrow", [1, TCH], F32))
            rstdrow = e(nc.sbuf_tensor("rstdrow", [1, TCH], F32))
            mrrow = e(nc.sbuf_tensor("mrrow", [1, TCH], F32))
            Rb = e(nc.sbuf_tensor("Rb", [128, TCH], BF16))
            Mb = e(nc.sbuf_tensor("Mb", [128, TCH], BF16))
            outc2 = [[e(nc.sbuf_tensor(f"outc{j}_{k}", [128, 512], F32)) for k in range(2)]
                     for j in range(2)]
            ps_r0 = e(nc.psum_tensor("ps_r0", [1, TCH], F32))
            ps_r1 = e(nc.psum_tensor("ps_r1", [1, TCH], F32))
            ps_bc = e(nc.psum_tensor("ps_bc", [128, TCH], F32))
            ps_a = e(nc.psum_tensor("ps_a", [128, 512], F32))
            ps_s = e(nc.psum_tensor("ps_s", [128, 512], F32))
            ps_av = e(nc.psum_tensor("ps_av", [65, 512], F32))

            p = Prog(nc)
            G = {}

            def gw(name):
                return list(G.get(name, []))

            w_sm = p.op("g", lambda en: en.dma_start(
                wcd[...].rearrange("p (a w) -> p a w", a=8),
                wcd_d[...].rearrange("(a p) w -> p a w", p=128)), dma=True, stream="sm")
            w_sm = p.op("g", lambda en: en.dma_start(
                wq[...].rearrange("p (a w) -> p a w", a=4),
                wq_d[...].rearrange("(a p) w -> p a w", p=128)), dma=True, stream="sm")
            w_sm = p.op("g", lambda en: en.dma_start(
                wkv[...].rearrange("p (a w) -> p a w", a=4),
                wkv_d[...].rearrange("(a p) w -> p a w", p=128)), dma=True, stream="sm")
            for d_ in range(4):
                w_sm = p.op("g", (lambda d_: lambda en: en.dma_start(
                    mdg[d_][...], mdiag_c[d_][...]))(d_), dma=True, stream="sm")
            c1 = p.op("v", lambda en: en.memset(onesc[...], 1.0))
            c2 = p.op("v", lambda en: en.memset(onesf[...], 1.0))
            c3 = p.op("v", lambda en: en.memset(onc[...], 1.0))
            c4 = p.op("v", lambda en: en.memset(bcol[...], BIAS))
            c5 = p.op("v", lambda en: en.memset(
                vb[...].rearrange("p (a w) -> p a w", a=64)[:, :, 64:65], 1.0))
            CON = [w_sm, c1, c2, c3, c4, c5]

            for ch in range(NCH):
                b = ch % 2
                t_x = p.op("y", (lambda ch, b: lambda en: en.dma_start(
                    xch[b][...].rearrange("p (a w) -> p a w", a=8),
                    x_d[:, TCH * ch:TCH * (ch + 1)].rearrange("(a p) w -> p a w", p=128)))(ch, b),
                    wait=gw(f"xch{b}"), dma=True, stream=f"x{b}")
                tprev = None
                for j in range(8):
                    sb = j % 2
                    tc = p.op("v", (lambda j, b, sb: lambda en: en.tensor_copy(
                        xcc[sb][...], xch[b][:, j * TCH:(j + 1) * TCH]))(j, b, sb),
                        wait=[t_x] + gw(f"xcc{sb}"))
                    tq = p.op("s", (lambda sb: lambda en: en.activation(
                        xsq[sb][...], xcc[sb][...], AF.Square))(sb),
                        wait=[tc] + gw(f"xsq{sb}"))
                    wl = [tc, c1] + ([tprev] if tprev else [])
                    if j == 0:
                        wl += gw("ps_r")
                    t1 = p.op("t", (lambda j, sb: lambda en: en.matmul(
                        ps_r0[0:1, :], onesc[:, 0:1], xcc[sb][...],
                        start=(j == 0), stop=(j == 7)))(j, sb), wait=wl)
                    t2 = p.op("t", (lambda j, sb: lambda en: en.matmul(
                        ps_r1[0:1, :], onesc[:, 1:2], xsq[sb][...],
                        start=(j == 0), stop=(j == 7)))(j, sb), wait=[tq, t1])
                    G[f"xcc{sb}"] = [t1, tq]
                    G[f"xsq{sb}"] = [t2]
                    tprev = t2
                t_mu = p.op("v", lambda en: en.tensor_scalar_mul(
                    murow[...], ps_r0[0:1, :], 1.0 / D), wait=[tprev])
                t_m2 = p.op("v", lambda en: en.tensor_mul(
                    musq[...], murow[...], murow[...]), wait=[t_mu])
                t_var = p.op("v", lambda en: en.scalar_tensor_tensor(
                    varrow[...], ps_r1[0:1, :], 1.0 / D, musq[...],
                    ALU.mult, ALU.subtract), wait=[t_m2, tprev])
                G["ps_r"] = [t_var]
                t_ve = p.op("v", lambda en: en.tensor_scalar_add(
                    varrow[...], varrow[...], 1e-5), wait=[t_var])
                t_ln = p.op("s", lambda en: en.activation(
                    rstdrow[...], varrow[...], AF.Ln), wait=[t_ve])
                t_rs = p.op("s", lambda en: en.activation(
                    rstdrow[...], rstdrow[...], AF.Exp, scale=-0.5), wait=[t_ln])
                t_mr = p.op("v", lambda en: en.tensor_mul(
                    mrrow[...], murow[...], rstdrow[...]), wait=[t_rs, t_mu])
                tRb, tMb = [], []
                for q_ in range(TCH // 512):
                    pass
                tb = p.op("t", lambda en: en.matmul(
                    ps_bc[:, 0:512], onesf[0:1, :], rstdrow[0:1, 0:512],
                    start=True, stop=True), wait=[t_rs, c2] + gw("ps_bc"))
                tR = p.op("s", lambda en: en.activation(
                    Rb[:, 0:512], ps_bc[:, 0:512], AF.Identity), wait=[tb] + gw("Rb"))
                tb2 = p.op("t", lambda en: en.matmul(
                    ps_bc[:, 0:512], onesf[0:1, :], mrrow[0:1, 0:512],
                    start=True, stop=True), wait=[t_mr, tR])
                tM = p.op("s", lambda en: en.activation(
                    Mb[:, 0:512], ps_bc[:, 0:512], AF.Identity, scale=-1.0),
                    wait=[tb2] + gw("Mb"))
                G["ps_bc"] = [tM]
                t_h = []
                for j in range(8):
                    tm = p.op("v", (lambda j, b: lambda en: en.tensor_mul(
                        h1[:, j * TCH:(j + 1) * TCH], xch[b][:, j * TCH:(j + 1) * TCH],
                        Rb[:, 0:TCH]))(j, b), wait=[tR, t_x] + gw("h1"))
                    ta = p.op("v", (lambda j: lambda en: en.tensor_add(
                        h1[:, j * TCH:(j + 1) * TCH], h1[:, j * TCH:(j + 1) * TCH],
                        Mb[:, 0:TCH]))(j), wait=[tm, tM])
                    t_h.append(ta)
                G[f"xch{b}"] = t_h
                G["Rb"] = t_h
                G["Mb"] = t_h
                t_lat = []
                for m in range(4):
                    last = None
                    for kk in range(8):
                        wl = [t_h[kk], CON[0]] + ([last] if last else [])
                        if kk == 0:
                            wl += gw("ps_a")
                        last = p.op("t", (lambda m, kk, ch: lambda en: en.matmul(
                            ps_a[:, 0:512], wcd[:, kk * LAT + 128 * m:kk * LAT + 128 * (m + 1)],
                            h1[:, kk * TCH:(kk + 1) * TCH],
                            start=(kk == 0), stop=(kk == 7)))(m, kk, ch), wait=wl)
                    tev = p.op("s", (lambda m: lambda en: en.activation(
                        lat[:, m * TCH:(m + 1) * TCH],
                        ps_a[:, 0:512], AF.Identity))(m), wait=[last] + (gw("lat") if m == 0 else []))
                    G["ps_a"] = [tev]
                    t_lat.append(tev)
                G["h1"] = [t_lat[-1]]
                qreaders = []
                for m in range(2):
                    last = None
                    for kk in range(4):
                        wl = [t_lat[kk], CON[0]] + ([last] if last else [])
                        if kk == 0:
                            wl += gw("ps_a")
                        last = p.op("t", (lambda m, kk: lambda en: en.matmul(
                            ps_a[:, 0:512], wq[:, kk * 256 + 128 * m:kk * 256 + 128 * (m + 1)],
                            lat[:, kk * TCH:(kk + 1) * TCH],
                            start=(kk == 0), stop=(kk == 3)))(m, kk), wait=wl)
                    tev0 = p.op("s", (lambda m, ch: lambda en: en.activation(
                        qh[2 * m][0:64, TCH * ch:TCH * (ch + 1)],
                        ps_a[0:64, 0:512], AF.Identity))(m, ch), wait=[last])
                    tev1 = p.op("s", (lambda m, ch: lambda en: en.activation(
                        qh[2 * m + 1][0:64, TCH * ch:TCH * (ch + 1)],
                        ps_a[64:128, 0:512], AF.Identity))(m, ch), wait=[last, tev0])
                    G["ps_a"] = [tev0, tev1]
                    qreaders += [tev0, tev1]
                last = None
                for kk in range(4):
                    wl = [t_lat[kk], CON[0]] + ([last] if last else [])
                    if kk == 0:
                        wl += gw("ps_a")
                    last = p.op("t", (lambda kk: lambda en: en.matmul(
                        ps_a[0:64, 0:512], wkv[:, kk * 128:kk * 128 + 64],
                        lat[:, kk * TCH:(kk + 1) * TCH],
                        start=(kk == 0), stop=(kk == 3)))(kk), wait=wl)
                tev = p.op("s", (lambda ch: lambda en: en.activation(
                    kb[0:64, TCH * ch:TCH * (ch + 1)], ps_a[0:64, 0:512],
                    AF.Identity))(ch), wait=[last])
                G["ps_a"] = [tev]
                for tb_ in range(4):
                    beta = 4 * ch + tb_
                    last = None
                    for kk in range(4):
                        wl = [t_lat[kk], CON[0]] + ([last] if last else [])
                        if kk == 0:
                            wl += gw("ps_s")
                        last = p.op("t", (lambda tb_, kk: lambda en: en.matmul(
                            ps_s[:, 0:64],
                            lat[:, kk * TCH + 128 * tb_:kk * TCH + 128 * (tb_ + 1)],
                            wkv[:, kk * 128 + 64:kk * 128 + 128],
                            start=(kk == 0), stop=(kk == 3)))(tb_, kk), wait=wl)
                    tev = p.op("s", (lambda beta: lambda en: en.activation(
                        vb[:, beta * 65:beta * 65 + 64], ps_s[:, 0:64],
                        AF.Identity))(beta), wait=[last])
                    G["ps_s"] = [tev]
                G["lat"] = [last] + qreaders

            p.phase_barrier()
            for tau in range(NCH):
                nsig = 4 * tau + 4
                tnos = {0: [], 1: []}
                for h in range(4):
                    hb = 64 * (h % 2)
                    jq = h // 2
                    last_av = None
                    for sg in range(nsig):
                        ab = sg % 2
                        t_s = p.op("t", (lambda h, tau, sg: lambda en: en.matmul(
                            ps_s[:, 0:512],
                            kb[0:64, 128 * sg:128 * (sg + 1)],
                            qh[h][0:64, TCH * tau:TCH * (tau + 1)],
                            start=True, stop=True))(h, tau, sg),
                            wait=gw("ps_s"))
                        te_ = p.op("s", (lambda ab: lambda en: en.activation(
                            Ab[ab][:, 0:512], ps_s[:, 0:512], AF.Exp,
                            scale=0.125, bias=bcol[...]))(ab),
                            wait=[t_s, CON[4]] + gw(f"Ab{ab}"))
                        G["ps_s"] = [te_]
                        tmask = None
                        delta = sg - 4 * tau
                        if 0 <= delta <= 3:
                            tmask = p.op("v", (lambda ab, delta: lambda en: en.tensor_mul(
                                Ab[ab][:, 0:512], Ab[ab][:, 0:512], mdg[delta][...]))(ab, delta),
                                wait=[te_, CON[0]])
                        rd = tmask or te_
                        wl = [rd, CON[5]] + ([last_av] if last_av else [])
                        if sg == 0:
                            wl += gw("ps_av")
                        last_av = p.op("t", (lambda ab, sg, nsig: lambda en: en.matmul(
                            ps_av[:, 0:512], vb[:, sg * 65:(sg + 1) * 65],
                            Ab[ab][:, 0:512],
                            start=(sg == 0), stop=(sg == nsig - 1)))(ab, sg, nsig), wait=wl)
                        G[f"Ab{ab}"] = [last_av]
                    trc = p.op("v", lambda en: en.reciprocal(
                        rrow[0:1, 0:512], ps_av[64:65, 0:512]), wait=[last_av] + gw("rrow"))
                    tbc = p.op("t", lambda en: en.matmul(
                        ps_bc[0:64, 0:512], onc[...], rrow[0:1, 0:512],
                        start=True, stop=True), wait=[trc, c3] + gw("ps_bc2"))
                    G["rrow"] = [tbc]
                    tev = p.op("s", lambda en: en.activation(
                        oev[...], ps_av[0:64, 0:512], AF.Identity),
                        wait=[last_av, trc] + gw("oev"))
                    G["ps_av"] = [tev]
                    cb = tau % 2
                    tno = p.op("v", (lambda hb, jq, cb: lambda en: en.tensor_mul(
                        outc2[jq][cb][hb:hb + 64, :], oev[...],
                        ps_bc[0:64, 0:512]))(hb, jq, cb),
                        wait=[tev, tbc] + gw(f"outc{jq}{cb}"))
                    G["ps_bc2"] = [tno]
                    G["oev"] = [tno]
                    tnos[jq].append(tno)
                cb = tau % 2
                for jq in range(2):
                    tst = p.op("y", (lambda jq, tau, cb: lambda en: en.dma_start(
                        ao_d[128 * jq:128 * (jq + 1), TCH * tau:TCH * (tau + 1)],
                        outc2[jq][cb][...]))(jq, tau, cb),
                        wait=tnos[jq], dma=True, stream=f"st{jq}{cb}")
                    G[f"outc{jq}{cb}"] = [tst]

            p.op("v", lambda en: en.engine_nop(), wait=p.all_done())
            p.emit(est)
        return nc
    return build_attna

build_attna = _ns_attna()

def _ns_attn():
    D = 1024
    HAL = 257
    NT = 2048
    NK = 2304
    NB = 18
    NTAU = 16
    QSCALE = 8.0


    def build_ffn_attn():
        nc = bass.Bass("TRN2", target_bir_lowering=False)
        x_d = nc.dram_tensor("x", [D, HAL + NT], F32, kind="ExternalInput")
        w_d = {nm: nc.dram_tensor(nm, [D, D], F32, kind="ExternalInput")
               for nm in ["Wq", "Wk", "Wv", "Wo"]}
        qs_d = nc.dram_tensor("qs", [128, 1], F32, kind="ExternalInput")
        ks_d = nc.dram_tensor("ks", [128, 1], F32, kind="ExternalInput")
        csq_d = nc.dram_tensor("cs_q", [128, NT], F32, kind="ExternalInput")
        ssq_d = nc.dram_tensor("ss_q", [128, NT], F32, kind="ExternalInput")
        csk_d = nc.dram_tensor("cs_k", [128, NK], F32, kind="ExternalInput")
        ssk_d = nc.dram_tensor("ss_k", [128, NK], F32, kind="ExternalInput")
        von_d = nc.dram_tensor("vones", [128, NB], F32, kind="ExternalInput")
        y_d = nc.dram_tensor("y", [D, NT], F32, kind="ExternalOutput")

        swap = np.zeros((128, 128), np.float32)
        for i in range(128):
            part = i + 32 if (i % 64) < 32 else i - 32
            swap[part, i] = -1.0 if (i % 64) < 32 else 1.0
        P_c = nc.inline_tensor(swap.astype(ml_dtypes.bfloat16), name="Pswap")
        ident_c = nc.inline_tensor(np.eye(128, dtype=ml_dtypes.bfloat16), name="id128")
        u = np.arange(128)
        m0_c = nc.inline_tensor((u[:, None] >= u[None, :]).astype(ml_dtypes.bfloat16),
                                name="mask_s0")
        m2_c = nc.inline_tensor((u[:, None] <= u[None, :]).astype(ml_dtypes.bfloat16),
                                name="mask_s2")
        indj_np = np.zeros((8, 128, 16), np.float32)
        for j in range(8):
            indj_np[j, 0:64, 2 * j] = 1.0
            indj_np[j, 64:128, 2 * j + 1] = 1.0
        indj_c = [nc.inline_tensor(indj_np[j].astype(ml_dtypes.bfloat16), name=f"indj{j}")
                  for j in range(8)]
        ind2_np = np.zeros((8, 16, 128), np.float32)
        for j in range(8):
            ind2_np[j, 2 * j, 0:64] = 1.0
            ind2_np[j, 2 * j + 1, 64:128] = 1.0
        ind2_c = nc.inline_tensor(ind2_np.reshape(8 * 16, 128).astype(ml_dtypes.bfloat16),
                                  name="indh2")

        with ExitStack() as est:
            e = est.enter_context
            qs = e(nc.sbuf_tensor("qss", [128, 1], F32))
            ks = e(nc.sbuf_tensor("kss", [128, 1], F32))
            cst = e(nc.sbuf_tensor("cst", [128, NK], BF16))
            sst = e(nc.sbuf_tensor("sst", [128, NK], BF16))
            von = e(nc.sbuf_tensor("von", [128, NB], BF16))
            Psw = e(nc.sbuf_tensor("Psw", [128, 128], BF16))
            idn = e(nc.sbuf_tensor("idn", [128, 128], BF16))
            msk0 = e(nc.sbuf_tensor("msk0", [128, 128], BF16))
            msk2 = e(nc.sbuf_tensor("msk2", [128, 128], BF16))
            indj = [e(nc.sbuf_tensor(f"indjs{j}", [128, 16], BF16)) for j in range(8)]
            ind2 = e(nc.sbuf_tensor("ind2", [16, 8 * 128], BF16))
            ones65 = e(nc.sbuf_tensor("ones65", [65, 128], BF16))
            onc = e(nc.sbuf_tensor("onc", [1, 64], F32))
            wbuf = e(nc.sbuf_tensor("wbuf", [128, 8 * D], BF16))
            wbuf2 = e(nc.sbuf_tensor("wbuf2", [128, 8 * D], BF16))
            qrot = e(nc.sbuf_tensor("qrot", [128, 8 * NT], BF16))
            krot = e(nc.sbuf_tensor("krot", [128, 8 * NK], BF16))
            vbuf = e(nc.sbuf_tensor("vbuf", [128, NB * D], BF16))
            xbch = [e(nc.sbuf_tensor(f"xbch{k}", [128, 8 * 512], BF16)) for k in range(2)]
            attb = [e(nc.sbuf_tensor(f"attb{k}", [128, 8 * 128], BF16)) for k in range(2)]
            A_T = e(nc.sbuf_tensor("A_T", [128, 16 * 384], BF16))
            negM = e(nc.sbuf_tensor("negM", [128, 16], BF16))
            negMr = e(nc.sbuf_tensor("negMr", [65, 16 * 128], BF16))
            rno = e(nc.sbuf_tensor("rno", [16, NT], BF16))
            rnk = e(nc.sbuf_tensor("rnk", [16, NK], BF16))
            rrow = e(nc.sbuf_tensor("rrow", [1, 512], F32))
            oev = e(nc.sbuf_tensor("oev", [64, 512], BF16))
            sqs = [e(nc.sbuf_tensor(f"sqs{k}", [128, 512], BF16)) for k in range(2)]
            tmp0 = e(nc.sbuf_tensor("tmp0", [128, 512], BF16))
            tmp1 = e(nc.sbuf_tensor("tmp1", [128, 512], BF16))
            tmp2 = e(nc.sbuf_tensor("tmp2", [128, 512], BF16))
            xres = [e(nc.sbuf_tensor(f"xres{k}", [128, 128], F32)) for k in range(2)]
            outb = [e(nc.sbuf_tensor(f"outbf{k}", [128, 128], F32)) for k in range(2)]
            ps_q = e(nc.psum_tensor("ps_q", [128, 512], F32))
            ps_w = e(nc.psum_tensor("ps_w", [128, 512], F32))
            ps_n = e(nc.psum_tensor("ps_n", [16, 512], F32))
            ps_bc = e(nc.psum_tensor("ps_bc", [128, 512], F32))
            ps_1 = e(nc.psum_tensor("ps_1", [128, 384], F32))
            ps_2 = e(nc.psum_tensor("ps_2", [128, 384], F32))
            ps_av = e(nc.psum_tensor("ps_av", [64, 512], F32))
            ps_dn = e(nc.psum_tensor("ps_dn", [1, 512], F32))

            p = Prog(nc)
            G = {}

            def gw(name):
                return list(G.get(name, []))

            w_sm = None
            smalls = [(qs, qs_d), (ks, ks_d), (von, von_d), (Psw, P_c), (idn, ident_c),
                      (msk0, m0_c), (msk2, m2_c)]
            smalls += [(indj[j], indj_c[j]) for j in range(8)]
            for (sb_, dr_) in smalls:
                w_sm = p.op("g", (lambda sb_, dr_: lambda en: en.dma_start(
                    sb_[...], dr_[...]))(sb_, dr_), dma=True, stream="sm")
            w_sm = p.op("g", lambda en: en.dma_start(
                ind2[...].rearrange("p (a w) -> p a w", a=8),
                ind2_c[...].rearrange("(a p) w -> p a w", p=16)), dma=True, stream="sm")
            c2 = p.op("v", lambda en: en.memset(ones65[...], 1.0))
            c3 = p.op("v", lambda en: en.memset(onc[...], 1.0))
            CON = [w_sm, c2, c3]

            def load_w(nm, buf):
                return p.op("g", (lambda nm, buf: lambda en: en.dma_start(
                    buf[...].rearrange("p (a w) -> p a w", a=8),
                    w_d[nm][...].rearrange("(a p) w -> p a w", p=128)))(nm, buf),
                    dma=True, stream="wst")

            def load_tbl(cs_src, ss_src, ncols):
                t1 = p.op("g", (lambda cs_src, ncols: lambda en: en.dma_start(
                    cst[:, 0:ncols], cs_src[...]))(cs_src, ncols), dma=True, stream="tbl")
                t2 = p.op("g", (lambda ss_src, ncols: lambda en: en.dma_start(
                    sst[:, 0:ncols], ss_src[...]))(ss_src, ncols), dma=True, stream="tbl")
                return [t1, t2]

            def emit_proj(dst, wtok, wbufx, scol, ncols, xoff, rows_out, tbl_toks):
                chunks = []
                o = 0
                while o < ncols:
                    cw = min(512, ncols - o)
                    chunks.append((o, cw))
                    o += cw
                for ci, (o, cw) in enumerate(chunks):
                    b = ci % 2
                    p.op("g", (lambda o, cw, b: lambda en: en.dma_start(
                        xbch[b][:, 0:4 * cw].rearrange("p (a w) -> p a w", a=4),
                        x_d[0:512, xoff + o:xoff + o + cw].rearrange(
                            "(a p) w -> p a w", p=128)))(o, cw, b),
                        wait=gw(f"xbch{b}"), dma=True, stream=f"xc{b}")
                    t_xc = p.op("g", (lambda o, cw, b: lambda en: en.dma_start(
                        xbch[b][:, 4 * cw:8 * cw].rearrange("p (a w) -> p a w", a=4),
                        x_d[512:1024, xoff - 1 + o:xoff - 1 + o + cw].rearrange(
                            "(a p) w -> p a w", p=128)))(o, cw, b),
                        wait=gw(f"xbch{b}"), dma=True, stream=f"xc{b}")
                    t3s = []
                    t_nmm = None
                    xc_readers = []
                    for j in range(8):
                        sb = j % 2
                        last = None
                        for kk in range(8):
                            wl = [t_xc, wtok] + ([last] if last else [])
                            if j == 0 and kk == 0:
                                wl += gw("ps_q")
                            last = p.op("t", (lambda j, kk, cw, b: lambda en: en.matmul(
                                ps_q[:, 0:cw],
                                wbufx[:, kk * D + 128 * j:kk * D + 128 * (j + 1)],
                                xbch[b][:, kk * cw:(kk + 1) * cw],
                                start=(kk == 0), stop=(kk == 7)))(j, kk, cw, b), wait=wl)
                        xc_readers.append(last)
                        tsq = p.op("s", (lambda cw, sb: lambda en: en.activation(
                            sqs[sb][:, 0:cw], ps_q[:, 0:cw], AF.Square))(cw, sb),
                            wait=[last] + gw(f"sqs{sb}"))
                        tsc = p.op("s", (lambda cw, scol: lambda en: en.activation(
                            tmp0[:, 0:cw], ps_q[:, 0:cw], AF.Identity,
                            scale=scol[...]))(cw, scol), wait=[last, tsq, CON[0]] + gw("tmp0"))
                        G["ps_q"] = [tsc]
                        wl = [tsq, CON[0]] + ([t_nmm] if t_nmm else [])
                        if j == 0:
                            wl += gw("ps_n")
                        t_nmm = p.op("t", (lambda j, cw, sb: lambda en: en.matmul(
                            ps_n[0:16, 0:cw], indj[j][...], sqs[sb][:, 0:cw],
                            start=(j == 0), stop=(j == 7)))(j, cw, sb), wait=wl)
                        G[f"sqs{sb}"] = [t_nmm]
                        tsw = p.op("t", (lambda cw: lambda en: en.matmul(
                            ps_w[:, 0:cw], Psw[...], tmp0[:, 0:cw],
                            start=True, stop=True))(cw), wait=[tsc, CON[0]] + gw("ps_w"))
                        t1 = p.op("v", (lambda o, cw: lambda en: en.tensor_mul(
                            tmp1[:, 0:cw], tmp0[:, 0:cw], cst[:, o:o + cw]))(o, cw),
                            wait=[tsc] + tbl_toks + gw("tmp1"))
                        G["tmp0"] = [t1]
                        t2 = p.op("v", (lambda o, cw: lambda en: en.tensor_mul(
                            tmp2[:, 0:cw], ps_w[:, 0:cw], sst[:, o:o + cw]))(o, cw),
                            wait=[tsw] + tbl_toks + gw("tmp2"))
                        G["ps_w"] = [t2]
                        t3 = p.op("v", (lambda j, o, cw, dst, ncols: lambda en: en.tensor_add(
                            dst[:, j * ncols + o:j * ncols + o + cw],
                            tmp1[:, 0:cw], tmp2[:, 0:cw]))(j, o, cw, dst, ncols),
                            wait=[t1, t2])
                        G["tmp1"] = [t3]
                        G["tmp2"] = [t3]
                        t3s.append(t3)
                    G[f"xbch{b}"] = xc_readers
                    tmx = p.op("v", (lambda o, cw, rows_out: lambda en: en.tensor_scalar_max(
                        rows_out[0:16, o:o + cw], ps_n[0:16, 0:cw], 1e-24))(o, cw, rows_out),
                        wait=[t_nmm])
                    G["ps_n"] = [tmx]
                    tl = p.op("s", (lambda o, cw, rows_out: lambda en: en.activation(
                        rows_out[0:16, o:o + cw], rows_out[0:16, o:o + cw], AF.Ln))(o, cw, rows_out),
                        wait=[tmx])
                    tex = p.op("s", (lambda o, cw, rows_out: lambda en: en.activation(
                        rows_out[0:16, o:o + cw], rows_out[0:16, o:o + cw], AF.Exp,
                        scale=-0.5))(o, cw, rows_out), wait=[tl])
                    for j in range(8):
                        tb = p.op("t", (lambda j, o, cw, rows_out: lambda en: en.matmul(
                            ps_bc[:, 0:cw], ind2[:, 128 * j:128 * (j + 1)],
                            rows_out[0:16, o:o + cw], start=True, stop=True))(j, o, cw, rows_out),
                            wait=[tex, CON[0]] + gw("ps_bc"))
                        tm = p.op("v", (lambda j, o, cw, dst, ncols: lambda en: en.tensor_mul(
                            dst[:, j * ncols + o:j * ncols + o + cw],
                            dst[:, j * ncols + o:j * ncols + o + cw],
                            ps_bc[:, 0:cw]))(j, o, cw, dst, ncols), wait=[tb, t3s[j]])
                        G["ps_bc"] = [tm]

            w_wq = load_w("Wq", wbuf)
            tbl = load_tbl(csq_d, ssq_d, NT)
            emit_proj(qrot, w_wq, wbuf, qs, NT, HAL, rno, tbl)
            p.phase_barrier()
            w_wk = load_w("Wk", wbuf2)
            tbl = load_tbl(csk_d, ssk_d, NK)
            emit_proj(krot, w_wk, wbuf2, ks, NK, 1, rnk, tbl)

            p.phase_barrier()
            w_wv = load_w("Wv", wbuf)
            for beta in range(NB):
                b = beta % 2
                p.op("g", (lambda beta, b: lambda en: en.dma_start(
                    xbch[b][:, 0:4 * 128].rearrange("p (a w) -> p a w", a=4),
                    x_d[0:512, 1 + 128 * beta:1 + 128 * (beta + 1)].rearrange(
                        "(a p) w -> p a w", p=128)))(beta, b),
                    wait=gw(f"xbch{b}"), dma=True, stream=f"xc{b}")
                t_xc = p.op("g", (lambda beta, b: lambda en: en.dma_start(
                    xbch[b][:, 4 * 128:8 * 128].rearrange("p (a w) -> p a w", a=4),
                    x_d[512:1024, 128 * beta:128 * (beta + 1)].rearrange(
                        "(a p) w -> p a w", p=128)))(beta, b),
                    wait=gw(f"xbch{b}"), dma=True, stream=f"xc{b}")
                readers = []
                for half in range(2):
                    ps = ps_bc if half == 0 else ps_q
                    last = None
                    for kk in range(8):
                        wl = [t_xc, w_wv] + ([last] if last else [])
                        if kk == 0:
                            wl += gw("ps_bc" if half == 0 else "ps_q")
                        last = p.op("t", (lambda b, half, kk, ps: lambda en: en.matmul(
                            ps[:, 0:512],
                            xbch[b][:, kk * 128:(kk + 1) * 128],
                            wbuf[:, kk * D + half * 512:kk * D + (half + 1) * 512],
                            start=(kk == 0), stop=(kk == 7)))(b, half, kk, ps), wait=wl)
                    readers.append(last)
                    tev = p.op("s", (lambda beta, half, ps: lambda en: en.activation(
                        vbuf[:, beta * D + half * 512:beta * D + (half + 1) * 512],
                        ps[:, 0:512], AF.Identity))(beta, half, ps), wait=[last])
                    G["ps_bc" if half == 0 else "ps_q"] = [tev]
                    tv_last = tev
                G[f"xbch{b}"] = readers

            p.phase_barrier()
            w_wo = load_w("Wo", wbuf2)
            for tau in range(NTAU):
                ab = tau % 2
                w0 = 1 + 128 * tau
                t_red = {}
                for h in range(16):
                    hb = 64 * (h % 2)
                    jt = h // 2
                    t_s1 = p.op("t", (lambda hb, jt, tau, w0: lambda en: en.matmul(
                        ps_1[:, 0:384],
                        qrot[hb:hb + 64, jt * NT + 128 * tau:jt * NT + 128 * (tau + 1)],
                        krot[hb:hb + 64, jt * NK + w0 - 1:jt * NK + w0 - 1 + 384],
                        start=True, stop=True))(hb, jt, tau, w0), wait=gw("ps_1"))
                    t_red[h] = p.op("v", (lambda h: lambda en: en.reduce_max(
                        negM[:, h:h + 1], ps_1[:, 0:384], AX.X, negate=True))(h),
                        wait=[t_s1] + gw("negM"))
                    G["ps_1"] = [t_red[h]]
                negm_readers = []
                for q4 in range(4):
                    last = None
                    for hh in range(4):
                        h = 4 * q4 + hh
                        wl = [t_red[h], CON[0]] + ([last] if last else [])
                        if hh == 0:
                            wl += gw("ps_n")
                        last = p.op("t", (lambda h, hh: lambda en: en.matmul(
                            ps_n[0:1, 128 * hh:128 * (hh + 1)], negM[:, h:h + 1],
                            idn[...], start=True, stop=True))(h, hh), wait=wl)
                    negm_readers.append(last)
                    te0 = p.op("s", (lambda q4: lambda en: en.activation(
                        negMr[0:1, 512 * q4:512 * (q4 + 1)], ps_n[0:1, 0:512],
                        AF.Identity))(q4), wait=[last] + gw("negMr"))
                    te1 = p.op("s", (lambda q4: lambda en: en.activation(
                        negMr[64:65, 512 * q4:512 * (q4 + 1)], ps_n[0:1, 0:512],
                        AF.Identity))(q4), wait=[last, te0])
                    G["ps_n"] = [te0, te1]
                    G[f"negMr{q4}"] = [te0, te1]
                G["negM"] = negm_readers
                t_exp = {}
                for h in range(16):
                    hb = 64 * (h % 2)
                    jt = h // 2
                    last = None
                    for sg in range(3):
                        wl = G[f"negMr{h // 4}"] + ([last] if last else [])
                        if sg == 0:
                            wl += gw("ps_2")
                        last = p.op("t", (lambda hb, jt, sg, tau, w0: lambda en: en.matmul(
                            ps_2[:, 128 * sg:128 * (sg + 1)],
                            krot[hb:hb + 64, jt * NK + w0 - 1 + 128 * sg:jt * NK + w0 - 1 + 128 * (sg + 1)],
                            qrot[hb:hb + 64, jt * NT + 128 * tau:jt * NT + 128 * (tau + 1)],
                            start=True, stop=False))(hb, jt, sg, tau, w0), wait=wl)
                        last = p.op("t", (lambda hb, h, sg: lambda en: en.matmul(
                            ps_2[:, 128 * sg:128 * (sg + 1)],
                            ones65[hb:hb + 1, 0:128],
                            negMr[hb:hb + 1, 128 * h:128 * (h + 1)],
                            start=False, stop=True))(hb, h, sg), wait=[last, CON[1]])
                    t_exp[h] = p.op("s", (lambda h: lambda en: en.activation(
                        A_T[:, 384 * h:384 * (h + 1)], ps_2[:, 0:384], AF.Exp,
                        scale=QSCALE))(h), wait=[last] + gw("A_T"))
                    G["ps_2"] = [t_exp[h]]
                expall = [t_exp[h] for h in range(16)]
                G["negMr"] = expall
                tm0 = p.op("v", lambda en: en.tensor_mul(
                    A_T[...].rearrange("p (h s) -> p h s", h=16)[:, :, 0:128],
                    A_T[...].rearrange("p (h s) -> p h s", h=16)[:, :, 0:128],
                    msk0[...].rearrange("p (o w) -> p o w", o=1).broadcast_to((128, 16, 128))),
                    wait=expall + [CON[0]])
                tm2 = p.op("v", lambda en: en.tensor_mul(
                    A_T[...].rearrange("p (h s) -> p h s", h=16)[:, :, 256:384],
                    A_T[...].rearrange("p (h s) -> p h s", h=16)[:, :, 256:384],
                    msk2[...].rearrange("p (o w) -> p o w", o=1).broadcast_to((128, 16, 128))),
                    wait=expall)
                at_readers = []
                for q4 in range(4):
                    last_av = None
                    last_dn = None
                    for hh in range(4):
                        h = 4 * q4 + hh
                        for sg in range(3):
                            beta = tau + sg
                            wl = [tm0, tm2, tv_last] + ([last_av] if last_av else [])
                            if hh == 0 and sg == 0:
                                wl += gw("ps_av")
                            last_av = p.op("t", (lambda h, hh, sg, beta: lambda en: en.matmul(
                                ps_av[0:64, 128 * hh:128 * (hh + 1)],
                                vbuf[:, beta * D + 64 * h:beta * D + 64 * (h + 1)],
                                A_T[:, 384 * h + 128 * sg:384 * h + 128 * (sg + 1)],
                                start=(sg == 0), stop=(sg == 2)))(h, hh, sg, beta), wait=wl)
                            wl2 = [tm0, tm2, CON[0]] + ([last_dn] if last_dn else [])
                            if hh == 0 and sg == 0:
                                wl2 += gw("ps_dn")
                            last_dn = p.op("t", (lambda h, hh, sg, beta: lambda en: en.matmul(
                                ps_dn[0:1, 128 * hh:128 * (hh + 1)],
                                von[:, beta:beta + 1],
                                A_T[:, 384 * h + 128 * sg:384 * h + 128 * (sg + 1)],
                                start=(sg == 0), stop=(sg == 2)))(h, hh, sg, beta), wait=wl2)
                    trc = p.op("v", lambda en: en.reciprocal(
                        rrow[0:1, 0:512], ps_dn[0:1, 0:512]), wait=[last_dn] + gw("rrow"))
                    G["ps_dn"] = [trc]
                    tbc = p.op("t", lambda en: en.matmul(
                        ps_bc[0:64, 0:512], onc[...], rrow[0:1, 0:512],
                        start=True, stop=True), wait=[trc, CON[2]] + gw("ps_bc"))
                    G["rrow"] = [tbc]
                    tev = p.op("s", lambda en: en.activation(
                        oev[...], ps_av[0:64, 0:512], AF.Identity),
                        wait=[last_av] + gw("oev"))
                    G["ps_av"] = [tev]
                    tne = p.op("v", (lambda q4, ab: lambda en: en.tensor_mul(
                        attb[ab][0:64, 256 * q4:256 * q4 + 256].rearrange(
                            "p (r w) -> p r w", r=2),
                        oev[...].rearrange("p (r c w) -> p r c w", r=2, c=2)[:, :, 0, :],
                        ps_bc[0:64, 0:512].rearrange(
                            "p (r c w) -> p r c w", r=2, c=2)[:, :, 0, :]))(q4, ab),
                        wait=[tev, tbc] + gw(f"attb{ab}"))
                    tno = p.op("v", (lambda q4, ab: lambda en: en.tensor_mul(
                        attb[ab][64:128, 256 * q4:256 * q4 + 256].rearrange(
                            "p (r w) -> p r w", r=2),
                        oev[...].rearrange("p (r c w) -> p r c w", r=2, c=2)[:, :, 1, :],
                        ps_bc[0:64, 0:512].rearrange(
                            "p (r c w) -> p r c w", r=2, c=2)[:, :, 1, :]))(q4, ab),
                        wait=[tev, tbc])
                    G["ps_bc"] = [tne, tno]
                    G["oev"] = [tne, tno]
                    at_readers += [tne, tno]
                G["A_T"] = at_readers
                G[f"attb_w{ab}"] = at_readers
                wo_readers = []
                for n in range(8):
                    rb = n % 2
                    t_xr = p.op("y", (lambda n, tau, rb: lambda en: en.dma_start(
                        xres[rb][...],
                        x_d[128 * n:128 * (n + 1), HAL + 128 * tau:HAL + 128 * (tau + 1)]))(n, tau, rb),
                        wait=gw(f"xres{rb}"), dma=True, stream=f"xr{rb}")
                    last = None
                    for kk in range(8):
                        wl = at_readers + [w_wo] + ([last] if last else [])
                        if kk == 0:
                            wl += gw("ps_q")
                        last = p.op("t", (lambda n, kk, ab: lambda en: en.matmul(
                            ps_q[:, 0:128],
                            wbuf2[:, kk * D + 128 * n:kk * D + 128 * (n + 1)],
                            attb[ab][:, 128 * kk:128 * (kk + 1)],
                            start=(kk == 0), stop=(kk == 7)))(n, kk, ab), wait=wl)
                    wo_readers.append(last)
                    tad = p.op("v", (lambda rb: lambda en: en.tensor_add(
                        outb[rb][...], ps_q[:, 0:128], xres[rb][...]))(rb),
                        wait=[last, t_xr] + gw(f"outb{rb}"))
                    G["ps_q"] = [tad]
                    G[f"xres{rb}"] = [tad]
                    tst = p.op("y", (lambda n, tau, rb: lambda en: en.dma_start(
                        y_d[128 * n:128 * (n + 1), 128 * tau:128 * (tau + 1)],
                        outb[rb][...]))(n, tau, rb), wait=[tad], dma=True, stream=f"yo{rb}")
                    G[f"outb{rb}"] = [tst]
                G[f"attb{ab}"] = wo_readers

            p.op("v", lambda en: en.engine_nop(), wait=p.all_done())
            p.emit(est)
        return nc


    def attn_inputs(x_own_cm, x_halo_cm, Wqkv, q_scale, k_scale, Wo, pos0, seq_start):
        DIMH = 64
        WINDOW = 256
        xin = np.concatenate([x_halo_cm, x_own_cm], axis=1).astype(np.float32)
        Wq = Wqkv[:, 0:1024]
        Wk = Wqkv[:, 1024:2048]
        Wv = Wqkv[:, 2048:3072]
        qs = np.tile(q_scale, 2).reshape(128, 1).astype(np.float32)
        ks = np.tile(k_scale, 2).reshape(128, 1).astype(np.float32)

        def tables(positions, inv):
            half = np.arange(0, DIMH, 2, dtype=np.float32)
            inv_freq = 1.0 / 10000.0 ** (half / DIMH)
            t = positions.astype(np.float32)
            freqs = t[:, None] * inv_freq[None, :]
            freqs = np.concatenate([freqs, freqs], -1)
            base_scale = (half + 0.4 * DIMH) / (1.4 * DIMH)
            power = (t - 8192 // 2) / (WINDOW // 2)
            scale = base_scale[None, :] ** power[:, None]
            scale = np.concatenate([scale, scale], -1)
            if inv:
                scale = 1.0 / scale
            cs = np.cos(freqs) * scale
            ss = np.sin(freqs) * scale
            cs = np.tile(cs.T, (2, 1)).astype(np.float32)
            ss = np.tile(ss.T, (2, 1)).astype(np.float32)
            return np.ascontiguousarray(cs), np.ascontiguousarray(ss)

        pos_q = pos0 + np.arange(2048)
        pos_k = pos0 - 256 + np.arange(2304)
        cs_q, ss_q = tables(pos_q, inv=False)
        cs_k, ss_k = tables(pos_k, inv=True)
        vones = np.zeros((128, NB), np.float32)
        for beta in range(NB):
            for pp in range(128):
                kpos = pos0 - 256 + 128 * beta + pp
                vones[pp, beta] = 1.0 if (kpos >= 0 or not seq_start) else 0.0
        if not seq_start:
            vones[:] = 1.0
        return {
            "x": xin,
            "Wq": np.ascontiguousarray(Wq.astype(np.float32)),
            "Wk": np.ascontiguousarray(Wk.astype(np.float32)),
            "Wv": np.ascontiguousarray(Wv.astype(np.float32)),
            "Wo": np.ascontiguousarray(Wo.astype(np.float32)),
            "qs": qs, "ks": ks,
            "cs_q": cs_q, "ss_q": ss_q, "cs_k": cs_k, "ss_k": ss_k,
            "vones": vones,
        }
    return build_ffn_attn, attn_inputs

build_ffn_attn, attn_inputs = _ns_attn()

def _ns_ffn():
    D = 1024
    DI = 2730
    NMB = 22
    TCH = 1024


    def mblk_sz(m):
        return 42 if m == NMB - 1 else 128


    def build_ffn_ffn(nt=2048, eps=1e-5):
        nc = bass.Bass("TRN2", target_bir_lowering=False)
        x_d = nc.dram_tensor("x", [D, nt + 1], F32, kind="ExternalInput")
        w1a_d = nc.dram_tensor("W1a", [D, DI], F32, kind="ExternalInput")
        w1g_d = nc.dram_tensor("W1g", [D, DI], F32, kind="ExternalInput")
        b1a_d = nc.dram_tensor("b1a", [128, NMB], F32, kind="ExternalInput")
        b1g_d = nc.dram_tensor("b1g", [128, NMB], F32, kind="ExternalInput")
        w2_d = nc.dram_tensor("W2", [DI, D], F32, kind="ExternalInput")
        b2_d = nc.dram_tensor("b2r", [1, D], F32, kind="ExternalInput")
        lng_d = nc.dram_tensor("lng", [128, 8], F32, kind="ExternalInput")
        lnb_d = nc.dram_tensor("lnb", [128, 8], F32, kind="ExternalInput")
        y_d = nc.dram_tensor("y", [D, nt], F32, kind="ExternalOutput")

        nchunk = nt // TCH
        with ExitStack() as st:
            e = st.enter_context
            x = [e(nc.sbuf_tensor(f"xt{i}", [128, nt + 1], F32)) for i in range(8)]
            lng = e(nc.sbuf_tensor("lngs", [128, 8], F32))
            lnb = e(nc.sbuf_tensor("lnbs", [128, 8], F32))
            b1a = e(nc.sbuf_tensor("b1as", [128, NMB], F32))
            b1g = e(nc.sbuf_tensor("b1gs", [128, NMB], F32))
            b2r = e(nc.sbuf_tensor("b2rs", [1, D], BF16))
            onesr = e(nc.sbuf_tensor("onesr", [1, TCH], BF16))
            ones2 = e(nc.sbuf_tensor("ones2", [128, 2], BF16))
            onesf = e(nc.sbuf_tensor("onesf", [1, 128], F32))
            xn = [e(nc.sbuf_tensor(f"xn{i}", [128, TCH], BF16)) for i in range(8)]
            xcast = [e(nc.sbuf_tensor(f"xcast{k}", [128, TCH], BF16)) for k in range(2)]
            xsq = [e(nc.sbuf_tensor(f"xsq{k}", [128, TCH], BF16)) for k in range(2)]
            hbuf = e(nc.sbuf_tensor("hbuf", [128, NMB * TCH], BF16))
            gelu_s = e(nc.sbuf_tensor("gelu_s", [128, TCH], BF16))
            a_s = e(nc.sbuf_tensor("a_s", [128, TCH], BF16))
            w1buf = [e(nc.sbuf_tensor(f"w1b{k}", [128, 16 * 128], BF16)) for k in range(2)]
            w2buf = [e(nc.sbuf_tensor(f"w2b{k}", [128, NMB * 128], BF16)) for k in range(2)]
            outbuf = [e(nc.sbuf_tensor(f"outb{k}", [128, TCH], F32)) for k in range(2)]
            murow = e(nc.sbuf_tensor("murow", [1, TCH], F32))
            musq = e(nc.sbuf_tensor("musq", [1, TCH], F32))
            varrow = e(nc.sbuf_tensor("varrow", [1, TCH], F32))
            lnvrow = e(nc.sbuf_tensor("lnvrow", [1, TCH], F32))
            rstdrow = e(nc.sbuf_tensor("rstdrow", [1, TCH], F32))
            mrrow = e(nc.sbuf_tensor("mrrow", [1, TCH], F32))
            Rb = e(nc.sbuf_tensor("Rb", [128, TCH], BF16))
            Mb = e(nc.sbuf_tensor("Mb", [128, TCH], BF16))
            ps_r0 = e(nc.psum_tensor("ps_r0", [1, TCH], F32))
            ps_r1 = e(nc.psum_tensor("ps_r1", [1, TCH], F32))
            ps_A = e(nc.psum_tensor("ps_A", [128, TCH], F32))
            ps_B = e(nc.psum_tensor("ps_B", [128, TCH], F32))

            p = Prog(nc)
            for i in range(8):
                w_x_all = p.op("y", (lambda i: lambda en: en.dma_start(
                    x[i][...], x_d[128 * i:128 * (i + 1), :]))(i), dma=True, stream="x")
            w_x = [w_x_all] * 8
            for (sb_, dr_) in [(lng, lng_d), (lnb, lnb_d), (b1a, b1a_d), (b1g, b1g_d)]:
                w_sm = p.op("y", (lambda sb_, dr_: lambda en: en.dma_start(
                    sb_[...], dr_[...]))(sb_, dr_), dma=True, stream="sm")
            w_lng = w_lnb = w_b1a = w_b1g = w_sm
            w_b2 = p.op("g", lambda en: en.dma_start(b2r[...], b2_d[...]), dma=True, stream="b2")
            c1 = p.op("v", lambda en: en.memset(onesr[...], 1.0))
            c2 = p.op("v", lambda en: en.memset(ones2[...], 1.0))
            c3 = p.op("v", lambda en: en.memset(onesf[...], 1.0))

            pv_var = None
            pv_psA_readers = []
            pv_h_last = None
            pv_w2_last = None

            for ch in range(nchunk):
                if ch > 0:
                    p.phase_barrier()
                c0 = ch * TCH

                def xsh_ap(i):
                    off = 1 if i < 4 else 0
                    return x[i][:, c0 + off: c0 + off + TCH]

                def xres_ap(i):
                    return x[i][:, c0 + 1:c0 + 1 + TCH]

                tprev_mm = None
                tc_list = []
                for i in range(8):
                    b = i % 2
                    wl = [w_x[i], c1, c2]
                    if i >= 2:
                        wl.append(tprev_mm)
                    tc = p.op("v", (lambda xv, b: lambda en: en.tensor_copy(
                        xcast[b][...], xv))(xsh_ap(i), b), wait=wl)
                    tsq = p.op("s", (lambda b: lambda en: en.activation(
                        xsq[b][...], xcast[b][...], AF.Square))(b), wait=[tc])
                    mmw = [tc]
                    if tprev_mm:
                        mmw.append(tprev_mm)
                    if i == 0 and pv_var is not None:
                        mmw.append(pv_var)
                    for q in range(TCH // 512):
                        t1 = p.op("t", (lambda b, i, q: lambda en: en.matmul(
                            ps_r0[0:1, q * 512:(q + 1) * 512], ones2[:, 0:1],
                            xcast[b][:, q * 512:(q + 1) * 512],
                            start=(i == 0), stop=(i == 7)))(b, i, q), wait=mmw)
                        mmw = [t1]
                    for q in range(TCH // 512):
                        t2 = p.op("t", (lambda b, i, q: lambda en: en.matmul(
                            ps_r1[0:1, q * 512:(q + 1) * 512], ones2[:, 1:2],
                            xsq[b][:, q * 512:(q + 1) * 512],
                            start=(i == 0), stop=(i == 7)))(b, i, q), wait=[tsq, t1, t2] if i else [tsq, t1])
                    tc_list.append(tc)
                    tprev_mm = t2
                t_mu = p.op("v", lambda en: en.tensor_scalar_mul(
                    murow[...], ps_r0[0:1, :], 1.0 / D), wait=[tprev_mm])
                t_m2 = p.op("v", lambda en: en.tensor_mul(
                    musq[...], murow[...], murow[...]), wait=[t_mu])
                t_var = p.op("v", lambda en: en.scalar_tensor_tensor(
                    varrow[...], ps_r1[0:1, :], 1.0 / D, musq[...],
                    ALU.mult, ALU.subtract), wait=[t_m2, tprev_mm])
                pv_var = t_var
                t_vpe = p.op("v", lambda en: en.tensor_scalar_add(
                    varrow[...], varrow[...], eps), wait=[t_var])
                t_lnv = p.op("s", lambda en: en.activation(
                    lnvrow[...], varrow[...], AF.Ln), wait=[t_vpe])
                t_rstd = p.op("s", lambda en: en.activation(
                    rstdrow[...], lnvrow[...], AF.Exp, scale=-0.5), wait=[t_lnv])
                t_mr = p.op("v", lambda en: en.tensor_mul(
                    mrrow[...], murow[...], rstdrow[...]), wait=[t_rstd, t_mu])
                tRb, tMb = [], []
                for q in range(TCH // 512):
                    bw = [t_rstd, c1] + pv_psA_readers
                    tb = p.op("t", (lambda q: lambda en: en.matmul(
                        ps_A[:, q * 512:(q + 1) * 512], onesf[0:1, :],
                        rstdrow[0:1, q * 512:(q + 1) * 512], start=True, stop=True))(q),
                        wait=bw + [c3])
                    tRb.append(p.op("s", (lambda q: lambda en: en.activation(
                        Rb[:, q * 512:(q + 1) * 512], ps_A[:, q * 512:(q + 1) * 512],
                        AF.Identity))(q), wait=[tb]))
                for q in range(TCH // 512):
                    tb = p.op("t", (lambda q: lambda en: en.matmul(
                        ps_A[:, q * 512:(q + 1) * 512], onesf[0:1, :],
                        mrrow[0:1, q * 512:(q + 1) * 512], start=True, stop=True))(q),
                        wait=[t_mr, tRb[q], c3])
                    tMb.append(p.op("s", (lambda q: lambda en: en.activation(
                        Mb[:, q * 512:(q + 1) * 512], ps_A[:, q * 512:(q + 1) * 512],
                        AF.Identity, scale=-1.0))(q), wait=[tb]))
                t_xn = []
                for i in range(8):
                    tm = p.op("v", (lambda xv, i: lambda en: en.tensor_mul(
                        xn[i][...], xv, Rb[...]))(xsh_ap(i), i),
                        wait=tRb + [w_x[i]])
                    ta = p.op("v", (lambda i: lambda en: en.tensor_add(
                        xn[i][...], xn[i][...], Mb[...]))(i), wait=[tm] + tMb)
                    tg = p.op("s", (lambda i: lambda en: en.activation(
                        xn[i][...], xn[i][...], AF.Identity,
                        scale=lng[:, i:i + 1], bias=lnb[:, i:i + 1]))(i),
                        wait=[ta, w_lng, w_lnb])
                    t_xn.append(tg)

                p.phase_barrier()
                t_w1 = {}
                t_h = {}
                for m in range(NMB):
                    b = m % 2
                    mw = mblk_sz(m)
                    wwait = [t_h[m - 2]] if m >= 2 else ([pv_w2_last] if pv_w2_last else [])
                    t_w1a = p.op("g", (lambda m, b, mw: lambda en: en.dma_start(
                        w1buf[b][:, 0:8 * mw].rearrange("p (a w) -> p a w", a=8),
                        w1a_d[:, 128 * m:128 * m + mw].rearrange("(a p) w -> p a w", p=128)))(m, b, mw),
                        wait=wwait, dma=True, stream=f"w1p{b}")
                    t_w1g = p.op("g", (lambda m, b, mw: lambda en: en.dma_start(
                        w1buf[b][:, 8 * mw:16 * mw].rearrange("p (a w) -> p a w", a=8),
                        w1g_d[:, 128 * m:128 * m + mw].rearrange("(a p) w -> p a w", p=128)))(m, b, mw),
                        wait=wwait, dma=True, stream=f"w1p{b}")
                    psw = [t_h[m - 1]] if m >= 1 else ([pv_w2_last] if pv_w2_last else [])
                    last = None
                    for q in range(TCH // 512):
                        for k in range(8):
                            wl = [t_w1g, t_xn[k]] + psw + ([last] if last else []) + \
                                (tMb if m == 0 else [])
                            last = p.op("t", (lambda b, mw, q, k: lambda en: en.matmul(
                                ps_A[0:mw, q * 512:(q + 1) * 512],
                                w1buf[b][:, k * mw:(k + 1) * mw],
                                xn[k][:, q * 512:(q + 1) * 512],
                                start=(k == 0), stop=(k == 7)))(b, mw, q, k), wait=wl)
                    t_a = last
                    for q in range(TCH // 512):
                        for k in range(8):
                            wl = [t_w1g, t_xn[k]] + psw + [last]
                            last = p.op("t", (lambda b, mw, q, k: lambda en: en.matmul(
                                ps_B[0:mw, q * 512:(q + 1) * 512],
                                w1buf[b][:, (8 + k) * mw:(9 + k) * mw],
                                xn[k][:, q * 512:(q + 1) * 512],
                                start=(k == 0), stop=(k == 7)))(b, mw, q, k), wait=wl)
                    t_g = last
                    tge = p.op("s", (lambda m, mw: lambda en: en.activation(
                        gelu_s[0:mw, :], ps_B[0:mw, :], AF.Gelu,
                        bias=b1g[0:mw, m:m + 1]))(m, mw), wait=[t_g, w_b1g])
                    tas = p.op("s", (lambda m, mw: lambda en: en.activation(
                        a_s[0:mw, :], ps_A[0:mw, :], AF.Identity,
                        bias=b1a[0:mw, m:m + 1]))(m, mw), wait=[t_a, w_b1a, tge])
                    hw = [tge, tas]
                    if pv_w2_last is not None:
                        hw.append(pv_w2_last)
                    t_h[m] = p.op("v", (lambda m, mw: lambda en: en.tensor_mul(
                        hbuf[0:mw, m * TCH:(m + 1) * TCH], a_s[0:mw, :],
                        gelu_s[0:mw, :]))(m, mw), wait=hw)
                pv_h_last = t_h[NMB - 1]

                p.phase_barrier()
                t_out_dma = {}
                tadds = []
                for n in range(8):
                    b = n % 2
                    wwait = [t_out_dma[n - 2]] if n >= 2 else []
                    t_w2a = p.op("g", (lambda n, b: lambda en: en.dma_start(
                        w2buf[b][:, 0:21 * 128].rearrange("p (a w) -> p a w", a=21),
                        w2_d[0:2688, 128 * n:128 * (n + 1)].rearrange("(a p) w -> p a w", p=128)))(n, b),
                        wait=wwait, dma=True, stream=f"w2p{b}")
                    t_w2 = p.op("g", (lambda n, b: lambda en: en.dma_start(
                        w2buf[b][0:42, 21 * 128:22 * 128],
                        w2_d[2688:2730, 128 * n:128 * (n + 1)]))(n, b),
                        wait=wwait, dma=True, stream=f"w2p{b}")
                    last = None
                    for q in range(TCH // 512):
                        for m in range(NMB):
                            mw = mblk_sz(m)
                            wl = [t_w2, t_w2a, t_h[m]] + ([last] if last else [])
                            if n == 0 and q == 0 and m == 0:
                                wl.append(t_h[NMB - 1])
                            if n >= 1 and q == 0 and m == 0:
                                wl.append(tadds[n - 1])
                            last = p.op("t", (lambda b, q, m, mw: lambda en: en.matmul(
                                ps_A[:, q * 512:(q + 1) * 512],
                                w2buf[b][0:mw, m * 128:(m + 1) * 128],
                                hbuf[0:mw, m * TCH + q * 512:m * TCH + (q + 1) * 512],
                                start=(m == 0), stop=False))(b, q, m, mw), wait=wl)
                        last = p.op("t", (lambda n, q: lambda en: en.matmul(
                            ps_A[:, q * 512:(q + 1) * 512],
                            b2r[0:1, 128 * n:128 * (n + 1)],
                            onesr[0:1, q * 512:(q + 1) * 512],
                            start=False, stop=True))(n, q), wait=[last, w_b2, c1])
                    pv_w2_last = last
                    tadd = p.op("v", (lambda xv, b: lambda en: en.tensor_add(
                        outbuf[b][...], ps_A[...], xv))(xres_ap(n), b),
                        wait=[last] + wwait)
                    tadds.append(tadd)
                    t_out_dma[n] = p.op("y", (lambda n, b, c0: lambda en: en.dma_start(
                        y_d[128 * n:128 * (n + 1), c0:c0 + TCH], outbuf[b][...]))(n, b, c0),
                        wait=[tadd], dma=True, stream=f"outp{b}")
                pv_psA_readers = tadds[-2:]

            p.op("v", lambda en: en.engine_nop(), wait=p.all_done())
            p.emit(st)
        return nc


    def ffn_inputs(x_own, x_halo_col, W1, b1, W2, b2, g, bln):
        nt = x_own.shape[1]
        xin = np.concatenate([x_halo_col.reshape(-1, 1), x_own], axis=1).astype(np.float32)
        b1a = np.zeros((128, NMB), np.float32)
        b1g = np.zeros((128, NMB), np.float32)
        for m in range(NMB):
            mw = mblk_sz(m)
            b1a[:mw, m] = b1[128 * m:128 * m + mw]
            b1g[:mw, m] = b1[DI + 128 * m:DI + 128 * m + mw]
        return {
            "x": np.ascontiguousarray(xin),
            "W1a": np.ascontiguousarray(W1[:, :DI].astype(np.float32)),
            "W1g": np.ascontiguousarray(W1[:, DI:].astype(np.float32)),
            "b1a": b1a, "b1g": b1g,
            "W2": np.ascontiguousarray(W2.astype(np.float32)),
            "b2r": np.ascontiguousarray(b2.reshape(1, -1).astype(np.float32)),
            "lng": np.ascontiguousarray(g.reshape(8, 128).T.astype(np.float32)),
            "lnb": np.ascontiguousarray(bln.reshape(8, 128).T.astype(np.float32)),
        }
    return build_ffn_ffn, ffn_inputs

build_ffn_ffn, ffn_inputs = _ns_ffn()

def _ns_lru():
    D = 1024
    NST = 256
    NT = 2048
    NCK = 16
    TCH = 512


    def _ln_alloc(nc, e):
        t = {}
        t["onesc"] = e(nc.sbuf_tensor("l_onesc", [128, 2], BF16))
        t["onesf"] = e(nc.sbuf_tensor("l_onesf", [1, 128], F32))
        t["xcc"] = [e(nc.sbuf_tensor(f"l_xcc{k}", [128, TCH], BF16)) for k in range(2)]
        t["xsq"] = [e(nc.sbuf_tensor(f"l_xsq{k}", [128, TCH], BF16)) for k in range(2)]
        for nm in ["mu", "musq", "var", "rstd", "mr"]:
            t[nm] = e(nc.sbuf_tensor(f"l_{nm}", [1, TCH], F32))
        t["Rb"] = e(nc.sbuf_tensor("l_Rb", [128, TCH], BF16))
        t["Mb"] = e(nc.sbuf_tensor("l_Mb", [128, TCH], BF16))
        t["ps_r0"] = e(nc.psum_tensor("l_psr0", [1, TCH], F32))
        t["ps_r1"] = e(nc.psum_tensor("l_psr1", [1, TCH], F32))
        t["ps_bc"] = e(nc.psum_tensor("l_psbc", [128, TCH], F32))
        return t


    def _ln_init(p, t):
        c1 = p.op("v", lambda en: en.memset(t["onesc"][...], 1.0))
        c2 = p.op("v", lambda en: en.memset(t["onesf"][...], 1.0))
        return [c1, c2]


    def _ln_chunk(p, G, gw, t, cons, xap, h3ap):
        tprev = None
        for j in range(8):
            sb = j % 2
            tc = p.op("v", (lambda j, sb: lambda en: en.tensor_copy(
                t["xcc"][sb][...], xap(j)))(j, sb), wait=cons + gw(f"lxcc{sb}"))
            tq = p.op("s", (lambda sb: lambda en: en.activation(
                t["xsq"][sb][...], t["xcc"][sb][...], AF.Square))(sb),
                wait=[tc] + gw(f"lxsq{sb}"))
            wl = [tc] + cons + ([tprev] if tprev else [])
            if j == 0:
                wl += gw("lpsr")
            t1 = p.op("t", (lambda j, sb: lambda en: en.matmul(
                t["ps_r0"][0:1, :], t["onesc"][:, 0:1], t["xcc"][sb][...],
                start=(j == 0), stop=(j == 7)))(j, sb), wait=wl)
            t2 = p.op("t", (lambda j, sb: lambda en: en.matmul(
                t["ps_r1"][0:1, :], t["onesc"][:, 1:2], t["xsq"][sb][...],
                start=(j == 0), stop=(j == 7)))(j, sb), wait=[tq, t1])
            G[f"lxcc{sb}"] = [t1, tq]
            G[f"lxsq{sb}"] = [t2]
            tprev = t2
        t_mu = p.op("v", lambda en: en.tensor_scalar_mul(
            t["mu"][...], t["ps_r0"][0:1, :], 1.0 / D), wait=[tprev])
        t_m2 = p.op("v", lambda en: en.tensor_mul(
            t["musq"][...], t["mu"][...], t["mu"][...]), wait=[t_mu])
        t_var = p.op("v", lambda en: en.scalar_tensor_tensor(
            t["var"][...], t["ps_r1"][0:1, :], 1.0 / D, t["musq"][...],
            ALU.mult, ALU.subtract), wait=[t_m2, tprev])
        G["lpsr"] = [t_var]
        t_ve = p.op("v", lambda en: en.tensor_scalar_add(
            t["var"][...], t["var"][...], 1e-5), wait=[t_var])
        t_ln = p.op("s", lambda en: en.activation(
            t["rstd"][...], t["var"][...], AF.Ln), wait=[t_ve])
        t_rs = p.op("s", lambda en: en.activation(
            t["rstd"][...], t["rstd"][...], AF.Exp, scale=-0.5), wait=[t_ln])
        t_mr = p.op("v", lambda en: en.tensor_mul(
            t["mr"][...], t["mu"][...], t["rstd"][...]), wait=[t_rs, t_mu])
        tb = p.op("t", lambda en: en.matmul(
            t["ps_bc"][:, 0:512], t["onesf"][0:1, :], t["rstd"][0:1, 0:512],
            start=True, stop=True), wait=[t_rs] + cons + gw("lpsbc"))
        tR = p.op("s", lambda en: en.activation(
            t["Rb"][:, 0:512], t["ps_bc"][:, 0:512], AF.Identity),
            wait=[tb] + gw("lRb"))
        tb2 = p.op("t", lambda en: en.matmul(
            t["ps_bc"][:, 0:512], t["onesf"][0:1, :], t["mr"][0:1, 0:512],
            start=True, stop=True), wait=[t_mr, tR])
        tM = p.op("s", lambda en: en.activation(
            t["Mb"][:, 0:512], t["ps_bc"][:, 0:512], AF.Identity, scale=-1.0),
            wait=[tb2] + gw("lMb"))
        G["lpsbc"] = [tM]
        t_h = []
        for j in range(8):
            tm = p.op("v", (lambda j: lambda en: en.tensor_mul(
                h3ap(j), xap(j), t["Rb"][:, 0:512]))(j), wait=[tR] + gw("lh3"))
            ta = p.op("v", (lambda j: lambda en: en.tensor_add(
                h3ap(j), h3ap(j), t["Mb"][:, 0:512]))(j), wait=[tm, tM])
            t_h.append(ta)
        G["lRb"] = t_h
        G["lMb"] = t_h
        G["lh3"] = []
        return t_h


    def _lam_cols(p, G, gw, nc, e, nu2, th2, w_in):
        r = {}
        for nm in ["nuv", "th", "eneg", "s2", "sth", "cth", "lamR", "lamI",
                   "gam", "tmp"]:
            r[nm] = e(nc.sbuf_tensor(f"rc_{nm}", [128, 2], F32))
        t1 = p.op("s", lambda en: en.activation(r["nuv"][...], nu2[...], AF.Exp),
                  wait=w_in)
        t2 = p.op("s", lambda en: en.activation(r["th"][...], th2[...], AF.Exp),
                  wait=w_in)
        t3 = p.op("v", lambda en: en.tensor_single_scalar(
            r["tmp"][...], r["th"][...], float(np.pi), ALU.is_gt), wait=[t2])
        t4 = p.op("v", lambda en: en.scalar_tensor_tensor(
            r["th"][...], r["tmp"][...], float(-2.0 * np.pi), r["th"][...],
            ALU.mult, ALU.add), wait=[t3])
        t5 = p.op("s", lambda en: en.activation(r["sth"][...], r["th"][...], AF.Sin),
                  wait=[t4])
        t6 = p.op("s", lambda en: en.activation(r["s2"][...], r["th"][...], AF.Sin,
                                                scale=0.5), wait=[t4, t5])
        t7 = p.op("v", lambda en: en.tensor_mul(r["s2"][...], r["s2"][...],
                                                r["s2"][...]), wait=[t6])
        t8 = p.op("v", lambda en: en.tensor_scalar(
            r["cth"][...], r["s2"][...], -2.0, 1.0, ALU.mult, ALU.add), wait=[t7])
        t9 = p.op("s", lambda en: en.activation(r["eneg"][...], r["nuv"][...], AF.Exp,
                                                scale=-1.0), wait=[t1, t6])
        ta = p.op("v", lambda en: en.tensor_mul(r["lamR"][...], r["eneg"][...],
                                                r["cth"][...]), wait=[t9, t8])
        tb = p.op("v", lambda en: en.tensor_mul(r["lamI"][...], r["eneg"][...],
                                                r["sth"][...]), wait=[t9, t5])
        tc = p.op("v", lambda en: en.tensor_mul(r["tmp"][...], r["eneg"][...],
                                                r["eneg"][...]), wait=[t9])
        td = p.op("v", lambda en: en.tensor_scalar(
            r["tmp"][...], r["tmp"][...], -1.0, 1.0, ALU.mult, ALU.add), wait=[tc])
        te = p.op("s", lambda en: en.activation(r["gam"][...], r["tmp"][...], AF.Sqrt),
                  wait=[td])
        G["lam"] = [ta, tb, te]
        return r


    def _build_pow(p, G, gw, nc, e, r, sign, width, name):
        TR = e(nc.sbuf_tensor(f"{name}R", [128, 2 * width], F32))
        TI = e(nc.sbuf_tensor(f"{name}I", [128, 2 * width], F32))
        t1r = e(nc.sbuf_tensor(f"{name}t1", [128, width], F32))
        t2r = e(nc.sbuf_tensor(f"{name}t2", [128, width], F32))
        outs = []
        for tile in range(2):
            off = tile * width
            lamR = r["lamR"][:, tile:tile + 1]
            lamI = r["lamI"][:, tile:tile + 1]
            last = [p.op("v", (lambda off: lambda en: en.memset(
                TR[:, off:off + 1], 1.0))(off), wait=gw("lam")),
                p.op("v", (lambda off: lambda en: en.memset(
                    TI[:, off:off + 1], 0.0))(off), wait=gw("lam"))]
            if sign > 0:
                last.append(p.op("v", (lambda off, lamR: lambda en: en.tensor_copy(
                    TR[:, off + 1:off + 2], lamR))(off, lamR), wait=gw("lam")))
                last.append(p.op("v", (lambda off, lamI: lambda en: en.tensor_copy(
                    TI[:, off + 1:off + 2], lamI))(off, lamI), wait=gw("lam")))
            else:
                a1 = p.op("v", (lambda lamR: lambda en: en.tensor_mul(
                    t1r[:, 0:1], lamR, lamR))(lamR), wait=gw("lam") + gw(f"{name}tmp"))
                a2 = p.op("v", (lambda lamI: lambda en: en.tensor_mul(
                    t2r[:, 0:1], lamI, lamI))(lamI), wait=gw("lam") + gw(f"{name}tmp"))
                a3 = p.op("v", lambda en: en.tensor_add(
                    t1r[:, 0:1], t1r[:, 0:1], t2r[:, 0:1]), wait=[a1, a2])
                a4 = p.op("v", lambda en: en.reciprocal(
                    t1r[:, 1:2], t1r[:, 0:1]), wait=[a3])
                a5 = p.op("v", (lambda off, lamR: lambda en: en.tensor_mul(
                    TR[:, off + 1:off + 2], lamR, t1r[:, 1:2]))(off, lamR), wait=[a4])
                a6 = p.op("v", (lambda off, lamI: lambda en: en.tensor_mul(
                    TI[:, off + 1:off + 2], lamI, t1r[:, 1:2]))(off, lamI), wait=[a4])
                a7 = p.op("v", (lambda off: lambda en: en.tensor_scalar_mul(
                    TI[:, off + 1:off + 2], TI[:, off + 1:off + 2], -1.0))(off),
                    wait=[a6])
                last += [a5, a7]
            filled = 2
            while filled < width:
                s = min(filled - 1, width - filled)
                cR = TR[:, off + filled - 1:off + filled]
                cI = TI[:, off + filled - 1:off + filled]
                m1 = p.op("v", (lambda off, s, cR: lambda en: en.tensor_scalar_mul(
                    t1r[:, 0:s], TR[:, off + 1:off + 1 + s], cR))(off, s, cR), wait=last)
                m2 = p.op("v", (lambda off, s, cI: lambda en: en.tensor_scalar_mul(
                    t2r[:, 0:s], TI[:, off + 1:off + 1 + s], cI))(off, s, cI), wait=last)
                m3 = p.op("v", (lambda off, s, filled: lambda en: en.tensor_sub(
                    TR[:, off + filled:off + filled + s], t1r[:, 0:s],
                    t2r[:, 0:s]))(off, s, filled), wait=[m1, m2])
                m4 = p.op("v", (lambda off, s, cI: lambda en: en.tensor_scalar_mul(
                    t1r[:, 0:s], TR[:, off + 1:off + 1 + s], cI))(off, s, cI), wait=[m3])
                m5 = p.op("v", (lambda off, s, cR: lambda en: en.tensor_scalar_mul(
                    t2r[:, 0:s], TI[:, off + 1:off + 1 + s], cR))(off, s, cR), wait=[m4])
                m6 = p.op("v", (lambda off, s, filled: lambda en: en.tensor_add(
                    TI[:, off + filled:off + filled + s], t1r[:, 0:s],
                    t2r[:, 0:s]))(off, s, filled), wait=[m5])
                last = [m3, m6]
                filled += s
            outs += last
            G[f"{name}tmp"] = last
        G[name] = outs
        return TR, TI


    def build_lru1():
        nc = bass.Bass("TRN2", target_bir_lowering=False)
        x_d = nc.dram_tensor("x3", [D, NT], F32, kind="ExternalInput")
        nu_d = nc.dram_tensor("nu2", [128, 2], F32, kind="ExternalInput")
        th_d = nc.dram_tensor("th2", [128, 2], F32, kind="ExternalInput")
        br_d = nc.dram_tensor("BreT", [D, NST], F32, kind="ExternalInput")
        bi_d = nc.dram_tensor("BimT", [D, NST], F32, kind="ExternalInput")
        acc_d = nc.dram_tensor("acc", [NT, 512], F32, kind="ExternalOutput")

        triu_c = nc.inline_tensor(
            np.triu(np.ones((128, 128), np.float32)).astype(np.float32), name="triu")
        id_c = nc.inline_tensor(np.eye(128, dtype=np.float32), name="idf32")

        with ExitStack() as est:
            e = est.enter_context
            nu2 = e(nc.sbuf_tensor("nu2s", [128, 2], F32))
            th2 = e(nc.sbuf_tensor("th2s", [128, 2], F32))
            brb = e(nc.sbuf_tensor("brb", [128, 8 * NST], BF16))
            bib = e(nc.sbuf_tensor("bib", [128, 8 * NST], BF16))
            triu = e(nc.sbuf_tensor("trius", [128, 128], F32))
            idf = e(nc.sbuf_tensor("idfs", [128, 128], F32))
            xch = [e(nc.sbuf_tensor(f"xch{k}", [128, 8 * TCH], F32)) for k in range(2)]
            h3 = e(nc.sbuf_tensor("h3c", [128, 8 * TCH], BF16))
            LnT = [e(nc.sbuf_tensor(f"LnT{ri}", [128, NST], F32)) for ri in range(2)]
            scl = e(nc.sbuf_tensor("scl", [128, 512], F32))
            accs = [e(nc.sbuf_tensor(f"accs{k}", [128, 512], F32)) for k in range(2)]
            lnt = _ln_alloc(nc, e)
            ps_bu = e(nc.psum_tensor("ps_bu", [128, 512], F32))
            ps_ac = e(nc.psum_tensor("ps_ac", [128, 512], F32))
            ps_tr = e(nc.psum_tensor("ps_tr", [128, 256], F32))

            p = Prog(nc)
            G = {}

            def gw(n):
                return list(G.get(n, []))

            w_in = []
            for wi, (sb_, dr_) in enumerate([(nu2, nu_d), (th2, th_d),
                                             (triu, triu_c), (idf, id_c)]):
                w_in.append(p.op("y", (lambda sb_, dr_: lambda en: en.dma_start(
                    sb_[...], dr_[...]))(sb_, dr_), dma=True, stream=f"sm{wi}"))
            w_b = p.op("g", lambda en: en.dma_start(
                brb[...].rearrange("p (a w) -> p a w", a=8),
                br_d[...].rearrange("(a p) w -> p a w", p=128)), dma=True, stream="wb")
            w_b = p.op("g", lambda en: en.dma_start(
                bib[...].rearrange("p (a w) -> p a w", a=8),
                bi_d[...].rearrange("(a p) w -> p a w", p=128)), dma=True, stream="wb")
            cons = _ln_init(p, lnt)
            G["in_rows"] = [w_in[0], w_in[1]]
            r = _lam_cols(p, G, gw, nc, e, nu2, th2, [w_in[0], w_in[1]])
            TnR, TnI = _build_pow(p, G, gw, nc, e, r, -1, 128, "Ln")
            ttoks = []
            for ri, (T, LT) in enumerate([(TnR, LnT[0]), (TnI, LnT[1])]):
                for tile in range(2):
                    tt = p.op("t", (lambda T, tile: lambda en: en.transpose(
                        ps_tr[:, 0:128], T[:, tile * 128:(tile + 1) * 128],
                        idf[...]))(T, tile), wait=G["Ln"] + [w_in[3]] + gw("ps_tr"))
                    tv = p.op("s", (lambda LT, tile: lambda en: en.activation(
                        LT[:, tile * 128:(tile + 1) * 128], ps_tr[:, 0:128],
                        AF.Identity))(LT, tile), wait=[tt])
                    G["ps_tr"] = [tv]
                    ttoks.append(tv)
            G["LnT"] = ttoks

            for ch in range(4):
                b = ch % 2
                t_x = p.op("y", (lambda ch, b: lambda en: en.dma_start(
                    xch[b][...].rearrange("p (a w) -> p a w", a=8),
                    x_d[:, TCH * ch:TCH * (ch + 1)].rearrange("(a p) w -> p a w", p=128)))(ch, b),
                    wait=gw(f"xch{b}"), dma=True, stream=f"x{b}")
                t_h = _ln_chunk(
                    p, G, gw, lnt, cons + [t_x],
                    (lambda b: lambda j: xch[b][:, j * TCH:(j + 1) * TCH])(b),
                    lambda j: h3[:, j * TCH:(j + 1) * TCH])
                G[f"xch{b}"] = t_h
                readers = []
                for tb in range(4):
                    c_idx = 4 * ch + tb
                    last = None
                    for ri, wsb in enumerate([brb, bib]):
                        for kk in range(8):
                            wl = [t_h[kk], w_b] + ([last] if last else [])
                            if ri == 0 and kk == 0:
                                wl += gw("ps_bu")
                            last = p.op("t", (lambda ri, kk, tb, wsb: lambda en: en.matmul(
                                ps_bu[:, ri * 256:(ri + 1) * 256],
                                h3[:, kk * TCH + 128 * tb:kk * TCH + 128 * (tb + 1)],
                                wsb[:, kk * NST:(kk + 1) * NST],
                                start=(kk == 0), stop=(kk == 7)))(ri, kk, tb, wsb), wait=wl)
                    m1 = p.op("v", lambda en: en.tensor_mul(
                        scl[:, 0:256], ps_bu[:, 0:256], LnT[0][...]),
                        wait=[last] + G["LnT"] + gw("scl"))
                    m2 = p.op("v", lambda en: en.tensor_mul(
                        scl[:, 256:512], ps_bu[:, 256:512], LnT[1][...]), wait=[last, m1])
                    m3 = p.op("v", lambda en: en.tensor_sub(
                        scl[:, 0:256], scl[:, 0:256], scl[:, 256:512]), wait=[m2])
                    m4 = p.op("v", lambda en: en.tensor_mul(
                        scl[:, 256:512], ps_bu[:, 0:256], LnT[1][...]), wait=[m3, last])
                    m5 = p.op("v", lambda en: en.scalar_tensor_tensor(
                        scl[:, 256:512], ps_bu[:, 256:512], 1.0, scl[:, 256:512],
                        ALU.mult, ALU.bypass), wait=[m4]) if False else None
                    m5 = p.op("v", (lambda b: lambda en: en.tensor_mul(
                        accs[b][:, 0:256], ps_bu[:, 256:512], LnT[0][...]))(b),
                        wait=[m4] + gw(f"accs{b}"))
                    m6 = p.op("v", (lambda b: lambda en: en.tensor_add(
                        scl[:, 256:512], scl[:, 256:512], accs[b][:, 0:256]))(b),
                        wait=[m5])
                    G["ps_bu"] = [m4, m6, m1, m2]
                    tcs = p.op("t", lambda en: en.matmul(
                        ps_ac[:, 0:512], triu[...], scl[:, 0:512],
                        start=True, stop=True), wait=[m3, m6, w_in[2]] + gw("ps_ac"))
                    G["scl"] = [tcs]
                    tev = p.op("s", (lambda b: lambda en: en.activation(
                        accs[b][:, 0:512], ps_ac[:, 0:512], AF.Identity))(b),
                        wait=[tcs, m5])
                    G["ps_ac"] = [tev]
                    tst = p.op("y", (lambda c_idx, b: lambda en: en.dma_start(
                        acc_d[128 * c_idx:128 * (c_idx + 1), :], accs[b][...]))(c_idx, b),
                        wait=[tev], dma=True, stream=f"st{b}")
                    G[f"accs{b}"] = [tst]
                    readers.append(tst)
                G["lh3"] = readers + [G["ps_bu"][0]]

            p.op("v", lambda en: en.engine_nop(), wait=p.all_done())
            p.emit(est)
        return nc


    def build_lru2a():
        nc = bass.Bass("TRN2", target_bir_lowering=False)
        x_d = nc.dram_tensor("x3", [D, NT], F32, kind="ExternalInput")
        acc_d = nc.dram_tensor("acc", [NT, 512], F32, kind="ExternalInput")
        alr_d = nc.dram_tensor("aclTR", [128, 2 * 64], F32, kind="ExternalInput")
        ali_d = nc.dram_tensor("aclTI", [128, 2 * 64], F32, kind="ExternalInput")
        sel_d = nc.dram_tensor("Ssel", [64, 16], F32, kind="ExternalInput")
        nu_d = nc.dram_tensor("nu2", [128, 2], F32, kind="ExternalInput")
        th_d = nc.dram_tensor("th2", [128, 2], F32, kind="ExternalInput")
        cr_d = nc.dram_tensor("CreT", [NST, D], F32, kind="ExternalInput")
        ci_d = nc.dram_tensor("CimT", [NST, D], F32, kind="ExternalInput")
        dm_d = nc.dram_tensor("Dmat", [D, D], F32, kind="ExternalInput")
        y_d = nc.dram_tensor("yt", [D, NT], F32, kind="ExternalOutput")
        h3_d = nc.dram_tensor("h3o", [D, NT], F32, kind="ExternalOutput")

        id_c = nc.inline_tensor(np.eye(128, dtype=np.float32), name="idf32")
        idb_c = nc.inline_tensor(np.eye(128, dtype=ml_dtypes.bfloat16), name="idb16")
        ec_np = np.zeros((16, 16 * 128), np.float32)
        for c in range(16):
            ec_np[c, 128 * c:128 * (c + 1)] = 1.0
        ec_c = nc.inline_tensor(ec_np, name="ec16")

        with ExitStack() as est:
            e = est.enter_context
            nu2 = e(nc.sbuf_tensor("nu2s", [128, 2], F32))
            th2 = e(nc.sbuf_tensor("th2s", [128, 2], F32))
            alr = e(nc.sbuf_tensor("alrs", [128, 2 * 64], F32))
            ali = e(nc.sbuf_tensor("alis", [128, 2 * 64], F32))
            sel = e(nc.sbuf_tensor("sels", [64, 16], F32))
            idf = e(nc.sbuf_tensor("idfs", [128, 128], F32))
            idb = e(nc.sbuf_tensor("idbs", [128, 128], BF16))
            ec16 = e(nc.sbuf_tensor("ec16s", [16, 16 * 128], F32))
            crb = e(nc.sbuf_tensor("crb", [128, 2 * D], BF16))
            cib = e(nc.sbuf_tensor("cib", [128, 2 * D], BF16))
            dmb = e(nc.sbuf_tensor("dmb", [128, 8 * D], BF16))
            xch = [e(nc.sbuf_tensor(f"xch{k}", [128, 8 * TCH], F32)) for k in range(2)]
            h3f = e(nc.sbuf_tensor("h3f", [128, 8 * NT], BF16))
            LpT = [e(nc.sbuf_tensor(f"LpT{ri}", [128, NST], F32)) for ri in range(2)]
            lamA = [e(nc.sbuf_tensor(f"lamA{ri}", [128, 2 * 64], F32)) for ri in range(2)]
            acp = e(nc.sbuf_tensor("acp", [64, 512], F32))
            mycar = e(nc.sbuf_tensor("mycar", [16, 512], F32))
            accb = [e(nc.sbuf_tensor(f"accb{k}", [128, 512], F32)) for k in range(2)]
            accp = e(nc.sbuf_tensor("accp", [128, 512], F32))
            stw = e(nc.sbuf_tensor("stw", [128, 512], BF16))
            stT = [e(nc.sbuf_tensor(f"stT{ri}", [128, 2 * NT], BF16)) for ri in range(2)]
            ytb = [e(nc.sbuf_tensor(f"ytb{k}", [128, 512], F32)) for k in range(2)]
            lnt = _ln_alloc(nc, e)
            ps_a = e(nc.psum_tensor("ps_a2", [128, 512], F32))
            ps_tr = e(nc.psum_tensor("ps_tr2", [128, 256], F32))
            ps_y = e(nc.psum_tensor("ps_y", [128, 512], F32))

            p = Prog(nc)
            G = {}

            def gw(n):
                return list(G.get(n, []))

            w_in = []
            for wi, (sb_, dr_) in enumerate([(nu2, nu_d), (th2, th_d), (alr, alr_d),
                                             (ali, ali_d), (sel, sel_d), (idf, id_c),
                                             (idb, idb_c), (ec16, ec_c)]):
                w_in.append(p.op("y", (lambda sb_, dr_: lambda en: en.dma_start(
                    sb_[...], dr_[...]))(sb_, dr_), dma=True, stream=f"sm{wi}"))
            w_c = []
            for wi, (sb_, dr_) in enumerate([(crb, cr_d), (cib, ci_d)]):
                w_c.append(p.op("g", (lambda sb_, dr_: lambda en: en.dma_start(
                    sb_[...].rearrange("p (a w) -> p a w", a=2),
                    dr_[...].rearrange("(a p) w -> p a w", p=128)))(sb_, dr_),
                    dma=True, stream=f"wc{wi}"))
            w_dm = p.op("g", lambda en: en.dma_start(
                dmb[...].rearrange("p (a w) -> p a w", a=8),
                dm_d[...].rearrange("(a p) w -> p a w", p=128)), dma=True, stream="wdm")
            cons = _ln_init(p, lnt)

            r = _lam_cols(p, G, gw, nc, e, nu2, th2, [w_in[0], w_in[1]])
            TpR, TpI = _build_pow(p, G, gw, nc, e, r, +1, 129, "Lp")
            tcw = []
            for tile in range(2):
                t1 = p.op("v", (lambda tile: lambda en: en.tensor_scalar_mul(
                    crb[:, tile * D:(tile + 1) * D], crb[:, tile * D:(tile + 1) * D],
                    r["gam"][:, tile:tile + 1]))(tile), wait=[w_c[0]] + G["lam"])
                t2 = p.op("v", (lambda tile: lambda en: en.tensor_scalar(
                    cib[:, tile * D:(tile + 1) * D], cib[:, tile * D:(tile + 1) * D],
                    r["gam"][:, tile:tile + 1], -1.0, ALU.mult, ALU.mult))(tile),
                    wait=[w_c[1]] + G["lam"])
                tcw += [t1, t2]
            lcs = []
            lR = e(nc.sbuf_tensor("lR", [128, 2 * 64], F32))
            lI = e(nc.sbuf_tensor("lI", [128, 2 * 64], F32))
            t1r = e(nc.sbuf_tensor("ch_t1", [128, 2 * 64], F32))
            t2r = e(nc.sbuf_tensor("ch_t2", [128, 2 * 64], F32))
            for tile in range(2):
                o = tile * 64
                pb = tile * 129
                l7R = TpR[:, pb + 127:pb + 128]
                l7I = TpI[:, pb + 127:pb + 128]
                wl = [w_in[2], w_in[3]] + G["Lp"]
                m1 = p.op("v", (lambda o, l7R: lambda en: en.tensor_scalar_mul(
                    t1r[:, o:o + 64], alr[:, o:o + 64], l7R))(o, l7R), wait=wl)
                m2 = p.op("v", (lambda o, l7I: lambda en: en.tensor_scalar_mul(
                    t2r[:, o:o + 64], ali[:, o:o + 64], l7I))(o, l7I), wait=wl)
                m3 = p.op("v", (lambda o: lambda en: en.tensor_sub(
                    lR[:, o:o + 64], t1r[:, o:o + 64], t2r[:, o:o + 64]))(o),
                    wait=[m1, m2])
                m4 = p.op("v", (lambda o, l7I: lambda en: en.tensor_scalar_mul(
                    t1r[:, o:o + 64], alr[:, o:o + 64], l7I))(o, l7I), wait=[m3])
                m5 = p.op("v", (lambda o, l7R: lambda en: en.tensor_scalar_mul(
                    t2r[:, o:o + 64], ali[:, o:o + 64], l7R))(o, l7R), wait=[m3])
                m6 = p.op("v", (lambda o: lambda en: en.tensor_add(
                    lI[:, o:o + 64], t1r[:, o:o + 64], t2r[:, o:o + 64]))(o),
                    wait=[m4, m5])
                lcs += [m3, m6]
            AR = e(nc.sbuf_tensor("AR", [128, 2 * 64], F32))
            AI = e(nc.sbuf_tensor("AI", [128, 2 * 64], F32))
            za = p.op("v", lambda en: en.memset(AR[...], 0.0), wait=lcs)
            zb = p.op("v", lambda en: en.memset(AI[...], 0.0), wait=lcs)
            pwR = e(nc.sbuf_tensor("pwR", [128, 8], F32))
            pwI = e(nc.sbuf_tensor("pwI", [128, 8], F32))
            qs_ = []
            for tile in range(2):
                po = tile * 4
                pb = tile * 129
                lWR = TpR[:, pb + 128:pb + 129]
                lWI = TpI[:, pb + 128:pb + 129]
                q0 = p.op("v", (lambda po: lambda en: en.memset(
                    pwR[:, po:po + 1], 1.0))(po))
                q1 = p.op("v", (lambda po: lambda en: en.memset(
                    pwI[:, po:po + 1], 0.0))(po))
                q2 = p.op("v", (lambda po, lWR: lambda en: en.tensor_copy(
                    pwR[:, po + 1:po + 2], lWR))(po, lWR), wait=G["Lp"])
                q3 = p.op("v", (lambda po, lWI: lambda en: en.tensor_copy(
                    pwI[:, po + 1:po + 2], lWI))(po, lWI), wait=G["Lp"])
                qt = [q0, q1, q2, q3]
                for k in [2, 3]:
                    a1 = p.op("v", (lambda po, k, lWR: lambda en: en.tensor_scalar_mul(
                        t1r[:, 0:1], pwR[:, po + k - 1:po + k], lWR))(po, k, lWR),
                        wait=qt + lcs)
                    a2 = p.op("v", (lambda po, k, lWI: lambda en: en.tensor_scalar_mul(
                        t2r[:, 0:1], pwI[:, po + k - 1:po + k], lWI))(po, k, lWI),
                        wait=qt + lcs)
                    a3 = p.op("v", (lambda po, k: lambda en: en.tensor_sub(
                        pwR[:, po + k:po + k + 1], t1r[:, 0:1], t2r[:, 0:1]))(po, k),
                        wait=[a1, a2])
                    a4 = p.op("v", (lambda po, k, lWI: lambda en: en.tensor_scalar_mul(
                        t1r[:, 0:1], pwR[:, po + k - 1:po + k], lWI))(po, k, lWI),
                        wait=[a3])
                    a5 = p.op("v", (lambda po, k, lWR: lambda en: en.tensor_scalar_mul(
                        t2r[:, 0:1], pwI[:, po + k - 1:po + k], lWR))(po, k, lWR),
                        wait=[a3])
                    a6 = p.op("v", (lambda po, k: lambda en: en.tensor_add(
                        pwI[:, po + k:po + k + 1], t1r[:, 0:1], t2r[:, 0:1]))(po, k),
                        wait=[a4, a5])
                    qt = [a3, a6]
                qs_ += qt
            chain_t = [za, zb] + qs_
            for k in range(1, 5):
                for tile in range(2):
                    o = tile * 64
                    po = tile * 4
                    cnt = 64 - k
                    b1 = p.op("v", (lambda o, po, k, cnt: lambda en: en.tensor_scalar_mul(
                        t1r[:, 0:cnt], lR[:, o:o + cnt], pwR[:, po + k - 1:po + k]))(o, po, k, cnt),
                        wait=chain_t)
                    b2 = p.op("v", (lambda o, po, k, cnt: lambda en: en.tensor_scalar_mul(
                        t2r[:, 0:cnt], lI[:, o:o + cnt], pwI[:, po + k - 1:po + k]))(o, po, k, cnt),
                        wait=chain_t)
                    b3 = p.op("v", (lambda cnt: lambda en: en.tensor_sub(
                        t1r[:, 0:cnt], t1r[:, 0:cnt], t2r[:, 0:cnt]))(cnt),
                        wait=[b1, b2])
                    b4 = p.op("v", (lambda o, k, cnt: lambda en: en.tensor_add(
                        AR[:, o + k:o + 64], AR[:, o + k:o + 64], t1r[:, 0:cnt]))(o, k, cnt),
                        wait=[b3])
                    b5 = p.op("v", (lambda o, po, k, cnt: lambda en: en.tensor_scalar_mul(
                        t1r[:, 0:cnt], lR[:, o:o + cnt], pwI[:, po + k - 1:po + k]))(o, po, k, cnt),
                        wait=[b4])
                    b6 = p.op("v", (lambda o, po, k, cnt: lambda en: en.tensor_scalar_mul(
                        t2r[:, 0:cnt], lI[:, o:o + cnt], pwR[:, po + k - 1:po + k]))(o, po, k, cnt),
                        wait=[b4])
                    b7 = p.op("v", (lambda cnt: lambda en: en.tensor_add(
                        t1r[:, 0:cnt], t1r[:, 0:cnt], t2r[:, 0:cnt]))(cnt),
                        wait=[b5, b6])
                    b8 = p.op("v", (lambda o, k, cnt: lambda en: en.tensor_add(
                        AI[:, o + k:o + 64], AI[:, o + k:o + 64], t1r[:, 0:cnt]))(o, k, cnt),
                        wait=[b7])
                    chain_t = [b4, b8]
            la_t = []
            for tile in range(2):
                o = tile * 64
                lamRc = r["lamR"][:, tile:tile + 1]
                lamIc = r["lamI"][:, tile:tile + 1]
                c1_ = p.op("v", (lambda o, lamRc: lambda en: en.tensor_scalar_mul(
                    t1r[:, 0:64], AR[:, o:o + 64], lamRc))(o, lamRc),
                    wait=chain_t + la_t)
                c2_ = p.op("v", (lambda o, lamIc: lambda en: en.tensor_scalar_mul(
                    t2r[:, 0:64], AI[:, o:o + 64], lamIc))(o, lamIc),
                    wait=chain_t + la_t)
                c3_ = p.op("v", (lambda o: lambda en: en.tensor_sub(
                    lamA[0][:, o:o + 64], t1r[:, 0:64], t2r[:, 0:64]))(o), wait=[c1_, c2_])
                c4_ = p.op("v", (lambda o, lamIc: lambda en: en.tensor_scalar_mul(
                    t1r[:, 0:64], AR[:, o:o + 64], lamIc))(o, lamIc), wait=[c3_])
                c5_ = p.op("v", (lambda o, lamRc: lambda en: en.tensor_scalar_mul(
                    t2r[:, 0:64], AI[:, o:o + 64], lamRc))(o, lamRc), wait=[c3_])
                c6_ = p.op("v", (lambda o: lambda en: en.tensor_add(
                    lamA[1][:, o:o + 64], t1r[:, 0:64], t2r[:, 0:64]))(o), wait=[c4_, c5_])
                la_t += [c3_, c6_]
            tr_t = []
            for ri in range(2):
                for tile in range(2):
                    tt = p.op("t", (lambda ri, tile: lambda en: en.transpose(
                        ps_tr[0:64, 0:128],
                        lamA[ri][...].rearrange("p (a w) -> p a w", a=2)[:, tile, :],
                        idf[...]))(ri, tile), wait=la_t + [w_in[5]] + gw("ps_tr"))
                    tv = p.op("s", (lambda ri, tile: lambda en: en.activation(
                        acp[0:64, (2 * ri + tile) * 128:(2 * ri + tile + 1) * 128],
                        ps_tr[0:64, 0:128], AF.Identity))(ri, tile), wait=[tt])
                    G["ps_tr"] = [tv]
                    tr_t.append(tv)
            tsel = p.op("t", lambda en: en.matmul(
                ps_a[0:16, 0:512], sel[...], acp[0:64, 0:512],
                start=True, stop=True), wait=tr_t + [w_in[4]] + gw("ps_a"))
            tmc = p.op("s", lambda en: en.activation(
                mycar[...], ps_a[0:16, 0:512], AF.Identity), wait=[tsel])
            G["ps_a"] = [tmc]
            lp_t = []
            for (TT_, dstl) in [(TpR, LpT[0]), (TpI, LpT[1])]:
                for tile in range(2):
                    tt = p.op("t", (lambda TT_, tile: lambda en: en.transpose(
                        ps_tr[:, 0:128],
                        TT_[:, tile * 129:tile * 129 + 128],
                        idf[...]))(TT_, tile), wait=G["Lp"] + [w_in[5]] + gw("ps_tr"))
                    tv = p.op("s", (lambda dstl, tile: lambda en: en.activation(
                        dstl[:, tile * 128:(tile + 1) * 128], ps_tr[:, 0:128],
                        AF.Identity))(dstl, tile), wait=[tt])
                    G["ps_tr"] = [tv]
                    lp_t.append(tv)

            h3done = []
            for ch in range(4):
                b = ch % 2
                t_x = p.op("y", (lambda ch, b: lambda en: en.dma_start(
                    xch[b][...].rearrange("p (a w) -> p a w", a=8),
                    x_d[:, TCH * ch:TCH * (ch + 1)].rearrange("(a p) w -> p a w", p=128)))(ch, b),
                    wait=gw(f"xch{b}"), dma=True, stream=f"x{b}")
                t_h = _ln_chunk(
                    p, G, gw, lnt, cons + [t_x],
                    (lambda b: lambda j: xch[b][:, j * TCH:(j + 1) * TCH])(b),
                    (lambda ch: lambda j: h3f[:, j * NT + TCH * ch:j * NT + TCH * (ch + 1)])(ch))
                G[f"xch{b}"] = t_h
                G["lh3"] = []
                h3done += t_h
                for j in range(8):
                    p.op("g", (lambda j, ch: lambda en: en.dma_start(
                        h3_d[128 * j:128 * (j + 1), TCH * ch:TCH * (ch + 1)],
                        h3f[:, j * NT + TCH * ch:j * NT + TCH * (ch + 1)]))(j, ch),
                        wait=[t_h[j]], dma=True, stream=f"h3s{ch % 2}")

            p.phase_barrier()
            st_t = []
            for c in range(16):
                b = c % 2
                t_ac = p.op("y", (lambda c, b: lambda en: en.dma_start(
                    accb[b][...], acc_d[128 * c:128 * (c + 1), :]))(c, b),
                    wait=gw(f"accb{b}"), dma=True, stream=f"ac{b}")
                tbc = p.op("t", (lambda c: lambda en: en.matmul(
                    ps_a[:, 0:512], ec16[:, 128 * c:128 * (c + 1)], mycar[...],
                    start=True, stop=True))(c), wait=[tmc, w_in[7]] + gw("ps_a"))
                ta1 = p.op("v", (lambda b: lambda en: en.tensor_add(
                    accp[:, 0:512], accb[b][...], ps_a[:, 0:512]))(b),
                    wait=[t_ac, tbc] + gw("accp"))
                G["ps_a"] = [ta1]
                s1 = p.op("v", (lambda b: lambda en: en.tensor_mul(
                    ytb[b][:, 0:256], accp[:, 0:256], LpT[0][...]))(b),
                    wait=[ta1] + lp_t + gw(f"ytb{b}"))
                s2 = p.op("v", (lambda b: lambda en: en.tensor_mul(
                    ytb[b][:, 256:512], accp[:, 256:512], LpT[1][...]))(b),
                    wait=[ta1] + lp_t)
                s3 = p.op("v", (lambda b: lambda en: en.tensor_sub(
                    ytb[b][:, 0:256], ytb[b][:, 0:256], ytb[b][:, 256:512]))(b),
                    wait=[s2, s1])
                sR = p.op("v", (lambda b: lambda en: en.tensor_copy(
                    stw[:, 0:256], ytb[b][:, 0:256]))(b), wait=[s3] + gw("stw"))
                s4 = p.op("v", (lambda b: lambda en: en.tensor_mul(
                    ytb[b][:, 0:256], accp[:, 0:256], LpT[1][...]))(b), wait=[sR])
                s5 = p.op("v", (lambda b: lambda en: en.tensor_mul(
                    ytb[b][:, 256:512], accp[:, 256:512], LpT[0][...]))(b), wait=[s3])
                s6 = p.op("v", (lambda b: lambda en: en.tensor_add(
                    ytb[b][:, 0:256], ytb[b][:, 0:256], ytb[b][:, 256:512]))(b),
                    wait=[s4, s5])
                sI = p.op("v", (lambda b: lambda en: en.tensor_copy(
                    stw[:, 256:512], ytb[b][:, 0:256]))(b), wait=[s6])
                G["accp"] = [s4, s5, sR]
                G[f"accb{b}"] = [ta1]
                G[f"ytb{b}"] = [sR, sI, s6]
                my_tr = []
                for ri in range(2):
                    for tile in range(2):
                        tt = p.op("t", (lambda ri, tile: lambda en: en.transpose(
                            ps_tr[...].bitcast(BF16)[:, 0:128],
                            stw[:, ri * 256 + tile * 128:ri * 256 + (tile + 1) * 128],
                            idb[...]))(ri, tile),
                            wait=[sR, sI, w_in[6]] + gw("ps_tr"))
                        tv = p.op("s", (lambda ri, tile, c: lambda en: en.activation(
                            stT[ri][:, tile * NT + 128 * c:tile * NT + 128 * (c + 1)],
                            ps_tr[...].bitcast(BF16)[:, 0:128], AF.Identity))(ri, tile, c),
                            wait=[tt])
                        G["ps_tr"] = [tv]
                        my_tr.append(tv)
                st_t += my_tr
                G["stw"] = my_tr

            p.phase_barrier()
            for ch in range(4):
                b = ch % 2
                for m in range(8):
                    last = None
                    for ri in range(2):
                        for tile in range(2):
                            csb = crb if ri == 0 else cib
                            wl = [last] if last else gw("ps_y")
                            last = p.op("t", (lambda m, ri, tile, ch, csb: lambda en: en.matmul(
                                ps_y[:, 0:512],
                                csb[:, tile * D + 128 * m:tile * D + 128 * (m + 1)],
                                stT[ri][:, tile * NT + TCH * ch:tile * NT + TCH * (ch + 1)],
                                start=(ri == 0 and tile == 0), stop=False))(m, ri, tile, ch, csb),
                                wait=wl)
                    for kk in range(8):
                        last = p.op("t", (lambda m, kk, ch: lambda en: en.matmul(
                            ps_y[:, 0:512],
                            dmb[:, kk * D + 128 * m:kk * D + 128 * (m + 1)],
                            h3f[:, kk * NT + TCH * ch:kk * NT + TCH * (ch + 1)],
                            start=False, stop=(kk == 7)))(m, kk, ch),
                            wait=[last])
                    tev = p.op("v", (lambda b: lambda en: en.tensor_copy(
                        ytb[b][...], ps_y[:, 0:512]))(b), wait=[last] + gw(f"ytb{b}"))
                    G["ps_y"] = [tev]
                    tst = p.op("y", (lambda m, ch, b: lambda en: en.dma_start(
                        y_d[128 * m:128 * (m + 1), TCH * ch:TCH * (ch + 1)],
                        ytb[b][...]))(m, ch, b), wait=[tev], dma=True, stream=f"ys{b}")
                    G[f"ytb{b}"] = [tst]

            p.op("v", lambda en: en.engine_nop(), wait=p.all_done())
            p.emit(est)
        return nc


    def build_lru2b():
        nc = bass.Bass("TRN2", target_bir_lowering=False)
        y_di = nc.dram_tensor("yt", [D, NT], F32, kind="ExternalInput")
        w1_d = nc.dram_tensor("W1", [D, 4096], F32, kind="ExternalInput")
        w2_d = nc.dram_tensor("W2", [4096, D], F32, kind="ExternalInput")
        x_d = nc.dram_tensor("x3", [D, NT], F32, kind="ExternalInput")
        h3_d = nc.dram_tensor("h3o", [D, NT], F32, kind="ExternalInput")
        rs_d = nc.dram_tensor("rescol", [128, 1], F32, kind="ExternalInput")
        o_d = nc.dram_tensor("x4p", [D, NT], F32, kind="ExternalOutput")

        NMB = 32
        TC2 = 1024
        C2 = 0.044715
        C1 = 0.7978845608028654

        with ExitStack() as est:
            e = est.enter_context
            yb = [e(nc.sbuf_tensor(f"yb{i}", [128, TC2], BF16)) for i in range(8)]
            hbuf = e(nc.sbuf_tensor("hbuf", [128, NMB * TC2], BF16))
            hb_s = e(nc.sbuf_tensor("hb_s", [128, TC2], BF16))
            hs_s = e(nc.sbuf_tensor("hs_s", [128, TC2], BF16))
            arg_s = e(nc.sbuf_tensor("arg_s", [128, TC2], BF16))
            th_s = e(nc.sbuf_tensor("th_s", [128, TC2], BF16))
            w1b = [e(nc.sbuf_tensor(f"w1b{k}", [128, 8 * 128], BF16)) for k in range(2)]
            w2b = [e(nc.sbuf_tensor(f"w2b{k}", [128, NMB * 128], BF16)) for k in range(2)]
            rsc = e(nc.sbuf_tensor("rsc", [128, 1], F32))
            sigc = e(nc.sbuf_tensor("sigc", [128, 1], F32))
            xrb = [e(nc.sbuf_tensor(f"xrb{k}", [128, TC2], F32)) for k in range(2)]
            hrb = [e(nc.sbuf_tensor(f"hrb{k}", [128, TC2], F32)) for k in range(2)]
            ob = [e(nc.sbuf_tensor(f"ob{k}", [128, TC2], F32)) for k in range(2)]
            ps_A = e(nc.psum_tensor("ps_A", [128, TC2], F32))
            ps_B = e(nc.psum_tensor("ps_B", [128, TC2], F32))

            p = Prog(nc)
            G = {}

            def gw(n):
                return list(G.get(n, []))

            w_rs = p.op("y", lambda en: en.dma_start(rsc[...], rs_d[...]),
                        dma=True, stream="sm")
            t_sg = p.op("s", lambda en: en.activation(
                sigc[...], rsc[...], AF.Sigmoid), wait=[w_rs])
            t_sg2 = p.op("v", lambda en: en.tensor_scalar_mul(
                sigc[...], sigc[...], 0.5), wait=[t_sg])

            for ci in range(NT // TC2):
                c0 = ci * TC2
                t_y = []
                for j in range(8):
                    t_y.append(p.op("g", (lambda j, c0: lambda en: en.dma_start(
                        yb[j][...], y_di[128 * j:128 * (j + 1), c0:c0 + TC2]))(j, c0),
                        wait=gw(f"yb{j}"), dma=True, stream=f"yl{j}"))
                t_h = {}
                for m in range(NMB):
                    b = m % 2
                    wwait = [t_h[m - 2]] if m >= 2 else gw("w1pv")
                    t_w1 = p.op("g", (lambda m, b: lambda en: en.dma_start(
                        w1b[b][...].rearrange("p (a w) -> p a w", a=8),
                        w1_d[:, 128 * m:128 * (m + 1)].rearrange("(a p) w -> p a w", p=128)))(m, b),
                        wait=wwait, dma=True, stream=f"w1p{b}")
                    psw = [t_h[m - 1]] if m >= 1 else gw("psA_prev")
                    last = None
                    for q in range(TC2 // 512):
                        for kk in range(8):
                            wl = [t_w1, t_y[kk]] + psw + ([last] if last else [])
                            last = p.op("t", (lambda b, q, kk: lambda en: en.matmul(
                                ps_A[:, q * 512:(q + 1) * 512],
                                w1b[b][:, kk * 128:(kk + 1) * 128],
                                yb[kk][:, q * 512:(q + 1) * 512],
                                start=(kk == 0), stop=(kk == 7)))(b, q, kk), wait=wl)
                    te1 = p.op("s", (lambda: lambda en: en.activation(
                        hb_s[...], ps_A[...], AF.Identity))(), wait=[last] + gw("hb_s"))
                    te2 = p.op("s", (lambda: lambda en: en.activation(
                        hs_s[...], ps_A[...], AF.Square))(), wait=[last, te1])
                    tv1 = p.op("v", (lambda: lambda en: en.tensor_mul(
                        arg_s[...], hs_s[...], hb_s[...]))(), wait=[te1, te2] + gw("arg_s"))
                    tv2 = p.op("v", (lambda: lambda en: en.scalar_tensor_tensor(
                        arg_s[...], arg_s[...], C2, hb_s[...], ALU.mult, ALU.add))(),
                        wait=[tv1])
                    te3 = p.op("s", (lambda: lambda en: en.activation(
                        th_s[...], arg_s[...], AF.Tanh, scale=C1))(), wait=[tv2] + gw("th_s"))
                    tv3 = p.op("v", (lambda: lambda en: en.tensor_scalar_add(
                        th_s[...], th_s[...], 1.0))(), wait=[te3])
                    t_h[m] = p.op("v", (lambda m: lambda en: en.tensor_mul(
                        hbuf[:, m * TC2:(m + 1) * TC2], hb_s[...], th_s[...]))(m),
                        wait=[tv3, te1] + gw(f"hbuf{m}"))
                    G["hb_s"] = [t_h[m]]
                    G["arg_s"] = [tv2]
                    G["th_s"] = [t_h[m]]
                G["w1pv"] = [t_h[NMB - 2], t_h[NMB - 1]]
                G["psA_prev"] = [t_h[NMB - 1]]

                for n in range(8):
                    b = n % 2
                    wwait = gw(f"w2rd{b}")
                    t_w2 = p.op("g", (lambda n, b: lambda en: en.dma_start(
                        w2b[b][...].rearrange("p (a w) -> p a w", a=NMB),
                        w2_d[:, 128 * n:128 * (n + 1)].rearrange("(a p) w -> p a w", p=128)))(n, b),
                        wait=wwait, dma=True, stream=f"w2p{b}")
                    t_xr = p.op("y", (lambda n, c0, b: lambda en: en.dma_start(
                        xrb[b][...], x_d[128 * n:128 * (n + 1), c0:c0 + TC2]))(n, c0, b),
                        wait=gw(f"xrb{b}"), dma=True, stream=f"xr{b}")
                    t_hr = p.op("y", (lambda n, c0, b: lambda en: en.dma_start(
                        hrb[b][...], h3_d[128 * n:128 * (n + 1), c0:c0 + TC2]))(n, c0, b),
                        wait=gw(f"hrb{b}"), dma=True, stream=f"hr{b}")
                    last = None
                    rd = []
                    for q in range(TC2 // 512):
                        for m in range(NMB):
                            wl = [t_w2, t_h[m]] + ([last] if last else [])
                            if n >= 1 and q == 0 and m == 0:
                                wl += gw("psB_rd")
                            last = p.op("t", (lambda b, q, m: lambda en: en.matmul(
                                ps_B[:, q * 512:(q + 1) * 512],
                                w2b[b][:, m * 128:(m + 1) * 128],
                                hbuf[:, m * TC2 + q * 512:m * TC2 + (q + 1) * 512],
                                start=(m == 0), stop=(m == NMB - 1)))(b, q, m), wait=wl)
                    G[f"w2rd{b}"] = [last]
                    tz = p.op("v", (lambda b: lambda en: en.tensor_scalar_mul(
                        ob[b][...], ps_B[...], sigc[...]))(b),
                        wait=[last, t_sg2] + gw(f"ob{b}"))
                    G["psB_rd"] = [tz]
                    ta1 = p.op("v", (lambda b: lambda en: en.tensor_add(
                        ob[b][...], ob[b][...], xrb[b][...]))(b), wait=[tz, t_xr])
                    ta2 = p.op("v", (lambda b: lambda en: en.tensor_add(
                        ob[b][...], ob[b][...], hrb[b][...]))(b), wait=[ta1, t_hr])
                    G[f"xrb{b}"] = [ta1]
                    G[f"hrb{b}"] = [ta2]
                    tst = p.op("y", (lambda n, c0, b: lambda en: en.dma_start(
                        o_d[128 * n:128 * (n + 1), c0:c0 + TC2], ob[b][...]))(n, c0, b),
                        wait=[ta2], dma=True, stream=f"os{b}")
                    G[f"ob{b}"] = [tst]
                for j in range(8):
                    G[f"yb{j}"] = [t_h[NMB - 1]]
                for m in range(NMB):
                    G[f"hbuf{m}"] = [G["psB_rd"][0]]

            p.op("v", lambda en: en.engine_nop(), wait=p.all_done())
            p.emit(est)
        return nc
    return build_lru1, build_lru2a, build_lru2b, _ln_alloc, _ln_init, _ln_chunk

build_lru1, build_lru2a, build_lru2b, _ln_alloc, _ln_init, _ln_chunk = _ns_lru()

def _ns_proj():
    D = 1024
    NT = 2048
    TCH = 512


    def build_proj():
        nc = bass.Bass("TRN2", target_bir_lowering=False)
        x_d = nc.dram_tensor("x0", [D, NT], F32, kind="ExternalInput")
        a_d = nc.dram_tensor("aog", [D, NT], F32, kind="ExternalInput")
        w_d = nc.dram_tensor("Wp", [D, D], F32, kind="ExternalInput")
        o_d = nc.dram_tensor("x1", [D, NT], F32, kind="ExternalOutput")

        with ExitStack() as est:
            e = est.enter_context
            wpb = e(nc.sbuf_tensor("wpb", [128, 8 * D], BF16))
            ab = [e(nc.sbuf_tensor(f"ab{k}", [128, 8 * TCH], BF16)) for k in range(2)]
            xb = [e(nc.sbuf_tensor(f"xb{k}", [128, TCH], F32)) for k in range(2)]
            ob = [e(nc.sbuf_tensor(f"ob{k}", [128, TCH], F32)) for k in range(2)]
            ps = e(nc.psum_tensor("ps_p", [128, TCH], F32))

            p = Prog(nc)
            G = {}

            def gw(n):
                return list(G.get(n, []))

            w_w = p.op("g", lambda en: en.dma_start(
                wpb[...].rearrange("p (a w) -> p a w", a=8),
                w_d[...].rearrange("(a p) w -> p a w", p=128)), dma=True, stream="wp")
            for ch in range(4):
                b = ch % 2
                t_a = p.op("g", (lambda ch, b: lambda en: en.dma_start(
                    ab[b][...].rearrange("p (a w) -> p a w", a=8),
                    a_d[:, TCH * ch:TCH * (ch + 1)].rearrange("(a p) w -> p a w", p=128)))(ch, b),
                    wait=gw(f"ab{b}"), dma=True, stream=f"al{b}")
                rds = []
                for m in range(8):
                    mb = m % 2
                    t_x = p.op("y", (lambda ch, m, mb: lambda en: en.dma_start(
                        xb[mb][...], x_d[128 * m:128 * (m + 1), TCH * ch:TCH * (ch + 1)]))(ch, m, mb),
                        wait=gw(f"xb{mb}"), dma=True, stream=f"xl{mb}")
                    last = None
                    for kk in range(8):
                        wl = [t_a, w_w] + ([last] if last else [])
                        if kk == 0:
                            wl += gw("ps")
                        last = p.op("t", (lambda b, m, kk: lambda en: en.matmul(
                            ps[...], wpb[:, kk * D + 128 * m:kk * D + 128 * (m + 1)],
                            ab[b][:, kk * TCH:(kk + 1) * TCH],
                            start=(kk == 0), stop=(kk == 7)))(b, m, kk), wait=wl)
                    rds.append(last)
                    tad = p.op("v", (lambda mb: lambda en: en.tensor_add(
                        ob[mb][...], ps[...], xb[mb][...]))(mb),
                        wait=[last, t_x] + gw(f"ob{mb}"))
                    G["ps"] = [tad]
                    G[f"xb{mb}"] = [tad]
                    tst = p.op("y", (lambda ch, m, mb: lambda en: en.dma_start(
                        o_d[128 * m:128 * (m + 1), TCH * ch:TCH * (ch + 1)],
                        ob[mb][...]))(ch, m, mb), wait=[tad], dma=True, stream=f"os{mb}")
                    G[f"ob{mb}"] = [tst]
                G[f"ab{b}"] = rds

            p.op("v", lambda en: en.engine_nop(), wait=p.all_done())
            p.emit(est)
        return nc


    def build_addln():
        nc = bass.Bass("TRN2", target_bir_lowering=False)
        xa_d = nc.dram_tensor("xa", [D, NT], F32, kind="ExternalInput")
        xb_d = nc.dram_tensor("xb", [D, NT], F32, kind="ExternalInput")
        s_d = nc.dram_tensor("s", [D, NT], F32, kind="ExternalOutput")
        l_d = nc.dram_tensor("sln", [D, NT], F32, kind="ExternalOutput")

        with ExitStack() as est:
            e = est.enter_context
            xab = [e(nc.sbuf_tensor(f"xab{k}", [128, 8 * TCH], F32)) for k in range(2)]
            xbb = [e(nc.sbuf_tensor(f"xbb{k}", [128, 8 * TCH], F32)) for k in range(2)]
            snb = [e(nc.sbuf_tensor(f"snb{k}", [128, 8 * TCH], F32)) for k in range(2)]
            lnb = [e(nc.sbuf_tensor(f"lnb{k}", [128, 8 * TCH], F32)) for k in range(2)]
            lnt = _ln_alloc(nc, e)

            p = Prog(nc)
            G = {}

            def gw(n):
                return list(G.get(n, []))

            cons = _ln_init(p, lnt)
            for ch in range(4):
                b = ch % 2
                t_a = p.op("y", (lambda ch, b: lambda en: en.dma_start(
                    xab[b][...].rearrange("p (a w) -> p a w", a=8),
                    xa_d[:, TCH * ch:TCH * (ch + 1)].rearrange("(a p) w -> p a w", p=128)))(ch, b),
                    wait=gw(f"xab{b}"), dma=True, stream=f"la{b}")
                t_b = p.op("g", (lambda ch, b: lambda en: en.dma_start(
                    xbb[b][...].rearrange("p (a w) -> p a w", a=8),
                    xb_d[:, TCH * ch:TCH * (ch + 1)].rearrange("(a p) w -> p a w", p=128)))(ch, b),
                    wait=gw(f"xbb{b}"), dma=True, stream=f"lb{b}")
                t_s = []
                t_st = []
                for j in range(8):
                    ts = p.op("v", (lambda b, j: lambda en: en.tensor_add(
                        snb[b][:, j * TCH:(j + 1) * TCH], xab[b][:, j * TCH:(j + 1) * TCH],
                        xbb[b][:, j * TCH:(j + 1) * TCH]))(b, j),
                        wait=[t_a, t_b] + gw(f"snb{b}{j}"))
                    t_s.append(ts)
                    t_st.append(p.op("y", (lambda b, j, ch: lambda en: en.dma_start(
                        s_d[128 * j:128 * (j + 1), TCH * ch:TCH * (ch + 1)],
                        snb[b][:, j * TCH:(j + 1) * TCH]))(b, j, ch),
                        wait=[ts], dma=True, stream=f"ss{b}"))
                G[f"xab{b}"] = t_s
                G[f"xbb{b}"] = t_s
                G["lh3"] = gw(f"lnb{b}")
                t_h = _ln_chunk(
                    p, G, gw, lnt, cons + t_s,
                    (lambda b: lambda j: snb[b][:, j * TCH:(j + 1) * TCH])(b),
                    (lambda b: lambda j: lnb[b][:, j * TCH:(j + 1) * TCH])(b))
                G["lh3"] = []
                t_l = []
                for j in range(8):
                    tl = p.op("g", (lambda b, j, ch: lambda en: en.dma_start(
                        l_d[128 * j:128 * (j + 1), TCH * ch:TCH * (ch + 1)],
                        lnb[b][:, j * TCH:(j + 1) * TCH]))(b, j, ch),
                        wait=[t_h[j]], dma=True, stream=f"ls{b}")
                    t_l.append(tl)
                for j in range(8):
                    G[f"snb{b}{j}"] = [t_h[j], t_st[-1]]
                G[f"lnb{b}"] = t_l
                G[f"xab{b}"] = t_h + [G["lpsr"][0]]
                G[f"xbb{b}"] = G[f"xab{b}"]

            p.op("v", lambda en: en.engine_nop(), wait=p.all_done())
            p.emit(est)
        return nc
    return build_proj, build_addln

build_proj, build_addln = _ns_proj()

_PROGS = {}


def _prog(name, fn):
    if name not in _PROGS:
        _PROGS[name] = fn()
    return _PROGS[name]


_CORES = list(range(8))


def _gather(res, name):
    return [np.concatenate([res.results[4 * b + j][name] for j in range(4)], 1)
            for b in range(2)]


def kernel(**inputs):
    X = np.asarray(inputs["L0_attn_a_x"], np.float32)
    Wcd = np.ascontiguousarray(np.asarray(inputs["L0_attn_a_Wc_down"], np.float32))
    Wqu = np.asarray(inputs["L0_attn_a_Wq_up"], np.float32)
    Wkv = np.ascontiguousarray(np.asarray(inputs["L0_attn_a_Wkv_up"], np.float32))
    Wcp = np.ascontiguousarray(np.asarray(inputs["L0_attn_a_Wc_proj"], np.float32))
    nu_log = np.asarray(inputs["L0_attn_b_nu_log"], np.float32)
    theta_log = np.asarray(inputs["L0_attn_b_theta_log"], np.float32)
    B_re = np.asarray(inputs["L0_attn_b_B_re"], np.float32)
    B_im = np.asarray(inputs["L0_attn_b_B_im"], np.float32)
    C_re = np.asarray(inputs["L0_attn_b_C_re"], np.float32)
    C_im = np.asarray(inputs["L0_attn_b_C_im"], np.float32)
    D_mat = np.asarray(inputs["L0_attn_b_D_mat"], np.float32)
    W1b_ = np.ascontiguousarray(np.asarray(inputs["L0_attn_b_W1"], np.float32))
    W2b_ = np.ascontiguousarray(np.asarray(inputs["L0_attn_b_W2"], np.float32))
    res_logit = np.float32(np.asarray(inputs["L0_attn_b_res_logit"]).reshape(()))
    Wqkv = np.asarray(inputs["L0_ffn_Wqkv"], np.float32)
    q_scale = np.asarray(inputs["L0_ffn_q_scale"], np.float32)
    k_scale = np.asarray(inputs["L0_ffn_k_scale"], np.float32)
    Wo = np.asarray(inputs["L0_ffn_Wo"], np.float32)
    ln_g = np.asarray(inputs["L0_ffn_ln_g"], np.float32)
    ln_b = np.asarray(inputs["L0_ffn_ln_b"], np.float32)
    W1f = np.asarray(inputs["L0_ffn_W1"], np.float32)
    b1f = np.asarray(inputs["L0_ffn_b1"], np.float32)
    W2f = np.asarray(inputs["L0_ffn_W2"], np.float32)
    b2f = np.asarray(inputs["L0_ffn_b2"], np.float32)

    Xcm = [np.ascontiguousarray(X[b].T) for b in range(2)]

    nc = _prog("attna", build_attna)
    ins = []
    for c in _CORES:
        b, j = c // 4, c % 4
        ins.append({"x0": Xcm[b], "Wc_down": Wcd,
                    "Wq_sl": np.ascontiguousarray(Wqu[:, 256 * j:256 * (j + 1)]),
                    "Wkv": Wkv})
    res = run_bass_kernel_spmd(nc, ins, core_ids=_CORES)
    aog = [np.concatenate([res.results[4 * b + j]["ao"] for j in range(4)], 0)
           for b in range(2)]

    nc = _prog("proj", build_proj)
    ins = []
    for c in _CORES:
        b, j = c // 4, c % 4
        sl = slice(2048 * j, 2048 * (j + 1))
        ins.append({"x0": np.ascontiguousarray(Xcm[b][:, sl]),
                    "aog": np.ascontiguousarray(aog[b][:, sl]), "Wp": Wcp})
    res = run_bass_kernel_spmd(nc, ins, core_ids=_CORES)
    X1 = _gather(res, "x1")

    nc_addln = _prog("addln", build_addln)

    def addln(A, Bm):
        ins = []
        for c in _CORES:
            b, j = c // 4, c % 4
            sl = slice(2048 * j, 2048 * (j + 1))
            xb = (np.zeros((1024, 2048), np.float32) if Bm is None
                  else np.ascontiguousarray(Bm[b][:, sl]))
            ins.append({"xa": np.ascontiguousarray(A[b][:, sl]), "xb": xb})
        res = run_bass_kernel_spmd(nc_addln, ins, core_ids=_CORES)
        return _gather(res, "s"), _gather(res, "sln")

    nc_attn = _prog("attn", build_ffn_attn)
    nc_ffn = _prog("ffn", build_ffn_ffn)

    def ffn_block(Y):
        for i in range(6):
            ins = []
            for c in _CORES:
                b, j = c // 4, c % 4
                pos0 = 2048 * j
                own = np.ascontiguousarray(Y[b][:, pos0:pos0 + 2048])
                halo = (np.zeros((1024, 257), np.float32) if j == 0
                        else np.ascontiguousarray(Y[b][:, pos0 - 257:pos0]))
                ins.append(attn_inputs(own, halo, Wqkv[i], q_scale[i],
                                       k_scale[i], Wo[i], pos0, j == 0))
            res = run_bass_kernel_spmd(nc_attn, ins, core_ids=_CORES)
            Y = _gather(res, "y")
            ins = []
            for c in _CORES:
                b, j = c // 4, c % 4
                pos0 = 2048 * j
                own = np.ascontiguousarray(Y[b][:, pos0:pos0 + 2048])
                halo = (np.zeros(1024, np.float32) if j == 0
                        else np.ascontiguousarray(Y[b][:, pos0 - 1]))
                ins.append(ffn_inputs(own, halo, W1f[i], b1f[i], W2f[i],
                                      b2f[i], ln_g[i], ln_b[i]))
            res = run_bass_kernel_spmd(nc_ffn, ins, core_ids=_CORES)
            Y = _gather(res, "y")
        return Y

    _, Y0 = addln(X1, None)
    Y6 = ffn_block(Y0)
    X2, _ = addln(X1, Y6)

    nu2 = np.ascontiguousarray(nu_log.reshape(2, 128).T)
    th2 = np.ascontiguousarray(theta_log.reshape(2, 128).T)
    BreT = np.ascontiguousarray(B_re.T)
    BimT = np.ascontiguousarray(B_im.T)
    nc = _prog("lru1", build_lru1)
    ins = []
    for c in _CORES:
        b, j = c // 4, c % 4
        sl = slice(2048 * j, 2048 * (j + 1))
        ins.append({"x3": np.ascontiguousarray(X2[b][:, sl]),
                    "nu2": nu2, "th2": th2, "BreT": BreT, "BimT": BimT})
    res = run_bass_kernel_spmd(nc, ins, core_ids=_CORES)
    accs = [np.asarray(res.results[c]["acc"]) for c in _CORES]
    acl = []
    for b in range(2):
        Lall = np.concatenate([accs[4 * b + j][127::128] for j in range(4)], 0)
        aR = np.ascontiguousarray(
            Lall[:, 0:256].T.reshape(2, 128, 64).transpose(1, 0, 2).reshape(128, 128))
        aI = np.ascontiguousarray(
            Lall[:, 256:512].T.reshape(2, 128, 64).transpose(1, 0, 2).reshape(128, 128))
        acl.append((aR, aI))
    nc = _prog("lru2a", build_lru2a)
    ins = []
    for c in _CORES:
        b, j = c // 4, c % 4
        sl = slice(2048 * j, 2048 * (j + 1))
        ssel = np.zeros((64, 16), np.float32)
        for g in range(16):
            ssel[16 * j + g, g] = 1.0
        ins.append({"x3": np.ascontiguousarray(X2[b][:, sl]), "acc": accs[c],
                    "aclTR": acl[b][0], "aclTI": acl[b][1], "Ssel": ssel,
                    "nu2": nu2, "th2": th2,
                    "CreT": np.ascontiguousarray(C_re.T),
                    "CimT": np.ascontiguousarray(C_im.T),
                    "Dmat": np.ascontiguousarray(D_mat.T)})
    res = run_bass_kernel_spmd(nc, ins, core_ids=_CORES)
    yts = [np.asarray(res.results[c]["yt"]) for c in _CORES]
    h3s = [np.asarray(res.results[c]["h3o"]) for c in _CORES]
    nc = _prog("lru2b", build_lru2b)
    ins = []
    for c in _CORES:
        b, j = c // 4, c % 4
        sl = slice(2048 * j, 2048 * (j + 1))
        ins.append({"yt": yts[c], "W1": W1b_, "W2": W2b_,
                    "x3": np.ascontiguousarray(X2[b][:, sl]), "h3o": h3s[c],
                    "rescol": np.full((128, 1), res_logit, np.float32)})
    res = run_bass_kernel_spmd(nc, ins, core_ids=_CORES)
    X4 = _gather(res, "x4p")

    _, Y0b = addln(X4, None)
    Y6b = ffn_block(Y0b)
    X5, _ = addln(X4, Y6b)

    return np.ascontiguousarray(
        np.stack([X5[b].T for b in range(2)], 0).astype(np.float32))


# version 04
# speedup vs baseline: 2.2791x; 1.2805x over previous
from contextlib import ExitStack
import numpy as np
import ml_dtypes
import concourse.bass as bass
import concourse.mybir as mybir
from concourse.bass_utils import run_bass_kernel_spmd

F32 = mybir.dt.float32
BF16 = mybir.dt.bfloat16
I32 = mybir.dt.int32
AF = mybir.ActivationFunctionType
ALU = mybir.AluOpType
AX = mybir.AxisListType
NEG = -30000.0
ELIDE = True


class Prog:

    def __init__(self, nc, name="prog"):
        self.nc = nc
        self.items = []
        self.sem_vals = {}
        self.sems = {}
        self._semnames = set()
        self._pending_barrier = None
        self._barrier_applied = set()

    def phase_barrier(self):
        self._pending_barrier = self.all_done()
        self._barrier_applied = set()

    def _bump(self, semname, n):
        v = self.sem_vals.get(semname, 0) + n
        self.sem_vals[semname] = v
        self._semnames.add(semname)
        return (semname, v)

    def op(self, eng, fn, wait=(), dma=False, stream=None):
        if dma:
            semname = f"d_{stream}" if stream else f"d_{eng}"
        else:
            semname = f"c_{eng}"
        tok = self._bump(semname, 16 if dma else 1)
        wait = list(wait)
        if self._pending_barrier is not None and eng not in self._barrier_applied:
            wait = wait + [self._pending_barrier]
            self._barrier_applied.add(eng)
        seen = {}
        for w in wait:
            if w is None:
                continue
            if isinstance(w, list):
                for (s, v) in w:
                    if s not in seen or seen[s] < v:
                        seen[s] = v
            else:
                s, v = w
                if s not in seen or seen[s] < v:
                    seen[s] = v
        self.items.append((eng, fn, sorted(seen.items()), (semname, 16 if dma else 1)))
        return tok

    def all_done(self):
        return [(s, v) for s, v in self.sem_vals.items()]

    def emit(self, stack):
        nc = self.nc
        for s in sorted(self._semnames):
            self.sems[s] = stack.enter_context(nc.semaphore(s))
        block = stack.enter_context(nc.Block())
        prog = self
        elide = getattr(self, "elide", ELIDE)

        def mk(engkey):
            def section(eng):
                own = f"c_{engkey}"
                own_count = 0
                hw = {}
                if isinstance(elide, (set, frozenset)):
                    el_own = engkey in elide
                    el_dom = True
                elif elide:
                    el_own = engkey != "v"
                    el_dom = True
                else:
                    el_own = el_dom = False
                for (e, fn, waits, _inc) in prog.items:
                    if e != engkey:
                        continue
                    for (s, v) in waits:
                        if el_own and s == own and v <= own_count:
                            continue
                        if el_dom:
                            if hw.get(s, -1) >= v and not (s == own and not el_own):
                                continue
                            hw[s] = v
                        eng.wait_ge(prog.sems[s], v)
                    inst = fn(eng)
                    if inst is not None:
                        inst.then_inc(prog.sems[_inc[0]], _inc[1])
                        if _inc[0] == own:
                            own_count += _inc[1]
            return section

        block.tensor(mk("t"))
        block.vector(mk("v"))
        block.scalar(mk("s"))
        block.gpsimd(mk("g"))
        block.sync(mk("y"))


def ceil_div(a, b):
    return (a + b - 1) // b


def bfloat(x):
    return np.asarray(x).astype(ml_dtypes.bfloat16)


def f32(x):
    return np.asarray(x, dtype=np.float32)


class OffsetDram:

    def __init__(self, handle, col_off, shape):
        self.h = handle
        self.off = col_off
        self.shape = shape

    def __getitem__(self, idx):
        if idx is Ellipsis:
            return self.h[:, self.off:self.off + self.shape[1]]
        s0, s1 = idx
        a = (s1.start or 0) + self.off
        b = (s1.stop if s1.stop is not None else self.shape[1]) + self.off
        return self.h[s0, a:b]


class NCProxy:

    def __init__(self, nc, pref, io):
        self._nc = nc
        self._pref = pref
        self._io = io

    def dram_tensor(self, name, shape, dt, kind=None, **kw):
        if name in self._io:
            return self._io[name]
        return self._nc.dram_tensor(self._pref + name, shape, dt, kind=kind, **kw)

    def sbuf_tensor(self, name, shape, dt, **kw):
        return self._nc.sbuf_tensor(self._pref + name, shape, dt, **kw)

    def psum_tensor(self, name, shape, dt, **kw):
        return self._nc.psum_tensor(self._pref + name, shape, dt, **kw)

    def inline_tensor(self, arr, name=None):
        nm = self._pref + (name or f"il{id(arr) % 99999}")
        return self._nc.inline_tensor(arr, name=nm)

    def __getattr__(self, a):
        return getattr(self._nc, a)


class FusedCtx:
    def __init__(self, nc, p):
        self.nc = nc
        self.p = p
        self.pref = ""
        self.io = {}

    def stage(self, pref, io):
        self.pref = pref
        self.io = io


CTX = None


def ctx_nc():
    if CTX is None:
        return bass.Bass("TRN2", target_bir_lowering=False)
    return NCProxy(CTX.nc, CTX.pref, CTX.io)


def ctx_prog(nc):
    if CTX is None:
        return Prog(nc)
    return CTX.p


def ctx_emit(p, est):
    if CTX is None:
        p.emit(est)
    else:
        p.phase_barrier()

def _ns_attna():
    D = 1024
    LAT = 512
    NSEQ = 8192
    TCH = 512
    NCH = NSEQ // TCH
    BIAS = -40.0


    def build_attna():
        nc = ctx_nc()
        x_d = nc.dram_tensor("x0", [D, NSEQ], F32, kind="ExternalInput")
        wcd_d = nc.dram_tensor("Wc_down", [D, LAT], F32, kind="ExternalInput")
        wq_d = nc.dram_tensor("Wq_sl", [LAT, 256], F32, kind="ExternalInput")
        wkv_d = nc.dram_tensor("Wkv", [LAT, 128], F32, kind="ExternalInput")
        ao_d = nc.dram_tensor("ao", [256, NSEQ], F32, kind="ExternalOutput")

        u = np.arange(128)
        mdiag_c = []
        for delta in range(4):
            tt = np.arange(512)
            m = ((128 * delta + u[:, None]) <= tt[None, :]).astype(ml_dtypes.bfloat16)
            mdiag_c.append(nc.inline_tensor(m, name=f"mdiag{delta}"))

        with ExitStack() as est:
            e = est.enter_context
            wcd = e(nc.sbuf_tensor("wcd", [128, 8 * LAT], BF16))
            wq = e(nc.sbuf_tensor("wq", [128, 4 * 256], BF16))
            wkv = e(nc.sbuf_tensor("wkv", [128, 4 * 128], BF16))
            lat = e(nc.sbuf_tensor("lat", [128, 4 * TCH], BF16))
            qh = [e(nc.sbuf_tensor(f"qh{h}", [64, NSEQ], BF16)) for h in range(4)]
            kb = e(nc.sbuf_tensor("kb", [64, NSEQ], BF16))
            vb = e(nc.sbuf_tensor("vb", [128, 64 * 65], BF16))
            xch = [e(nc.sbuf_tensor(f"xch{k}", [128, 8 * TCH], F32)) for k in range(2)]
            xcc = [e(nc.sbuf_tensor(f"xcc{k}", [128, TCH], BF16)) for k in range(2)]
            xsq = [e(nc.sbuf_tensor(f"xsq{k}", [128, TCH], BF16)) for k in range(2)]
            h1 = e(nc.sbuf_tensor("h1", [128, 8 * TCH], BF16))
            mdg = [e(nc.sbuf_tensor(f"mdg{d}", [128, 512], BF16)) for d in range(4)]
            Ab = [e(nc.sbuf_tensor(f"Ab{k}", [128, 512], BF16)) for k in range(4)]
            oev = e(nc.sbuf_tensor("oev", [64, 512], BF16))
            rrow = e(nc.sbuf_tensor("rrow", [1, 512], F32))
            onesc = e(nc.sbuf_tensor("onesc", [128, 2], BF16))
            onesf = e(nc.sbuf_tensor("onesf", [1, 128], F32))
            onc = e(nc.sbuf_tensor("onc", [1, 64], F32))
            bcol = e(nc.sbuf_tensor("bcol", [128, 1], F32))
            murow = e(nc.sbuf_tensor("murow", [1, TCH], F32))
            musq = e(nc.sbuf_tensor("musq", [1, TCH], F32))
            varrow = e(nc.sbuf_tensor("varrow", [1, TCH], F32))
            rstdrow = e(nc.sbuf_tensor("rstdrow", [1, TCH], F32))
            mrrow = e(nc.sbuf_tensor("mrrow", [1, TCH], F32))
            Rb = e(nc.sbuf_tensor("Rb", [128, TCH], BF16))
            Mb = e(nc.sbuf_tensor("Mb", [128, TCH], BF16))
            outc2 = [[e(nc.sbuf_tensor(f"outc{j}_{k}", [128, 512], F32)) for k in range(2)]
                     for j in range(2)]
            ps_r0 = e(nc.psum_tensor("ps_r0", [1, TCH], F32))
            ps_r1 = e(nc.psum_tensor("ps_r1", [1, TCH], F32))
            ps_bc = e(nc.psum_tensor("ps_bc", [128, TCH], F32))
            ps_a = e(nc.psum_tensor("ps_a", [128, 512], F32))
            ps_s = e(nc.psum_tensor("ps_s", [128, 512], F32))
            ps_sB = e(nc.psum_tensor("ps_sB", [128, 512], F32))
            ps_av = e(nc.psum_tensor("ps_av", [65, 512], F32))

            p = ctx_prog(nc)
            G = {}

            def gw(name):
                return list(G.get(name, []))

            w_sm = p.op("g", lambda en: en.dma_start(
                wcd[...].rearrange("p (a w) -> p a w", a=8),
                wcd_d[...].rearrange("(a p) w -> p a w", p=128)), dma=True, stream="sm")
            w_sm = p.op("g", lambda en: en.dma_start(
                wq[...].rearrange("p (a w) -> p a w", a=4),
                wq_d[...].rearrange("(a p) w -> p a w", p=128)), dma=True, stream="sm")
            w_sm = p.op("g", lambda en: en.dma_start(
                wkv[...].rearrange("p (a w) -> p a w", a=4),
                wkv_d[...].rearrange("(a p) w -> p a w", p=128)), dma=True, stream="sm")
            for d_ in range(4):
                w_sm = p.op("g", (lambda d_: lambda en: en.dma_start(
                    mdg[d_][...], mdiag_c[d_][...]))(d_), dma=True, stream="sm")
            c1 = p.op("v", lambda en: en.memset(onesc[...], 1.0))
            c2 = p.op("v", lambda en: en.memset(onesf[...], 1.0))
            c3 = p.op("v", lambda en: en.memset(onc[...], 1.0))
            c4 = p.op("v", lambda en: en.memset(bcol[...], BIAS))
            c5 = p.op("v", lambda en: en.memset(
                vb[...].rearrange("p (a w) -> p a w", a=64)[:, :, 64:65], 1.0))
            CON = [w_sm, c1, c2, c3, c4, c5]

            for ch in range(NCH):
                b = ch % 2
                t_x = p.op("y", (lambda ch, b: lambda en: en.dma_start(
                    xch[b][...].rearrange("p (a w) -> p a w", a=8),
                    x_d[:, TCH * ch:TCH * (ch + 1)].rearrange("(a p) w -> p a w", p=128)))(ch, b),
                    wait=gw(f"xch{b}"), dma=True, stream=f"x{b}")
                tprev = None
                for j in range(8):
                    sb = j % 2
                    tc = p.op("v", (lambda j, b, sb: lambda en: en.tensor_copy(
                        xcc[sb][...], xch[b][:, j * TCH:(j + 1) * TCH]))(j, b, sb),
                        wait=[t_x] + gw(f"xcc{sb}"))
                    tq = p.op("s", (lambda sb: lambda en: en.activation(
                        xsq[sb][...], xcc[sb][...], AF.Square))(sb),
                        wait=[tc] + gw(f"xsq{sb}"))
                    wl = [tc, c1] + ([tprev] if tprev else [])
                    if j == 0:
                        wl += gw("ps_r")
                    t1 = p.op("t", (lambda j, sb: lambda en: en.matmul(
                        ps_r0[0:1, :], onesc[:, 0:1], xcc[sb][...],
                        start=(j == 0), stop=(j == 7)))(j, sb), wait=wl)
                    t2 = p.op("t", (lambda j, sb: lambda en: en.matmul(
                        ps_r1[0:1, :], onesc[:, 1:2], xsq[sb][...],
                        start=(j == 0), stop=(j == 7)))(j, sb), wait=[tq, t1])
                    G[f"xcc{sb}"] = [t1, tq]
                    G[f"xsq{sb}"] = [t2]
                    tprev = t2
                t_mu = p.op("v", lambda en: en.tensor_scalar_mul(
                    murow[...], ps_r0[0:1, :], 1.0 / D), wait=[tprev])
                t_m2 = p.op("v", lambda en: en.tensor_mul(
                    musq[...], murow[...], murow[...]), wait=[t_mu])
                t_var = p.op("v", lambda en: en.scalar_tensor_tensor(
                    varrow[...], ps_r1[0:1, :], 1.0 / D, musq[...],
                    ALU.mult, ALU.subtract), wait=[t_m2, tprev])
                G["ps_r"] = [t_var]
                t_ve = p.op("v", lambda en: en.tensor_scalar_add(
                    varrow[...], varrow[...], 1e-5), wait=[t_var])
                t_ln = p.op("s", lambda en: en.activation(
                    rstdrow[...], varrow[...], AF.Ln), wait=[t_ve])
                t_rs = p.op("s", lambda en: en.activation(
                    rstdrow[...], rstdrow[...], AF.Exp, scale=-0.5), wait=[t_ln])
                t_mr = p.op("v", lambda en: en.tensor_mul(
                    mrrow[...], murow[...], rstdrow[...]), wait=[t_rs, t_mu])
                tRb, tMb = [], []
                for q_ in range(TCH // 512):
                    pass
                tb = p.op("t", lambda en: en.matmul(
                    ps_bc[:, 0:512], onesf[0:1, :], rstdrow[0:1, 0:512],
                    start=True, stop=True), wait=[t_rs, c2] + gw("ps_bc"))
                tR = p.op("s", lambda en: en.activation(
                    Rb[:, 0:512], ps_bc[:, 0:512], AF.Identity), wait=[tb] + gw("Rb"))
                tb2 = p.op("t", lambda en: en.matmul(
                    ps_bc[:, 0:512], onesf[0:1, :], mrrow[0:1, 0:512],
                    start=True, stop=True), wait=[t_mr, tR])
                tM = p.op("s", lambda en: en.activation(
                    Mb[:, 0:512], ps_bc[:, 0:512], AF.Identity, scale=-1.0),
                    wait=[tb2] + gw("Mb"))
                G["ps_bc"] = [tM]
                t_h = []
                for j in range(8):
                    tm = p.op("v", (lambda j, b: lambda en: en.tensor_mul(
                        h1[:, j * TCH:(j + 1) * TCH], xch[b][:, j * TCH:(j + 1) * TCH],
                        Rb[:, 0:TCH]))(j, b), wait=[tR, t_x] + gw("h1"))
                    ta = p.op("v", (lambda j: lambda en: en.tensor_add(
                        h1[:, j * TCH:(j + 1) * TCH], h1[:, j * TCH:(j + 1) * TCH],
                        Mb[:, 0:TCH]))(j), wait=[tm, tM])
                    t_h.append(ta)
                G[f"xch{b}"] = t_h
                G["Rb"] = t_h
                G["Mb"] = t_h
                t_lat = []
                for m in range(4):
                    last = None
                    for kk in range(8):
                        wl = [t_h[kk], CON[0]] + ([last] if last else [])
                        if kk == 0:
                            wl += gw("ps_a")
                        last = p.op("t", (lambda m, kk, ch: lambda en: en.matmul(
                            ps_a[:, 0:512], wcd[:, kk * LAT + 128 * m:kk * LAT + 128 * (m + 1)],
                            h1[:, kk * TCH:(kk + 1) * TCH],
                            start=(kk == 0), stop=(kk == 7)))(m, kk, ch), wait=wl)
                    tev = p.op("s", (lambda m: lambda en: en.activation(
                        lat[:, m * TCH:(m + 1) * TCH],
                        ps_a[:, 0:512], AF.Identity))(m), wait=[last] + (gw("lat") if m == 0 else []))
                    G["ps_a"] = [tev]
                    t_lat.append(tev)
                G["h1"] = [t_lat[-1]]
                qreaders = []
                for m in range(2):
                    last = None
                    for kk in range(4):
                        wl = [t_lat[kk], CON[0]] + ([last] if last else [])
                        if kk == 0:
                            wl += gw("ps_a")
                        last = p.op("t", (lambda m, kk: lambda en: en.matmul(
                            ps_a[:, 0:512], wq[:, kk * 256 + 128 * m:kk * 256 + 128 * (m + 1)],
                            lat[:, kk * TCH:(kk + 1) * TCH],
                            start=(kk == 0), stop=(kk == 3)))(m, kk), wait=wl)
                    tev0 = p.op("s", (lambda m, ch: lambda en: en.activation(
                        qh[2 * m][0:64, TCH * ch:TCH * (ch + 1)],
                        ps_a[0:64, 0:512], AF.Identity))(m, ch), wait=[last])
                    tev1 = p.op("s", (lambda m, ch: lambda en: en.activation(
                        qh[2 * m + 1][0:64, TCH * ch:TCH * (ch + 1)],
                        ps_a[64:128, 0:512], AF.Identity))(m, ch), wait=[last, tev0])
                    G["ps_a"] = [tev0, tev1]
                    qreaders += [tev0, tev1]
                last = None
                for kk in range(4):
                    wl = [t_lat[kk], CON[0]] + ([last] if last else [])
                    if kk == 0:
                        wl += gw("ps_a")
                    last = p.op("t", (lambda kk: lambda en: en.matmul(
                        ps_a[0:64, 0:512], wkv[:, kk * 128:kk * 128 + 64],
                        lat[:, kk * TCH:(kk + 1) * TCH],
                        start=(kk == 0), stop=(kk == 3)))(kk), wait=wl)
                tev = p.op("s", (lambda ch: lambda en: en.activation(
                    kb[0:64, TCH * ch:TCH * (ch + 1)], ps_a[0:64, 0:512],
                    AF.Identity))(ch), wait=[last])
                G["ps_a"] = [tev]
                for tb_ in range(4):
                    beta = 4 * ch + tb_
                    last = None
                    for kk in range(4):
                        wl = [t_lat[kk], CON[0]] + ([last] if last else [])
                        if kk == 0:
                            wl += gw("ps_s")
                        last = p.op("t", (lambda tb_, kk: lambda en: en.matmul(
                            ps_s[:, 0:64],
                            lat[:, kk * TCH + 128 * tb_:kk * TCH + 128 * (tb_ + 1)],
                            wkv[:, kk * 128 + 64:kk * 128 + 128],
                            start=(kk == 0), stop=(kk == 3)))(tb_, kk), wait=wl)
                    tev = p.op("s", (lambda beta: lambda en: en.activation(
                        vb[:, beta * 65:beta * 65 + 64], ps_s[:, 0:64],
                        AF.Identity))(beta), wait=[last])
                    G["ps_s"] = [tev]
                G["lat"] = [last] + qreaders

            p.phase_barrier()
            for tau in range(NCH):
                nsig = 4 * tau + 4
                tnos = {0: [], 1: []}
                for h in range(4):
                    hb = 64 * (h % 2)
                    jq = h // 2
                    last_av = None
                    rd_q = {}

                    def emit_st(sg):
                        ab = sg % 4
                        pb = sg % 2
                        psp = ps_s if pb == 0 else ps_sB
                        t_s = p.op("t", (lambda h, tau, sg, psp: lambda en: en.matmul(
                            psp[:, 0:512],
                            kb[0:64, 128 * sg:128 * (sg + 1)],
                            qh[h][0:64, TCH * tau:TCH * (tau + 1)],
                            start=True, stop=True))(h, tau, sg, psp),
                            wait=gw(f"ps_sp{pb}"))
                        te_ = p.op("s", (lambda ab, psp: lambda en: en.activation(
                            Ab[ab][:, 0:512], psp[:, 0:512], AF.Exp,
                            scale=0.125, bias=bcol[...]))(ab, psp),
                            wait=[t_s, CON[4]] + gw(f"Ab{ab}"))
                        G[f"ps_sp{pb}"] = [te_]
                        tmask = None
                        delta = sg - 4 * tau
                        if 0 <= delta <= 3:
                            tmask = p.op("v", (lambda ab, delta: lambda en: en.tensor_mul(
                                Ab[ab][:, 0:512], Ab[ab][:, 0:512], mdg[delta][...]))(ab, delta),
                                wait=[te_, CON[0]])
                        rd_q[sg] = tmask or te_

                    def emit_av(sg):
                        nonlocal last_av
                        ab = sg % 4
                        wl = [rd_q[sg], CON[5]] + ([last_av] if last_av else [])
                        if sg == 0:
                            wl += gw("ps_av")
                        last_av = p.op("t", (lambda ab, sg, nsig: lambda en: en.matmul(
                            ps_av[:, 0:512], vb[:, sg * 65:(sg + 1) * 65],
                            Ab[ab][:, 0:512],
                            start=(sg == 0), stop=(sg == nsig - 1)))(ab, sg, nsig), wait=wl)
                        G[f"Ab{ab}"] = [last_av]

                    for sg in range(nsig):
                        emit_st(sg)
                        if sg >= 1:
                            emit_av(sg - 1)
                    emit_av(nsig - 1)
                    trc = p.op("v", lambda en: en.reciprocal(
                        rrow[0:1, 0:512], ps_av[64:65, 0:512]), wait=[last_av] + gw("rrow"))
                    tbc = p.op("t", lambda en: en.matmul(
                        ps_bc[0:64, 0:512], onc[...], rrow[0:1, 0:512],
                        start=True, stop=True), wait=[trc, c3] + gw("ps_bc2"))
                    G["rrow"] = [tbc]
                    tev = p.op("s", lambda en: en.activation(
                        oev[...], ps_av[0:64, 0:512], AF.Identity),
                        wait=[last_av, trc] + gw("oev"))
                    G["ps_av"] = [tev]
                    cb = tau % 2
                    tno = p.op("v", (lambda hb, jq, cb: lambda en: en.tensor_mul(
                        outc2[jq][cb][hb:hb + 64, :], oev[...],
                        ps_bc[0:64, 0:512]))(hb, jq, cb),
                        wait=[tev, tbc] + gw(f"outc{jq}{cb}"))
                    G["ps_bc2"] = [tno]
                    G["oev"] = [tno]
                    tnos[jq].append(tno)
                cb = tau % 2
                for jq in range(2):
                    tst = p.op("y", (lambda jq, tau, cb: lambda en: en.dma_start(
                        ao_d[128 * jq:128 * (jq + 1), TCH * tau:TCH * (tau + 1)],
                        outc2[jq][cb][...]))(jq, tau, cb),
                        wait=tnos[jq], dma=True, stream=f"st{jq}{cb}")
                    G[f"outc{jq}{cb}"] = [tst]

            p.op("v", lambda en: en.engine_nop(), wait=p.all_done())
            ctx_emit(p, est)
        return nc
    return build_attna

build_attna = _ns_attna()

def _ns_attn():
    D = 1024
    HAL = 257
    NT = 2048
    NK = 2304
    NB = 18
    NTAU = 16
    QSCALE = 8.0


    def build_ffn_attn():
        nc = ctx_nc()
        x_d = nc.dram_tensor("x", [D, HAL + NT], F32, kind="ExternalInput")
        w_d = {nm: nc.dram_tensor(nm, [D, D], F32, kind="ExternalInput")
               for nm in ["Wq", "Wk", "Wv", "Wo"]}
        qs_d = nc.dram_tensor("qs", [128, 1], F32, kind="ExternalInput")
        ks_d = nc.dram_tensor("ks", [128, 1], F32, kind="ExternalInput")
        csq_d = nc.dram_tensor("cs_q", [128, NT], F32, kind="ExternalInput")
        ssq_d = nc.dram_tensor("ss_q", [128, NT], F32, kind="ExternalInput")
        csk_d = nc.dram_tensor("cs_k", [128, NK], F32, kind="ExternalInput")
        ssk_d = nc.dram_tensor("ss_k", [128, NK], F32, kind="ExternalInput")
        von_d = nc.dram_tensor("vones", [128, NB], F32, kind="ExternalInput")
        y_d = nc.dram_tensor("y", [D, NT], F32, kind="ExternalOutput")

        swap = np.zeros((128, 128), np.float32)
        for i in range(128):
            part = i + 32 if (i % 64) < 32 else i - 32
            swap[part, i] = -1.0 if (i % 64) < 32 else 1.0
        P_c = nc.inline_tensor(swap.astype(ml_dtypes.bfloat16), name="Pswap")
        ident_c = nc.inline_tensor(np.eye(128, dtype=ml_dtypes.bfloat16), name="id128")
        u = np.arange(128)
        m0_c = nc.inline_tensor((u[:, None] >= u[None, :]).astype(ml_dtypes.bfloat16),
                                name="mask_s0")
        m2_c = nc.inline_tensor((u[:, None] <= u[None, :]).astype(ml_dtypes.bfloat16),
                                name="mask_s2")
        indj_np = np.zeros((8, 128, 16), np.float32)
        for j in range(8):
            indj_np[j, 0:64, 2 * j] = 1.0
            indj_np[j, 64:128, 2 * j + 1] = 1.0
        indj_c = [nc.inline_tensor(indj_np[j].astype(ml_dtypes.bfloat16), name=f"indj{j}")
                  for j in range(8)]
        ind2_np = np.zeros((8, 16, 128), np.float32)
        for j in range(8):
            ind2_np[j, 2 * j, 0:64] = 1.0
            ind2_np[j, 2 * j + 1, 64:128] = 1.0
        ind2_c = nc.inline_tensor(ind2_np.reshape(8 * 16, 128).astype(ml_dtypes.bfloat16),
                                  name="indh2")

        with ExitStack() as est:
            e = est.enter_context
            qs = e(nc.sbuf_tensor("qss", [128, 1], F32))
            ks = e(nc.sbuf_tensor("kss", [128, 1], F32))
            von = e(nc.sbuf_tensor("von", [128, NB], BF16))
            idn = e(nc.sbuf_tensor("idn", [128, 128], BF16))
            msk0 = e(nc.sbuf_tensor("msk0", [128, 128], BF16))
            msk2 = e(nc.sbuf_tensor("msk2", [128, 128], BF16))
            ones65 = e(nc.sbuf_tensor("ones65", [65, 128], BF16))
            onc = e(nc.sbuf_tensor("onc", [1, 64], F32))
            wbuf = e(nc.sbuf_tensor("wbuf", [128, 8 * D], BF16))
            qrot = e(nc.sbuf_tensor("qrot", [128, 8 * NT], BF16))
            krot = e(nc.sbuf_tensor("krot", [128, 8 * NK], BF16))
            vbuf = e(nc.sbuf_tensor("vbuf", [128, NB * D], BF16))
            est2 = ExitStack()
            e2 = est2.enter_context
            cst = e2(nc.sbuf_tensor("cst", [128, NK], BF16))
            sst = e2(nc.sbuf_tensor("sst", [128, NK], BF16))
            Psw = e2(nc.sbuf_tensor("Psw", [128, 128], BF16))
            indj = [e2(nc.sbuf_tensor(f"indjs{j}", [128, 16], BF16)) for j in range(8)]
            ind2 = e2(nc.sbuf_tensor("ind2", [16, 8 * 128], BF16))
            wbuf2 = e2(nc.sbuf_tensor("wbuf2", [128, 8 * D], BF16))
            rno = e2(nc.sbuf_tensor("rno", [16, NT], BF16))
            rnk = e2(nc.sbuf_tensor("rnk", [16, NK], BF16))
            sqs = [e2(nc.sbuf_tensor(f"sqs{k}", [128, 512], BF16)) for k in range(2)]
            tmp0 = e2(nc.sbuf_tensor("tmp0", [128, 512], BF16))
            tmp1 = e2(nc.sbuf_tensor("tmp1", [128, 512], BF16))
            tmp2 = e2(nc.sbuf_tensor("tmp2", [128, 512], BF16))
            xbch = [e2(nc.sbuf_tensor(f"xbch{k}", [128, 8 * 512], BF16)) for k in range(2)]
            est2.close()
            attb4 = e(nc.sbuf_tensor("attb4", [128, 4096], BF16))
            A_T = e(nc.sbuf_tensor("A_T", [128, 16 * 384], BF16))
            negM = e(nc.sbuf_tensor("negM", [128, 16], BF16))
            negMr = e(nc.sbuf_tensor("negMr", [65, 16 * 128], BF16))
            rrow = e(nc.sbuf_tensor("rrow", [1, 512], F32))
            oev = e(nc.sbuf_tensor("oev", [64, 512], BF16))
            xres = [e(nc.sbuf_tensor(f"xres{k}", [128, 512], F32)) for k in range(2)]
            outb = [e(nc.sbuf_tensor(f"outbf{k}", [128, 512], F32)) for k in range(2)]
            ps_q = e(nc.psum_tensor("ps_q", [128, 512], F32))
            ps_w = e(nc.psum_tensor("ps_w", [128, 512], F32))
            ps_n = e(nc.psum_tensor("ps_n", [16, 512], F32))
            ps_bc = e(nc.psum_tensor("ps_bc", [128, 512], F32))
            ps_1 = e(nc.psum_tensor("ps_1", [128, 384], F32))
            ps_2 = e(nc.psum_tensor("ps_2", [128, 384], F32))
            ps_av = e(nc.psum_tensor("ps_av", [64, 512], F32))
            ps_dn = e(nc.psum_tensor("ps_dn", [1, 512], F32))

            p = ctx_prog(nc)
            G = {}

            def gw(name):
                return list(G.get(name, []))

            w_sm = None
            smalls = [(qs, qs_d), (ks, ks_d), (von, von_d), (Psw, P_c), (idn, ident_c),
                      (msk0, m0_c), (msk2, m2_c)]
            smalls += [(indj[j], indj_c[j]) for j in range(8)]
            for (sb_, dr_) in smalls:
                w_sm = p.op("g", (lambda sb_, dr_: lambda en: en.dma_start(
                    sb_[...], dr_[...]))(sb_, dr_), dma=True, stream="sm")
            w_sm = p.op("g", lambda en: en.dma_start(
                ind2[...].rearrange("p (a w) -> p a w", a=8),
                ind2_c[...].rearrange("(a p) w -> p a w", p=16)), dma=True, stream="sm")
            c2 = p.op("v", lambda en: en.memset(ones65[...], 1.0))
            c3 = p.op("v", lambda en: en.memset(onc[...], 1.0))
            CON = [w_sm, c2, c3]

            def load_w(nm, buf):
                return p.op("g", (lambda nm, buf: lambda en: en.dma_start(
                    buf[...].rearrange("p (a w) -> p a w", a=8),
                    w_d[nm][...].rearrange("(a p) w -> p a w", p=128)))(nm, buf),
                    dma=True, stream="wst")

            def load_tbl(cs_src, ss_src, ncols):
                t1 = p.op("g", (lambda cs_src, ncols: lambda en: en.dma_start(
                    cst[:, 0:ncols], cs_src[...]))(cs_src, ncols), dma=True, stream="tbl")
                t2 = p.op("g", (lambda ss_src, ncols: lambda en: en.dma_start(
                    sst[:, 0:ncols], ss_src[...]))(ss_src, ncols), dma=True, stream="tbl")
                return [t1, t2]

            def emit_proj(dst, wtok, wbufx, scol, ncols, xoff, rows_out, tbl_toks):
                chunks = []
                o = 0
                while o < ncols:
                    cw = min(512, ncols - o)
                    chunks.append((o, cw))
                    o += cw
                for ci, (o, cw) in enumerate(chunks):
                    b = ci % 2
                    p.op("g", (lambda o, cw, b: lambda en: en.dma_start(
                        xbch[b][:, 0:4 * cw].rearrange("p (a w) -> p a w", a=4),
                        x_d[0:512, xoff + o:xoff + o + cw].rearrange(
                            "(a p) w -> p a w", p=128)))(o, cw, b),
                        wait=gw(f"xbch{b}"), dma=True, stream=f"xc{b}")
                    t_xc = p.op("g", (lambda o, cw, b: lambda en: en.dma_start(
                        xbch[b][:, 4 * cw:8 * cw].rearrange("p (a w) -> p a w", a=4),
                        x_d[512:1024, xoff - 1 + o:xoff - 1 + o + cw].rearrange(
                            "(a p) w -> p a w", p=128)))(o, cw, b),
                        wait=gw(f"xbch{b}"), dma=True, stream=f"xc{b}")
                    t3s = []
                    t_nmm = None
                    xc_readers = []
                    for j in range(8):
                        sb = j % 2
                        last = None
                        for kk in range(8):
                            wl = [t_xc, wtok] + ([last] if last else [])
                            if j == 0 and kk == 0:
                                wl += gw("ps_q")
                            last = p.op("t", (lambda j, kk, cw, b: lambda en: en.matmul(
                                ps_q[:, 0:cw],
                                wbufx[:, kk * D + 128 * j:kk * D + 128 * (j + 1)],
                                xbch[b][:, kk * cw:(kk + 1) * cw],
                                start=(kk == 0), stop=(kk == 7)))(j, kk, cw, b), wait=wl)
                        xc_readers.append(last)
                        tsq = p.op("s", (lambda cw, sb: lambda en: en.activation(
                            sqs[sb][:, 0:cw], ps_q[:, 0:cw], AF.Square))(cw, sb),
                            wait=[last] + gw(f"sqs{sb}"))
                        tsc = p.op("s", (lambda cw, scol: lambda en: en.activation(
                            tmp0[:, 0:cw], ps_q[:, 0:cw], AF.Identity,
                            scale=scol[...]))(cw, scol), wait=[last, tsq, CON[0]] + gw("tmp0"))
                        G["ps_q"] = [tsc]
                        wl = [tsq, CON[0]] + ([t_nmm] if t_nmm else [])
                        if j == 0:
                            wl += gw("ps_n")
                        t_nmm = p.op("t", (lambda j, cw, sb: lambda en: en.matmul(
                            ps_n[0:16, 0:cw], indj[j][...], sqs[sb][:, 0:cw],
                            start=(j == 0), stop=(j == 7)))(j, cw, sb), wait=wl)
                        G[f"sqs{sb}"] = [t_nmm]
                        tsw = p.op("t", (lambda cw: lambda en: en.matmul(
                            ps_w[:, 0:cw], Psw[...], tmp0[:, 0:cw],
                            start=True, stop=True))(cw), wait=[tsc, CON[0]] + gw("ps_w"))
                        t1 = p.op("v", (lambda o, cw: lambda en: en.tensor_mul(
                            tmp1[:, 0:cw], tmp0[:, 0:cw], cst[:, o:o + cw]))(o, cw),
                            wait=[tsc] + tbl_toks + gw("tmp1"))
                        G["tmp0"] = [t1]
                        t2 = p.op("v", (lambda o, cw: lambda en: en.tensor_mul(
                            tmp2[:, 0:cw], ps_w[:, 0:cw], sst[:, o:o + cw]))(o, cw),
                            wait=[tsw] + tbl_toks + gw("tmp2"))
                        G["ps_w"] = [t2]
                        t3 = p.op("v", (lambda j, o, cw, dst, ncols: lambda en: en.tensor_add(
                            dst[:, j * ncols + o:j * ncols + o + cw],
                            tmp1[:, 0:cw], tmp2[:, 0:cw]))(j, o, cw, dst, ncols),
                            wait=[t1, t2])
                        G["tmp1"] = [t3]
                        G["tmp2"] = [t3]
                        t3s.append(t3)
                    G[f"xbch{b}"] = xc_readers
                    tmx = p.op("v", (lambda o, cw, rows_out: lambda en: en.tensor_scalar_max(
                        rows_out[0:16, o:o + cw], ps_n[0:16, 0:cw], 1e-24))(o, cw, rows_out),
                        wait=[t_nmm])
                    G["ps_n"] = [tmx]
                    tl = p.op("s", (lambda o, cw, rows_out: lambda en: en.activation(
                        rows_out[0:16, o:o + cw], rows_out[0:16, o:o + cw], AF.Ln))(o, cw, rows_out),
                        wait=[tmx])
                    tex = p.op("s", (lambda o, cw, rows_out: lambda en: en.activation(
                        rows_out[0:16, o:o + cw], rows_out[0:16, o:o + cw], AF.Exp,
                        scale=-0.5))(o, cw, rows_out), wait=[tl])
                    for j in range(8):
                        tb = p.op("t", (lambda j, o, cw, rows_out: lambda en: en.matmul(
                            ps_bc[:, 0:cw], ind2[:, 128 * j:128 * (j + 1)],
                            rows_out[0:16, o:o + cw], start=True, stop=True))(j, o, cw, rows_out),
                            wait=[tex, CON[0]] + gw("ps_bc"))
                        tm = p.op("v", (lambda j, o, cw, dst, ncols: lambda en: en.tensor_mul(
                            dst[:, j * ncols + o:j * ncols + o + cw],
                            dst[:, j * ncols + o:j * ncols + o + cw],
                            ps_bc[:, 0:cw]))(j, o, cw, dst, ncols), wait=[tb, t3s[j]])
                        G["ps_bc"] = [tm]

            w_wq = load_w("Wq", wbuf)
            tbl = load_tbl(csq_d, ssq_d, NT)
            emit_proj(qrot, w_wq, wbuf, qs, NT, HAL, rno, tbl)
            p.phase_barrier()
            w_wk = load_w("Wk", wbuf2)
            tbl = load_tbl(csk_d, ssk_d, NK)
            emit_proj(krot, w_wk, wbuf2, ks, NK, 1, rnk, tbl)

            p.phase_barrier()
            w_wv = load_w("Wv", wbuf)
            for beta in range(NB):
                b = beta % 2
                p.op("g", (lambda beta, b: lambda en: en.dma_start(
                    xbch[b][:, 0:4 * 128].rearrange("p (a w) -> p a w", a=4),
                    x_d[0:512, 1 + 128 * beta:1 + 128 * (beta + 1)].rearrange(
                        "(a p) w -> p a w", p=128)))(beta, b),
                    wait=gw(f"xbch{b}"), dma=True, stream=f"xc{b}")
                t_xc = p.op("g", (lambda beta, b: lambda en: en.dma_start(
                    xbch[b][:, 4 * 128:8 * 128].rearrange("p (a w) -> p a w", a=4),
                    x_d[512:1024, 128 * beta:128 * (beta + 1)].rearrange(
                        "(a p) w -> p a w", p=128)))(beta, b),
                    wait=gw(f"xbch{b}"), dma=True, stream=f"xc{b}")
                readers = []
                for half in range(2):
                    ps = ps_bc if half == 0 else ps_q
                    last = None
                    for kk in range(8):
                        wl = [t_xc, w_wv] + ([last] if last else [])
                        if kk == 0:
                            wl += gw("ps_bc" if half == 0 else "ps_q")
                        last = p.op("t", (lambda b, half, kk, ps: lambda en: en.matmul(
                            ps[:, 0:512],
                            xbch[b][:, kk * 128:(kk + 1) * 128],
                            wbuf[:, kk * D + half * 512:kk * D + (half + 1) * 512],
                            start=(kk == 0), stop=(kk == 7)))(b, half, kk, ps), wait=wl)
                    readers.append(last)
                    tev = p.op("s", (lambda beta, half, ps: lambda en: en.activation(
                        vbuf[:, beta * D + half * 512:beta * D + (half + 1) * 512],
                        ps[:, 0:512], AF.Identity))(beta, half, ps), wait=[last])
                    G["ps_bc" if half == 0 else "ps_q"] = [tev]
                    tv_last = tev
                G[f"xbch{b}"] = readers

            p.phase_barrier()
            w_wo = load_w("Wo", wbuf)
            grp_read = []
            for tau in range(NTAU):
                ab = tau % 2
                w0 = 1 + 128 * tau
                t_red = {}
                t_s1d = {}

                def em_s1(h):
                    hb = 64 * (h % 2)
                    jt = h // 2
                    pp = ps_1 if h % 2 == 0 else ps_w
                    t_s1d[h] = p.op("t", (lambda hb, jt, tau, w0, pp: lambda en: en.matmul(
                        pp[:, 0:384],
                        qrot[hb:hb + 64, jt * NT + 128 * tau:jt * NT + 128 * (tau + 1)],
                        krot[hb:hb + 64, jt * NK + w0 - 1:jt * NK + w0 - 1 + 384],
                        start=True, stop=True))(hb, jt, tau, w0, pp),
                        wait=gw("g_ps1" if h % 2 == 0 else "g_psw"))

                def em_red(h):
                    pp = ps_1 if h % 2 == 0 else ps_w
                    t_red[h] = p.op("v", (lambda h, pp: lambda en: en.reduce_max(
                        negM[:, h:h + 1], pp[:, 0:384], AX.X, negate=True))(h, pp),
                        wait=[t_s1d[h]] + gw("negM"))
                    G["g_ps1" if h % 2 == 0 else "g_psw"] = [t_red[h]]

                for h in range(16):
                    em_s1(h)
                    if h >= 1:
                        em_red(h - 1)
                em_red(15)
                negm_readers = []
                for q4 in range(4):
                    last = None
                    for hh in range(4):
                        h = 4 * q4 + hh
                        wl = [t_red[h], CON[0]] + ([last] if last else [])
                        if hh == 0:
                            wl += gw("ps_n")
                        last = p.op("t", (lambda h, hh: lambda en: en.matmul(
                            ps_n[0:1, 128 * hh:128 * (hh + 1)], negM[:, h:h + 1],
                            idn[...], start=True, stop=True))(h, hh), wait=wl)
                    negm_readers.append(last)
                    te0 = p.op("s", (lambda q4: lambda en: en.activation(
                        negMr[0:1, 512 * q4:512 * (q4 + 1)], ps_n[0:1, 0:512],
                        AF.Identity))(q4), wait=[last] + gw("negMr"))
                    te1 = p.op("s", (lambda q4: lambda en: en.activation(
                        negMr[64:65, 512 * q4:512 * (q4 + 1)], ps_n[0:1, 0:512],
                        AF.Identity))(q4), wait=[last, te0])
                    G["ps_n"] = [te0, te1]
                    G[f"negMr{q4}"] = [te0, te1]
                G["negM"] = negm_readers
                t_exp = {}
                mmend = {}

                def em_p2(h):
                    hb = 64 * (h % 2)
                    jt = h // 2
                    pp = ps_2 if h % 2 == 0 else ps_1
                    gk = "g_ps2" if h % 2 == 0 else "g_ps1"
                    last = None
                    for sg in range(3):
                        wl = G[f"negMr{h // 4}"] + ([last] if last else [])
                        if sg == 0:
                            wl += gw(gk)
                        last = p.op("t", (lambda hb, jt, sg, tau, w0, pp: lambda en: en.matmul(
                            pp[:, 128 * sg:128 * (sg + 1)],
                            krot[hb:hb + 64, jt * NK + w0 - 1 + 128 * sg:jt * NK + w0 - 1 + 128 * (sg + 1)],
                            qrot[hb:hb + 64, jt * NT + 128 * tau:jt * NT + 128 * (tau + 1)],
                            start=True, stop=False))(hb, jt, sg, tau, w0, pp), wait=wl)
                        last = p.op("t", (lambda hb, h, sg, pp: lambda en: en.matmul(
                            pp[:, 128 * sg:128 * (sg + 1)],
                            ones65[hb:hb + 1, 0:128],
                            negMr[hb:hb + 1, 128 * h:128 * (h + 1)],
                            start=False, stop=True))(hb, h, sg, pp), wait=[last, CON[1]])
                    mmend[h] = last

                def em_exp(h):
                    pp = ps_2 if h % 2 == 0 else ps_1
                    gk = "g_ps2" if h % 2 == 0 else "g_ps1"
                    t_exp[h] = p.op("s", (lambda h, pp: lambda en: en.activation(
                        A_T[:, 384 * h:384 * (h + 1)], pp[:, 0:384], AF.Exp,
                        scale=QSCALE))(h, pp), wait=[mmend[h]] + gw("A_T"))
                    G[gk] = [t_exp[h]]

                for h in range(16):
                    em_p2(h)
                    if h >= 1:
                        em_exp(h - 1)
                em_exp(15)
                expall = [t_exp[h] for h in range(16)]
                G["negMr"] = expall
                tm0 = p.op("v", lambda en: en.tensor_mul(
                    A_T[...].rearrange("p (h s) -> p h s", h=16)[:, :, 0:128],
                    A_T[...].rearrange("p (h s) -> p h s", h=16)[:, :, 0:128],
                    msk0[...].rearrange("p (o w) -> p o w", o=1).broadcast_to((128, 16, 128))),
                    wait=expall + [CON[0]])
                tm2 = p.op("v", lambda en: en.tensor_mul(
                    A_T[...].rearrange("p (h s) -> p h s", h=16)[:, :, 256:384],
                    A_T[...].rearrange("p (h s) -> p h s", h=16)[:, :, 256:384],
                    msk2[...].rearrange("p (o w) -> p o w", o=1).broadcast_to((128, 16, 128))),
                    wait=expall)
                at_readers = []
                taum = tau % 4
                for q4 in range(4):
                    last_av = None
                    last_dn = None
                    for hh in range(4):
                        h = 4 * q4 + hh
                        for sg in range(3):
                            beta = tau + sg
                            wl = [tm0, tm2, tv_last] + ([last_av] if last_av else [])
                            if hh == 0 and sg == 0:
                                wl += gw("ps_av")
                            last_av = p.op("t", (lambda h, hh, sg, beta: lambda en: en.matmul(
                                ps_av[0:64, 128 * hh:128 * (hh + 1)],
                                vbuf[:, beta * D + 64 * h:beta * D + 64 * (h + 1)],
                                A_T[:, 384 * h + 128 * sg:384 * h + 128 * (sg + 1)],
                                start=(sg == 0), stop=(sg == 2)))(h, hh, sg, beta), wait=wl)
                    for sg in range(3):
                        beta = tau + sg
                        wl2 = [tm0, tm2, CON[0]] + ([last_dn] if last_dn else [])
                        if sg == 0:
                            wl2 += gw("ps_dn")
                        last_dn = p.op("t", (lambda q4, sg, beta: lambda en: en.matmul(
                            ps_dn[0:1, 0:512],
                            von[:, beta:beta + 1],
                            A_T[...].rearrange("p (h x) -> p h x", h=16)[
                                :, 4 * q4:4 * q4 + 4, 128 * sg:128 * (sg + 1)],
                            start=(sg == 0), stop=(sg == 2)))(q4, sg, beta), wait=wl2)
                    trc = p.op("v", lambda en: en.reciprocal(
                        rrow[0:1, 0:512], ps_dn[0:1, 0:512]), wait=[last_dn] + gw("rrow"))
                    G["ps_dn"] = [trc]
                    tbc = p.op("t", lambda en: en.matmul(
                        ps_bc[0:64, 0:512], onc[...], rrow[0:1, 0:512],
                        start=True, stop=True), wait=[trc, CON[2]] + gw("ps_bc"))
                    G["rrow"] = [tbc]
                    tev = p.op("s", lambda en: en.activation(
                        oev[...], ps_av[0:64, 0:512], AF.Identity),
                        wait=[last_av] + gw("oev"))
                    G["ps_av"] = [tev]
                    a4v = attb4[...].rearrange("p (k t w) -> p k t w", k=8, t=4)
                    tne = p.op("v", (lambda q4, taum: lambda en: en.tensor_mul(
                        a4v[0:64, 2 * q4:2 * q4 + 2, taum, :],
                        oev[...].rearrange("p (r c w) -> p r c w", r=2, c=2)[:, :, 0, :],
                        ps_bc[0:64, 0:512].rearrange(
                            "p (r c w) -> p r c w", r=2, c=2)[:, :, 0, :]))(q4, taum),
                        wait=[tev, tbc] + gw("attb4"))
                    tno = p.op("v", (lambda q4, taum: lambda en: en.tensor_mul(
                        a4v[64:128, 2 * q4:2 * q4 + 2, taum, :],
                        oev[...].rearrange("p (r c w) -> p r c w", r=2, c=2)[:, :, 1, :],
                        ps_bc[0:64, 0:512].rearrange(
                            "p (r c w) -> p r c w", r=2, c=2)[:, :, 1, :]))(q4, taum),
                        wait=[tev, tbc])
                    G["ps_bc"] = [tne, tno]
                    G["oev"] = [tne, tno]
                    at_readers += [tne, tno]
                G["A_T"] = at_readers
                grp_read += at_readers
                if taum == 3:
                    gt = tau - 3
                    wo_readers = []
                    for n in range(8):
                        rb = n % 2
                        t_xr = p.op("y", (lambda n, gt, rb: lambda en: en.dma_start(
                            xres[rb][...],
                            x_d[128 * n:128 * (n + 1), HAL + 128 * gt:HAL + 128 * gt + 512]))(n, gt, rb),
                            wait=gw(f"xres{rb}"), dma=True, stream=f"xr{rb}")
                        last = None
                        for kk in range(8):
                            wl = grp_read + [w_wo] + ([last] if last else [])
                            if kk == 0:
                                wl += gw("ps_q")
                            last = p.op("t", (lambda n, kk: lambda en: en.matmul(
                                ps_q[:, 0:512],
                                wbuf[:, kk * D + 128 * n:kk * D + 128 * (n + 1)],
                                attb4[:, 512 * kk:512 * (kk + 1)],
                                start=(kk == 0), stop=(kk == 7)))(n, kk), wait=wl)
                        wo_readers.append(last)
                        tad = p.op("v", (lambda rb: lambda en: en.tensor_add(
                            outb[rb][...], ps_q[:, 0:512], xres[rb][...]))(rb),
                            wait=[last, t_xr] + gw(f"outb{rb}"))
                        G["ps_q"] = [tad]
                        G[f"xres{rb}"] = [tad]
                        tst = p.op("y", (lambda n, gt, rb: lambda en: en.dma_start(
                            y_d[128 * n:128 * (n + 1), 128 * gt:128 * gt + 512],
                            outb[rb][...]))(n, gt, rb), wait=[tad], dma=True, stream=f"yo{rb}")
                        G[f"outb{rb}"] = [tst]
                    G["attb4"] = wo_readers
                    grp_read = []

            p.op("v", lambda en: en.engine_nop(), wait=p.all_done())
            ctx_emit(p, est)
        return nc


    def attn_inputs(x_own_cm, x_halo_cm, Wqkv, q_scale, k_scale, Wo, pos0, seq_start):
        DIMH = 64
        WINDOW = 256
        xin = np.concatenate([x_halo_cm, x_own_cm], axis=1).astype(np.float32)
        Wq = Wqkv[:, 0:1024]
        Wk = Wqkv[:, 1024:2048]
        Wv = Wqkv[:, 2048:3072]
        qs = np.tile(q_scale, 2).reshape(128, 1).astype(np.float32)
        ks = np.tile(k_scale, 2).reshape(128, 1).astype(np.float32)

        def tables(positions, inv):
            half = np.arange(0, DIMH, 2, dtype=np.float32)
            inv_freq = 1.0 / 10000.0 ** (half / DIMH)
            t = positions.astype(np.float32)
            freqs = t[:, None] * inv_freq[None, :]
            freqs = np.concatenate([freqs, freqs], -1)
            base_scale = (half + 0.4 * DIMH) / (1.4 * DIMH)
            power = (t - 8192 // 2) / (WINDOW // 2)
            scale = base_scale[None, :] ** power[:, None]
            scale = np.concatenate([scale, scale], -1)
            if inv:
                scale = 1.0 / scale
            cs = np.cos(freqs) * scale
            ss = np.sin(freqs) * scale
            cs = np.tile(cs.T, (2, 1)).astype(np.float32)
            ss = np.tile(ss.T, (2, 1)).astype(np.float32)
            return np.ascontiguousarray(cs), np.ascontiguousarray(ss)

        pos_q = pos0 + np.arange(2048)
        pos_k = pos0 - 256 + np.arange(2304)
        cs_q, ss_q = tables(pos_q, inv=False)
        cs_k, ss_k = tables(pos_k, inv=True)
        vones = np.zeros((128, NB), np.float32)
        for beta in range(NB):
            for pp in range(128):
                kpos = pos0 - 256 + 128 * beta + pp
                vones[pp, beta] = 1.0 if (kpos >= 0 or not seq_start) else 0.0
        if not seq_start:
            vones[:] = 1.0
        return {
            "x": xin,
            "Wq": np.ascontiguousarray(Wq.astype(np.float32)),
            "Wk": np.ascontiguousarray(Wk.astype(np.float32)),
            "Wv": np.ascontiguousarray(Wv.astype(np.float32)),
            "Wo": np.ascontiguousarray(Wo.astype(np.float32)),
            "qs": qs, "ks": ks,
            "cs_q": cs_q, "ss_q": ss_q, "cs_k": cs_k, "ss_k": ss_k,
            "vones": vones,
        }
    return build_ffn_attn, attn_inputs

build_ffn_attn, attn_inputs = _ns_attn()

def _ns_ffn():
    D = 1024
    DI = 2730
    NMB = 22
    TCH = 1024


    def mblk_sz(m):
        return 42 if m == NMB - 1 else 128


    def build_ffn_ffn(nt=2048, eps=1e-5):
        nc = ctx_nc()
        x_d = nc.dram_tensor("x", [D, nt + 1], F32, kind="ExternalInput")
        w1a_d = nc.dram_tensor("W1a", [D, DI], F32, kind="ExternalInput")
        w1g_d = nc.dram_tensor("W1g", [D, DI], F32, kind="ExternalInput")
        b1a_d = nc.dram_tensor("b1a", [128, NMB], F32, kind="ExternalInput")
        b1g_d = nc.dram_tensor("b1g", [128, NMB], F32, kind="ExternalInput")
        w2_d = nc.dram_tensor("W2", [DI, D], F32, kind="ExternalInput")
        b2_d = nc.dram_tensor("b2r", [1, D], F32, kind="ExternalInput")
        lng_d = nc.dram_tensor("lng", [128, 8], F32, kind="ExternalInput")
        lnb_d = nc.dram_tensor("lnb", [128, 8], F32, kind="ExternalInput")
        y_d = nc.dram_tensor("y", [D, nt], F32, kind="ExternalOutput")

        nchunk = nt // TCH
        with ExitStack() as st:
            e = st.enter_context
            x = [e(nc.sbuf_tensor(f"xt{i}", [128, nt + 1], F32)) for i in range(8)]
            lng = e(nc.sbuf_tensor("lngs", [128, 8], F32))
            lnb = e(nc.sbuf_tensor("lnbs", [128, 8], F32))
            b1a = e(nc.sbuf_tensor("b1as", [128, NMB], F32))
            b1g = e(nc.sbuf_tensor("b1gs", [128, NMB], F32))
            b2r = e(nc.sbuf_tensor("b2rs", [1, D], BF16))
            onesr = e(nc.sbuf_tensor("onesr", [1, TCH], BF16))
            ones2 = e(nc.sbuf_tensor("ones2", [128, 2], BF16))
            onesf = e(nc.sbuf_tensor("onesf", [1, 128], F32))
            xn = [e(nc.sbuf_tensor(f"xn{i}", [128, TCH], BF16)) for i in range(8)]
            xcast = [e(nc.sbuf_tensor(f"xcast{k}", [128, TCH], BF16)) for k in range(2)]
            xsq = [e(nc.sbuf_tensor(f"xsq{k}", [128, TCH], BF16)) for k in range(2)]
            hbuf = e(nc.sbuf_tensor("hbuf", [128, NMB * TCH], BF16))
            gelu_s = e(nc.sbuf_tensor("gelu_s", [128, TCH], BF16))
            a_s = e(nc.sbuf_tensor("a_s", [128, TCH], BF16))
            w1buf = [e(nc.sbuf_tensor(f"w1b{k}", [128, 16 * 128], BF16)) for k in range(2)]
            w2buf = [e(nc.sbuf_tensor(f"w2b{k}", [128, NMB * 128], BF16)) for k in range(2)]
            outbuf = [e(nc.sbuf_tensor(f"outb{k}", [128, TCH], F32)) for k in range(2)]
            murow = e(nc.sbuf_tensor("murow", [1, TCH], F32))
            musq = e(nc.sbuf_tensor("musq", [1, TCH], F32))
            varrow = e(nc.sbuf_tensor("varrow", [1, TCH], F32))
            lnvrow = e(nc.sbuf_tensor("lnvrow", [1, TCH], F32))
            rstdrow = e(nc.sbuf_tensor("rstdrow", [1, TCH], F32))
            mrrow = e(nc.sbuf_tensor("mrrow", [1, TCH], F32))
            Rb = e(nc.sbuf_tensor("Rb", [128, TCH], BF16))
            Mb = e(nc.sbuf_tensor("Mb", [128, TCH], BF16))
            ps_r0 = e(nc.psum_tensor("ps_r0", [1, TCH], F32))
            ps_r1 = e(nc.psum_tensor("ps_r1", [1, TCH], F32))
            ps_A = e(nc.psum_tensor("ps_A", [128, TCH], F32))
            ps_B = e(nc.psum_tensor("ps_B", [128, TCH], F32))

            p = ctx_prog(nc)
            for i in range(8):
                w_x_all = p.op("y", (lambda i: lambda en: en.dma_start(
                    x[i][...], x_d[128 * i:128 * (i + 1), :]))(i), dma=True, stream="x")
            w_x = [w_x_all] * 8
            for (sb_, dr_) in [(lng, lng_d), (lnb, lnb_d), (b1a, b1a_d), (b1g, b1g_d)]:
                w_sm = p.op("y", (lambda sb_, dr_: lambda en: en.dma_start(
                    sb_[...], dr_[...]))(sb_, dr_), dma=True, stream="sm")
            w_lng = w_lnb = w_b1a = w_b1g = w_sm
            w_b2 = p.op("g", lambda en: en.dma_start(b2r[...], b2_d[...]), dma=True, stream="b2")
            c1 = p.op("v", lambda en: en.memset(onesr[...], 1.0))
            c2 = p.op("v", lambda en: en.memset(ones2[...], 1.0))
            c3 = p.op("v", lambda en: en.memset(onesf[...], 1.0))

            pv_var = None
            pv_psA_readers = []
            pv_h_last = None
            pv_w2_last = None

            for ch in range(nchunk):
                if ch > 0:
                    p.phase_barrier()
                c0 = ch * TCH

                def xsh_ap(i):
                    off = 1 if i < 4 else 0
                    return x[i][:, c0 + off: c0 + off + TCH]

                def xres_ap(i):
                    return x[i][:, c0 + 1:c0 + 1 + TCH]

                tprev_mm = None
                tc_list = []
                for i in range(8):
                    b = i % 2
                    wl = [w_x[i], c1, c2]
                    if i >= 2:
                        wl.append(tprev_mm)
                    tc = p.op("v", (lambda xv, b: lambda en: en.tensor_copy(
                        xcast[b][...], xv))(xsh_ap(i), b), wait=wl)
                    tsq = p.op("s", (lambda b: lambda en: en.activation(
                        xsq[b][...], xcast[b][...], AF.Square))(b), wait=[tc])
                    mmw = [tc]
                    if tprev_mm:
                        mmw.append(tprev_mm)
                    if i == 0 and pv_var is not None:
                        mmw.append(pv_var)
                    for q in range(TCH // 512):
                        t1 = p.op("t", (lambda b, i, q: lambda en: en.matmul(
                            ps_r0[0:1, q * 512:(q + 1) * 512], ones2[:, 0:1],
                            xcast[b][:, q * 512:(q + 1) * 512],
                            start=(i == 0), stop=(i == 7)))(b, i, q), wait=mmw)
                        mmw = [t1]
                    for q in range(TCH // 512):
                        t2 = p.op("t", (lambda b, i, q: lambda en: en.matmul(
                            ps_r1[0:1, q * 512:(q + 1) * 512], ones2[:, 1:2],
                            xsq[b][:, q * 512:(q + 1) * 512],
                            start=(i == 0), stop=(i == 7)))(b, i, q), wait=[tsq, t1, t2] if i else [tsq, t1])
                    tc_list.append(tc)
                    tprev_mm = t2
                t_mu = p.op("v", lambda en: en.tensor_scalar_mul(
                    murow[...], ps_r0[0:1, :], 1.0 / D), wait=[tprev_mm])
                t_m2 = p.op("v", lambda en: en.tensor_mul(
                    musq[...], murow[...], murow[...]), wait=[t_mu])
                t_var = p.op("v", lambda en: en.scalar_tensor_tensor(
                    varrow[...], ps_r1[0:1, :], 1.0 / D, musq[...],
                    ALU.mult, ALU.subtract), wait=[t_m2, tprev_mm])
                pv_var = t_var
                t_vpe = p.op("v", lambda en: en.tensor_scalar_add(
                    varrow[...], varrow[...], eps), wait=[t_var])
                t_lnv = p.op("s", lambda en: en.activation(
                    lnvrow[...], varrow[...], AF.Ln), wait=[t_vpe])
                t_rstd = p.op("s", lambda en: en.activation(
                    rstdrow[...], lnvrow[...], AF.Exp, scale=-0.5), wait=[t_lnv])
                t_mr = p.op("v", lambda en: en.tensor_mul(
                    mrrow[...], murow[...], rstdrow[...]), wait=[t_rstd, t_mu])
                tRb, tMb = [], []
                for q in range(TCH // 512):
                    bw = [t_rstd, c1] + pv_psA_readers
                    tb = p.op("t", (lambda q: lambda en: en.matmul(
                        ps_A[:, q * 512:(q + 1) * 512], onesf[0:1, :],
                        rstdrow[0:1, q * 512:(q + 1) * 512], start=True, stop=True))(q),
                        wait=bw + [c3])
                    tRb.append(p.op("s", (lambda q: lambda en: en.activation(
                        Rb[:, q * 512:(q + 1) * 512], ps_A[:, q * 512:(q + 1) * 512],
                        AF.Identity))(q), wait=[tb]))
                for q in range(TCH // 512):
                    tb = p.op("t", (lambda q: lambda en: en.matmul(
                        ps_A[:, q * 512:(q + 1) * 512], onesf[0:1, :],
                        mrrow[0:1, q * 512:(q + 1) * 512], start=True, stop=True))(q),
                        wait=[t_mr, tRb[q], c3])
                    tMb.append(p.op("s", (lambda q: lambda en: en.activation(
                        Mb[:, q * 512:(q + 1) * 512], ps_A[:, q * 512:(q + 1) * 512],
                        AF.Identity, scale=-1.0))(q), wait=[tb]))
                t_xn = []
                for i in range(8):
                    tm = p.op("v", (lambda xv, i: lambda en: en.tensor_mul(
                        xn[i][...], xv, Rb[...]))(xsh_ap(i), i),
                        wait=tRb + [w_x[i]])
                    ta = p.op("v", (lambda i: lambda en: en.tensor_add(
                        xn[i][...], xn[i][...], Mb[...]))(i), wait=[tm] + tMb)
                    tg = p.op("s", (lambda i: lambda en: en.activation(
                        xn[i][...], xn[i][...], AF.Identity,
                        scale=lng[:, i:i + 1], bias=lnb[:, i:i + 1]))(i),
                        wait=[ta, w_lng, w_lnb])
                    t_xn.append(tg)

                p.phase_barrier()
                t_w1 = {}
                t_h = {}
                for m in range(NMB):
                    b = m % 2
                    mw = mblk_sz(m)
                    wwait = [t_h[m - 2]] if m >= 2 else ([pv_w2_last] if pv_w2_last else [])
                    t_w1a = p.op("g", (lambda m, b, mw: lambda en: en.dma_start(
                        w1buf[b][:, 0:8 * mw].rearrange("p (a w) -> p a w", a=8),
                        w1a_d[:, 128 * m:128 * m + mw].rearrange("(a p) w -> p a w", p=128)))(m, b, mw),
                        wait=wwait, dma=True, stream=f"w1p{b}")
                    t_w1g = p.op("g", (lambda m, b, mw: lambda en: en.dma_start(
                        w1buf[b][:, 8 * mw:16 * mw].rearrange("p (a w) -> p a w", a=8),
                        w1g_d[:, 128 * m:128 * m + mw].rearrange("(a p) w -> p a w", p=128)))(m, b, mw),
                        wait=wwait, dma=True, stream=f"w1p{b}")
                    psw = [t_h[m - 1]] if m >= 1 else ([pv_w2_last] if pv_w2_last else [])
                    last = None
                    for q in range(TCH // 512):
                        for k in range(8):
                            wl = [t_w1g, t_xn[k]] + psw + ([last] if last else []) + \
                                (tMb if m == 0 else [])
                            last = p.op("t", (lambda b, mw, q, k: lambda en: en.matmul(
                                ps_A[0:mw, q * 512:(q + 1) * 512],
                                w1buf[b][:, k * mw:(k + 1) * mw],
                                xn[k][:, q * 512:(q + 1) * 512],
                                start=(k == 0), stop=(k == 7)))(b, mw, q, k), wait=wl)
                    t_a = last
                    for q in range(TCH // 512):
                        for k in range(8):
                            wl = [t_w1g, t_xn[k]] + psw + [last]
                            last = p.op("t", (lambda b, mw, q, k: lambda en: en.matmul(
                                ps_B[0:mw, q * 512:(q + 1) * 512],
                                w1buf[b][:, (8 + k) * mw:(9 + k) * mw],
                                xn[k][:, q * 512:(q + 1) * 512],
                                start=(k == 0), stop=(k == 7)))(b, mw, q, k), wait=wl)
                    t_g = last
                    tge = p.op("s", (lambda m, mw: lambda en: en.activation(
                        gelu_s[0:mw, :], ps_B[0:mw, :], AF.Gelu,
                        bias=b1g[0:mw, m:m + 1]))(m, mw), wait=[t_g, w_b1g])
                    tas = p.op("s", (lambda m, mw: lambda en: en.activation(
                        a_s[0:mw, :], ps_A[0:mw, :], AF.Identity,
                        bias=b1a[0:mw, m:m + 1]))(m, mw), wait=[t_a, w_b1a, tge])
                    hw = [tge, tas]
                    if pv_w2_last is not None:
                        hw.append(pv_w2_last)
                    t_h[m] = p.op("v", (lambda m, mw: lambda en: en.tensor_mul(
                        hbuf[0:mw, m * TCH:(m + 1) * TCH], a_s[0:mw, :],
                        gelu_s[0:mw, :]))(m, mw), wait=hw)
                pv_h_last = t_h[NMB - 1]

                p.phase_barrier()
                t_out_dma = {}
                tadds = []
                for n in range(8):
                    b = n % 2
                    wwait = [t_out_dma[n - 2]] if n >= 2 else []
                    t_w2a = p.op("g", (lambda n, b: lambda en: en.dma_start(
                        w2buf[b][:, 0:21 * 128].rearrange("p (a w) -> p a w", a=21),
                        w2_d[0:2688, 128 * n:128 * (n + 1)].rearrange("(a p) w -> p a w", p=128)))(n, b),
                        wait=wwait, dma=True, stream=f"w2p{b}")
                    t_w2 = p.op("g", (lambda n, b: lambda en: en.dma_start(
                        w2buf[b][0:42, 21 * 128:22 * 128],
                        w2_d[2688:2730, 128 * n:128 * (n + 1)]))(n, b),
                        wait=wwait, dma=True, stream=f"w2p{b}")
                    last = None
                    for q in range(TCH // 512):
                        for m in range(NMB):
                            mw = mblk_sz(m)
                            wl = [t_w2, t_w2a, t_h[m]] + ([last] if last else [])
                            if n == 0 and q == 0 and m == 0:
                                wl.append(t_h[NMB - 1])
                            if n >= 1 and q == 0 and m == 0:
                                wl.append(tadds[n - 1])
                            last = p.op("t", (lambda b, q, m, mw: lambda en: en.matmul(
                                ps_A[:, q * 512:(q + 1) * 512],
                                w2buf[b][0:mw, m * 128:(m + 1) * 128],
                                hbuf[0:mw, m * TCH + q * 512:m * TCH + (q + 1) * 512],
                                start=(m == 0), stop=False))(b, q, m, mw), wait=wl)
                        last = p.op("t", (lambda n, q: lambda en: en.matmul(
                            ps_A[:, q * 512:(q + 1) * 512],
                            b2r[0:1, 128 * n:128 * (n + 1)],
                            onesr[0:1, q * 512:(q + 1) * 512],
                            start=False, stop=True))(n, q), wait=[last, w_b2, c1])
                    pv_w2_last = last
                    tadd = p.op("v", (lambda xv, b: lambda en: en.tensor_add(
                        outbuf[b][...], ps_A[...], xv))(xres_ap(n), b),
                        wait=[last] + wwait)
                    tadds.append(tadd)
                    t_out_dma[n] = p.op("y", (lambda n, b, c0: lambda en: en.dma_start(
                        y_d[128 * n:128 * (n + 1), c0:c0 + TCH], outbuf[b][...]))(n, b, c0),
                        wait=[tadd], dma=True, stream=f"outp{b}")
                pv_psA_readers = tadds[-2:]

            p.op("v", lambda en: en.engine_nop(), wait=p.all_done())
            ctx_emit(p, st)
        return nc


    def ffn_inputs(x_own, x_halo_col, W1, b1, W2, b2, g, bln):
        nt = x_own.shape[1]
        xin = np.concatenate([x_halo_col.reshape(-1, 1), x_own], axis=1).astype(np.float32)
        b1a = np.zeros((128, NMB), np.float32)
        b1g = np.zeros((128, NMB), np.float32)
        for m in range(NMB):
            mw = mblk_sz(m)
            b1a[:mw, m] = b1[128 * m:128 * m + mw]
            b1g[:mw, m] = b1[DI + 128 * m:DI + 128 * m + mw]
        return {
            "x": np.ascontiguousarray(xin),
            "W1a": np.ascontiguousarray(W1[:, :DI].astype(np.float32)),
            "W1g": np.ascontiguousarray(W1[:, DI:].astype(np.float32)),
            "b1a": b1a, "b1g": b1g,
            "W2": np.ascontiguousarray(W2.astype(np.float32)),
            "b2r": np.ascontiguousarray(b2.reshape(1, -1).astype(np.float32)),
            "lng": np.ascontiguousarray(g.reshape(8, 128).T.astype(np.float32)),
            "lnb": np.ascontiguousarray(bln.reshape(8, 128).T.astype(np.float32)),
        }
    return build_ffn_ffn, ffn_inputs

build_ffn_ffn, ffn_inputs = _ns_ffn()

def _ns_lru():
    D = 1024
    NST = 256
    NT = 2048
    NCK = 16
    TCH = 512


    def _ln_alloc(nc, e):
        t = {}
        t["onesc"] = e(nc.sbuf_tensor("l_onesc", [128, 2], BF16))
        t["onesf"] = e(nc.sbuf_tensor("l_onesf", [1, 128], F32))
        t["xcc"] = [e(nc.sbuf_tensor(f"l_xcc{k}", [128, TCH], BF16)) for k in range(2)]
        t["xsq"] = [e(nc.sbuf_tensor(f"l_xsq{k}", [128, TCH], BF16)) for k in range(2)]
        for nm in ["mu", "musq", "var", "rstd", "mr"]:
            t[nm] = e(nc.sbuf_tensor(f"l_{nm}", [1, TCH], F32))
        t["Rb"] = e(nc.sbuf_tensor("l_Rb", [128, TCH], BF16))
        t["Mb"] = e(nc.sbuf_tensor("l_Mb", [128, TCH], BF16))
        t["ps_r0"] = e(nc.psum_tensor("l_psr0", [1, TCH], F32))
        t["ps_r1"] = e(nc.psum_tensor("l_psr1", [1, TCH], F32))
        t["ps_bc"] = e(nc.psum_tensor("l_psbc", [128, TCH], F32))
        return t


    def _ln_init(p, t):
        c1 = p.op("v", lambda en: en.memset(t["onesc"][...], 1.0))
        c2 = p.op("v", lambda en: en.memset(t["onesf"][...], 1.0))
        return [c1, c2]


    def _ln_chunk(p, G, gw, t, cons, xap, h3ap):
        tprev = None
        for j in range(8):
            sb = j % 2
            tc = p.op("v", (lambda j, sb: lambda en: en.tensor_copy(
                t["xcc"][sb][...], xap(j)))(j, sb), wait=cons + gw(f"lxcc{sb}"))
            tq = p.op("s", (lambda sb: lambda en: en.activation(
                t["xsq"][sb][...], t["xcc"][sb][...], AF.Square))(sb),
                wait=[tc] + gw(f"lxsq{sb}"))
            wl = [tc] + cons + ([tprev] if tprev else [])
            if j == 0:
                wl += gw("lpsr")
            t1 = p.op("t", (lambda j, sb: lambda en: en.matmul(
                t["ps_r0"][0:1, :], t["onesc"][:, 0:1], t["xcc"][sb][...],
                start=(j == 0), stop=(j == 7)))(j, sb), wait=wl)
            t2 = p.op("t", (lambda j, sb: lambda en: en.matmul(
                t["ps_r1"][0:1, :], t["onesc"][:, 1:2], t["xsq"][sb][...],
                start=(j == 0), stop=(j == 7)))(j, sb), wait=[tq, t1])
            G[f"lxcc{sb}"] = [t1, tq]
            G[f"lxsq{sb}"] = [t2]
            tprev = t2
        t_mu = p.op("v", lambda en: en.tensor_scalar_mul(
            t["mu"][...], t["ps_r0"][0:1, :], 1.0 / D), wait=[tprev])
        t_m2 = p.op("v", lambda en: en.tensor_mul(
            t["musq"][...], t["mu"][...], t["mu"][...]), wait=[t_mu])
        t_var = p.op("v", lambda en: en.scalar_tensor_tensor(
            t["var"][...], t["ps_r1"][0:1, :], 1.0 / D, t["musq"][...],
            ALU.mult, ALU.subtract), wait=[t_m2, tprev])
        G["lpsr"] = [t_var]
        t_ve = p.op("v", lambda en: en.tensor_scalar_add(
            t["var"][...], t["var"][...], 1e-5), wait=[t_var])
        t_ln = p.op("s", lambda en: en.activation(
            t["rstd"][...], t["var"][...], AF.Ln), wait=[t_ve])
        t_rs = p.op("s", lambda en: en.activation(
            t["rstd"][...], t["rstd"][...], AF.Exp, scale=-0.5), wait=[t_ln])
        t_mr = p.op("v", lambda en: en.tensor_mul(
            t["mr"][...], t["mu"][...], t["rstd"][...]), wait=[t_rs, t_mu])
        tb = p.op("t", lambda en: en.matmul(
            t["ps_bc"][:, 0:512], t["onesf"][0:1, :], t["rstd"][0:1, 0:512],
            start=True, stop=True), wait=[t_rs] + cons + gw("lpsbc"))
        tR = p.op("s", lambda en: en.activation(
            t["Rb"][:, 0:512], t["ps_bc"][:, 0:512], AF.Identity),
            wait=[tb] + gw("lRb"))
        tb2 = p.op("t", lambda en: en.matmul(
            t["ps_bc"][:, 0:512], t["onesf"][0:1, :], t["mr"][0:1, 0:512],
            start=True, stop=True), wait=[t_mr, tR])
        tM = p.op("s", lambda en: en.activation(
            t["Mb"][:, 0:512], t["ps_bc"][:, 0:512], AF.Identity, scale=-1.0),
            wait=[tb2] + gw("lMb"))
        G["lpsbc"] = [tM]
        t_h = []
        for j in range(8):
            tm = p.op("v", (lambda j: lambda en: en.tensor_mul(
                h3ap(j), xap(j), t["Rb"][:, 0:512]))(j), wait=[tR] + gw("lh3"))
            ta = p.op("v", (lambda j: lambda en: en.tensor_add(
                h3ap(j), h3ap(j), t["Mb"][:, 0:512]))(j), wait=[tm, tM])
            t_h.append(ta)
        G["lRb"] = t_h
        G["lMb"] = t_h
        G["lh3"] = []
        return t_h


    def _lam_cols(p, G, gw, nc, e, nu2, th2, w_in):
        r = {}
        for nm in ["nuv", "th", "eneg", "s2", "sth", "cth", "lamR", "lamI",
                   "gam", "tmp"]:
            r[nm] = e(nc.sbuf_tensor(f"rc_{nm}", [128, 2], F32))
        t1 = p.op("s", lambda en: en.activation(r["nuv"][...], nu2[...], AF.Exp),
                  wait=w_in)
        t2 = p.op("s", lambda en: en.activation(r["th"][...], th2[...], AF.Exp),
                  wait=w_in)
        t3 = p.op("v", lambda en: en.tensor_single_scalar(
            r["tmp"][...], r["th"][...], float(np.pi), ALU.is_gt), wait=[t2])
        t4 = p.op("v", lambda en: en.scalar_tensor_tensor(
            r["th"][...], r["tmp"][...], float(-2.0 * np.pi), r["th"][...],
            ALU.mult, ALU.add), wait=[t3])
        t5 = p.op("s", lambda en: en.activation(r["sth"][...], r["th"][...], AF.Sin),
                  wait=[t4])
        t6 = p.op("s", lambda en: en.activation(r["s2"][...], r["th"][...], AF.Sin,
                                                scale=0.5), wait=[t4, t5])
        t7 = p.op("v", lambda en: en.tensor_mul(r["s2"][...], r["s2"][...],
                                                r["s2"][...]), wait=[t6])
        t8 = p.op("v", lambda en: en.tensor_scalar(
            r["cth"][...], r["s2"][...], -2.0, 1.0, ALU.mult, ALU.add), wait=[t7])
        t9 = p.op("s", lambda en: en.activation(r["eneg"][...], r["nuv"][...], AF.Exp,
                                                scale=-1.0), wait=[t1, t6])
        ta = p.op("v", lambda en: en.tensor_mul(r["lamR"][...], r["eneg"][...],
                                                r["cth"][...]), wait=[t9, t8])
        tb = p.op("v", lambda en: en.tensor_mul(r["lamI"][...], r["eneg"][...],
                                                r["sth"][...]), wait=[t9, t5])
        tc = p.op("v", lambda en: en.tensor_mul(r["tmp"][...], r["eneg"][...],
                                                r["eneg"][...]), wait=[t9])
        td = p.op("v", lambda en: en.tensor_scalar(
            r["tmp"][...], r["tmp"][...], -1.0, 1.0, ALU.mult, ALU.add), wait=[tc])
        te = p.op("s", lambda en: en.activation(r["gam"][...], r["tmp"][...], AF.Sqrt),
                  wait=[td])
        G["lam"] = [ta, tb, te]
        return r


    def _build_pow(p, G, gw, nc, e, r, sign, width, name):
        TR = e(nc.sbuf_tensor(f"{name}R", [128, 2 * width], F32))
        TI = e(nc.sbuf_tensor(f"{name}I", [128, 2 * width], F32))
        t1r = e(nc.sbuf_tensor(f"{name}t1", [128, width], F32))
        t2r = e(nc.sbuf_tensor(f"{name}t2", [128, width], F32))
        outs = []
        for tile in range(2):
            off = tile * width
            lamR = r["lamR"][:, tile:tile + 1]
            lamI = r["lamI"][:, tile:tile + 1]
            last = [p.op("v", (lambda off: lambda en: en.memset(
                TR[:, off:off + 1], 1.0))(off), wait=gw("lam")),
                p.op("v", (lambda off: lambda en: en.memset(
                    TI[:, off:off + 1], 0.0))(off), wait=gw("lam"))]
            if sign > 0:
                last.append(p.op("v", (lambda off, lamR: lambda en: en.tensor_copy(
                    TR[:, off + 1:off + 2], lamR))(off, lamR), wait=gw("lam")))
                last.append(p.op("v", (lambda off, lamI: lambda en: en.tensor_copy(
                    TI[:, off + 1:off + 2], lamI))(off, lamI), wait=gw("lam")))
            else:
                a1 = p.op("v", (lambda lamR: lambda en: en.tensor_mul(
                    t1r[:, 0:1], lamR, lamR))(lamR), wait=gw("lam") + gw(f"{name}tmp"))
                a2 = p.op("v", (lambda lamI: lambda en: en.tensor_mul(
                    t2r[:, 0:1], lamI, lamI))(lamI), wait=gw("lam") + gw(f"{name}tmp"))
                a3 = p.op("v", lambda en: en.tensor_add(
                    t1r[:, 0:1], t1r[:, 0:1], t2r[:, 0:1]), wait=[a1, a2])
                a4 = p.op("v", lambda en: en.reciprocal(
                    t1r[:, 1:2], t1r[:, 0:1]), wait=[a3])
                a5 = p.op("v", (lambda off, lamR: lambda en: en.tensor_mul(
                    TR[:, off + 1:off + 2], lamR, t1r[:, 1:2]))(off, lamR), wait=[a4])
                a6 = p.op("v", (lambda off, lamI: lambda en: en.tensor_mul(
                    TI[:, off + 1:off + 2], lamI, t1r[:, 1:2]))(off, lamI), wait=[a4])
                a7 = p.op("v", (lambda off: lambda en: en.tensor_scalar_mul(
                    TI[:, off + 1:off + 2], TI[:, off + 1:off + 2], -1.0))(off),
                    wait=[a6])
                last += [a5, a7]
            filled = 2
            while filled < width:
                s = min(filled - 1, width - filled)
                cR = TR[:, off + filled - 1:off + filled]
                cI = TI[:, off + filled - 1:off + filled]
                m1 = p.op("v", (lambda off, s, cR: lambda en: en.tensor_scalar_mul(
                    t1r[:, 0:s], TR[:, off + 1:off + 1 + s], cR))(off, s, cR), wait=last)
                m2 = p.op("v", (lambda off, s, cI: lambda en: en.tensor_scalar_mul(
                    t2r[:, 0:s], TI[:, off + 1:off + 1 + s], cI))(off, s, cI), wait=last)
                m3 = p.op("v", (lambda off, s, filled: lambda en: en.tensor_sub(
                    TR[:, off + filled:off + filled + s], t1r[:, 0:s],
                    t2r[:, 0:s]))(off, s, filled), wait=[m1, m2])
                m4 = p.op("v", (lambda off, s, cI: lambda en: en.tensor_scalar_mul(
                    t1r[:, 0:s], TR[:, off + 1:off + 1 + s], cI))(off, s, cI), wait=[m3])
                m5 = p.op("v", (lambda off, s, cR: lambda en: en.tensor_scalar_mul(
                    t2r[:, 0:s], TI[:, off + 1:off + 1 + s], cR))(off, s, cR), wait=[m4])
                m6 = p.op("v", (lambda off, s, filled: lambda en: en.tensor_add(
                    TI[:, off + filled:off + filled + s], t1r[:, 0:s],
                    t2r[:, 0:s]))(off, s, filled), wait=[m5])
                last = [m3, m6]
                filled += s
            outs += last
            G[f"{name}tmp"] = last
        G[name] = outs
        return TR, TI


    def build_lru1():
        nc = ctx_nc()
        x_d = nc.dram_tensor("x3", [D, NT], F32, kind="ExternalInput")
        nu_d = nc.dram_tensor("nu2", [128, 2], F32, kind="ExternalInput")
        th_d = nc.dram_tensor("th2", [128, 2], F32, kind="ExternalInput")
        br_d = nc.dram_tensor("BreT", [D, NST], F32, kind="ExternalInput")
        bi_d = nc.dram_tensor("BimT", [D, NST], F32, kind="ExternalInput")
        acc_d = nc.dram_tensor("acc", [NT, 512], F32, kind="ExternalOutput")

        triu_c = nc.inline_tensor(
            np.triu(np.ones((128, 128), np.float32)).astype(np.float32), name="triu")
        id_c = nc.inline_tensor(np.eye(128, dtype=np.float32), name="idf32")

        with ExitStack() as est:
            e = est.enter_context
            nu2 = e(nc.sbuf_tensor("nu2s", [128, 2], F32))
            th2 = e(nc.sbuf_tensor("th2s", [128, 2], F32))
            brb = e(nc.sbuf_tensor("brb", [128, 8 * NST], BF16))
            bib = e(nc.sbuf_tensor("bib", [128, 8 * NST], BF16))
            triu = e(nc.sbuf_tensor("trius", [128, 128], F32))
            idf = e(nc.sbuf_tensor("idfs", [128, 128], F32))
            xch = [e(nc.sbuf_tensor(f"xch{k}", [128, 8 * TCH], F32)) for k in range(2)]
            h3 = e(nc.sbuf_tensor("h3c", [128, 8 * TCH], BF16))
            LnT = [e(nc.sbuf_tensor(f"LnT{ri}", [128, NST], F32)) for ri in range(2)]
            scl = e(nc.sbuf_tensor("scl", [128, 512], F32))
            accs = [e(nc.sbuf_tensor(f"accs{k}", [128, 512], F32)) for k in range(2)]
            lnt = _ln_alloc(nc, e)
            ps_bu = e(nc.psum_tensor("ps_bu", [128, 512], F32))
            ps_ac = e(nc.psum_tensor("ps_ac", [128, 512], F32))
            ps_tr = e(nc.psum_tensor("ps_tr", [128, 256], F32))

            p = ctx_prog(nc)
            G = {}

            def gw(n):
                return list(G.get(n, []))

            w_in = []
            for wi, (sb_, dr_) in enumerate([(nu2, nu_d), (th2, th_d),
                                             (triu, triu_c), (idf, id_c)]):
                w_in.append(p.op("y", (lambda sb_, dr_: lambda en: en.dma_start(
                    sb_[...], dr_[...]))(sb_, dr_), dma=True, stream=f"sm{wi}"))
            w_b = p.op("g", lambda en: en.dma_start(
                brb[...].rearrange("p (a w) -> p a w", a=8),
                br_d[...].rearrange("(a p) w -> p a w", p=128)), dma=True, stream="wb")
            w_b = p.op("g", lambda en: en.dma_start(
                bib[...].rearrange("p (a w) -> p a w", a=8),
                bi_d[...].rearrange("(a p) w -> p a w", p=128)), dma=True, stream="wb")
            cons = _ln_init(p, lnt)
            G["in_rows"] = [w_in[0], w_in[1]]
            r = _lam_cols(p, G, gw, nc, e, nu2, th2, [w_in[0], w_in[1]])
            TnR, TnI = _build_pow(p, G, gw, nc, e, r, -1, 128, "Ln")
            ttoks = []
            for ri, (T, LT) in enumerate([(TnR, LnT[0]), (TnI, LnT[1])]):
                for tile in range(2):
                    tt = p.op("t", (lambda T, tile: lambda en: en.transpose(
                        ps_tr[:, 0:128], T[:, tile * 128:(tile + 1) * 128],
                        idf[...]))(T, tile), wait=G["Ln"] + [w_in[3]] + gw("ps_tr"))
                    tv = p.op("s", (lambda LT, tile: lambda en: en.activation(
                        LT[:, tile * 128:(tile + 1) * 128], ps_tr[:, 0:128],
                        AF.Identity))(LT, tile), wait=[tt])
                    G["ps_tr"] = [tv]
                    ttoks.append(tv)
            G["LnT"] = ttoks

            for ch in range(4):
                b = ch % 2
                t_x = p.op("y", (lambda ch, b: lambda en: en.dma_start(
                    xch[b][...].rearrange("p (a w) -> p a w", a=8),
                    x_d[:, TCH * ch:TCH * (ch + 1)].rearrange("(a p) w -> p a w", p=128)))(ch, b),
                    wait=gw(f"xch{b}"), dma=True, stream=f"x{b}")
                t_h = _ln_chunk(
                    p, G, gw, lnt, cons + [t_x],
                    (lambda b: lambda j: xch[b][:, j * TCH:(j + 1) * TCH])(b),
                    lambda j: h3[:, j * TCH:(j + 1) * TCH])
                G[f"xch{b}"] = t_h
                readers = []
                for tb in range(4):
                    c_idx = 4 * ch + tb
                    last = None
                    for ri, wsb in enumerate([brb, bib]):
                        for kk in range(8):
                            wl = [t_h[kk], w_b] + ([last] if last else [])
                            if ri == 0 and kk == 0:
                                wl += gw("ps_bu")
                            last = p.op("t", (lambda ri, kk, tb, wsb: lambda en: en.matmul(
                                ps_bu[:, ri * 256:(ri + 1) * 256],
                                h3[:, kk * TCH + 128 * tb:kk * TCH + 128 * (tb + 1)],
                                wsb[:, kk * NST:(kk + 1) * NST],
                                start=(kk == 0), stop=(kk == 7)))(ri, kk, tb, wsb), wait=wl)
                    m1 = p.op("v", lambda en: en.tensor_mul(
                        scl[:, 0:256], ps_bu[:, 0:256], LnT[0][...]),
                        wait=[last] + G["LnT"] + gw("scl"))
                    m2 = p.op("v", lambda en: en.tensor_mul(
                        scl[:, 256:512], ps_bu[:, 256:512], LnT[1][...]), wait=[last, m1])
                    m3 = p.op("v", lambda en: en.tensor_sub(
                        scl[:, 0:256], scl[:, 0:256], scl[:, 256:512]), wait=[m2])
                    m4 = p.op("v", lambda en: en.tensor_mul(
                        scl[:, 256:512], ps_bu[:, 0:256], LnT[1][...]), wait=[m3, last])
                    m5 = p.op("v", lambda en: en.scalar_tensor_tensor(
                        scl[:, 256:512], ps_bu[:, 256:512], 1.0, scl[:, 256:512],
                        ALU.mult, ALU.bypass), wait=[m4]) if False else None
                    m5 = p.op("v", (lambda b: lambda en: en.tensor_mul(
                        accs[b][:, 0:256], ps_bu[:, 256:512], LnT[0][...]))(b),
                        wait=[m4] + gw(f"accs{b}"))
                    m6 = p.op("v", (lambda b: lambda en: en.tensor_add(
                        scl[:, 256:512], scl[:, 256:512], accs[b][:, 0:256]))(b),
                        wait=[m5])
                    G["ps_bu"] = [m4, m6, m1, m2]
                    tcs = p.op("t", lambda en: en.matmul(
                        ps_ac[:, 0:512], triu[...], scl[:, 0:512],
                        start=True, stop=True), wait=[m3, m6, w_in[2]] + gw("ps_ac"))
                    G["scl"] = [tcs]
                    tev = p.op("s", (lambda b: lambda en: en.activation(
                        accs[b][:, 0:512], ps_ac[:, 0:512], AF.Identity))(b),
                        wait=[tcs, m5])
                    G["ps_ac"] = [tev]
                    tst = p.op("y", (lambda c_idx, b: lambda en: en.dma_start(
                        acc_d[128 * c_idx:128 * (c_idx + 1), :], accs[b][...]))(c_idx, b),
                        wait=[tev], dma=True, stream=f"st{b}")
                    G[f"accs{b}"] = [tst]
                    readers.append(tst)
                G["lh3"] = readers + [G["ps_bu"][0]]

            p.op("v", lambda en: en.engine_nop(), wait=p.all_done())
            ctx_emit(p, est)
        return nc


    def build_lru2a():
        nc = ctx_nc()
        x_d = nc.dram_tensor("x3", [D, NT], F32, kind="ExternalInput")
        acc_d = nc.dram_tensor("acc", [NT, 512], F32, kind="ExternalInput")
        alr_d = nc.dram_tensor("aclTR", [128, 2 * 64], F32, kind="ExternalInput")
        ali_d = nc.dram_tensor("aclTI", [128, 2 * 64], F32, kind="ExternalInput")
        sel_d = nc.dram_tensor("Ssel", [64, 16], F32, kind="ExternalInput")
        nu_d = nc.dram_tensor("nu2", [128, 2], F32, kind="ExternalInput")
        th_d = nc.dram_tensor("th2", [128, 2], F32, kind="ExternalInput")
        cr_d = nc.dram_tensor("CreT", [NST, D], F32, kind="ExternalInput")
        ci_d = nc.dram_tensor("CimT", [NST, D], F32, kind="ExternalInput")
        dm_d = nc.dram_tensor("Dmat", [D, D], F32, kind="ExternalInput")
        y_d = nc.dram_tensor("yt", [D, NT], F32, kind="ExternalOutput")
        h3_d = nc.dram_tensor("h3o", [D, NT], F32, kind="ExternalOutput")

        id_c = nc.inline_tensor(np.eye(128, dtype=np.float32), name="idf32")
        idb_c = nc.inline_tensor(np.eye(128, dtype=ml_dtypes.bfloat16), name="idb16")
        ec_np = np.zeros((16, 16 * 128), np.float32)
        for c in range(16):
            ec_np[c, 128 * c:128 * (c + 1)] = 1.0
        ec_c = nc.inline_tensor(ec_np, name="ec16")

        with ExitStack() as est:
            e = est.enter_context
            nu2 = e(nc.sbuf_tensor("nu2s", [128, 2], F32))
            th2 = e(nc.sbuf_tensor("th2s", [128, 2], F32))
            alr = e(nc.sbuf_tensor("alrs", [128, 2 * 64], F32))
            ali = e(nc.sbuf_tensor("alis", [128, 2 * 64], F32))
            sel = e(nc.sbuf_tensor("sels", [64, 16], F32))
            idf = e(nc.sbuf_tensor("idfs", [128, 128], F32))
            idb = e(nc.sbuf_tensor("idbs", [128, 128], BF16))
            ec16 = e(nc.sbuf_tensor("ec16s", [16, 16 * 128], F32))
            crb = e(nc.sbuf_tensor("crb", [128, 2 * D], BF16))
            cib = e(nc.sbuf_tensor("cib", [128, 2 * D], BF16))
            dmb = e(nc.sbuf_tensor("dmb", [128, 8 * D], BF16))
            xch = [e(nc.sbuf_tensor(f"xch{k}", [128, 8 * TCH], F32)) for k in range(2)]
            h3f = e(nc.sbuf_tensor("h3f", [128, 8 * NT], BF16))
            LpT = [e(nc.sbuf_tensor(f"LpT{ri}", [128, NST], F32)) for ri in range(2)]
            lamA = [e(nc.sbuf_tensor(f"lamA{ri}", [128, 2 * 64], F32)) for ri in range(2)]
            acp = e(nc.sbuf_tensor("acp", [64, 512], F32))
            mycar = e(nc.sbuf_tensor("mycar", [16, 512], F32))
            accb = [e(nc.sbuf_tensor(f"accb{k}", [128, 512], F32)) for k in range(2)]
            accp = e(nc.sbuf_tensor("accp", [128, 512], F32))
            stw = e(nc.sbuf_tensor("stw", [128, 512], BF16))
            stT = [e(nc.sbuf_tensor(f"stT{ri}", [128, 2 * NT], BF16)) for ri in range(2)]
            ytb = [e(nc.sbuf_tensor(f"ytb{k}", [128, 512], F32)) for k in range(2)]
            lnt = _ln_alloc(nc, e)
            ps_a = e(nc.psum_tensor("ps_a2", [128, 512], F32))
            ps_tr = e(nc.psum_tensor("ps_tr2", [128, 256], F32))
            ps_y = e(nc.psum_tensor("ps_y", [128, 512], F32))

            p = ctx_prog(nc)
            G = {}

            def gw(n):
                return list(G.get(n, []))

            w_in = []
            for wi, (sb_, dr_) in enumerate([(nu2, nu_d), (th2, th_d), (alr, alr_d),
                                             (ali, ali_d), (sel, sel_d), (idf, id_c),
                                             (idb, idb_c), (ec16, ec_c)]):
                w_in.append(p.op("y", (lambda sb_, dr_: lambda en: en.dma_start(
                    sb_[...], dr_[...]))(sb_, dr_), dma=True, stream=f"sm{wi}"))
            w_c = []
            for wi, (sb_, dr_) in enumerate([(crb, cr_d), (cib, ci_d)]):
                w_c.append(p.op("g", (lambda sb_, dr_: lambda en: en.dma_start(
                    sb_[...].rearrange("p (a w) -> p a w", a=2),
                    dr_[...].rearrange("(a p) w -> p a w", p=128)))(sb_, dr_),
                    dma=True, stream=f"wc{wi}"))
            w_dm = p.op("g", lambda en: en.dma_start(
                dmb[...].rearrange("p (a w) -> p a w", a=8),
                dm_d[...].rearrange("(a p) w -> p a w", p=128)), dma=True, stream="wdm")
            cons = _ln_init(p, lnt)

            r = _lam_cols(p, G, gw, nc, e, nu2, th2, [w_in[0], w_in[1]])
            TpR, TpI = _build_pow(p, G, gw, nc, e, r, +1, 129, "Lp")
            tcw = []
            for tile in range(2):
                t1 = p.op("v", (lambda tile: lambda en: en.tensor_scalar_mul(
                    crb[:, tile * D:(tile + 1) * D], crb[:, tile * D:(tile + 1) * D],
                    r["gam"][:, tile:tile + 1]))(tile), wait=[w_c[0]] + G["lam"])
                t2 = p.op("v", (lambda tile: lambda en: en.tensor_scalar(
                    cib[:, tile * D:(tile + 1) * D], cib[:, tile * D:(tile + 1) * D],
                    r["gam"][:, tile:tile + 1], -1.0, ALU.mult, ALU.mult))(tile),
                    wait=[w_c[1]] + G["lam"])
                tcw += [t1, t2]
            lcs = []
            lR = e(nc.sbuf_tensor("lR", [128, 2 * 64], F32))
            lI = e(nc.sbuf_tensor("lI", [128, 2 * 64], F32))
            t1r = e(nc.sbuf_tensor("ch_t1", [128, 2 * 64], F32))
            t2r = e(nc.sbuf_tensor("ch_t2", [128, 2 * 64], F32))
            for tile in range(2):
                o = tile * 64
                pb = tile * 129
                l7R = TpR[:, pb + 127:pb + 128]
                l7I = TpI[:, pb + 127:pb + 128]
                wl = [w_in[2], w_in[3]] + G["Lp"]
                m1 = p.op("v", (lambda o, l7R: lambda en: en.tensor_scalar_mul(
                    t1r[:, o:o + 64], alr[:, o:o + 64], l7R))(o, l7R), wait=wl)
                m2 = p.op("v", (lambda o, l7I: lambda en: en.tensor_scalar_mul(
                    t2r[:, o:o + 64], ali[:, o:o + 64], l7I))(o, l7I), wait=wl)
                m3 = p.op("v", (lambda o: lambda en: en.tensor_sub(
                    lR[:, o:o + 64], t1r[:, o:o + 64], t2r[:, o:o + 64]))(o),
                    wait=[m1, m2])
                m4 = p.op("v", (lambda o, l7I: lambda en: en.tensor_scalar_mul(
                    t1r[:, o:o + 64], alr[:, o:o + 64], l7I))(o, l7I), wait=[m3])
                m5 = p.op("v", (lambda o, l7R: lambda en: en.tensor_scalar_mul(
                    t2r[:, o:o + 64], ali[:, o:o + 64], l7R))(o, l7R), wait=[m3])
                m6 = p.op("v", (lambda o: lambda en: en.tensor_add(
                    lI[:, o:o + 64], t1r[:, o:o + 64], t2r[:, o:o + 64]))(o),
                    wait=[m4, m5])
                lcs += [m3, m6]
            AR = e(nc.sbuf_tensor("AR", [128, 2 * 64], F32))
            AI = e(nc.sbuf_tensor("AI", [128, 2 * 64], F32))
            za = p.op("v", lambda en: en.memset(AR[...], 0.0), wait=lcs)
            zb = p.op("v", lambda en: en.memset(AI[...], 0.0), wait=lcs)
            pwR = e(nc.sbuf_tensor("pwR", [128, 8], F32))
            pwI = e(nc.sbuf_tensor("pwI", [128, 8], F32))
            qs_ = []
            for tile in range(2):
                po = tile * 4
                pb = tile * 129
                lWR = TpR[:, pb + 128:pb + 129]
                lWI = TpI[:, pb + 128:pb + 129]
                q0 = p.op("v", (lambda po: lambda en: en.memset(
                    pwR[:, po:po + 1], 1.0))(po))
                q1 = p.op("v", (lambda po: lambda en: en.memset(
                    pwI[:, po:po + 1], 0.0))(po))
                q2 = p.op("v", (lambda po, lWR: lambda en: en.tensor_copy(
                    pwR[:, po + 1:po + 2], lWR))(po, lWR), wait=G["Lp"])
                q3 = p.op("v", (lambda po, lWI: lambda en: en.tensor_copy(
                    pwI[:, po + 1:po + 2], lWI))(po, lWI), wait=G["Lp"])
                qt = [q0, q1, q2, q3]
                for k in [2, 3]:
                    a1 = p.op("v", (lambda po, k, lWR: lambda en: en.tensor_scalar_mul(
                        t1r[:, 0:1], pwR[:, po + k - 1:po + k], lWR))(po, k, lWR),
                        wait=qt + lcs)
                    a2 = p.op("v", (lambda po, k, lWI: lambda en: en.tensor_scalar_mul(
                        t2r[:, 0:1], pwI[:, po + k - 1:po + k], lWI))(po, k, lWI),
                        wait=qt + lcs)
                    a3 = p.op("v", (lambda po, k: lambda en: en.tensor_sub(
                        pwR[:, po + k:po + k + 1], t1r[:, 0:1], t2r[:, 0:1]))(po, k),
                        wait=[a1, a2])
                    a4 = p.op("v", (lambda po, k, lWI: lambda en: en.tensor_scalar_mul(
                        t1r[:, 0:1], pwR[:, po + k - 1:po + k], lWI))(po, k, lWI),
                        wait=[a3])
                    a5 = p.op("v", (lambda po, k, lWR: lambda en: en.tensor_scalar_mul(
                        t2r[:, 0:1], pwI[:, po + k - 1:po + k], lWR))(po, k, lWR),
                        wait=[a3])
                    a6 = p.op("v", (lambda po, k: lambda en: en.tensor_add(
                        pwI[:, po + k:po + k + 1], t1r[:, 0:1], t2r[:, 0:1]))(po, k),
                        wait=[a4, a5])
                    qt = [a3, a6]
                qs_ += qt
            chain_t = [za, zb] + qs_
            for k in range(1, 5):
                for tile in range(2):
                    o = tile * 64
                    po = tile * 4
                    cnt = 64 - k
                    b1 = p.op("v", (lambda o, po, k, cnt: lambda en: en.tensor_scalar_mul(
                        t1r[:, 0:cnt], lR[:, o:o + cnt], pwR[:, po + k - 1:po + k]))(o, po, k, cnt),
                        wait=chain_t)
                    b2 = p.op("v", (lambda o, po, k, cnt: lambda en: en.tensor_scalar_mul(
                        t2r[:, 0:cnt], lI[:, o:o + cnt], pwI[:, po + k - 1:po + k]))(o, po, k, cnt),
                        wait=chain_t)
                    b3 = p.op("v", (lambda cnt: lambda en: en.tensor_sub(
                        t1r[:, 0:cnt], t1r[:, 0:cnt], t2r[:, 0:cnt]))(cnt),
                        wait=[b1, b2])
                    b4 = p.op("v", (lambda o, k, cnt: lambda en: en.tensor_add(
                        AR[:, o + k:o + 64], AR[:, o + k:o + 64], t1r[:, 0:cnt]))(o, k, cnt),
                        wait=[b3])
                    b5 = p.op("v", (lambda o, po, k, cnt: lambda en: en.tensor_scalar_mul(
                        t1r[:, 0:cnt], lR[:, o:o + cnt], pwI[:, po + k - 1:po + k]))(o, po, k, cnt),
                        wait=[b4])
                    b6 = p.op("v", (lambda o, po, k, cnt: lambda en: en.tensor_scalar_mul(
                        t2r[:, 0:cnt], lI[:, o:o + cnt], pwR[:, po + k - 1:po + k]))(o, po, k, cnt),
                        wait=[b4])
                    b7 = p.op("v", (lambda cnt: lambda en: en.tensor_add(
                        t1r[:, 0:cnt], t1r[:, 0:cnt], t2r[:, 0:cnt]))(cnt),
                        wait=[b5, b6])
                    b8 = p.op("v", (lambda o, k, cnt: lambda en: en.tensor_add(
                        AI[:, o + k:o + 64], AI[:, o + k:o + 64], t1r[:, 0:cnt]))(o, k, cnt),
                        wait=[b7])
                    chain_t = [b4, b8]
            la_t = []
            for tile in range(2):
                o = tile * 64
                lamRc = r["lamR"][:, tile:tile + 1]
                lamIc = r["lamI"][:, tile:tile + 1]
                c1_ = p.op("v", (lambda o, lamRc: lambda en: en.tensor_scalar_mul(
                    t1r[:, 0:64], AR[:, o:o + 64], lamRc))(o, lamRc),
                    wait=chain_t + la_t)
                c2_ = p.op("v", (lambda o, lamIc: lambda en: en.tensor_scalar_mul(
                    t2r[:, 0:64], AI[:, o:o + 64], lamIc))(o, lamIc),
                    wait=chain_t + la_t)
                c3_ = p.op("v", (lambda o: lambda en: en.tensor_sub(
                    lamA[0][:, o:o + 64], t1r[:, 0:64], t2r[:, 0:64]))(o), wait=[c1_, c2_])
                c4_ = p.op("v", (lambda o, lamIc: lambda en: en.tensor_scalar_mul(
                    t1r[:, 0:64], AR[:, o:o + 64], lamIc))(o, lamIc), wait=[c3_])
                c5_ = p.op("v", (lambda o, lamRc: lambda en: en.tensor_scalar_mul(
                    t2r[:, 0:64], AI[:, o:o + 64], lamRc))(o, lamRc), wait=[c3_])
                c6_ = p.op("v", (lambda o: lambda en: en.tensor_add(
                    lamA[1][:, o:o + 64], t1r[:, 0:64], t2r[:, 0:64]))(o), wait=[c4_, c5_])
                la_t += [c3_, c6_]
            tr_t = []
            for ri in range(2):
                for tile in range(2):
                    tt = p.op("t", (lambda ri, tile: lambda en: en.transpose(
                        ps_tr[0:64, 0:128],
                        lamA[ri][...].rearrange("p (a w) -> p a w", a=2)[:, tile, :],
                        idf[...]))(ri, tile), wait=la_t + [w_in[5]] + gw("ps_tr"))
                    tv = p.op("s", (lambda ri, tile: lambda en: en.activation(
                        acp[0:64, (2 * ri + tile) * 128:(2 * ri + tile + 1) * 128],
                        ps_tr[0:64, 0:128], AF.Identity))(ri, tile), wait=[tt])
                    G["ps_tr"] = [tv]
                    tr_t.append(tv)
            tsel = p.op("t", lambda en: en.matmul(
                ps_a[0:16, 0:512], sel[...], acp[0:64, 0:512],
                start=True, stop=True), wait=tr_t + [w_in[4]] + gw("ps_a"))
            tmc = p.op("s", lambda en: en.activation(
                mycar[...], ps_a[0:16, 0:512], AF.Identity), wait=[tsel])
            G["ps_a"] = [tmc]
            lp_t = []
            for (TT_, dstl) in [(TpR, LpT[0]), (TpI, LpT[1])]:
                for tile in range(2):
                    tt = p.op("t", (lambda TT_, tile: lambda en: en.transpose(
                        ps_tr[:, 0:128],
                        TT_[:, tile * 129:tile * 129 + 128],
                        idf[...]))(TT_, tile), wait=G["Lp"] + [w_in[5]] + gw("ps_tr"))
                    tv = p.op("s", (lambda dstl, tile: lambda en: en.activation(
                        dstl[:, tile * 128:(tile + 1) * 128], ps_tr[:, 0:128],
                        AF.Identity))(dstl, tile), wait=[tt])
                    G["ps_tr"] = [tv]
                    lp_t.append(tv)

            h3done = []
            for ch in range(4):
                b = ch % 2
                t_x = p.op("y", (lambda ch, b: lambda en: en.dma_start(
                    xch[b][...].rearrange("p (a w) -> p a w", a=8),
                    x_d[:, TCH * ch:TCH * (ch + 1)].rearrange("(a p) w -> p a w", p=128)))(ch, b),
                    wait=gw(f"xch{b}"), dma=True, stream=f"x{b}")
                t_h = _ln_chunk(
                    p, G, gw, lnt, cons + [t_x],
                    (lambda b: lambda j: xch[b][:, j * TCH:(j + 1) * TCH])(b),
                    (lambda ch: lambda j: h3f[:, j * NT + TCH * ch:j * NT + TCH * (ch + 1)])(ch))
                G[f"xch{b}"] = t_h
                G["lh3"] = []
                h3done += t_h
                for j in range(8):
                    p.op("g", (lambda j, ch: lambda en: en.dma_start(
                        h3_d[128 * j:128 * (j + 1), TCH * ch:TCH * (ch + 1)],
                        h3f[:, j * NT + TCH * ch:j * NT + TCH * (ch + 1)]))(j, ch),
                        wait=[t_h[j]], dma=True, stream=f"h3s{ch % 2}")

            p.phase_barrier()
            st_t = []
            for c in range(16):
                b = c % 2
                t_ac = p.op("y", (lambda c, b: lambda en: en.dma_start(
                    accb[b][...], acc_d[128 * c:128 * (c + 1), :]))(c, b),
                    wait=gw(f"accb{b}"), dma=True, stream=f"ac{b}")
                tbc = p.op("t", (lambda c: lambda en: en.matmul(
                    ps_a[:, 0:512], ec16[:, 128 * c:128 * (c + 1)], mycar[...],
                    start=True, stop=True))(c), wait=[tmc, w_in[7]] + gw("ps_a"))
                ta1 = p.op("v", (lambda b: lambda en: en.tensor_add(
                    accp[:, 0:512], accb[b][...], ps_a[:, 0:512]))(b),
                    wait=[t_ac, tbc] + gw("accp"))
                G["ps_a"] = [ta1]
                s1 = p.op("v", (lambda b: lambda en: en.tensor_mul(
                    ytb[b][:, 0:256], accp[:, 0:256], LpT[0][...]))(b),
                    wait=[ta1] + lp_t + gw(f"ytb{b}"))
                s2 = p.op("v", (lambda b: lambda en: en.tensor_mul(
                    ytb[b][:, 256:512], accp[:, 256:512], LpT[1][...]))(b),
                    wait=[ta1] + lp_t)
                s3 = p.op("v", (lambda b: lambda en: en.tensor_sub(
                    ytb[b][:, 0:256], ytb[b][:, 0:256], ytb[b][:, 256:512]))(b),
                    wait=[s2, s1])
                sR = p.op("v", (lambda b: lambda en: en.tensor_copy(
                    stw[:, 0:256], ytb[b][:, 0:256]))(b), wait=[s3] + gw("stw"))
                s4 = p.op("v", (lambda b: lambda en: en.tensor_mul(
                    ytb[b][:, 0:256], accp[:, 0:256], LpT[1][...]))(b), wait=[sR])
                s5 = p.op("v", (lambda b: lambda en: en.tensor_mul(
                    ytb[b][:, 256:512], accp[:, 256:512], LpT[0][...]))(b), wait=[s3])
                s6 = p.op("v", (lambda b: lambda en: en.tensor_add(
                    ytb[b][:, 0:256], ytb[b][:, 0:256], ytb[b][:, 256:512]))(b),
                    wait=[s4, s5])
                sI = p.op("v", (lambda b: lambda en: en.tensor_copy(
                    stw[:, 256:512], ytb[b][:, 0:256]))(b), wait=[s6])
                G["accp"] = [s4, s5, sR]
                G[f"accb{b}"] = [ta1]
                G[f"ytb{b}"] = [sR, sI, s6]
                my_tr = []
                for ri in range(2):
                    for tile in range(2):
                        tt = p.op("t", (lambda ri, tile: lambda en: en.transpose(
                            ps_tr[...].bitcast(BF16)[:, 0:128],
                            stw[:, ri * 256 + tile * 128:ri * 256 + (tile + 1) * 128],
                            idb[...]))(ri, tile),
                            wait=[sR, sI, w_in[6]] + gw("ps_tr"))
                        tv = p.op("s", (lambda ri, tile, c: lambda en: en.activation(
                            stT[ri][:, tile * NT + 128 * c:tile * NT + 128 * (c + 1)],
                            ps_tr[...].bitcast(BF16)[:, 0:128], AF.Identity))(ri, tile, c),
                            wait=[tt])
                        G["ps_tr"] = [tv]
                        my_tr.append(tv)
                st_t += my_tr
                G["stw"] = my_tr

            p.phase_barrier()
            for ch in range(4):
                b = ch % 2
                for m in range(8):
                    last = None
                    for ri in range(2):
                        for tile in range(2):
                            csb = crb if ri == 0 else cib
                            wl = [last] if last else gw("ps_y")
                            last = p.op("t", (lambda m, ri, tile, ch, csb: lambda en: en.matmul(
                                ps_y[:, 0:512],
                                csb[:, tile * D + 128 * m:tile * D + 128 * (m + 1)],
                                stT[ri][:, tile * NT + TCH * ch:tile * NT + TCH * (ch + 1)],
                                start=(ri == 0 and tile == 0), stop=False))(m, ri, tile, ch, csb),
                                wait=wl)
                    for kk in range(8):
                        last = p.op("t", (lambda m, kk, ch: lambda en: en.matmul(
                            ps_y[:, 0:512],
                            dmb[:, kk * D + 128 * m:kk * D + 128 * (m + 1)],
                            h3f[:, kk * NT + TCH * ch:kk * NT + TCH * (ch + 1)],
                            start=False, stop=(kk == 7)))(m, kk, ch),
                            wait=[last])
                    tev = p.op("v", (lambda b: lambda en: en.tensor_copy(
                        ytb[b][...], ps_y[:, 0:512]))(b), wait=[last] + gw(f"ytb{b}"))
                    G["ps_y"] = [tev]
                    tst = p.op("y", (lambda m, ch, b: lambda en: en.dma_start(
                        y_d[128 * m:128 * (m + 1), TCH * ch:TCH * (ch + 1)],
                        ytb[b][...]))(m, ch, b), wait=[tev], dma=True, stream=f"ys{b}")
                    G[f"ytb{b}"] = [tst]

            p.op("v", lambda en: en.engine_nop(), wait=p.all_done())
            ctx_emit(p, est)
        return nc


    def build_lru2b():
        nc = ctx_nc()
        y_di = nc.dram_tensor("yt", [D, NT], F32, kind="ExternalInput")
        w1_d = nc.dram_tensor("W1", [D, 4096], F32, kind="ExternalInput")
        w2_d = nc.dram_tensor("W2", [4096, D], F32, kind="ExternalInput")
        x_d = nc.dram_tensor("x3", [D, NT], F32, kind="ExternalInput")
        h3_d = nc.dram_tensor("h3o", [D, NT], F32, kind="ExternalInput")
        rs_d = nc.dram_tensor("rescol", [128, 1], F32, kind="ExternalInput")
        o_d = nc.dram_tensor("x4p", [D, NT], F32, kind="ExternalOutput")

        NMB = 32
        TC2 = 1024
        C2 = 0.044715
        C1 = 0.7978845608028654

        with ExitStack() as est:
            e = est.enter_context
            yb = [e(nc.sbuf_tensor(f"yb{i}", [128, TC2], BF16)) for i in range(8)]
            hbuf = e(nc.sbuf_tensor("hbuf", [128, NMB * TC2], BF16))
            hb_s = e(nc.sbuf_tensor("hb_s", [128, TC2], BF16))
            hs_s = e(nc.sbuf_tensor("hs_s", [128, TC2], BF16))
            arg_s = e(nc.sbuf_tensor("arg_s", [128, TC2], BF16))
            th_s = e(nc.sbuf_tensor("th_s", [128, TC2], BF16))
            w1b = [e(nc.sbuf_tensor(f"w1b{k}", [128, 8 * 128], BF16)) for k in range(2)]
            w2b = [e(nc.sbuf_tensor(f"w2b{k}", [128, NMB * 128], BF16)) for k in range(2)]
            rsc = e(nc.sbuf_tensor("rsc", [128, 1], F32))
            sigc = e(nc.sbuf_tensor("sigc", [128, 1], F32))
            xrb = [e(nc.sbuf_tensor(f"xrb{k}", [128, TC2], F32)) for k in range(2)]
            hrb = [e(nc.sbuf_tensor(f"hrb{k}", [128, TC2], F32)) for k in range(2)]
            ob = [e(nc.sbuf_tensor(f"ob{k}", [128, TC2], F32)) for k in range(2)]
            ps_A = e(nc.psum_tensor("ps_A", [128, TC2], F32))
            ps_B = e(nc.psum_tensor("ps_B", [128, TC2], F32))

            p = ctx_prog(nc)
            G = {}

            def gw(n):
                return list(G.get(n, []))

            w_rs = p.op("y", lambda en: en.dma_start(rsc[...], rs_d[...]),
                        dma=True, stream="sm")
            t_sg = p.op("s", lambda en: en.activation(
                sigc[...], rsc[...], AF.Sigmoid), wait=[w_rs])
            t_sg2 = p.op("v", lambda en: en.tensor_scalar_mul(
                sigc[...], sigc[...], 0.5), wait=[t_sg])

            for ci in range(NT // TC2):
                c0 = ci * TC2
                t_y = []
                for j in range(8):
                    t_y.append(p.op("g", (lambda j, c0: lambda en: en.dma_start(
                        yb[j][...], y_di[128 * j:128 * (j + 1), c0:c0 + TC2]))(j, c0),
                        wait=gw(f"yb{j}"), dma=True, stream=f"yl{j}"))
                t_h = {}
                for m in range(NMB):
                    b = m % 2
                    wwait = [t_h[m - 2]] if m >= 2 else gw("w1pv")
                    t_w1 = p.op("g", (lambda m, b: lambda en: en.dma_start(
                        w1b[b][...].rearrange("p (a w) -> p a w", a=8),
                        w1_d[:, 128 * m:128 * (m + 1)].rearrange("(a p) w -> p a w", p=128)))(m, b),
                        wait=wwait, dma=True, stream=f"w1p{b}")
                    psw = [t_h[m - 1]] if m >= 1 else gw("psA_prev")
                    last = None
                    for q in range(TC2 // 512):
                        for kk in range(8):
                            wl = [t_w1, t_y[kk]] + psw + ([last] if last else [])
                            last = p.op("t", (lambda b, q, kk: lambda en: en.matmul(
                                ps_A[:, q * 512:(q + 1) * 512],
                                w1b[b][:, kk * 128:(kk + 1) * 128],
                                yb[kk][:, q * 512:(q + 1) * 512],
                                start=(kk == 0), stop=(kk == 7)))(b, q, kk), wait=wl)
                    te1 = p.op("s", (lambda: lambda en: en.activation(
                        hb_s[...], ps_A[...], AF.Identity))(), wait=[last] + gw("hb_s"))
                    te2 = p.op("s", (lambda: lambda en: en.activation(
                        hs_s[...], ps_A[...], AF.Square))(), wait=[last, te1])
                    tv1 = p.op("v", (lambda: lambda en: en.tensor_mul(
                        arg_s[...], hs_s[...], hb_s[...]))(), wait=[te1, te2] + gw("arg_s"))
                    tv2 = p.op("v", (lambda: lambda en: en.scalar_tensor_tensor(
                        arg_s[...], arg_s[...], C2, hb_s[...], ALU.mult, ALU.add))(),
                        wait=[tv1])
                    te3 = p.op("s", (lambda: lambda en: en.activation(
                        th_s[...], arg_s[...], AF.Tanh, scale=C1))(), wait=[tv2] + gw("th_s"))
                    tv3 = p.op("v", (lambda: lambda en: en.tensor_scalar_add(
                        th_s[...], th_s[...], 1.0))(), wait=[te3])
                    t_h[m] = p.op("v", (lambda m: lambda en: en.tensor_mul(
                        hbuf[:, m * TC2:(m + 1) * TC2], hb_s[...], th_s[...]))(m),
                        wait=[tv3, te1] + gw(f"hbuf{m}"))
                    G["hb_s"] = [t_h[m]]
                    G["arg_s"] = [tv2]
                    G["th_s"] = [t_h[m]]
                G["w1pv"] = [t_h[NMB - 2], t_h[NMB - 1]]
                G["psA_prev"] = [t_h[NMB - 1]]

                for n in range(8):
                    b = n % 2
                    wwait = gw(f"w2rd{b}")
                    t_w2 = p.op("g", (lambda n, b: lambda en: en.dma_start(
                        w2b[b][...].rearrange("p (a w) -> p a w", a=NMB),
                        w2_d[:, 128 * n:128 * (n + 1)].rearrange("(a p) w -> p a w", p=128)))(n, b),
                        wait=wwait, dma=True, stream=f"w2p{b}")
                    t_xr = p.op("y", (lambda n, c0, b: lambda en: en.dma_start(
                        xrb[b][...], x_d[128 * n:128 * (n + 1), c0:c0 + TC2]))(n, c0, b),
                        wait=gw(f"xrb{b}"), dma=True, stream=f"xr{b}")
                    t_hr = p.op("y", (lambda n, c0, b: lambda en: en.dma_start(
                        hrb[b][...], h3_d[128 * n:128 * (n + 1), c0:c0 + TC2]))(n, c0, b),
                        wait=gw(f"hrb{b}"), dma=True, stream=f"hr{b}")
                    last = None
                    rd = []
                    for q in range(TC2 // 512):
                        for m in range(NMB):
                            wl = [t_w2, t_h[m]] + ([last] if last else [])
                            if n >= 1 and q == 0 and m == 0:
                                wl += gw("psB_rd")
                            last = p.op("t", (lambda b, q, m: lambda en: en.matmul(
                                ps_B[:, q * 512:(q + 1) * 512],
                                w2b[b][:, m * 128:(m + 1) * 128],
                                hbuf[:, m * TC2 + q * 512:m * TC2 + (q + 1) * 512],
                                start=(m == 0), stop=(m == NMB - 1)))(b, q, m), wait=wl)
                    G[f"w2rd{b}"] = [last]
                    tz = p.op("v", (lambda b: lambda en: en.tensor_scalar_mul(
                        ob[b][...], ps_B[...], sigc[...]))(b),
                        wait=[last, t_sg2] + gw(f"ob{b}"))
                    G["psB_rd"] = [tz]
                    ta1 = p.op("v", (lambda b: lambda en: en.tensor_add(
                        ob[b][...], ob[b][...], xrb[b][...]))(b), wait=[tz, t_xr])
                    ta2 = p.op("v", (lambda b: lambda en: en.tensor_add(
                        ob[b][...], ob[b][...], hrb[b][...]))(b), wait=[ta1, t_hr])
                    G[f"xrb{b}"] = [ta1]
                    G[f"hrb{b}"] = [ta2]
                    tst = p.op("y", (lambda n, c0, b: lambda en: en.dma_start(
                        o_d[128 * n:128 * (n + 1), c0:c0 + TC2], ob[b][...]))(n, c0, b),
                        wait=[ta2], dma=True, stream=f"os{b}")
                    G[f"ob{b}"] = [tst]
                for j in range(8):
                    G[f"yb{j}"] = [t_h[NMB - 1]]
                for m in range(NMB):
                    G[f"hbuf{m}"] = [G["psB_rd"][0]]

            p.op("v", lambda en: en.engine_nop(), wait=p.all_done())
            ctx_emit(p, est)
        return nc
    return build_lru1, build_lru2a, build_lru2b, _ln_alloc, _ln_init, _ln_chunk

build_lru1, build_lru2a, build_lru2b, _ln_alloc, _ln_init, _ln_chunk = _ns_lru()

def _ns_proj():
    D = 1024
    NT = 2048
    TCH = 512


    def build_proj():
        nc = ctx_nc()
        x_d = nc.dram_tensor("x0", [D, NT], F32, kind="ExternalInput")
        a_d = nc.dram_tensor("aog", [D, NT], F32, kind="ExternalInput")
        w_d = nc.dram_tensor("Wp", [D, D], F32, kind="ExternalInput")
        o_d = nc.dram_tensor("x1", [D, NT], F32, kind="ExternalOutput")

        with ExitStack() as est:
            e = est.enter_context
            wpb = e(nc.sbuf_tensor("wpb", [128, 8 * D], BF16))
            ab = [e(nc.sbuf_tensor(f"ab{k}", [128, 8 * TCH], BF16)) for k in range(2)]
            xb = [e(nc.sbuf_tensor(f"xb{k}", [128, TCH], F32)) for k in range(2)]
            ob = [e(nc.sbuf_tensor(f"ob{k}", [128, TCH], F32)) for k in range(2)]
            ps = e(nc.psum_tensor("ps_p", [128, TCH], F32))

            p = ctx_prog(nc)
            G = {}

            def gw(n):
                return list(G.get(n, []))

            w_w = p.op("g", lambda en: en.dma_start(
                wpb[...].rearrange("p (a w) -> p a w", a=8),
                w_d[...].rearrange("(a p) w -> p a w", p=128)), dma=True, stream="wp")
            for ch in range(4):
                b = ch % 2
                t_a = p.op("g", (lambda ch, b: lambda en: en.dma_start(
                    ab[b][...].rearrange("p (a w) -> p a w", a=8),
                    a_d[:, TCH * ch:TCH * (ch + 1)].rearrange("(a p) w -> p a w", p=128)))(ch, b),
                    wait=gw(f"ab{b}"), dma=True, stream=f"al{b}")
                rds = []
                for m in range(8):
                    mb = m % 2
                    t_x = p.op("y", (lambda ch, m, mb: lambda en: en.dma_start(
                        xb[mb][...], x_d[128 * m:128 * (m + 1), TCH * ch:TCH * (ch + 1)]))(ch, m, mb),
                        wait=gw(f"xb{mb}"), dma=True, stream=f"xl{mb}")
                    last = None
                    for kk in range(8):
                        wl = [t_a, w_w] + ([last] if last else [])
                        if kk == 0:
                            wl += gw("ps")
                        last = p.op("t", (lambda b, m, kk: lambda en: en.matmul(
                            ps[...], wpb[:, kk * D + 128 * m:kk * D + 128 * (m + 1)],
                            ab[b][:, kk * TCH:(kk + 1) * TCH],
                            start=(kk == 0), stop=(kk == 7)))(b, m, kk), wait=wl)
                    rds.append(last)
                    tad = p.op("v", (lambda mb: lambda en: en.tensor_add(
                        ob[mb][...], ps[...], xb[mb][...]))(mb),
                        wait=[last, t_x] + gw(f"ob{mb}"))
                    G["ps"] = [tad]
                    G[f"xb{mb}"] = [tad]
                    tst = p.op("y", (lambda ch, m, mb: lambda en: en.dma_start(
                        o_d[128 * m:128 * (m + 1), TCH * ch:TCH * (ch + 1)],
                        ob[mb][...]))(ch, m, mb), wait=[tad], dma=True, stream=f"os{mb}")
                    G[f"ob{mb}"] = [tst]
                G[f"ab{b}"] = rds

            p.op("v", lambda en: en.engine_nop(), wait=p.all_done())
            ctx_emit(p, est)
        return nc


    def build_addln():
        nc = ctx_nc()
        xa_d = nc.dram_tensor("xa", [D, NT], F32, kind="ExternalInput")
        xb_d = nc.dram_tensor("xb", [D, NT], F32, kind="ExternalInput")
        s_d = nc.dram_tensor("s", [D, NT], F32, kind="ExternalOutput")
        l_d = nc.dram_tensor("sln", [D, NT], F32, kind="ExternalOutput")

        with ExitStack() as est:
            e = est.enter_context
            xab = [e(nc.sbuf_tensor(f"xab{k}", [128, 8 * TCH], F32)) for k in range(2)]
            xbb = [e(nc.sbuf_tensor(f"xbb{k}", [128, 8 * TCH], F32)) for k in range(2)]
            snb = [e(nc.sbuf_tensor(f"snb{k}", [128, 8 * TCH], F32)) for k in range(2)]
            lnb = [e(nc.sbuf_tensor(f"lnb{k}", [128, 8 * TCH], F32)) for k in range(2)]
            lnt = _ln_alloc(nc, e)

            p = ctx_prog(nc)
            G = {}

            def gw(n):
                return list(G.get(n, []))

            cons = _ln_init(p, lnt)
            for ch in range(4):
                b = ch % 2
                t_a = p.op("y", (lambda ch, b: lambda en: en.dma_start(
                    xab[b][...].rearrange("p (a w) -> p a w", a=8),
                    xa_d[:, TCH * ch:TCH * (ch + 1)].rearrange("(a p) w -> p a w", p=128)))(ch, b),
                    wait=gw(f"xab{b}"), dma=True, stream=f"la{b}")
                t_b = p.op("g", (lambda ch, b: lambda en: en.dma_start(
                    xbb[b][...].rearrange("p (a w) -> p a w", a=8),
                    xb_d[:, TCH * ch:TCH * (ch + 1)].rearrange("(a p) w -> p a w", p=128)))(ch, b),
                    wait=gw(f"xbb{b}"), dma=True, stream=f"lb{b}")
                t_s = []
                t_st = []
                for j in range(8):
                    ts = p.op("v", (lambda b, j: lambda en: en.tensor_add(
                        snb[b][:, j * TCH:(j + 1) * TCH], xab[b][:, j * TCH:(j + 1) * TCH],
                        xbb[b][:, j * TCH:(j + 1) * TCH]))(b, j),
                        wait=[t_a, t_b] + gw(f"snb{b}{j}"))
                    t_s.append(ts)
                    t_st.append(p.op("y", (lambda b, j, ch: lambda en: en.dma_start(
                        s_d[128 * j:128 * (j + 1), TCH * ch:TCH * (ch + 1)],
                        snb[b][:, j * TCH:(j + 1) * TCH]))(b, j, ch),
                        wait=[ts], dma=True, stream=f"ss{b}"))
                G[f"xab{b}"] = t_s
                G[f"xbb{b}"] = t_s
                G["lh3"] = gw(f"lnb{b}")
                t_h = _ln_chunk(
                    p, G, gw, lnt, cons + t_s,
                    (lambda b: lambda j: snb[b][:, j * TCH:(j + 1) * TCH])(b),
                    (lambda b: lambda j: lnb[b][:, j * TCH:(j + 1) * TCH])(b))
                G["lh3"] = []
                t_l = []
                for j in range(8):
                    tl = p.op("g", (lambda b, j, ch: lambda en: en.dma_start(
                        l_d[128 * j:128 * (j + 1), TCH * ch:TCH * (ch + 1)],
                        lnb[b][:, j * TCH:(j + 1) * TCH]))(b, j, ch),
                        wait=[t_h[j]], dma=True, stream=f"ls{b}")
                    t_l.append(tl)
                for j in range(8):
                    G[f"snb{b}{j}"] = [t_h[j], t_st[-1]]
                G[f"lnb{b}"] = t_l
                G[f"xab{b}"] = t_h + [G["lpsr"][0]]
                G[f"xbb{b}"] = G[f"xab{b}"]

            p.op("v", lambda en: en.engine_nop(), wait=p.all_done())
            ctx_emit(p, est)
        return nc
    return build_proj, build_addln

build_proj, build_addln = _ns_proj()

_PROGS = {}


def _prog(name, fn):
    if name not in _PROGS:
        _PROGS[name] = fn()
    return _PROGS[name]


_CORES = list(range(8))


def _gather(res, name):
    return [np.concatenate([res.results[4 * b + j][name] for j in range(4)], 1)
            for b in range(2)]


def kernel(**inputs):
    X = np.asarray(inputs["L0_attn_a_x"], np.float32)
    Wcd = np.ascontiguousarray(np.asarray(inputs["L0_attn_a_Wc_down"], np.float32))
    Wqu = np.asarray(inputs["L0_attn_a_Wq_up"], np.float32)
    Wkv = np.ascontiguousarray(np.asarray(inputs["L0_attn_a_Wkv_up"], np.float32))
    Wcp = np.ascontiguousarray(np.asarray(inputs["L0_attn_a_Wc_proj"], np.float32))
    nu_log = np.asarray(inputs["L0_attn_b_nu_log"], np.float32)
    theta_log = np.asarray(inputs["L0_attn_b_theta_log"], np.float32)
    B_re = np.asarray(inputs["L0_attn_b_B_re"], np.float32)
    B_im = np.asarray(inputs["L0_attn_b_B_im"], np.float32)
    C_re = np.asarray(inputs["L0_attn_b_C_re"], np.float32)
    C_im = np.asarray(inputs["L0_attn_b_C_im"], np.float32)
    D_mat = np.asarray(inputs["L0_attn_b_D_mat"], np.float32)
    W1b_ = np.ascontiguousarray(np.asarray(inputs["L0_attn_b_W1"], np.float32))
    W2b_ = np.ascontiguousarray(np.asarray(inputs["L0_attn_b_W2"], np.float32))
    res_logit = np.float32(np.asarray(inputs["L0_attn_b_res_logit"]).reshape(()))
    Wqkv = np.asarray(inputs["L0_ffn_Wqkv"], np.float32)
    q_scale = np.asarray(inputs["L0_ffn_q_scale"], np.float32)
    k_scale = np.asarray(inputs["L0_ffn_k_scale"], np.float32)
    Wo = np.asarray(inputs["L0_ffn_Wo"], np.float32)
    ln_g = np.asarray(inputs["L0_ffn_ln_g"], np.float32)
    ln_b = np.asarray(inputs["L0_ffn_ln_b"], np.float32)
    W1f = np.asarray(inputs["L0_ffn_W1"], np.float32)
    b1f = np.asarray(inputs["L0_ffn_b1"], np.float32)
    W2f = np.asarray(inputs["L0_ffn_W2"], np.float32)
    b2f = np.asarray(inputs["L0_ffn_b2"], np.float32)

    Xcm = [np.ascontiguousarray(X[b].T) for b in range(2)]

    nc = _prog("attna", build_attna)
    ins = []
    for c in _CORES:
        b, j = c // 4, c % 4
        ins.append({"x0": Xcm[b], "Wc_down": Wcd,
                    "Wq_sl": np.ascontiguousarray(Wqu[:, 256 * j:256 * (j + 1)]),
                    "Wkv": Wkv})
    res = run_bass_kernel_spmd(nc, ins, core_ids=_CORES)
    aog = [np.concatenate([res.results[4 * b + j]["ao"] for j in range(4)], 0)
           for b in range(2)]

    nc = _prog("proj", build_proj)
    ins = []
    for c in _CORES:
        b, j = c // 4, c % 4
        sl = slice(2048 * j, 2048 * (j + 1))
        ins.append({"x0": np.ascontiguousarray(Xcm[b][:, sl]),
                    "aog": np.ascontiguousarray(aog[b][:, sl]), "Wp": Wcp})
    res = run_bass_kernel_spmd(nc, ins, core_ids=_CORES)
    X1 = _gather(res, "x1")

    nc_addln = _prog("addln", build_addln)

    def addln(A, Bm):
        ins = []
        for c in _CORES:
            b, j = c // 4, c % 4
            sl = slice(2048 * j, 2048 * (j + 1))
            xb = (np.zeros((1024, 2048), np.float32) if Bm is None
                  else np.ascontiguousarray(Bm[b][:, sl]))
            ins.append({"xa": np.ascontiguousarray(A[b][:, sl]), "xb": xb})
        res = run_bass_kernel_spmd(nc_addln, ins, core_ids=_CORES)
        return _gather(res, "s"), _gather(res, "sln")

    nc_attn = _prog("attn", build_ffn_attn)
    nc_ffn = _prog("ffn", build_ffn_ffn)

    def ffn_block(Y):
        for i in range(6):
            ins = []
            for c in _CORES:
                b, j = c // 4, c % 4
                pos0 = 2048 * j
                own = np.ascontiguousarray(Y[b][:, pos0:pos0 + 2048])
                halo = (np.zeros((1024, 257), np.float32) if j == 0
                        else np.ascontiguousarray(Y[b][:, pos0 - 257:pos0]))
                ins.append(attn_inputs(own, halo, Wqkv[i], q_scale[i],
                                       k_scale[i], Wo[i], pos0, j == 0))
            res = run_bass_kernel_spmd(nc_attn, ins, core_ids=_CORES)
            Y = _gather(res, "y")
            ins = []
            for c in _CORES:
                b, j = c // 4, c % 4
                pos0 = 2048 * j
                own = np.ascontiguousarray(Y[b][:, pos0:pos0 + 2048])
                halo = (np.zeros(1024, np.float32) if j == 0
                        else np.ascontiguousarray(Y[b][:, pos0 - 1]))
                ins.append(ffn_inputs(own, halo, W1f[i], b1f[i], W2f[i],
                                      b2f[i], ln_g[i], ln_b[i]))
            res = run_bass_kernel_spmd(nc_ffn, ins, core_ids=_CORES)
            Y = _gather(res, "y")
        return Y

    _, Y0 = addln(X1, None)
    Y6 = ffn_block(Y0)
    X2, _ = addln(X1, Y6)

    nu2 = np.ascontiguousarray(nu_log.reshape(2, 128).T)
    th2 = np.ascontiguousarray(theta_log.reshape(2, 128).T)
    BreT = np.ascontiguousarray(B_re.T)
    BimT = np.ascontiguousarray(B_im.T)
    nc = _prog("lru1", build_lru1)
    ins = []
    for c in _CORES:
        b, j = c // 4, c % 4
        sl = slice(2048 * j, 2048 * (j + 1))
        ins.append({"x3": np.ascontiguousarray(X2[b][:, sl]),
                    "nu2": nu2, "th2": th2, "BreT": BreT, "BimT": BimT})
    res = run_bass_kernel_spmd(nc, ins, core_ids=_CORES)
    accs = [np.asarray(res.results[c]["acc"]) for c in _CORES]
    acl = []
    for b in range(2):
        Lall = np.concatenate([accs[4 * b + j][127::128] for j in range(4)], 0)
        aR = np.ascontiguousarray(
            Lall[:, 0:256].T.reshape(2, 128, 64).transpose(1, 0, 2).reshape(128, 128))
        aI = np.ascontiguousarray(
            Lall[:, 256:512].T.reshape(2, 128, 64).transpose(1, 0, 2).reshape(128, 128))
        acl.append((aR, aI))
    nc = _prog("lru2a", build_lru2a)
    ins = []
    for c in _CORES:
        b, j = c // 4, c % 4
        sl = slice(2048 * j, 2048 * (j + 1))
        ssel = np.zeros((64, 16), np.float32)
        for g in range(16):
            ssel[16 * j + g, g] = 1.0
        ins.append({"x3": np.ascontiguousarray(X2[b][:, sl]), "acc": accs[c],
                    "aclTR": acl[b][0], "aclTI": acl[b][1], "Ssel": ssel,
                    "nu2": nu2, "th2": th2,
                    "CreT": np.ascontiguousarray(C_re.T),
                    "CimT": np.ascontiguousarray(C_im.T),
                    "Dmat": np.ascontiguousarray(D_mat.T)})
    res = run_bass_kernel_spmd(nc, ins, core_ids=_CORES)
    yts = [np.asarray(res.results[c]["yt"]) for c in _CORES]
    h3s = [np.asarray(res.results[c]["h3o"]) for c in _CORES]
    nc = _prog("lru2b", build_lru2b)
    ins = []
    for c in _CORES:
        b, j = c // 4, c % 4
        sl = slice(2048 * j, 2048 * (j + 1))
        ins.append({"yt": yts[c], "W1": W1b_, "W2": W2b_,
                    "x3": np.ascontiguousarray(X2[b][:, sl]), "h3o": h3s[c],
                    "rescol": np.full((128, 1), res_logit, np.float32)})
    res = run_bass_kernel_spmd(nc, ins, core_ids=_CORES)
    X4 = _gather(res, "x4p")

    _, Y0b = addln(X4, None)
    Y6b = ffn_block(Y0b)
    X5, _ = addln(X4, Y6b)

    return np.ascontiguousarray(
        np.stack([X5[b].T for b in range(2)], 0).astype(np.float32))
